# version 05
# baseline (speedup 1.0000x reference)
import math
import contextlib
import numpy as np
import ml_dtypes
import concourse.bass as bass
import concourse.mybir as mybir
from concourse.bass_utils import run_bass_kernel_spmd

F32 = mybir.dt.float32
BF16 = mybir.dt.bfloat16
F16 = mybir.dt.float16
AF = mybir.ActivationFunctionType
ALU = mybir.AluOpType
AX = mybir.AxisListType

B, L, D, NH, dh = 4, 8192, 1024, 16, 64
HALO = 512
TLOC = 4096 + HALO
NT = TLOC // 128
C = 128
NCH = 4096 // C
WINDOW = 256
QK_SCALE = 8.0
DIM_INNER = 2730
NLAYER = 6
LN_EPS = 1e-5

DEBUG = {}


def _host_consts(core_id):
    half = core_id % 2
    gstart = 4096 * half - HALO
    c = {}
    c["ident128"] = np.eye(128, dtype=np.float32)
    Rm = np.zeros((128, 128), np.float32)
    for hb in range(2):
        o = 64 * hb
        for p in range(32):
            Rm[o + p + 32, o + p] = -1.0
            Rm[o + p, o + p + 32] = 1.0
    c["rotmat"] = Rm
    halfv = np.arange(0, dh, 2, dtype=np.float64)
    inv_freq = 1.0 / 10000.0 ** (halfv / dh)
    t = np.arange(gstart, gstart + TLOC, dtype=np.float64)
    freqs = t[:, None] * inv_freq[None, :]
    freqs = np.concatenate([freqs, freqs], -1)
    base_scale = (halfv + 0.4 * dh) / (1.4 * dh)
    power = (t - L // 2) / (WINDOW // 2)
    scale = base_scale[None, :] ** power[:, None]
    scale = np.concatenate([scale, scale], -1)
    cos, sin = np.cos(freqs), np.sin(freqs)
    for nm, tab in [("rotAQ", cos * scale), ("rotBQ", sin * scale),
                    ("rotAK", cos / scale), ("rotBK", sin / scale)]:
        tabf = np.asarray(tab.T, np.float32)
        c[nm] = np.concatenate([tabf, tabf], axis=0)
    r = np.arange(128)[:, None]
    cc_ = np.arange(384)[None, :]
    def mkmask(lo):
        allowed = (cc_ >= r) & (cc_ <= r + 256)
        if lo is not None:
            allowed &= (cc_ >= lo)
            full_row_dead = (~allowed).all(axis=1, keepdims=True)
            allowed = np.where(full_row_dead, (cc_ >= r) & (cc_ <= r + 256), allowed)
        return np.where(allowed, 0.0, -1e30).astype(np.float32)
    masks = [mkmask(None)]
    for qt in (4, 5):
        lo = 256 - (gstart + 128 * qt)
        masks.append(mkmask(lo if lo > 0 else None))
    mm = np.stack(masks)
    half_cat = np.concatenate([mm[:, 0:64, :], mm[:, 64:128, :]], axis=2)
    c["attn_masks"] = np.concatenate([half_cat, half_cat], axis=1)
    c["sel"] = np.full((128, 1), 1.0 if half == 1 else 0.0, np.float32)
    tl = np.arange(C, dtype=np.float32)
    tim = tl[:, None] - tl[None, :]
    c["d_strict"] = np.where(tim > 0, tim, 0.0).astype(np.float32)
    c["m_strict"] = (tim > 0).astype(np.float32)
    c["d_strictT"] = np.ascontiguousarray(c["d_strict"].T)
    c["m_strictT"] = np.ascontiguousarray(c["m_strict"].T)
    c["d_inclT"] = np.where(tim >= 0, tim, 0.0).astype(np.float32).T.copy()
    c["m_inclT"] = (tim >= 0).astype(np.float32).T.copy()
    c["tcol"] = (tl + 1.0).reshape(128, 1).astype(np.float32)
    c["tbarcol"] = (C - (tl + 1.0)).reshape(128, 1).astype(np.float32)
    i64p = np.zeros((128, 64), np.float32)
    i64p[0:64] = np.eye(64)
    i64p[64:128] = np.eye(64)
    c["i64pair"] = i64p
    c["ones_row"] = np.ones((1, 128), np.float32)
    co = np.zeros((8, 128, 16), np.float32)
    for j in range(8):
        co[j, 0:64, 2 * j] = 1.0
        co[j, 64:128, 2 * j + 1] = 1.0
    c["chunk_ones"] = co
    c["chunk_onesT"] = np.ascontiguousarray(co.transpose(0, 2, 1))
    c["ident64h"] = np.concatenate([np.eye(64, dtype=np.float32)] * 2, axis=0)
    return c


ENGINES = ("sync", "gpsimd", "tensor", "scalar", "vector")


class Sem:
    __slots__ = ("h", "v", "name")

    def __init__(self, handle, name):
        self.h = handle
        self.v = 0
        self.name = name

    def inc(self, n=1):
        self.v += n
        return self.v


class Prog:
    def __init__(self, nc):
        self.nc = nc
        self.ops = {e: [] for e in ENGINES}

    def add(self, engine, fn):
        self.ops[engine].append(fn)

    def wait(self, engine, sem, val):
        self.add(engine, lambda e, s=sem.h, v=val: e.wait_ge(s, v))

    def play(self, blk):
        for eng in ENGINES:
            ops = self.ops[eng]
            if not ops:
                continue
            def body(e, ops=ops):
                for fn in ops:
                    fn(e)
            getattr(blk, eng)(body)


class Bank:

    def __init__(self, prog, tensor_handle, pe_sem, ev_sem):
        self.prog = prog
        self.t = tensor_handle
        self.pe = pe_sem
        self.ev = ev_sem
        self.n_cons_expected = 0

    def produce(self, mm_fns, n_consumers):
        p = self.prog
        if self.n_cons_expected:
            p.wait("tensor", self.ev, self.ev.v)
        for fn in mm_fns[:-1]:
            p.add("tensor", fn)
        tgt = self.pe.inc()
        p.add("tensor", lambda e, fn=mm_fns[-1], s=self.pe.h: fn(e).then_inc(s, 1))
        self.n_cons_expected = n_consumers
        return tgt

    def consume(self, engine, fn, pe_tgt):
        p = self.prog
        p.wait(engine, self.pe, pe_tgt)
        p.add(engine, lambda e, fn=fn, s=self.ev.h: fn(e).then_inc(s, 1))
        self.ev.inc()
        self.n_cons_expected -= 1


def ceil_div(a, b):
    return (a + b - 1) // b


class Build:

    def __init__(self, with_transformer=True, n_layers=NLAYER, dbg=()):
        self.nc = bass.Bass("TRN2", target_bir_lowering=False)
        self.prog = Prog(self.nc)
        self.es = contextlib.ExitStack()
        self.es_dn = contextlib.ExitStack()
        self.es_tf = contextlib.ExitStack()
        self.with_transformer = with_transformer
        self.n_layers = n_layers
        self.dbg_names = set(dbg)
        self.dram = {}
        self.sb = {}
        self.psum = {}
        self.sems = {}

    def din(self, name, shape, dtype=F32):
        t = self.nc.dram_tensor(name, list(shape), dtype, kind="ExternalInput").ap()
        self.dram[name] = t
        return t

    def dout(self, name, shape, dtype=F32):
        t = self.nc.dram_tensor(name, list(shape), dtype, kind="ExternalOutput").ap()
        self.dram[name] = t
        return t

    def dint(self, name, shape, dtype=F32):
        t = self.nc.dram_tensor(name, list(shape), dtype, kind="Internal").ap()
        self.dram[name] = t
        return t

    def sbuf(self, name, shape, dtype=F32, scope=None):
        es = scope if scope is not None else self.es
        t = es.enter_context(self.nc.sbuf_tensor(name, list(shape), dtype))
        self.sb[name] = t
        return t

    def psum_t(self, name, shape, dtype=F32):
        t = self.es.enter_context(self.nc.psum_tensor(name, list(shape), dtype))
        self.psum[name] = t
        return t

    def sem(self, name):
        s = Sem(self.es.enter_context(self.nc.semaphore(name)), name)
        self.sems[name] = s
        return s

    def dbg_store(self, name, src_fn, shape, dtype=F32):
        if name not in self.dbg_names:
            return
        o = self.dout("dbg_" + name, shape, dtype)
        d = self.d_sy
        tgt = d.inc(16)
        self.prog.add("sync", lambda e, o=o, f=src_fn, s=d.h: e.dma_start(o, f()).then_inc(s, 16))


def build_common(bld: "Build"):
    nc, p = bld.nc, bld.prog
    bld.din("x_in", [TLOC, D])
    bld.din("Wt_d", [D, D])
    bld.din("ln_gamma", [1, D]); bld.din("ln_beta", [1, D])
    bld.din("eta", [1, 1]); bld.din("lam_logit", [1, 1])
    bld.din("sel", [128, 1])
    for nm in ("ident128", "rotmat", "d_strict", "m_strict", "d_strictT", "m_strictT",
               "d_inclT", "m_inclT"):
        bld.din(nm, [128, 128])
    bld.din("i64pair", [128, 64])
    bld.din("tcol", [128, 1]); bld.din("tbarcol", [128, 1])
    bld.din("ones_row", [1, 128])
    bld.din("attn_masks", [3, 128, 768])
    bld.din("chunk_ones", [8, 128, 16])
    for nm in ("rotAQ", "rotBQ", "rotAK", "rotBK"):
        bld.din(nm, [128, TLOC])
    bld.din("Wqkv", [NLAYER, D, 3 * D]); bld.din("Wo", [NLAYER, D, D])
    bld.din("W1", [NLAYER, D, 2 * DIM_INNER]); bld.din("W2", [NLAYER, DIM_INNER, D])
    bld.din("b1cols", [NLAYER, 2, 22, 128, 1]); bld.din("b2cols", [NLAYER, 8, 128, 1])
    bld.din("ffn_g", [NLAYER, 8, 128, 1]); bld.din("ffn_b", [NLAYER, 8, 128, 1])
    bld.din("qk_scale", [NLAYER, 2, 128, 1])
    bld.din("chunk_onesT", [8, 16, 128]); bld.din("ident64h", [128, 64])
    bld.din("ones_col128", [128, 1])
    bld.dout("out", [4096, D])
    bld.dint("sc_static", [4096, D], BF16)
    bld.dint("sc_o0", [4096, D], BF16)
    bld.dint("sc_ofin", [4096, D], BF16)
    bld.dint("sc_T", [NCH, 128, 8 * 128], BF16)
    bld.dint("sc_P", [NCH, 128, 8 * 64])
    bld.dint("cc_m_in", [128, 1024])
    bld.dint("cc_m_out", [256, 1024])
    bld.dint("sc_s", [4096, D], BF16)
    bld.dint("sc_p", [36, 128, 16 * 384], F16)
    bld.dint("cc_h_in", [D, HALO], F16)
    bld.dint("cc_h_out", [2 * D, HALO], F16)
    bld.dint("sc_wq", [NLAYER, D, 3 * D], F16)
    bld.dint("sc_w1", [NLAYER, D, 2 * DIM_INNER], BF16)
    bld.dint("sc_w2", [NLAYER, DIM_INNER, D], BF16)

    bld.s_pe = bld.sem("s_pe")
    bld.s_ve = bld.sem("s_ve")
    bld.s_ac = bld.sem("s_ac")
    bld.s_gp = bld.sem("s_gp")
    bld.d_sy = bld.sem("d_sy")
    bld.d_sc = bld.sem("d_sc")
    bld.d_gp = bld.sem("d_gp")
    bld.s_cc = bld.sem("s_cc")
    bld.s_sy = bld.sem("s_sy")
    bld.s_sg = bld.sem("s_sg")
    bld.s_gq = bld.sem("s_gq")
    bld.d_w1p = [bld.sem("d_w10"), bld.sem("d_w11")]
    bld.d_w2p = [bld.sem("d_w20"), bld.sem("d_w21")]
    bld.d_qk = [bld.sem("d_qk0"), bld.sem("d_qk1")]
    bld.d_rt = bld.sem("d_rt")
    bld.d_wb = bld.sem("d_wb")
    bld.wb_pend = {}

    bld.pbt = bld.psum_t("pbt", [128, 1024], BF16)
    bld.pb = [bld.psum_t(f"pb{i}", [128, 512], F32) for i in range(7)]

    bld.h = bld.sbuf("h", [128, 8 * TLOC], F16)
    bld.cols = bld.sbuf("cols", [128, 32], F32)
    bld.x2 = bld.sbuf("x2", [128, D], F32)
    bld.ident_bf = bld.sbuf("ident_bf", [128, 128], BF16)
    bld.rot_bf = bld.sbuf("rot_bf", [128, 128], BF16)
    bld.dn_cols = bld.sbuf("dn_cols", [128, 16], F32)
    bld.ones_row_sb = bld.sbuf("ones_row_sb", [1, 128], BF16)
    bld.ones_row_f32 = bld.sbuf("ones_row_f32", [1, 128], F32)
    bld.xrow = [bld.sbuf(f"xrow{i}", [128, D], F32, scope=bld.es_dn) for i in range(2)]
    bld.xn_tok = [bld.sbuf(f"xn_tok{i}", [128, D], BF16, scope=bld.es_dn) for i in range(2)]
    bld.xa_tok = [bld.sbuf(f"xa_tok{i}", [128, D], BF16, scope=bld.es_dn) for i in range(2)]
    bld.xa_fm = [bld.sbuf(f"xa_fm{i}", [128, 8 * 128], BF16, scope=bld.es_dn) for i in range(2)]
    bld.st_tok = [bld.sbuf(f"st_tok{i}", [128, D], BF16, scope=bld.es_dn) for i in range(2)]
    bld.Wt_sb = bld.sbuf("Wt_sb", [128, 8 * D], BF16, scope=bld.es_dn)
    bld.Ls = bld.sbuf("Ls", [128, 128], BF16, scope=bld.es_dn)
    bld.LsT = bld.sbuf("LsT", [128, 128], BF16, scope=bld.es_dn)
    bld.LiT = bld.sbuf("LiT", [128, 128], BF16, scope=bld.es_dn)
    bld.i64_bf = bld.sbuf("i64_bf", [128, 64], F32, scope=bld.es_dn)
    bld.consts_f32 = bld.sbuf("consts_f32", [128, 8 * 128], F32, scope=bld.es_dn)
    bld.X_all = bld.sbuf("X_all", [128, 16 * 128], BF16, scope=bld.es_dn)
    bld.Na = bld.sbuf("Na", [128, 16 * 128], BF16, scope=bld.es_dn)
    bld.NTa = bld.sbuf("NTa", [128, 16 * 128], BF16, scope=bld.es_dn)
    bld.Nb = bld.sbuf("Nb", [128, 16 * 128], BF16, scope=bld.es_dn)
    bld.NTb = bld.sbuf("NTb", [128, 16 * 128], BF16, scope=bld.es_dn)
    bld.Bt_all = bld.sbuf("Bt_all", [128, 16 * 128], BF16, scope=bld.es_dn)
    bld.LbK = bld.sbuf("LbK", [128, D], BF16, scope=bld.es_dn)
    bld.U_all = bld.sbuf("U_all", [128, D], BF16, scope=bld.es_dn)
    bld.Wfm = bld.sbuf("Wfm", [128, 8 * 128], BF16, scope=bld.es_dn)
    bld.Ttok = bld.sbuf("Ttok", [128, D], BF16, scope=bld.es_dn)
    bld.Tfm = bld.sbuf("Tfm", [128, 8 * 128], BF16, scope=bld.es_dn)
    bld.M_all = bld.sbuf("M_all", [128, 8 * 64], F32, scope=bld.es_dn)
    bld.MT_all = bld.sbuf("MT_all", [128, 8 * 64], F32, scope=bld.es_dn)
    bld.MTbf = bld.sbuf("MTbf", [128, 8 * 64], BF16, scope=bld.es_dn)
    bld.P_all = bld.sbuf("P_all", [128, 8 * 64], F32, scope=bld.es_dn)
    bld.PT_all = bld.sbuf("PT_all", [128, 8 * 64], F32, scope=bld.es_dn)
    bld.G_all = bld.sbuf("G_all", [128, 8 * 64], F32, scope=bld.es_dn)
    bld.o0_all = bld.sbuf("o0_all", [128, D], BF16, scope=bld.es_dn)
    bld.otmp = bld.sbuf("otmp", [128, 512], F32, scope=bld.es_dn)
    bld.Minit = bld.sbuf("Minit", [128, 8 * 64], F32, scope=bld.es_dn)
    bld.MinitT = bld.sbuf("MinitT", [128, 8 * 64], F32, scope=bld.es_dn)
    bld.MinitTbf = bld.sbuf("MinitTbf", [128, 8 * 64], BF16, scope=bld.es_dn)
    bld.Pld = bld.sbuf("Pld", [128, 8 * 64], F32, scope=bld.es_dn)
    bld.Tld = bld.sbuf("Tld", [128, 8 * 128], BF16, scope=bld.es_dn)
    bld.o0ld = bld.sbuf("o0ld", [128, D], BF16, scope=bld.es_dn)
    bld.dMbf = bld.sbuf("dMbf", [128, 8 * 64], BF16, scope=bld.es_dn)
    bld.gamma_b = bld.sbuf("gamma_b", [128, D], BF16, scope=bld.es_dn)
    bld.beta_b = bld.sbuf("beta_b", [128, D], BF16, scope=bld.es_dn)
    bld.grow = bld.sbuf("grow", [1, D], F32, scope=bld.es_dn)
    bld.brow = bld.sbuf("brow", [1, D], F32, scope=bld.es_dn)
    bld.yrow = bld.sbuf("yrow", [128, D], F32, scope=bld.es_dn)
    bld.st_ld = bld.sbuf("st_ld", [128, D], BF16, scope=bld.es_dn)
    bld.of_ld = bld.sbuf("of_ld", [128, D], BF16, scope=bld.es_dn)
    bld.s_bf = bld.sbuf("s_bf", [128, D], BF16, scope=bld.es_dn)
_ENG_SELF_SEM = {"vector": "s_ve", "scalar": "s_ac", "gpsimd": "s_gp"}


def E(bld, eng, fn, inc=None, n=1):
    ssn = _ENG_SELF_SEM.get(eng)
    if ssn is not None:
        sem = getattr(bld, ssn)
        if inc is None or inc is sem:
            W8(bld, eng, sem, sem.v)
            tgt = sem.inc(1)
            bld.prog.add(eng, lambda e, fn=fn, s=sem.h: fn(e).then_inc(s, 1))
            return tgt
    if inc is None:
        bld.prog.add(eng, fn)
        return None
    tgt = inc.inc(n)
    bld.prog.add(eng, lambda e, fn=fn, s=inc.h, n=n: fn(e).then_inc(s, n))
    return tgt


def W8(bld, eng, sem, val):
    if val is None or val <= 0:
        return
    bld.prog.wait(eng, sem, val)


def fence(bld, queue):
    dsem, msem = {"sync": (bld.d_sy, bld.s_sy), "scalar": (bld.d_sc, bld.s_sg),
                  "gpsimd": (bld.d_gp, bld.s_gq)}[queue]
    W8(bld, queue, dsem, dsem.v)
    tgt = msem.inc()
    bld.prog.add(queue, lambda e, s=msem.h: e.sem_inc(s, 1))
    return tgt


CL_LNLAM, CL_LAM, CL_LBAR, CL_BETA, CL_BLAM, CL_LAMC, CL_SEL, CL_NBETA = range(8)


def emit_setup(bld: Build):
    nc, p = bld.nc, bld.prog
    d = bld.dram
    gp_loads = [
        (bld.ident_bf[:], d["ident128"]),
        (bld.rot_bf[:], d["rotmat"]),
        (bld.ones_row_sb[:], d["ones_row"]),
    ]
    for j in range(8):
        gp_loads.append((bld.Wt_sb[:, j * D:(j + 1) * D], d["Wt_d"][128 * j:128 * (j + 1), :]))
    for dst, src in gp_loads:
        E(bld, "gpsimd", lambda e, a=dst, b=src: e.dma_start(a, b), inc=bld.d_gp, n=16)
    for t in (bld.M_all, bld.MT_all, bld.MTbf):
        E(bld, "gpsimd", lambda e, a=t: e.memset(a[:], 0.0), inc=bld.s_gp)
    gp_done = fence(bld, "gpsimd")
    gp_ms = bld.s_gp.v

    cf = bld.consts_f32
    sy_loads = [
        (cf[:, 0:128], d["d_strict"]), (cf[:, 128:256], d["m_strict"]),
        (cf[:, 256:384], d["d_strictT"]), (cf[:, 384:512], d["m_strictT"]),
        (cf[:, 512:640], d["d_inclT"]), (cf[:, 640:768], d["m_inclT"]),
        (bld.i64_bf[:], d["i64pair"][:, 0:64]),
        (bld.cols[0:128, 0:1], d["tcol"]), (bld.cols[0:128, 1:2], d["tbarcol"]),
        (bld.cols[0:1, 2:3], d["eta"]), (bld.cols[0:1, 3:4], d["lam_logit"]),
        (bld.dn_cols[:, CL_SEL:CL_SEL + 1], d["sel"]),
        (bld.grow[:], d["ln_gamma"]), (bld.brow[:], d["ln_beta"]),
        (bld.ones_row_f32[:], d["ones_row"]),
    ]
    for dst, src in sy_loads:
        E(bld, "sync", lambda e, a=dst, b=src: e.dma_start(a, b), inc=bld.d_sy, n=16)
    sy_done = fence(bld, "sync")

    ones_r = bld.ones_row_f32

    W8(bld, "scalar", bld.s_sy, sy_done)
    E(bld, "scalar", lambda e: e.activation(bld.cols[0:1, 4:5], bld.cols[0:1, 3:4], AF.Sigmoid))
    E(bld, "scalar", lambda e: e.activation(bld.cols[0:1, 5:6], bld.cols[0:1, 4:5], AF.Ln),
      inc=bld.s_ac)
    a_ms = bld.s_ac.v
    W8(bld, "vector", bld.s_ac, a_ms)
    E(bld, "vector", lambda e: e.tensor_scalar_max(bld.cols[0:1, 5:6], bld.cols[0:1, 5:6], -85.0))
    E(bld, "vector", lambda e: e.tensor_scalar(bld.cols[0:1, 6:7], bld.cols[0:1, 4:5], -1.0, 1.0,
                                               ALU.mult, ALU.add))
    E(bld, "vector", lambda e: e.tensor_mul(bld.cols[0:1, 7:8], bld.cols[0:1, 6:7], bld.cols[0:1, 2:3]))
    E(bld, "vector", lambda e: e.tensor_scalar_mul(bld.cols[0:1, 7:8], bld.cols[0:1, 7:8], 1.0 / 64.0))
    E(bld, "vector", lambda e: e.tensor_scalar_mul(bld.cols[0:1, 8:9], bld.cols[0:1, 5:6], 128.0),
      inc=bld.s_ve)
    v_ms = bld.s_ve.v
    W8(bld, "scalar", bld.s_ve, v_ms)
    E(bld, "scalar", lambda e: e.activation(bld.cols[0:1, 9:10], bld.cols[0:1, 8:9], AF.Exp),
      inc=bld.s_ac)
    a_ms = bld.s_ac.v

    W8(bld, "tensor", bld.s_ac, a_ms)
    W8(bld, "tensor", bld.s_sy, sy_done)
    ps = bld.pb[6]
    def bcast_mm(src_ap, col):
        E(bld, "tensor", lambda e, s=src_ap, c=col: e.matmul(
            ps[:, c:c + 1], ones_r[:], s, start=True, stop=True))
    bcast_mm(bld.cols[0:1, 5:6], 0)
    bcast_mm(bld.cols[0:1, 7:8], 1)
    bcast_mm(bld.cols[0:1, 9:10], 2)
    E(bld, "tensor", lambda e: e.matmul(ps[:, 3:4], ones_r[:], bld.cols[0:1, 7:8],
                                        start=True, stop=True), inc=bld.s_pe)
    pe_ms = bld.s_pe.v
    W8(bld, "vector", bld.s_pe, pe_ms)
    E(bld, "vector", lambda e: e.tensor_copy(bld.dn_cols[:, CL_LNLAM:CL_LNLAM + 1], ps[:, 0:1]))
    E(bld, "vector", lambda e: e.tensor_copy(bld.dn_cols[:, CL_BETA:CL_BETA + 1], ps[:, 1:2]))
    E(bld, "vector", lambda e: e.tensor_copy(bld.dn_cols[:, CL_LAMC:CL_LAMC + 1], ps[:, 2:3]))
    E(bld, "vector", lambda e: e.tensor_scalar_mul(bld.dn_cols[:, CL_NBETA:CL_NBETA + 1],
                                                   ps[:, 3:4], -1.0))
    E(bld, "vector", lambda e: e.tensor_mul(bld.cols[:, 10:11], bld.cols[:, 0:1],
                                            bld.dn_cols[:, CL_LNLAM:CL_LNLAM + 1]))
    E(bld, "vector", lambda e: e.tensor_mul(bld.cols[:, 11:12], bld.cols[:, 1:2],
                                            bld.dn_cols[:, CL_LNLAM:CL_LNLAM + 1]), inc=bld.s_ve)
    v_ms = bld.s_ve.v
    W8(bld, "scalar", bld.s_ve, v_ms)
    E(bld, "scalar", lambda e: e.activation(bld.dn_cols[:, CL_LAM:CL_LAM + 1], bld.cols[:, 10:11], AF.Exp))
    E(bld, "scalar", lambda e: e.activation(bld.dn_cols[:, CL_LBAR:CL_LBAR + 1], bld.cols[:, 11:12], AF.Exp),
      inc=bld.s_ac)
    a_ms = bld.s_ac.v
    W8(bld, "vector", bld.s_ac, a_ms)
    E(bld, "vector", lambda e: e.tensor_mul(bld.dn_cols[:, CL_BLAM:CL_BLAM + 1],
                                            bld.dn_cols[:, CL_BETA:CL_BETA + 1],
                                            bld.dn_cols[:, CL_LAM:CL_LAM + 1]))

    def mk_decay(dst, d_off, m_off, use_nbeta):
        E(bld, "vector", lambda e: e.tensor_scalar_mul(
            bld.x2[:, 0:128], cf[:, d_off:d_off + 128], bld.dn_cols[:, CL_LNLAM:CL_LNLAM + 1]),
            inc=bld.s_ve)
        W8(bld, "scalar", bld.s_ve, bld.s_ve.v)
        E(bld, "scalar", lambda e: e.activation(bld.x2[:, 128:256], bld.x2[:, 0:128], AF.Exp),
          inc=bld.s_ac)
        W8(bld, "vector", bld.s_ac, bld.s_ac.v)
        if use_nbeta:
            E(bld, "vector", lambda e: e.tensor_mul(bld.x2[:, 0:128], bld.x2[:, 128:256],
                                                    cf[:, m_off:m_off + 128]))
            E(bld, "vector", lambda e: e.tensor_scalar_mul(
                dst[:], bld.x2[:, 0:128], bld.dn_cols[:, CL_NBETA:CL_NBETA + 1]), inc=bld.s_ve)
        else:
            E(bld, "vector", lambda e: e.tensor_mul(dst[:], bld.x2[:, 128:256],
                                                    cf[:, m_off:m_off + 128]), inc=bld.s_ve)

    mk_decay(bld.Ls, 0, 128, True)
    mk_decay(bld.LsT, 256, 384, True)
    mk_decay(bld.LiT, 512, 640, False)
    bld.guards = getattr(bld, "guards", {})
    guard_set(bld, "x2", (bld.s_ve, bld.s_ve.v))

    W8(bld, "vector", bld.s_sy, sy_done)
    for pj in range(8):
        E(bld, "vector", lambda e, pj=pj: e.tensor_copy(bld.P_all[:, 64 * pj:64 * (pj + 1)],
                                                        bld.i64_bf[:, :]))
        E(bld, "vector", lambda e, pj=pj: e.tensor_copy(bld.PT_all[:, 64 * pj:64 * (pj + 1)],
                                                        bld.i64_bf[:, :]))
    E(bld, "vector", lambda e: e.tensor_copy(bld.PT_all[:, 0:1], bld.PT_all[:, 0:1]), inc=bld.s_ve)

    W8(bld, "tensor", bld.s_sy, sy_done)
    for half in range(2):
        W8(bld, "tensor", bld.s_ve, bld.s_ve.v)
        E(bld, "tensor", lambda e, hf=half: e.matmul(
            bld.pb[5][:, 0:512], ones_r[:], bld.grow[:, 512 * hf:512 * (hf + 1)],
            start=True, stop=True), inc=bld.s_pe)
        W8(bld, "vector", bld.s_pe, bld.s_pe.v)
        E(bld, "vector", lambda e, hf=half: e.tensor_copy(
            bld.gamma_b[:, 512 * hf:512 * (hf + 1)], bld.pb[5][:, 0:512]), inc=bld.s_ve)
        W8(bld, "tensor", bld.s_ve, bld.s_ve.v)
        E(bld, "tensor", lambda e, hf=half: e.matmul(
            bld.pb[5][:, 0:512], ones_r[:], bld.brow[:, 512 * hf:512 * (hf + 1)],
            start=True, stop=True), inc=bld.s_pe)
        W8(bld, "vector", bld.s_pe, bld.s_pe.v)
        E(bld, "vector", lambda e, hf=half: e.tensor_copy(
            bld.beta_b[:, 512 * hf:512 * (hf + 1)], bld.pb[5][:, 0:512]), inc=bld.s_ve)

    for eng in ("tensor", "vector", "scalar"):
        W8(bld, eng, bld.s_gq, gp_done)
        W8(bld, eng, bld.s_gp, gp_ms)


def guard_wait(bld, eng, key):
    for sem, val in bld.guards.get(key, ()):
        if sem.name.startswith("d_"):
            val = sem.v
        W8(bld, eng, sem, val)


def guard_set(bld, key, *pairs):
    bld.guards[key] = list(pairs)


def guard_add(bld, key, sem, val):
    bld.guards.setdefault(key, []).append((sem, val))


def emit_tile_prep(bld: Build, tile):
    p = bld.prog
    slot = tile % 2
    own = tile >= 4
    if not hasattr(bld, "guards"):
        bld.guards = {}
    cb = 12 + 10 * slot

    guard_wait(bld, "sync", f"xrow{slot}")
    E(bld, "sync", lambda e, t=tile, s=slot: e.dma_start(
        bld.xrow[s][:], bld.dram["x_in"][128 * t:128 * (t + 1), :]), inc=bld.d_sy, n=16)
    xv = fence(bld, "sync")

    W8(bld, "vector", bld.s_sy, xv)
    X = bld.xrow[slot]
    c = bld.cols
    E(bld, "vector", lambda e: e.reduce_sum(c[:, cb + 0:cb + 1], X[:], axis=AX.X), inc=bld.s_ve)
    W8(bld, "scalar", bld.s_sy, xv)
    guard_wait(bld, "scalar", "x2")
    sq_a = E(bld, "scalar", lambda e: e.activation(bld.x2[:], X[:], AF.Square), inc=bld.s_ac)
    W8(bld, "vector", bld.s_ac, sq_a)
    x2r = E(bld, "vector", lambda e: e.reduce_sum(c[:, cb + 1:cb + 2], bld.x2[:], axis=AX.X))
    guard_set(bld, "x2", (bld.s_ve, x2r))
    E(bld, "vector", lambda e: e.tensor_scalar_mul(c[:, cb + 2:cb + 3], c[:, cb + 0:cb + 1], 1.0 / D))
    E(bld, "vector", lambda e: e.tensor_scalar_mul(c[:, cb + 3:cb + 4], c[:, cb + 2:cb + 3], -1.0))
    E(bld, "vector", lambda e: e.tensor_scalar_mul(c[:, cb + 4:cb + 5], c[:, cb + 1:cb + 2], 1.0 / D))
    E(bld, "vector", lambda e: e.tensor_mul(c[:, cb + 5:cb + 6], c[:, cb + 2:cb + 3], c[:, cb + 2:cb + 3]))
    E(bld, "vector", lambda e: e.tensor_sub(c[:, cb + 6:cb + 7], c[:, cb + 4:cb + 5], c[:, cb + 5:cb + 6]))
    E(bld, "vector", lambda e: e.tensor_scalar_add(c[:, cb + 6:cb + 7], c[:, cb + 6:cb + 7], LN_EPS),
      inc=bld.s_ve)
    W8(bld, "scalar", bld.s_ve, bld.s_ve.v)
    sqrt_a = E(bld, "scalar", lambda e: e.activation(c[:, cb + 7:cb + 8], c[:, cb + 6:cb + 7], AF.Sqrt),
               inc=bld.s_ac)
    W8(bld, "vector", bld.s_ac, sqrt_a)
    E(bld, "vector", lambda e: e.reciprocal(c[:, cb + 8:cb + 9], c[:, cb + 7:cb + 8]))
    guard_wait(bld, "vector", f"xn{slot}")
    xn_v = E(bld, "vector", lambda e: e.tensor_scalar(
        bld.xn_tok[slot][:], X[:], c[:, cb + 3:cb + 4], c[:, cb + 8:cb + 9],
        ALU.add, ALU.mult), inc=bld.s_ve)
    guard_set(bld, f"xrow{slot}", (bld.s_ve, xn_v), (bld.s_ac, sq_a))
    tanh_a = None
    if own:
        W8(bld, "scalar", bld.s_ve, xn_v)
        guard_wait(bld, "scalar", f"xa{slot}")
        tanh_a = E(bld, "scalar", lambda e: e.activation(
            bld.xa_tok[slot][:], bld.xn_tok[slot][:], AF.Tanh), inc=bld.s_ac)

    guard_wait(bld, "tensor", "pbt")
    W8(bld, "tensor", bld.s_ve, xn_v)
    for j in range(8):
        inc = bld.s_pe if j == 7 else None
        E(bld, "tensor", lambda e, j=j: e.transpose(
            bld.pbt[:, 128 * j:128 * (j + 1)], bld.xn_tok[slot][:, 128 * j:128 * (j + 1)],
            bld.ident_bf[:]), inc=inc)
    trn_pe = bld.s_pe.v
    W8(bld, "scalar", bld.s_pe, trn_pe)
    hdst = bld.h[:].rearrange("p (c t) -> p c t", t=TLOC)[:, :, 128 * tile:128 * (tile + 1)]
    psrc = bld.pbt[:].rearrange("p (c t) -> p c t", t=128)
    h_a = E(bld, "scalar", lambda e, d2=hdst, s2=psrc: e.activation(d2, s2, AF.Copy), inc=bld.s_ac)
    guard_set(bld, "pbt", (bld.s_ac, h_a))
    guard_set(bld, f"xn{slot}", (bld.s_pe, trn_pe), (bld.s_ac, tanh_a if own else h_a))

    if not own:
        return

    guard_wait(bld, "tensor", "pbt")
    W8(bld, "tensor", bld.s_ac, tanh_a)
    guard_wait(bld, "tensor", f"xafm{slot}")
    for j in range(8):
        inc = bld.s_pe if j == 7 else None
        E(bld, "tensor", lambda e, j=j: e.transpose(
            bld.pbt[:, 128 * j:128 * (j + 1)], bld.xa_tok[slot][:, 128 * j:128 * (j + 1)],
            bld.ident_bf[:]), inc=inc)
    tra_pe = bld.s_pe.v
    W8(bld, "vector", bld.s_pe, tra_pe)
    xa_v = E(bld, "vector", lambda e: e.tensor_copy(bld.xa_fm[slot][:], bld.pbt[:]), inc=bld.s_ve)
    guard_set(bld, "pbt", (bld.s_ve, xa_v))
    guard_set(bld, f"xa{slot}", (bld.s_pe, tra_pe))

    for half in range(2):
        guard_wait(bld, "tensor", f"pb{half}")
        W8(bld, "tensor", bld.s_ve, xa_v)
        for j in range(8):
            inc = bld.s_pe if j == 7 else None
            E(bld, "tensor", lambda e, j=j, hf=half: e.matmul(
                bld.pb[hf][:, 0:512], bld.xa_fm[slot][:, 128 * j:128 * (j + 1)],
                bld.Wt_sb[:, 1024 * j + 512 * hf:1024 * j + 512 * hf + 512],
                start=(j == 0), stop=(j == 7)), inc=inc)
        st_pe = bld.s_pe.v
        W8(bld, "vector", bld.s_pe, st_pe)
        if half == 0:
            guard_wait(bld, "vector", f"st{slot}")
        ev = E(bld, "vector", lambda e, hf=half: e.tensor_copy(
            bld.st_tok[slot][:, 512 * hf:512 * (hf + 1)], bld.pb[hf][:, 0:512]), inc=bld.s_ve)
        guard_set(bld, f"pb{half}", (bld.s_ve, ev))
    st_v = bld.s_ve.v
    guard_set(bld, f"xafm{slot}", (bld.s_pe, bld.s_pe.v))
    W8(bld, "sync", bld.s_ve, st_v)
    r0 = (tile - 4) * 128
    E(bld, "sync", lambda e, r0=r0, s=slot: e.dma_start(
        bld.dram["sc_static"][r0:r0 + 128, :], bld.st_tok[s][:]), inc=bld.d_sy, n=16)
    stv = fence(bld, "sync")
    guard_set(bld, f"st{slot}", (bld.s_sy, stv))


def emit_dn_chunk(bld: Build, ch):
    p = bld.prog
    slot = (4 + ch) % 2
    xa_t = bld.xa_tok[slot]
    xa_f = bld.xa_fm[slot]
    st_t = bld.st_tok[slot]
    dcl = bld.dn_cols

    def lam_c(i):
        return dcl[:, i:i + 1]

    def hblk(h):
        return 128 * (h // 2), 64 * (h % 2)

    for grp in range(2):
        for bank, par in ((2, 0), (3, 1)):
            guard_wait(bld, "tensor", f"pb{bank}")
            if grp == 0 and bank == 2 and par == 0:
                pass
            heads = [8 * grp + 2 * i + par for i in range(4)]
            for ii, h in enumerate(heads):
                co, pb_ = hblk(h)
                inc = bld.s_pe if ii == 3 else None
                E(bld, "tensor", lambda e, co=co, pb_=pb_, ii=ii, bank=bank: e.matmul(
                    bld.pb[bank][:, 128 * ii:128 * (ii + 1)],
                    xa_f[pb_:pb_ + 64, co:co + 128], xa_f[pb_:pb_ + 64, co:co + 128],
                    start=True, stop=True, tile_position=(pb_, 0)), inc=inc)
            kk_pe = bld.s_pe.v
            W8(bld, "vector", bld.s_pe, kk_pe)
            psv = bld.pb[bank][:].rearrange("p (i x) -> p i x", x=128)
            if grp == 0 and bank == 2:
                guard_wait(bld, "vector", "NBt")
            for dst, msk in ((bld.Na, bld.Ls), (bld.NTa, bld.LsT), (bld.Bt_all, bld.LiT)):
                dv = dst[:].rearrange("p (hh pp x) -> p hh pp x", pp=2, x=128)[:, 4 * grp:4 * grp + 4, par, :]
                inc2 = bld.s_ve if msk is bld.LiT else None
                E(bld, "vector", lambda e, dv=dv, psv=psv, msk=msk: e.tensor_tensor(
                    dv, psv, msk[:].unsqueeze(1).broadcast_to([128, 4, 128]), ALU.mult),
                  inc=inc2)
            guard_set(bld, f"pb{bank}", (bld.s_ve, bld.s_ve.v))
    masks_v = bld.s_ve.v

    xv3 = bld.X_all[:].rearrange("p (h x) -> p h x", x=128)
    guard_wait(bld, "vector", "X_all")
    guard_wait(bld, "vector", "LbK")
    E(bld, "vector", lambda e: e.tensor_scalar_mul(
        xv3[:, :, 0:64], st_t[:].rearrange("p (h x) -> p h x", x=64), lam_c(CL_BETA)))
    E(bld, "vector", lambda e: e.tensor_scalar_mul(
        xv3[:, :, 64:128], xa_t[:].rearrange("p (h x) -> p h x", x=64), lam_c(CL_BLAM)))
    E(bld, "vector", lambda e: e.tensor_scalar_mul(bld.LbK[:], xa_t[:], lam_c(CL_LBAR)),
      inc=bld.s_ve)
    rhs_v = bld.s_ve.v
    guard_add(bld, f"xa{slot}", bld.s_ve, rhs_v)
    guard_add(bld, f"st{slot}", bld.s_ve, rhs_v)

    Ncur, NTcur, Nnxt, NTnxt = bld.Na, bld.NTa, bld.Nb, bld.NTb
    for j in range(7):
        for grp in range(4):
            bank = 2 + (grp % 2)
            guard_wait(bld, "tensor", f"pb{bank}")
            if j == 0:
                W8(bld, "tensor", bld.s_ve, rhs_v)
            for ii in range(4):
                h = 4 * grp + ii
                inc = bld.s_pe if ii == 3 else None
                E(bld, "tensor", lambda e, h=h, ii=ii, bank=bank, NT=NTcur: e.matmul(
                    bld.pb[bank][:, 128 * ii:128 * (ii + 1)],
                    NT[:, 128 * h:128 * (h + 1)], bld.X_all[:, 128 * h:128 * (h + 1)],
                    start=True, stop=True), inc=inc)
            app_pe = bld.s_pe.v
            W8(bld, "vector", bld.s_pe, app_pe)
            xslice = bld.X_all[:, 512 * grp:512 * (grp + 1)]
            E(bld, "vector", lambda e, xs=xslice, bank=bank: e.tensor_tensor(
                xs, xs, bld.pb[bank][:, 0:512], ALU.add), inc=bld.s_ve)
            guard_set(bld, f"pb{bank}", (bld.s_ve, bld.s_ve.v))
        if j < 6:
            for grp in range(4):
                guard_wait(bld, "tensor", "pb4")
                guard_wait(bld, "tensor", "pb5")
                for ii in range(4):
                    h = 4 * grp + ii
                    E(bld, "tensor", lambda e, h=h, ii=ii, N=Ncur, NT=NTcur: e.matmul(
                        bld.pb[4][:, 128 * ii:128 * (ii + 1)],
                        NT[:, 128 * h:128 * (h + 1)], N[:, 128 * h:128 * (h + 1)],
                        start=True, stop=True))
                for ii in range(4):
                    h = 4 * grp + ii
                    inc = bld.s_pe if ii == 3 else None
                    E(bld, "tensor", lambda e, h=h, ii=ii, N=Ncur, NT=NTcur: e.matmul(
                        bld.pb[5][:, 128 * ii:128 * (ii + 1)],
                        N[:, 128 * h:128 * (h + 1)], NT[:, 128 * h:128 * (h + 1)],
                        start=True, stop=True), inc=inc)
                sq_pe = bld.s_pe.v
                W8(bld, "vector", bld.s_pe, sq_pe)
                E(bld, "vector", lambda e, grp=grp, Nn=Nnxt: e.tensor_copy(
                    Nn[:, 512 * grp:512 * (grp + 1)], bld.pb[4][:, 0:512]))
                E(bld, "vector", lambda e, grp=grp, NTn=NTnxt: e.tensor_copy(
                    NTn[:, 512 * grp:512 * (grp + 1)], bld.pb[5][:, 0:512]), inc=bld.s_ve)
                guard_set(bld, "pb4", (bld.s_ve, bld.s_ve.v))
                guard_set(bld, "pb5", (bld.s_ve, bld.s_ve.v))
            Ncur, NTcur, Nnxt, NTnxt = Nnxt, NTnxt, Ncur, NTcur
    solve_v = bld.s_ve.v

    for rnd in range(2):
        guard_wait(bld, "tensor", "pbt")
        W8(bld, "tensor", bld.s_ve, solve_v)
        for pi in range(4):
            pj = 4 * rnd + pi
            for par in range(2):
                h = 2 * pj + par
                inc = bld.s_pe if (pi == 3 and par == 1) else None
                E(bld, "tensor", lambda e, h=h, pi=pi, par=par: e.transpose(
                    bld.pbt[64 * par:64 * (par + 1), 128 * pi:128 * (pi + 1)],
                    bld.X_all[:, 128 * h + 64:128 * (h + 1)], bld.ident_bf[:],
                    tile_position=(0, 64 * par)), inc=inc)
        wt_pe = bld.s_pe.v
        W8(bld, "vector", bld.s_pe, wt_pe)
        if rnd == 0:
            guard_wait(bld, "vector", "Wfm")
        E(bld, "vector", lambda e, rnd=rnd: e.tensor_scalar_mul(
            bld.Wfm[:, 512 * rnd:512 * (rnd + 1)], bld.pbt[:, 0:512], -1.0), inc=bld.s_ve)
        guard_set(bld, "pbt", (bld.s_ve, bld.s_ve.v))

    for par in range(2):
        bank = 6 if par == 0 else 5
        guard_wait(bld, "tensor", f"pb{bank}")
        W8(bld, "tensor", bld.s_ve, bld.s_ve.v)
        for pj in range(8):
            h = 2 * pj + par
            inc = bld.s_pe if pj == 7 else None
            E(bld, "tensor", lambda e, pj=pj, par=par, bank=bank: e.matmul(
                bld.pb[bank][:, 64 * pj:64 * (pj + 1)],
                bld.Wfm[64 * par:64 * (par + 1), 128 * pj:128 * (pj + 1)],
                bld.MTbf[64 * par:64 * (par + 1), 64 * pj:64 * (pj + 1)],
                start=True, stop=True, tile_position=(64 * par, 0)), inc=inc)
        u_pe = bld.s_pe.v
        W8(bld, "vector", bld.s_pe, u_pe)
        if par == 0:
            guard_wait(bld, "vector", "U_all")
        uv = bld.U_all[:].rearrange("p (hh pp x) -> p hh pp x", pp=2, x=64)[:, :, par, :]
        ub = bld.X_all[:].rearrange("p (hh pp x) -> p hh pp x", pp=2, x=128)[:, :, par, 0:64]
        pv = bld.pb[bank][:].rearrange("p (h x) -> p h x", x=64)
        E(bld, "vector", lambda e, uv=uv, ub=ub, pv=pv: e.tensor_tensor(uv, ub, pv, ALU.add),
          inc=bld.s_ve)
        guard_set(bld, f"pb{bank}", (bld.s_ve, bld.s_ve.v))
    u_v = bld.s_ve.v

    mtbf_readers_pe = []

    guard_wait(bld, "vector", "o0_all")
    for par in range(2):
        bank = 0 if par == 0 else 1
        guard_wait(bld, "tensor", f"pb{bank}")
        for pj in range(8):
            h = 2 * pj + par
            co, pb_ = hblk(h)
            inc = bld.s_pe if pj == 7 else None
            E(bld, "tensor", lambda e, pj=pj, par=par, bank=bank, co=co, pb_=pb_: e.matmul(
                bld.pb[bank][:, 64 * pj:64 * (pj + 1)],
                xa_f[pb_:pb_ + 64, co:co + 128],
                bld.MTbf[64 * par:64 * (par + 1), 64 * pj:64 * (pj + 1)],
                start=True, stop=True, tile_position=(pb_, 0)), inc=inc)
        mtbf_readers_pe.append(bld.s_pe.v)
    o1_pe = bld.s_pe.v
    o0v = bld.o0_all[:].rearrange("p (h x) -> p h x", x=64)
    for rnd in range(2):
        guard_wait(bld, "tensor", "pb4")
        W8(bld, "tensor", bld.s_ve, bld.s_ve.v)
        for i8 in range(8):
            h = 8 * rnd + i8
            inc = bld.s_pe if i8 == 7 else None
            E(bld, "tensor", lambda e, h=h, i8=i8: e.matmul(
                bld.pb[4][:, 64 * i8:64 * (i8 + 1)],
                bld.Bt_all[:, 128 * h:128 * (h + 1)], bld.U_all[:, 64 * h:64 * (h + 1)],
                start=True, stop=True), inc=inc)
        o2_pe = bld.s_pe.v
        W8(bld, "scalar", bld.s_pe, o2_pe)
        otv = bld.otmp[:].rearrange("p (hh pp x) -> p hh pp x", pp=2, x=64)
        for par in range(2):
            bank = 0 if par == 0 else 1
            pv = bld.pb[bank][:].rearrange("p (h x) -> p h x", x=64)[:, 4 * rnd:4 * rnd + 4]
            E(bld, "scalar", lambda e, par=par, pv=pv, otv=otv: e.activation(
                otv[:, :, par, :], pv, AF.Copy), inc=(bld.s_ac if par == 1 else None))
        ocp_a = bld.s_ac.v
        W8(bld, "vector", bld.s_ac, ocp_a)
        W8(bld, "vector", bld.s_pe, o2_pe)
        pv2 = bld.pb[4][:].rearrange("p (h x) -> p h x", x=64)
        E(bld, "vector", lambda e, rnd=rnd, pv2=pv2, o0v=o0v: e.scalar_tensor_tensor(
            o0v[:, 8 * rnd:8 * rnd + 8], bld.otmp[:].rearrange("p (h x) -> p h x", x=64),
            lam_c(CL_LAM), pv2, ALU.mult, ALU.add), inc=bld.s_ve)
        guard_set(bld, "pb4", (bld.s_ve, bld.s_ve.v))
        if rnd == 1:
            guard_set(bld, "pb0", (bld.s_ac, ocp_a))
            guard_set(bld, "pb1", (bld.s_ac, ocp_a))
    o0_v = bld.s_ve.v
    W8(bld, "sync", bld.s_ve, o0_v)
    E(bld, "sync", lambda e, ch=ch: e.dma_start(
        bld.dram["sc_o0"][128 * ch:128 * (ch + 1), :], bld.o0_all[:]), inc=bld.d_sy, n=16)
    guard_set(bld, "o0_all", (bld.s_sy, fence(bld, "sync")))

    for rnd in range(2):
        guard_wait(bld, "tensor", "pb4")
        for i8 in range(8):
            h = 8 * rnd + i8
            inc = bld.s_pe if i8 == 7 else None
            E(bld, "tensor", lambda e, h=h, i8=i8: e.matmul(
                bld.pb[4][:, 64 * i8:64 * (i8 + 1)],
                bld.Bt_all[:, 128 * h:128 * (h + 1)], bld.X_all[:, 128 * h + 64:128 * (h + 1)],
                start=True, stop=True), inc=inc)
        bw_pe = bld.s_pe.v
        W8(bld, "vector", bld.s_pe, bw_pe)
        if rnd == 0:
            guard_wait(bld, "vector", "Ttok")
            guard_wait(bld, "vector", "Tfm")
        ttv = bld.Ttok[:].rearrange("p (h x) -> p h x", x=64)[:, 8 * rnd:8 * rnd + 8]
        qv = xa_t[:].rearrange("p (h x) -> p h x", x=64)[:, 8 * rnd:8 * rnd + 8]
        pv2 = bld.pb[4][:].rearrange("p (h x) -> p h x", x=64)
        E(bld, "vector", lambda e, ttv=ttv, qv=qv, pv2=pv2: e.scalar_tensor_tensor(
            ttv, qv, lam_c(CL_LAM), pv2, ALU.mult, ALU.subtract), inc=bld.s_ve)
        guard_set(bld, "pb4", (bld.s_ve, bld.s_ve.v))
    tt_v = bld.s_ve.v
    guard_add(bld, f"xa{slot}", bld.s_ve, tt_v)
    for rnd in range(2):
        guard_wait(bld, "tensor", "pbt")
        W8(bld, "tensor", bld.s_ve, tt_v)
        for pi in range(4):
            pj = 4 * rnd + pi
            for par in range(2):
                h = 2 * pj + par
                inc = bld.s_pe if (pi == 3 and par == 1) else None
                E(bld, "tensor", lambda e, h=h, pi=pi, par=par: e.transpose(
                    bld.pbt[64 * par:64 * (par + 1), 128 * pi:128 * (pi + 1)],
                    bld.Ttok[:, 64 * h:64 * (h + 1)], bld.ident_bf[:],
                    tile_position=(0, 64 * par)), inc=inc)
        tf_pe = bld.s_pe.v
        W8(bld, "vector", bld.s_pe, tf_pe)
        E(bld, "vector", lambda e, rnd=rnd: e.tensor_copy(
            bld.Tfm[:, 512 * rnd:512 * (rnd + 1)], bld.pbt[:, 0:512]), inc=bld.s_ve)
        guard_set(bld, "pbt", (bld.s_ve, bld.s_ve.v))
    guard_set(bld, "Ttok", (bld.s_pe, bld.s_pe.v))
    tfm_v = bld.s_ve.v
    W8(bld, "sync", bld.s_ve, tfm_v)
    E(bld, "sync", lambda e, ch=ch: e.dma_start(bld.dram["sc_T"][ch], bld.Tfm[:]),
      inc=bld.d_sy, n=16)

    guard_wait(bld, "sync", "P_write")
    E(bld, "sync", lambda e, ch=ch: e.dma_start(bld.dram["sc_P"][ch], bld.P_all[:]),
      inc=bld.d_sy, n=16)
    psnap = fence(bld, "sync")

    guard_wait(bld, "tensor", "pb6")
    W8(bld, "tensor", bld.s_ve, tfm_v)
    for pj in range(8):
        for par in range(2):
            h = 2 * pj + par
            inc = bld.s_pe if (pj == 7 and par == 1) else None
            E(bld, "tensor", lambda e, h=h, pj=pj, par=par: e.matmul(
                bld.pb[6][64 * par:64 * (par + 1), 64 * pj:64 * (pj + 1)],
                bld.X_all[:, 128 * h + 64:128 * (h + 1)], bld.LbK[:, 64 * h:64 * (h + 1)],
                start=True, stop=True, tile_position=(0, 64 * par)), inc=inc)
    g_pe = bld.s_pe.v
    W8(bld, "vector", bld.s_pe, g_pe)
    E(bld, "vector", lambda e: e.scalar_tensor_tensor(
        bld.G_all[:].rearrange("p (h x) -> p h x", x=64),
        bld.i64_bf[:].unsqueeze(1).broadcast_to([128, 8, 64]),
        lam_c(CL_LAMC), bld.pb[6][:].rearrange("p (h x) -> p h x", x=64),
        ALU.mult, ALU.subtract), inc=bld.s_ve)
    g_v = bld.s_ve.v
    guard_set(bld, "pb6", (bld.s_ve, g_v))

    guard_wait(bld, "tensor", "pb6")
    for pj in range(8):
        for par in range(2):
            h = 2 * pj + par
            E(bld, "tensor", lambda e, h=h, pj=pj, par=par: e.matmul(
                bld.pb[6][64 * par:64 * (par + 1), 64 * pj:64 * (pj + 1)],
                bld.U_all[:, 64 * h:64 * (h + 1)], bld.LbK[:, 64 * h:64 * (h + 1)],
                start=True, stop=True, tile_position=(0, 64 * par)))
    guard_wait(bld, "tensor", "pb5")
    for pj in range(8):
        for par in range(2):
            h = 2 * pj + par
            inc = bld.s_pe if (pj == 7 and par == 1) else None
            E(bld, "tensor", lambda e, h=h, pj=pj, par=par: e.matmul(
                bld.pb[5][64 * par:64 * (par + 1), 64 * pj:64 * (pj + 1)],
                bld.LbK[:, 64 * h:64 * (h + 1)], bld.U_all[:, 64 * h:64 * (h + 1)],
                start=True, stop=True, tile_position=(0, 64 * par)), inc=inc)
    mm_pe = bld.s_pe.v
    for r in mtbf_readers_pe:
        pass
    W8(bld, "vector", bld.s_pe, mm_pe)
    E(bld, "vector", lambda e: e.scalar_tensor_tensor(
        bld.M_all[:], bld.M_all[:], lam_c(CL_LAMC), bld.pb[6][:], ALU.mult, ALU.add))
    E(bld, "vector", lambda e: e.scalar_tensor_tensor(
        bld.MT_all[:], bld.MT_all[:], lam_c(CL_LAMC), bld.pb[5][:], ALU.mult, ALU.add),
      inc=bld.s_ve)
    m_v = bld.s_ve.v
    guard_wait(bld, "vector", "MTbf_w")
    E(bld, "vector", lambda e: e.tensor_copy(bld.MTbf[:], bld.MT_all[:]), inc=bld.s_ve)
    guard_set(bld, "pb5", (bld.s_ve, bld.s_ve.v))
    guard_set(bld, "pb6", (bld.s_ve, bld.s_ve.v))

    W8(bld, "tensor", bld.s_ve, g_v)
    W8(bld, "tensor", bld.s_sy, psnap)
    guard_wait(bld, "tensor", "pb3")
    for pj in range(8):
        for par in range(2):
            E(bld, "tensor", lambda e, pj=pj, par=par: e.matmul(
                bld.pb[3][64 * par:64 * (par + 1), 64 * pj:64 * (pj + 1)],
                bld.PT_all[64 * par:64 * (par + 1), 64 * pj:64 * (pj + 1)],
                bld.G_all[64 * par:64 * (par + 1), 64 * pj:64 * (pj + 1)],
                start=True, stop=True, tile_position=(64 * par, 64 * par)))
    guard_wait(bld, "tensor", "pb2")
    for pj in range(8):
        for par in range(2):
            inc = bld.s_pe if (pj == 7 and par == 1) else None
            E(bld, "tensor", lambda e, pj=pj, par=par: e.matmul(
                bld.pb[2][64 * par:64 * (par + 1), 64 * pj:64 * (pj + 1)],
                bld.G_all[64 * par:64 * (par + 1), 64 * pj:64 * (pj + 1)],
                bld.PT_all[64 * par:64 * (par + 1), 64 * pj:64 * (pj + 1)],
                start=True, stop=True, tile_position=(64 * par, 64 * par)), inc=inc)
    p_pe = bld.s_pe.v
    W8(bld, "vector", bld.s_pe, p_pe)
    E(bld, "vector", lambda e: e.tensor_copy(bld.P_all[:], bld.pb[3][:]))
    E(bld, "vector", lambda e: e.tensor_copy(bld.PT_all[:], bld.pb[2][:]), inc=bld.s_ve)
    p_v = bld.s_ve.v
    guard_set(bld, "pb2", (bld.s_ve, p_v))
    guard_set(bld, "pb3", (bld.s_ve, p_v))
    guard_set(bld, "P_write", (bld.s_ve, p_v))
    guard_set(bld, "Wfm", (bld.s_pe, mm_pe))
    guard_set(bld, "U_all", (bld.s_pe, mm_pe))
    guard_set(bld, "LbK", (bld.s_pe, mm_pe))
    guard_set(bld, "X_all", (bld.s_pe, g_pe))
    guard_set(bld, "NBt", (bld.s_pe, g_pe))
    guard_set(bld, "MTbf_w", (bld.s_pe, mm_pe))
    guard_set(bld, "Tfm", (bld.s_sy, psnap))


def emit_dn_ag(bld: Build):
    p = bld.prog
    W8(bld, "sync", bld.s_ve, bld.s_ve.v)
    E(bld, "sync", lambda e: e.dma_start(bld.dram["cc_m_in"][:, 0:512], bld.M_all[:]),
      inc=bld.d_sy, n=16)
    E(bld, "sync", lambda e: e.dma_start(bld.dram["cc_m_in"][:, 512:1024], bld.MT_all[:]),
      inc=bld.d_sy, n=16)
    ag_in = fence(bld, "sync")
    W8(bld, "gpsimd", bld.s_sy, ag_in)
    cc = E(bld, "gpsimd", lambda e: e.collective_compute(
        "AllGather", ALU.bypass, replica_groups=[[0, 1], [2, 3], [4, 5], [6, 7]],
        ins=[bld.dram["cc_m_in"].opt()], outs=[bld.dram["cc_m_out"].opt()]), inc=bld.s_cc)
    W8(bld, "sync", bld.s_cc, cc)
    guard_wait(bld, "sync", "xrow0")
    E(bld, "sync", lambda e: e.dma_start(bld.xrow[0][:], bld.dram["cc_m_out"][0:128, :]),
      inc=bld.d_sy, n=16)
    ld = fence(bld, "sync")
    W8(bld, "vector", bld.s_sy, ld)
    E(bld, "vector", lambda e: e.tensor_scalar_mul(bld.Minit[:], bld.xrow[0][:, 0:512],
                                                   bld.dn_cols[:, CL_SEL:CL_SEL + 1]))
    E(bld, "vector", lambda e: e.tensor_scalar_mul(bld.MinitT[:], bld.xrow[0][:, 512:1024],
                                                   bld.dn_cols[:, CL_SEL:CL_SEL + 1]))
    E(bld, "vector", lambda e: e.tensor_copy(bld.MinitTbf[:], bld.MinitT[:]), inc=bld.s_ve)


def emit_dn_corr(bld: Build, ch):
    p = bld.prog
    guard_wait(bld, "sync", "corr_ld")
    E(bld, "sync", lambda e, ch=ch: e.dma_start(bld.Pld[:], bld.dram["sc_P"][ch]),
      inc=bld.d_sy, n=16)
    E(bld, "sync", lambda e, ch=ch: e.dma_start(bld.Tld[:], bld.dram["sc_T"][ch]),
      inc=bld.d_sy, n=16)
    E(bld, "sync", lambda e, ch=ch: e.dma_start(
        bld.o0ld[:], bld.dram["sc_o0"][128 * ch:128 * (ch + 1), :]), inc=bld.d_sy, n=16)
    l3 = fence(bld, "sync")
    guard_wait(bld, "tensor", "pb6")
    W8(bld, "tensor", bld.s_sy, l3)
    W8(bld, "tensor", bld.s_ve, bld.s_ve.v)
    for pj in range(8):
        for par in range(2):
            inc = bld.s_pe if (pj == 7 and par == 1) else None
            E(bld, "tensor", lambda e, pj=pj, par=par: e.matmul(
                bld.pb[6][64 * par:64 * (par + 1), 64 * pj:64 * (pj + 1)],
                bld.Pld[64 * par:64 * (par + 1), 64 * pj:64 * (pj + 1)],
                bld.MinitT[64 * par:64 * (par + 1), 64 * pj:64 * (pj + 1)],
                start=True, stop=True, tile_position=(64 * par, 64 * par)), inc=inc)
    dm_pe = bld.s_pe.v
    W8(bld, "vector", bld.s_pe, dm_pe)
    guard_wait(bld, "vector", "dMbf")
    dm_v = E(bld, "vector", lambda e: e.tensor_copy(bld.dMbf[:], bld.pb[6][:]), inc=bld.s_ve)
    guard_set(bld, "pb6", (bld.s_ve, dm_v))
    for par in range(2):
        bank = 2 if par == 0 else 3
        guard_wait(bld, "tensor", f"pb{bank}")
        W8(bld, "tensor", bld.s_ve, dm_v)
        W8(bld, "tensor", bld.s_sy, l3)
        for pj in range(8):
            inc = bld.s_pe if pj == 7 else None
            E(bld, "tensor", lambda e, pj=pj, par=par, bank=bank: e.matmul(
                bld.pb[bank][:, 64 * pj:64 * (pj + 1)],
                bld.Tld[64 * par:64 * (par + 1), 128 * pj:128 * (pj + 1)],
                bld.dMbf[64 * par:64 * (par + 1), 64 * pj:64 * (pj + 1)],
                start=True, stop=True, tile_position=(64 * par, 0)), inc=inc)
    do_pe = bld.s_pe.v
    guard_set(bld, "dMbf", (bld.s_pe, do_pe))
    W8(bld, "vector", bld.s_pe, do_pe)
    W8(bld, "vector", bld.s_sy, l3)
    guard_wait(bld, "vector", "o0_all")
    for par in range(2):
        bank = 2 if par == 0 else 3
        ov = bld.o0_all[:].rearrange("p (hh pp x) -> p hh pp x", pp=2, x=64)[:, :, par, :]
        iv = bld.o0ld[:].rearrange("p (hh pp x) -> p hh pp x", pp=2, x=64)[:, :, par, :]
        pv = bld.pb[bank][:].rearrange("p (h x) -> p h x", x=64)
        inc = bld.s_ve if par == 1 else None
        E(bld, "vector", lambda e, ov=ov, iv=iv, pv=pv: e.tensor_tensor(ov, iv, pv, ALU.add),
          inc=inc)
    of_v = bld.s_ve.v
    guard_set(bld, "pb2", (bld.s_ve, of_v))
    guard_set(bld, "pb3", (bld.s_ve, of_v))
    W8(bld, "sync", bld.s_ve, of_v)
    E(bld, "sync", lambda e, ch=ch: e.dma_start(
        bld.dram["sc_ofin"][128 * ch:128 * (ch + 1), :], bld.o0_all[:]), inc=bld.d_sy, n=16)
    st = fence(bld, "sync")
    guard_set(bld, "o0_all", (bld.s_sy, st))
    guard_set(bld, "corr_ld", (bld.s_pe, do_pe), (bld.s_ve, of_v))


def emit_s_tile(bld: Build, ti):
    p = bld.prog
    guard_wait(bld, "sync", "s_ld")
    E(bld, "sync", lambda e, ti=ti: e.dma_start(
        bld.st_ld[:], bld.dram["sc_static"][128 * ti:128 * (ti + 1), :]), inc=bld.d_sy, n=16)
    E(bld, "sync", lambda e, ti=ti: e.dma_start(
        bld.of_ld[:], bld.dram["sc_ofin"][128 * ti:128 * (ti + 1), :]), inc=bld.d_sy, n=16)
    l2 = fence(bld, "sync")
    W8(bld, "vector", bld.s_sy, l2)
    guard_wait(bld, "vector", "yrow")
    c = bld.cols
    cb = 12
    E(bld, "vector", lambda e: e.tensor_add(bld.yrow[:], bld.st_ld[:], bld.of_ld[:]))
    E(bld, "vector", lambda e: e.reduce_sum(c[:, cb + 0:cb + 1], bld.yrow[:], axis=AX.X),
      inc=bld.s_ve)
    y_v = bld.s_ve.v
    W8(bld, "scalar", bld.s_ve, y_v)
    guard_wait(bld, "scalar", "x2")
    sq_a = E(bld, "scalar", lambda e: e.activation(bld.x2[:], bld.yrow[:], AF.Square), inc=bld.s_ac)
    W8(bld, "vector", bld.s_ac, sq_a)
    x2r = E(bld, "vector", lambda e: e.reduce_sum(c[:, cb + 1:cb + 2], bld.x2[:], axis=AX.X))
    guard_set(bld, "x2", (bld.s_ve, x2r))
    E(bld, "vector", lambda e: e.tensor_scalar_mul(c[:, cb + 2:cb + 3], c[:, cb + 0:cb + 1], 1.0 / D))
    E(bld, "vector", lambda e: e.tensor_scalar_mul(c[:, cb + 3:cb + 4], c[:, cb + 2:cb + 3], -1.0))
    E(bld, "vector", lambda e: e.tensor_scalar_mul(c[:, cb + 4:cb + 5], c[:, cb + 1:cb + 2], 1.0 / D))
    E(bld, "vector", lambda e: e.tensor_mul(c[:, cb + 5:cb + 6], c[:, cb + 2:cb + 3], c[:, cb + 2:cb + 3]))
    E(bld, "vector", lambda e: e.tensor_sub(c[:, cb + 6:cb + 7], c[:, cb + 4:cb + 5], c[:, cb + 5:cb + 6]))
    E(bld, "vector", lambda e: e.tensor_scalar_add(c[:, cb + 6:cb + 7], c[:, cb + 6:cb + 7], LN_EPS),
      inc=bld.s_ve)
    W8(bld, "scalar", bld.s_ve, bld.s_ve.v)
    sr_a = E(bld, "scalar", lambda e: e.activation(c[:, cb + 7:cb + 8], c[:, cb + 6:cb + 7], AF.Sqrt),
             inc=bld.s_ac)
    W8(bld, "vector", bld.s_ac, sr_a)
    E(bld, "vector", lambda e: e.reciprocal(c[:, cb + 8:cb + 9], c[:, cb + 7:cb + 8]))
    E(bld, "vector", lambda e: e.tensor_scalar(bld.yrow[:], bld.yrow[:], c[:, cb + 3:cb + 4],
                                               c[:, cb + 8:cb + 9], ALU.add, ALU.mult))
    E(bld, "vector", lambda e: e.tensor_mul(bld.yrow[:], bld.yrow[:], bld.gamma_b[:]))
    guard_wait(bld, "vector", "s_bf")
    sv = E(bld, "vector", lambda e: e.tensor_add(bld.s_bf[:], bld.yrow[:], bld.beta_b[:]),
           inc=bld.s_ve)
    guard_set(bld, "yrow", (bld.s_ve, sv))
    guard_set(bld, "s_ld", (bld.s_ve, sv))
    W8(bld, "sync", bld.s_ve, sv)
    E(bld, "sync", lambda e, ti=ti: e.dma_start(
        bld.dram["sc_s"][128 * ti:128 * (ti + 1), :], bld.s_bf[:]), inc=bld.d_sy, n=16)
    st = fence(bld, "sync")
    guard_set(bld, "s_bf", (bld.s_sy, st))


def emit_dn_phase(bld: Build, n_tiles=NT, n_chunks=NCH):
    for tile in range(n_tiles):
        emit_tile_prep(bld, tile)
        if 4 <= tile < 4 + n_chunks:
            emit_dn_chunk(bld, tile - 4)
    emit_dn_ag(bld)
    for ch in range(n_chunks):
        emit_dn_corr(bld, ch)
    for ti in range(n_chunks):
        emit_s_tile(bld, ti)


def finish_program(bld: Build):
    for q, s in (("sync", bld.d_sy), ("gpsimd", bld.d_gp), ("scalar", bld.d_sc)):
        if s.v:
            W8(bld, q, s, s.v)


def build_dn_standalone(n_tiles=NT, n_chunks=NCH):
    bld = Build(with_transformer=False)
    build_common(bld)
    emit_setup(bld)
    emit_dn_phase(bld, n_tiles, n_chunks)
    sdone = fence(bld, "sync")
    W8(bld, "gpsimd", bld.s_sy, sdone)
    nrows = 128 * n_chunks
    step = 128
    for i in range(0, nrows, step):
        E(bld, "gpsimd", lambda e, i=i: e.dma_start(
            bld.dram["out"][i:i + step, :],
            bld.dram["sc_s"][i:i + step, :]), inc=bld.d_gp, n=16)
    finish_program(bld)
    with bld.es, bld.nc.Block() as blk:
        bld.prog.play(blk)
    return bld


def core_input_map(inputs, core_id):
    b, half = core_id // 2, core_id % 2
    gstart = 4096 * half - HALO
    x = np.asarray(inputs["L0_attn_x"][b], np.float32)
    xloc = np.zeros((TLOC, D), np.float32)
    lo = max(0, gstart)
    xloc[lo - gstart:] = x[lo:gstart + TLOC]
    m = dict(_host_consts(core_id))
    m["x_in"] = xloc
    m["Wt_d"] = np.ascontiguousarray(np.asarray(inputs["L0_attn_W"], np.float32).T)
    m["ln_gamma"] = np.asarray(inputs["L0_attn_ln_gamma"], np.float32).reshape(1, D)
    m["ln_beta"] = np.asarray(inputs["L0_attn_ln_beta"], np.float32).reshape(1, D)
    m["eta"] = np.asarray(inputs["L0_attn_eta"], np.float32).reshape(1, 1)
    m["lam_logit"] = np.asarray(inputs["L0_attn_lam_logit"], np.float32).reshape(1, 1)
    m["Wqkv"] = np.asarray(inputs["L0_ffn_Wqkv"], np.float32)
    m["Wo"] = np.asarray(inputs["L0_ffn_Wo"], np.float32)
    m["W1"] = np.asarray(inputs["L0_ffn_W1"], np.float32)
    m["W2"] = np.asarray(inputs["L0_ffn_W2"], np.float32)
    b1 = np.asarray(inputs["L0_ffn_b1"], np.float32)
    b1a = np.pad(b1[:, :DIM_INNER], ((0, 0), (0, 22 * 128 - DIM_INNER))).reshape(NLAYER, 22, 128)
    b1g = np.pad(b1[:, DIM_INNER:], ((0, 0), (0, 22 * 128 - DIM_INNER))).reshape(NLAYER, 22, 128)
    m["b1cols"] = np.ascontiguousarray(np.stack([b1a, b1g], axis=1).reshape(NLAYER, 2, 22, 128, 1))
    m["b2cols"] = np.ascontiguousarray(
        np.asarray(inputs["L0_ffn_b2"], np.float32).reshape(NLAYER, 8, 128, 1))
    m["ffn_g"] = np.ascontiguousarray(
        np.asarray(inputs["L0_ffn_ln_g"], np.float32).reshape(NLAYER, 8, 128, 1))
    m["ffn_b"] = np.ascontiguousarray(
        np.asarray(inputs["L0_ffn_ln_b"], np.float32).reshape(NLAYER, 8, 128, 1))
    qs2 = np.tile(np.asarray(inputs["L0_ffn_q_scale"], np.float32), (1, 2))
    ks2 = np.tile(np.asarray(inputs["L0_ffn_k_scale"], np.float32), (1, 2))
    m["qk_scale"] = np.ascontiguousarray(
        np.stack([qs2, ks2], axis=1).reshape(NLAYER, 2, 128, 1))
    m["ones_col128"] = np.ones((128, 1), np.float32)
    return m


def phase_barrier(bld: Build):
    for q in ("sync", "scalar", "gpsimd"):
        fence(bld, q)
    sems = (bld.s_pe, bld.s_ve, bld.s_ac, bld.s_gp, bld.s_sy, bld.s_sg, bld.s_gq, bld.s_cc)
    for eng in ENGINES:
        for s in sems:
            if s.v:
                W8(bld, eng, s, s.v)


def tf_alloc(bld: Build):
    s = lambda n, sh, dt: bld.sbuf(n, sh, dt, scope=bld.es_tf)
    bld.wqk_buf = s("wqk_buf", [128, 8 * 512], F16)
    bld.wv_buf = s("wv_buf", [128, 8 * 512], F16)
    bld.wo_sb = s("wo_sb", [128, 8 * 1024], F16)
    bld.masks_sb = s("masks_sb", [128, 3 * 768], BF16)
    bld.i64h = s("i64h", [128, 64], BF16)
    bld.co_sb = s("co_sb", [128, 8 * 16], BF16)
    bld.coT_sb = s("coT_sb", [16, 8 * 128], BF16)
    bld.lcols = s("lcols", [128, 64], F32)
    bld.b1c = s("b1c", [128, 44], F32)
    bld.qbuf = s("qbuf", [128, 8 * 512], BF16)
    bld.kbuf = s("kbuf", [128, 8 * 768], BF16)
    bld.vbuf = s("vbuf", [128, 6 * 1024], F16)
    bld.rotA = s("rotA", [128, 512], BF16)
    bld.rotB = s("rotB", [128, 512], BF16)

    bld.q2t = s("q2t", [128, 512], BF16)
    bld.q2t2 = s("q2t2", [128, 512], BF16)
    bld.nrm = s("nrm", [16, 512], BF16)
    bld.Pbuf = s("Pbuf", [128, 4 * 384], BF16)
    bld.PTb = [s(f"PTb{i}", [128, 128], F16) for i in range(12)]
    bld.ofm = s("ofm", [128, 8 * 128], F16)
    bld.m_sb = s("m_sb", [128, 22 * 512], BF16)
    bld.xn_t = bld.qbuf
    bld.w1b = [s(f"w1b{i}", [128, 8 * 256], BF16) for i in range(2)]
    bld.w2b = [s(f"w2b{i}", [128, 1024], BF16) for i in range(2)]
    bld.ones16 = s("ones16", [128, 1], F16)
    bld.hsave = s("hsave", [128, 8 * 9], F16)
    bld.hdelta = s("hdelta", [128, 8], F16)


def emit_tf_setup(bld: Build):
    d = bld.dram
    E(bld, "gpsimd", lambda e: e.dma_start(
        bld.masks_sb[:].rearrange("p (s n) -> p s n", n=768),
        d["attn_masks"].rearrange("s p n -> p s n")), inc=bld.d_gp, n=16)
    E(bld, "gpsimd", lambda e: e.dma_start(bld.i64h[:], d["ident64h"]), inc=bld.d_gp, n=16)
    E(bld, "gpsimd", lambda e: e.dma_start(
        bld.co_sb[:].rearrange("p (j n) -> p j n", n=16),
        d["chunk_ones"].rearrange("j p n -> p j n")), inc=bld.d_gp, n=16)
    E(bld, "gpsimd", lambda e: e.dma_start(bld.ones16[:], d["ones_col128"]), inc=bld.d_gp, n=16)
    E(bld, "gpsimd", lambda e: e.memset(bld.kbuf[:], 0.0), inc=bld.s_gp)
    E(bld, "gpsimd", lambda e: e.memset(bld.vbuf[:], 0.0), inc=bld.s_gp)
    E(bld, "gpsimd", lambda e: e.dma_start(
        bld.coT_sb[:].rearrange("p (j n) -> p j n", n=128),
        d["chunk_onesT"].rearrange("j p n -> p j n")), inc=bld.d_gp, n=16)
    gq = fence(bld, "gpsimd")
    sy = fence(bld, "sync")
    for eng in ("tensor", "vector", "scalar"):
        W8(bld, eng, bld.s_gq, gq)
        W8(bld, eng, bld.s_sy, sy)
        W8(bld, eng, bld.s_gp, bld.s_gp.v)


LQ_QS, LQ_KS = 0, 1
LQ_G0 = 2
LQ_B0 = 10
LQ_B2 = 18


def emit_layer_loads(bld: Build, l):
    d = bld.dram

    guard_wait(bld, "gpsimd", "wo")
    for vc in range(8):
        bld.wo_tot = E(bld, "gpsimd", lambda e, vc=vc, l=l: e.dma_start(
            bld.wo_sb[:, 1024 * vc:1024 * (vc + 1)],
            d["Wo"][l][128 * vc:128 * (vc + 1), :]), inc=bld.d_gp, n=16)
    guard_wait(bld, "sync", "lcols")
    E(bld, "sync", lambda e, l=l: e.dma_start(bld.lcols[:, 0:1], d["qk_scale"][l][0]), inc=bld.d_sy, n=16)
    E(bld, "sync", lambda e, l=l: e.dma_start(bld.lcols[:, 1:2], d["qk_scale"][l][1]), inc=bld.d_sy, n=16)
    E(bld, "sync", lambda e, l=l: e.dma_start(
        bld.lcols[:, LQ_G0:LQ_G0 + 8].rearrange("p (c o) -> p c o", o=1),
        d["ffn_g"][l].rearrange("c p o -> p c o"), allow_slow_non_contiguous=True), inc=bld.d_sy, n=16)
    E(bld, "sync", lambda e, l=l: e.dma_start(
        bld.lcols[:, LQ_B0:LQ_B0 + 8].rearrange("p (c o) -> p c o", o=1),
        d["ffn_b"][l].rearrange("c p o -> p c o"), allow_slow_non_contiguous=True), inc=bld.d_sy, n=16)
    E(bld, "sync", lambda e, l=l: e.dma_start(
        bld.lcols[:, LQ_B2:LQ_B2 + 8].rearrange("p (c o) -> p c o", o=1),
        d["b2cols"][l].rearrange("c p o -> p c o"), allow_slow_non_contiguous=True), inc=bld.d_sy, n=16)
    E(bld, "sync", lambda e, l=l: e.dma_start(
        bld.b1c[:, 0:44].rearrange("p (s c o) -> p s c o", s=2, o=1),
        d["b1cols"][l].rearrange("s c p o -> p s c o"), allow_slow_non_contiguous=True), inc=bld.d_sy, n=16)
    return fence(bld, "sync")


def emit_halo_ag(bld: Build, l):
    d = bld.dram
    hv = bld.h[:].rearrange("p (c t) -> p c t", t=TLOC)
    W8(bld, "sync", bld.s_ve, bld.s_ve.v)
    E(bld, "sync", lambda e, hv=hv: e.dma_start(
        d["cc_h_in"].rearrange("(c p) t -> p c t", p=128), hv[:, :, TLOC - HALO:TLOC]),
      inc=bld.d_sy, n=16)
    agi = fence(bld, "sync")
    W8(bld, "gpsimd", bld.s_sy, agi)
    cc = E(bld, "gpsimd", lambda e: e.collective_compute(
        "AllGather", ALU.bypass, replica_groups=[[0, 1], [2, 3], [4, 5], [6, 7]],
        ins=[d["cc_h_in"].opt()], outs=[d["cc_h_out"].opt()]), inc=bld.s_cc)
    W8(bld, "sync", bld.s_cc, cc)
    guard_wait(bld, "sync", "vbuf")
    W8(bld, "sync", bld.s_pe, bld.s_pe.v)
    E(bld, "sync", lambda e: e.dma_start(
        bld.vbuf[:, 0:4096].rearrange("p (c t) -> p c t", t=512),
        d["cc_h_out"][0:1024, :].rearrange("(c p) t -> p c t", p=128)), inc=bld.d_sy, n=16)
    ld = fence(bld, "sync")
    W8(bld, "vector", bld.s_sy, ld)
    W8(bld, "vector", bld.s_pe, bld.s_pe.v)
    for kc in range(8):
        E(bld, "vector", lambda e, kc=kc, hv=hv: e.tensor_scalar_mul(
            hv[:, kc, 0:HALO], bld.vbuf[:, 512 * kc:512 * kc + HALO],
            bld.dn_cols[:, CL_SEL:CL_SEL + 1]))
    hf = E(bld, "vector", lambda e, hv=hv: e.tensor_copy(hv[:, 0, 0:1], hv[:, 0, 0:1]))
    guard_set(bld, "vbuf", (bld.s_ve, hf))
    return hf


def emit_attn_chunk(bld: Build, l, ci, pv_carry):
    d = bld.dram
    t0 = 512 * ci
    width = min(512, TLOC - t0)
    hv = bld.h[:].rearrange("p (c t) -> p c t", t=TLOC)


    guard_wait(bld, "vector", "ring")
    for kc in range(8):
        E(bld, "vector", lambda e, kc=kc: e.tensor_copy(
            bld.kbuf[:, 768 * kc:768 * kc + 256], bld.kbuf[:, 768 * kc + 512:768 * kc + 768]))
    for i in range(2):
        E(bld, "vector", lambda e, i=i: e.tensor_copy(
            bld.vbuf[:, 1024 * i:1024 * (i + 1)], bld.vbuf[:, 1024 * (i + 4):1024 * (i + 5)]))
    ring_v = bld.s_ve.v

    if ci >= 1:
        E(bld, "vector", lambda e, ci=ci, t0=t0: e.tensor_sub(
            bld.hdelta[:], bld.hsave[:].rearrange("p (c i) -> p c i", i=9)[:, :, ci - 1],
            bld.h[:].rearrange("p (c t) -> p c t", t=TLOC)[:, :, t0 - 1]), inc=bld.s_ve)
    delta_v = bld.s_ve.v

    nrm_bank = bld.pb[6]
    qkv_bufs = (bld.wqk_buf, bld.wv_buf)
    sq_v = {}
    tc_q = (ci == 0)

    def iss_qkv(idx, col0):
        p = idx % 2
        guard_wait(bld, "gpsimd", f"qkv{p}")
        pend = bld.wb_pend.pop(("qkv", p), None)
        if pend:
            W8(bld, "gpsimd", bld.d_wb, pend)
        src = d["Wqkv"] if tc_q else d["sc_wq"]
        tot = E(bld, "gpsimd", lambda e, l=l, col0=col0, p=p, src=src: e.dma_start(
            qkv_bufs[p][:], src[l].rearrange("(c p) n -> p c n", p=128)
            [:, :, col0:col0 + 512]), inc=bld.d_qk[p], n=16)
        if tc_q:
            W8(bld, "scalar", bld.d_qk[p], tot)
            wt = E(bld, "scalar", lambda e, l=l, col0=col0, p=p: e.dma_start(
                d["sc_wq"][l].rearrange("(c p) n -> p c n", p=128)[:, :, col0:col0 + 512],
                qkv_bufs[p][:].rearrange("p (c n) -> p c n", n=512)), inc=bld.d_wb, n=16)
            bld.wb_pend[("qkv", p)] = wt
        return tot

    for side, base_col in (("q", 0), ("k", 1024)):
        for mh in range(2):
          idx = (0 if side == "q" else 2) + mh
          qtot = iss_qkv(idx, base_col + 512 * mh)
          wbuf = qkv_bufs[idx % 2]
          W8(bld, "tensor", bld.d_qk[idx % 2], qtot)
          for m4 in range(4):
            m = 4 * mh + m4
            bank = bld.pb[m % 2]
            guard_wait(bld, "tensor", f"pb{m % 2}")
            if m == 0 and mh == 0 and side == "q":
                W8(bld, "tensor", bld.s_ve, ring_v)
            corr = ci >= 1
            if corr and m == 0:
                W8(bld, "tensor", bld.s_ve, delta_v)
            for kc in range(8):
                off = (1 if kc < 4 else 0)
                last = (kc == 7) and not corr
                inc = bld.s_pe if last else None
                E(bld, "tensor", lambda e, kc=kc, m4=m4, off=off, t0=t0, w=width,
                    bank=bank, last=last, wbuf=wbuf: e.matmul(
                    bank[:, 0:w], wbuf[:, 512 * kc + 128 * m4:512 * kc + 128 * (m4 + 1)],
                    bld.h[:, TLOC * kc + t0 - 1 + off: TLOC * kc + t0 - 1 + off + w],
                    start=(kc == 0), stop=last), inc=inc)
            if corr:
                for kc in range(4, 8):
                    last = kc == 7
                    inc = bld.s_pe if last else None
                    E(bld, "tensor", lambda e, kc=kc, m4=m4, bank=bank, last=last, wbuf=wbuf: e.matmul(
                        bank[:, 0:1], wbuf[:, 512 * kc + 128 * m4:512 * kc + 128 * (m4 + 1)],
                        bld.hdelta[:, kc:kc + 1], start=False, stop=last), inc=inc)
            mm_pe = bld.s_pe.v
            W8(bld, "vector", bld.s_pe, mm_pe)
            if side == "q":
                dst = bld.qbuf[:, 512 * m:512 * m + width]
            else:
                dst = bld.kbuf[:, 768 * m + 256:768 * m + 256 + width]
            ev = E(bld, "vector", lambda e, dst=dst, bank=bank, w=width: e.tensor_copy(
                dst, bank[:, 0:w]))
            W8(bld, "scalar", bld.s_ve, ev)
            slot = bld.q2t if m % 2 == 0 else bld.q2t2
            guard_wait(bld, "scalar", f"q2t{m % 2}")
            sq_v[m] = E(bld, "scalar", lambda e, dst=dst, w=width, slot=slot: e.activation(
                slot[:, 0:w], dst, AF.Square), inc=bld.s_ac)

            def nmm(mm, last):
                slot2 = bld.q2t if mm % 2 == 0 else bld.q2t2
                W8(bld, "tensor", bld.s_ac, sq_v[mm])
                if mm == 0:
                    guard_wait(bld, "tensor", "pb6")
                E(bld, "tensor", lambda e, mm=mm, w=width, slot2=slot2: e.matmul(
                    nrm_bank[0:16, 0:w], bld.co_sb[:, 16 * mm:16 * (mm + 1)], slot2[:, 0:w],
                    start=(mm == 0), stop=last), inc=(bld.s_pe if last else None))
                guard_set(bld, f"q2t{mm % 2}", (bld.s_pe, bld.s_pe.v))
            if m >= 1:
                nmm(m - 1, False)
            if m == 7:
                nmm(7, True)
            guard_set(bld, f"pb{m % 2}", (bld.s_ve, bld.s_ve.v))
          guard_set(bld, f"qkv{idx % 2}", (bld.s_pe, bld.s_pe.v))
        nq_pe = bld.s_pe.v
        W8(bld, "scalar", bld.s_pe, nq_pe)
        guard_wait(bld, "scalar", "nrm")
        sq2 = E(bld, "scalar", lambda e, w=width: e.activation(
            bld.nrm[:, 0:w], nrm_bank[0:16, 0:w], AF.Sqrt), inc=bld.s_ac)
        W8(bld, "vector", bld.s_ac, sq2)
        E(bld, "vector", lambda e, w=width: e.tensor_scalar_max(bld.nrm[:, 0:w], bld.nrm[:, 0:w], 1e-12))
        def _recip(e, w=width, bld=bld):
            with bld.nc.allow_low_precision(reason="l2norm recip in bf16 (bf16-matmul noise class)"):
                return e.reciprocal(bld.nrm[:, 0:w], bld.nrm[:, 0:w])
        rc = E(bld, "vector", _recip)
        guard_set(bld, "pb6", (bld.s_ac, sq2))
        scol = bld.lcols[:, LQ_QS:LQ_QS + 1] if side == "q" else bld.lcols[:, LQ_KS:LQ_KS + 1]
        rA, rB = bld.rotA, bld.rotB
        guard_wait(bld, "gpsimd", "rot")
        for dst, nm in ((rA, "rotAQ" if side == "q" else "rotAK"),
                        (rB, "rotBQ" if side == "q" else "rotBK")):
            rt_tot = E(bld, "gpsimd", lambda e, dst=dst, nm=nm, t0=t0, w=width: e.dma_start(
                dst[:, 0:w], d[nm][:, t0:t0 + w]), inc=bld.d_rt, n=16)
        for m in range(8):
            if side == "q":
                sl = bld.qbuf[:, 512 * m:512 * m + width]
            else:
                sl = bld.kbuf[:, 768 * m + 256:768 * m + 256 + width]
            guard_wait(bld, "tensor", "pb2")
            W8(bld, "tensor", bld.s_ve, rc)
            bc = E(bld, "tensor", lambda e, m=m, w=width: e.matmul(
                bld.pb[2][:, 0:w], bld.coT_sb[:, 128 * m:128 * (m + 1)], bld.nrm[:, 0:w],
                start=True, stop=True), inc=bld.s_pe)
            W8(bld, "vector", bld.s_pe, bc)
            E(bld, "vector", lambda e, sl=sl, scol=scol, w=width: e.scalar_tensor_tensor(
                sl, sl, scol, bld.pb[2][:, 0:w], ALU.mult, ALU.mult))
            guard_wait(bld, "tensor", "pb3")
            W8(bld, "tensor", bld.s_ve, bld.s_ve.v)
            rt = E(bld, "tensor", lambda e, sl=sl, w=width: e.matmul(
                bld.pb[3][:, 0:w], bld.rot_bf[:], sl, start=True, stop=True), inc=bld.s_pe)
            W8(bld, "vector", bld.s_pe, rt)
            W8(bld, "vector", bld.d_rt, rt_tot)
            guard_wait(bld, "vector", "q2t0")
            E(bld, "vector", lambda e, sl=sl, rA=rA, w=width: e.tensor_mul(
                bld.q2t[:, 0:w], sl, rA[:, 0:w]))
            E(bld, "vector", lambda e, sl=sl, rB=rB, w=width: e.tensor_tensor(
                sl, bld.pb[3][:, 0:w], rB[:, 0:w], ALU.mult))
            qr = E(bld, "vector", lambda e, sl=sl, w=width: e.tensor_add(sl, sl, bld.q2t[:, 0:w]))
            guard_set(bld, "pb2", (bld.s_ve, bld.s_ve.v))
            guard_set(bld, "pb3", (bld.s_ve, qr))
            guard_set(bld, "q2t0", (bld.s_ve, qr))
        guard_set(bld, "rot", (bld.s_ve, bld.s_ve.v))
    qk_done = bld.s_ve.v

    for half in range(2):
        idx = 4 + half
        vtot = iss_qkv(idx, 2048 + 512 * half)
        wbuf = qkv_bufs[idx % 2]
        W8(bld, "tensor", bld.d_qk[idx % 2], vtot)
        for tt in range(ceil_div(width, 128)):
            vt = 2 + tt
            bank = bld.pb[4 + tt % 2]
            guard_wait(bld, "tensor", f"pb{4 + tt % 2}")
            vcorr = (ci >= 1) and (tt == 0)
            for kc in range(8):
                off = (1 if kc < 4 else 0)
                last = (kc == 7) and not vcorr
                inc = bld.s_pe if last else None
                E(bld, "tensor", lambda e, kc=kc, tt=tt, off=off, t0=t0, bank=bank,
                    last=last, wbuf=wbuf: e.matmul(
                    bank[:, 0:512],
                    bld.h[:, TLOC * kc + t0 - 1 + off + 128 * tt: TLOC * kc + t0 - 1 + off + 128 * (tt + 1)],
                    wbuf[:, 512 * kc:512 * (kc + 1)],
                    start=(kc == 0), stop=last), inc=inc)
            if vcorr:
                for kc in range(4, 8):
                    last = kc == 7
                    inc = bld.s_pe if last else None
                    E(bld, "tensor", lambda e, kc=kc, bank=bank, last=last, wbuf=wbuf: e.matmul(
                        bank[0:1, 0:512], bld.hdelta[:, kc:kc + 1],
                        wbuf[:, 512 * kc:512 * (kc + 1)],
                        start=False, stop=last), inc=inc)
            v_pe = bld.s_pe.v
            W8(bld, "vector", bld.s_pe, v_pe)
            ev = E(bld, "vector", lambda e, vt=vt, hf=half, bank=bank: e.tensor_copy(
                bld.vbuf[:, 1024 * vt + 512 * hf:1024 * vt + 512 * (hf + 1)], bank[:, 0:512]))
            guard_set(bld, f"pb{4 + tt % 2}", (bld.s_ve, ev))
        guard_set(bld, f"qkv{idx % 2}", (bld.s_pe, bld.s_pe.v))
    v_done = bld.s_ve.v
    return qk_done, v_done


def emit_attn_qtile(bld: Build, l, ci, qt, qk_done, v_done):
    Q = 4 * ci + qt
    qoff = 128 * qt
    slot = {4: 1, 5: 2}.get(Q, 0)
    c = bld.cols
    pn_v, tr_v, ev_v = {}, {}, {}
    state = {"first_pv": True}

    def s_phase(h):
        par = h % 2
        pj = h // 2
        bank = bld.pb[2 + par]
        guard_wait(bld, "tensor", f"pb{2 + par}")
        if h < 2:
            W8(bld, "tensor", bld.s_ve, qk_done)
        E(bld, "tensor", lambda e, par=par, slot=slot: e.matmul(
            bld.pb[2 + par][0:64, 0:384], bld.i64h[64 * par:64 * (par + 1), :],
            bld.masks_sb[64 * par:64 * (par + 1), 768 * slot:768 * slot + 384],
            start=True, stop=False, tile_position=(64 * par, 0)))
        E(bld, "tensor", lambda e, par=par, slot=slot: e.matmul(
            bld.pb[2 + par][64:128, 0:384], bld.i64h[64 * par:64 * (par + 1), :],
            bld.masks_sb[64 * par:64 * (par + 1), 768 * slot + 384:768 * slot + 768],
            start=True, stop=False, tile_position=(64 * par, 64)))
        smm = E(bld, "tensor", lambda e, par=par, pj=pj, qoff=qoff: e.matmul(
            bld.pb[2 + par][:, 0:384],
            bld.qbuf[64 * par:64 * (par + 1), 512 * pj + qoff:512 * pj + qoff + 128],
            bld.kbuf[64 * par:64 * (par + 1), 768 * pj + qoff:768 * pj + qoff + 384],
            start=False, stop=True, tile_position=(64 * par, 0)), inc=bld.s_pe)
        W8(bld, "vector", bld.s_pe, smm)
        E(bld, "vector", lambda e, bank=bank: e.reduce_max(c[:, 12:13], bank[:, 0:384],
                                                           axis=AX.X, negate=True))
        nm = E(bld, "vector", lambda e: e.tensor_scalar_mul(c[:, 13:14], c[:, 12:13], 8.0))
        W8(bld, "scalar", bld.s_ve, nm)
        guard_wait(bld, "scalar", f"P{h % 4}")
        ex = E(bld, "scalar", lambda e, h=h, bank=bank: e.activation(
            bld.Pbuf[:, 384 * (h % 4):384 * (h % 4) + 384], bank[:, 0:384], AF.Exp,
            bias=c[:, 13:14], scale=8.0, accum_out=c[:, 14:15]), inc=bld.s_ac)
        W8(bld, "vector", bld.s_ac, ex)
        E(bld, "vector", lambda e: e.reciprocal(c[:, 15:16], c[:, 14:15]))
        pn_v[h] = E(bld, "vector", lambda e, h=h: e.tensor_scalar_mul(
            bld.Pbuf[:, 384 * (h % 4):384 * (h % 4) + 384],
            bld.Pbuf[:, 384 * (h % 4):384 * (h % 4) + 384], c[:, 15:16]))
        guard_set(bld, f"pb{2 + par}", (bld.s_ac, ex))

    def tr_phase(h):
        guard_wait(bld, "tensor", f"pb{h % 2}")
        W8(bld, "tensor", bld.s_ve, pn_v[h])
        for j in range(3):
            inc = bld.s_pe if j == 2 else None
            E(bld, "tensor", lambda e, h=h, j=j: e.transpose(
                bld.pb[h % 2][:].bitcast(BF16)[:, 128 * j:128 * (j + 1)],
                bld.Pbuf[:, 384 * (h % 4) + 128 * j:384 * (h % 4) + 128 * (j + 1)],
                bld.ident_bf[:]), inc=inc)
        tr_v[h] = bld.s_pe.v
        guard_set(bld, f"P{h % 4}", (bld.s_pe, tr_v[h]))
        W8(bld, "scalar", bld.s_pe, tr_v[h])
        for j in range(3):
            ev_v[h] = E(bld, "scalar", lambda e, h=h, j=j: e.activation(
                bld.PTb[(h % 4) * 3 + j][:],
                bld.pb[h % 2][:].bitcast(BF16)[:, 128 * j:128 * (j + 1)], AF.Copy),
                inc=bld.s_ac)
        guard_set(bld, f"pb{h % 2}", (bld.s_ac, ev_v[h]))

    def pv_phase(h):
        par = h % 2
        pj = h // 2
        bnk = 4 + (pj % 2)
        cslot = pj // 2
        if state["first_pv"]:
            guard_wait(bld, "tensor", "pb4")
            guard_wait(bld, "tensor", "pb5")
            W8(bld, "tensor", bld.s_ve, v_done)
            state["first_pv"] = False
        W8(bld, "tensor", bld.s_ac, ev_v[h])
        for kc in range(3):
            inc = bld.s_pe if kc == 2 else None
            E(bld, "tensor", lambda e, h=h, kc=kc, par=par, bnk=bnk, cslot=cslot: e.matmul(
                bld.pb[bnk][64 * par:64 * (par + 1), 128 * cslot:128 * (cslot + 1)],
                bld.vbuf[:, 1024 * (qt + kc) + 64 * h:1024 * (qt + kc) + 64 * (h + 1)],
                bld.PTb[(h % 4) * 3 + kc][:], start=(kc == 0), stop=(kc == 2),
                tile_position=(0, 64 * par)), inc=inc)

    for i in range(NH + 2):
        if i >= 2:
            tr_phase(i - 2)
        if i < NH:
            s_phase(i)
        if i >= 2:
            pv_phase(i - 2)
    pv_pe = bld.s_pe.v
    W8(bld, "vector", bld.s_pe, pv_pe)
    guard_wait(bld, "vector", "ofm")
    for b2 in range(2):
        dv = bld.ofm[:].rearrange("p (c b t) -> p c b t", b=2, t=128)[:, :, b2, :]
        E(bld, "vector", lambda e, dv=dv, b2=b2: e.tensor_copy(
            dv, bld.pb[4 + b2][:].rearrange("p (c t) -> p c t", t=128)))
    ev = bld.s_ve.v
    guard_set(bld, "pb4", (bld.s_ve, ev))
    guard_set(bld, "pb5", (bld.s_ve, ev))
    return ev


def emit_wo_qtile(bld: Build, l, ci, qt, ofm_v):
    Q = 4 * ci + qt
    t0 = 128 * Q
    for dm in range(8):
        bank = bld.pb[2 + dm % 2]
        guard_wait(bld, "tensor", f"pb{2 + dm % 2}")
        if dm == 0:
            W8(bld, "tensor", bld.s_ve, ofm_v)
            W8(bld, "tensor", bld.d_gp, bld.wo_tot)
        for vc in range(8):
            inc = bld.s_pe if vc == 7 else None
            E(bld, "tensor", lambda e, dm=dm, vc=vc, bank=bank: e.matmul(
                bank[:, 0:128], bld.wo_sb[:, 1024 * vc + 128 * dm:1024 * vc + 128 * (dm + 1)],
                bld.ofm[:, 128 * vc:128 * (vc + 1)],
                start=(vc == 0), stop=(vc == 7)), inc=inc)
        wo_pe = bld.s_pe.v
        W8(bld, "vector", bld.s_pe, wo_pe)
        rs = E(bld, "vector", lambda e, dm=dm, t0=t0, bank=bank: e.tensor_tensor(
            bld.h[:, TLOC * dm + t0:TLOC * dm + t0 + 128],
            bld.h[:, TLOC * dm + t0:TLOC * dm + t0 + 128], bank[:, 0:128], ALU.add))
        guard_set(bld, f"pb{2 + dm % 2}", (bld.s_ve, rs))
    guard_set(bld, "ofm", (bld.s_pe, bld.s_pe.v))
    return bld.s_ve.v



def emit_ffn_ttile(bld: Build, l, ti):
    d = bld.dram
    t0 = HALO + 512 * ti
    c2 = bld.x2
    NJ = 22

    w1tot, w2tot = {}, {}
    tc = (ti == 7)

    def iss_w1(jp):
        p = jp % 2
        jw = 128 if jp < 21 else DIM_INNER - 128 * 21
        buf = bld.w1b[p]
        guard_wait(bld, "gpsimd", f"w1b{p}")
        pend = bld.wb_pend.pop(("w1", p), None)
        if pend:
            W8(bld, "gpsimd", bld.d_wb, pend)
        srcw = d["W1"] if tc else d["sc_w1"]
        E(bld, "gpsimd", lambda e, l=l, jp=jp, jw=jw, buf=buf, srcw=srcw: e.dma_start(
            buf[:].rearrange("p (c s j) -> p c s j", s=2, j=128)[:, :, 0, 0:jw],
            srcw[l].rearrange("(c p) n -> p c n", p=128)[:, :, 128 * jp:128 * jp + jw]),
            inc=bld.d_w1p[p], n=16)
        w1tot[jp] = E(bld, "gpsimd", lambda e, l=l, jp=jp, jw=jw, buf=buf, srcw=srcw: e.dma_start(
            buf[:].rearrange("p (c s j) -> p c s j", s=2, j=128)[:, :, 1, 0:jw],
            srcw[l].rearrange("(c p) n -> p c n", p=128)
            [:, :, DIM_INNER + 128 * jp:DIM_INNER + 128 * jp + jw]),
            inc=bld.d_w1p[p], n=16)
        if tc:
            W8(bld, "scalar", bld.d_w1p[p], w1tot[jp])
            for s2 in range(2):
                wt = E(bld, "scalar", lambda e, l=l, jp=jp, jw=jw, buf=buf, s2=s2: e.dma_start(
                    d["sc_w1"][l].rearrange("(c p) n -> p c n", p=128)
                    [:, :, DIM_INNER * s2 + 128 * jp:DIM_INNER * s2 + 128 * jp + jw],
                    buf[:].rearrange("p (c s j) -> p c s j", s=2, j=128)[:, :, s2, 0:jw]),
                    inc=bld.d_wb, n=16)
            bld.wb_pend[("w1", p)] = wt

    def iss_w2(jp):
        p = jp % 2
        jw = 128 if jp < 21 else DIM_INNER - 128 * 21
        buf = bld.w2b[p]
        qn = "gpsimd" if tc else "scalar"
        guard_wait(bld, qn, f"w2b{p}")
        pend = bld.wb_pend.pop(("w2", p), None)
        if pend:
            W8(bld, qn, bld.d_wb, pend)
        if tc:
            w2tot[jp] = E(bld, "gpsimd", lambda e, l=l, jp=jp, jw=jw, buf=buf: e.dma_start(
                buf[0:jw, :], d["W2"][l][128 * jp:128 * jp + jw, :]), inc=bld.d_w2p[p], n=16)
            W8(bld, "scalar", bld.d_w2p[p], w2tot[jp])
            wt = E(bld, "scalar", lambda e, l=l, jp=jp, jw=jw, buf=buf: e.dma_start(
                d["sc_w2"][l][128 * jp:128 * jp + jw, :], buf[0:jw, :]), inc=bld.d_wb, n=16)
            bld.wb_pend[("w2", p)] = wt
        else:
            if jp < 2:
                W8(bld, "scalar", bld.d_wb, bld.d_wb.v)
            w2tot[jp] = E(bld, "scalar", lambda e, l=l, jp=jp, jw=jw, buf=buf: e.dma_start(
                buf[0:jw, :], d["sc_w2"][l][128 * jp:128 * jp + jw, :]), inc=bld.d_w2p[p], n=16)

    iss_w1(0); iss_w1(1); iss_w2(0); iss_w2(1)

    guard_wait(bld, "scalar", "xn_t")
    for kc in range(8):
        off = (1 if kc < 4 else 0)
        inc = bld.s_ac if kc == 7 else None
        E(bld, "scalar", lambda e, kc=kc, off=off, t0=t0: e.activation(
            bld.xn_t[:, 512 * kc:512 * (kc + 1)],
            bld.h[:, TLOC * kc + t0 - 1 + off:TLOC * kc + t0 - 1 + off + 512], AF.Square),
            inc=inc)
    sq_a = bld.s_ac.v
    guard_wait(bld, "tensor", "pb6")
    guard_wait(bld, "tensor", "pb2")
    W8(bld, "tensor", bld.s_ac, sq_a)
    for kc in range(8):
        off = (1 if kc < 4 else 0)
        E(bld, "tensor", lambda e, kc=kc, off=off, t0=t0: e.matmul(
            bld.pb[6][0:1, 0:512], bld.ones16[:],
            bld.h[:, TLOC * kc + t0 - 1 + off:TLOC * kc + t0 - 1 + off + 512],
            start=(kc == 0), stop=(kc == 7)))
    for kc in range(8):
        inc = bld.s_pe if kc == 7 else None
        E(bld, "tensor", lambda e, kc=kc: e.matmul(
            bld.pb[2][0:1, 0:512], bld.ones16[:], bld.xn_t[:, 512 * kc:512 * (kc + 1)],
            start=(kc == 0), stop=(kc == 7)), inc=inc)
    st_pe = bld.s_pe.v
    W8(bld, "vector", bld.s_pe, st_pe)
    guard_wait(bld, "vector", "x2rows")
    A = c2[0:1, 0:512]
    Brow = c2[0:1, 512:1024]
    E(bld, "vector", lambda e: e.tensor_scalar_mul(A, bld.pb[6][0:1, 0:512], 1.0 / D))
    E(bld, "vector", lambda e: e.tensor_scalar_mul(Brow, bld.pb[2][0:1, 0:512], 1.0 / D))
    E(bld, "vector", lambda e: e.tensor_mul(bld.pb[6][0:1, 0:512], A, A))
    E(bld, "vector", lambda e: e.tensor_sub(Brow, Brow, bld.pb[6][0:1, 0:512]))
    sv = E(bld, "vector", lambda e: e.tensor_scalar_add(Brow, Brow, LN_EPS))
    W8(bld, "scalar", bld.s_ve, sv)
    sa = E(bld, "scalar", lambda e: e.activation(bld.pb[6][0:1, 0:512], Brow, AF.Sqrt),
           inc=bld.s_ac)
    W8(bld, "vector", bld.s_ac, sa)
    E(bld, "vector", lambda e: e.reciprocal(Brow, bld.pb[6][0:1, 0:512]))
    E(bld, "vector", lambda e: e.tensor_mul(A, A, Brow))
    nr = E(bld, "vector", lambda e: e.tensor_scalar_mul(A, A, -1.0))
    guard_wait(bld, "tensor", "pb3")
    W8(bld, "tensor", bld.s_ve, nr)
    E(bld, "tensor", lambda e: e.matmul(bld.pb[2][:, 0:512], bld.ones_row_f32[:],
                                        Brow, start=True, stop=True))
    bc = E(bld, "tensor", lambda e: e.matmul(bld.pb[3][:, 0:512], bld.ones_row_f32[:],
                                             A, start=True, stop=True), inc=bld.s_pe)
    guard_set(bld, "x2rows", (bld.s_pe, bc))
    W8(bld, "vector", bld.s_pe, bc)
    lnt = bld.Pbuf[:, 0:512]
    for kc in range(8):
        off = (1 if kc < 4 else 0)
        E(bld, "vector", lambda e, kc=kc, off=off, t0=t0: e.tensor_tensor(
            lnt, bld.h[:, TLOC * kc + t0 - 1 + off:TLOC * kc + t0 - 1 + off + 512],
            bld.pb[2][:, 0:512], ALU.mult))
        E(bld, "vector", lambda e: e.tensor_tensor(lnt, lnt, bld.pb[3][:, 0:512], ALU.add))
        E(bld, "vector", lambda e, kc=kc: e.tensor_scalar(
            bld.xn_t[:, 512 * kc:512 * (kc + 1)], lnt,
            bld.lcols[:, LQ_G0 + kc:LQ_G0 + kc + 1], bld.lcols[:, LQ_B0 + kc:LQ_B0 + kc + 1],
            ALU.mult, ALU.add))
    xn_v = bld.s_ve.v
    guard_set(bld, "pb2", (bld.s_ve, xn_v))
    guard_set(bld, "pb3", (bld.s_ve, xn_v))
    guard_set(bld, "xn_t", (bld.s_ve, xn_v))

    if True:
      for jp in range(NJ):
        jw = 128 if jp < 21 else DIM_INNER - 128 * 21
        buf = bld.w1b[jp % 2]
        for s2, bank in ((0, 0), (1, 1)):
            guard_wait(bld, "tensor", f"pb{bank}")
            if s2 == 0:
                W8(bld, "tensor", bld.d_w1p[jp % 2], w1tot[jp])
                if jp == 0:
                    W8(bld, "tensor", bld.s_ve, xn_v)
            for kc in range(8):
                inc = bld.s_pe if kc == 7 else None
                E(bld, "tensor", lambda e, kc=kc, s2=s2, jw=jw, bank=bank, buf=buf: e.matmul(
                    bld.pb[bank][0:jw, 0:512], buf[:, 256 * kc + 128 * s2:256 * kc + 128 * s2 + jw],
                    bld.xn_t[:, 512 * kc:512 * (kc + 1)], start=(kc == 0), stop=(kc == 7)), inc=inc)
        hg_pe = bld.s_pe.v
        guard_set(bld, f"w1b{jp % 2}", (bld.s_pe, hg_pe))
        if jp + 2 < NJ:
            iss_w1(jp + 2)
        W8(bld, "scalar", bld.s_pe, hg_pe)
        guard_wait(bld, "scalar", "Pb2")
        g1 = E(bld, "scalar", lambda e, jp=jp, jw=jw: e.activation(
            bld.Pbuf[:, 512:512 + 512][0:jw, :], bld.pb[1][0:jw, 0:512], AF.Gelu,
            bias=bld.b1c[0:jw, 22 + jp:23 + jp]), inc=bld.s_ac)
        W8(bld, "vector", bld.s_ac, g1)
        guard_wait(bld, "vector", f"m{jp % 2}")
        mv = E(bld, "vector", lambda e, jp=jp, jw=jw: e.scalar_tensor_tensor(
            bld.m_sb[0:jw, 512 * jp:512 * (jp + 1)], bld.pb[0][0:jw, 0:512],
            bld.b1c[0:jw, jp:jp + 1], bld.Pbuf[:, 512:1024][0:jw, :], ALU.add, ALU.mult))
        guard_set(bld, "pb0", (bld.s_ve, mv))
        guard_set(bld, "pb1", (bld.s_ac, g1))
        guard_set(bld, "Pb2", (bld.s_ve, mv))
    m_v = bld.s_ve.v

    pbanks = [bld.pb[i][:] for i in range(7)] + [bld.pbt[:].bitcast(F32)]
    for b in range(7):
        guard_wait(bld, "tensor", f"pb{b}")
    guard_wait(bld, "tensor", "pbt")
    W8(bld, "tensor", bld.s_ve, m_v)
    for jp in range(NJ):
        jw = 128 if jp < 21 else DIM_INNER - 128 * 21
        buf = bld.w2b[jp % 2]
        W8(bld, "tensor", bld.d_w2p[jp % 2], w2tot[jp])
        for dm in range(8):
            inc = bld.s_pe if dm == 7 else None
            E(bld, "tensor", lambda e, jp=jp, jw=jw, dm=dm, buf=buf, pbanks=pbanks: e.matmul(
                pbanks[dm][:, 0:512], buf[0:jw, 128 * dm:128 * (dm + 1)],
                bld.m_sb[0:jw, 512 * jp:512 * (jp + 1)],
                start=(jp == 0), stop=(jp == NJ - 1)), inc=inc)
        guard_set(bld, f"w2b{jp % 2}", (bld.s_pe, bld.s_pe.v))
        if jp + 2 < NJ:
            iss_w2(jp + 2)
    w2_pe = bld.s_pe.v
    guard_set(bld, "w2b0", (bld.s_pe, w2_pe))
    guard_set(bld, "w2b1", (bld.s_pe, w2_pe))
    guard_set(bld, "m0", (bld.s_pe, w2_pe))
    guard_set(bld, "m1", (bld.s_pe, w2_pe))
    W8(bld, "vector", bld.s_pe, w2_pe)
    for dm in range(8):
        E(bld, "vector", lambda e, dm=dm, t0=t0, pbanks=pbanks: e.scalar_tensor_tensor(
            bld.h[:, TLOC * dm + t0:TLOC * dm + t0 + 512], pbanks[dm][:, 0:512],
            bld.lcols[:, LQ_B2 + dm:LQ_B2 + dm + 1],
            bld.h[:, TLOC * dm + t0:TLOC * dm + t0 + 512], ALU.add, ALU.add))
    res_v = bld.s_ve.v
    for b in range(7):
        guard_set(bld, f"pb{b}", (bld.s_ve, res_v))
    guard_set(bld, "pbt", (bld.s_ve, res_v))
    guard_set(bld, "pb4", (bld.s_ve, res_v))
    guard_set(bld, "pb5", (bld.s_ve, res_v))
    return res_v


def emit_layer(bld: Build, l, skip_ffn=False):
    sy = emit_layer_loads(bld, l)
    if l > 0:
        emit_halo_ag(bld, l)
    for eng in ("tensor", "vector", "scalar"):
        W8(bld, eng, bld.s_sy, sy)
    hsv = bld.h[:].rearrange("p (c t) -> p c t", t=TLOC)
    E(bld, "vector", lambda e, hsv=hsv: e.tensor_copy(
        bld.hsave[:].rearrange("p (c i) -> p c i", i=9), hsv[:, :, 511::512]), inc=bld.s_ve)
    for ci in range(9):
        qk_done, v_done = emit_attn_chunk(bld, l, ci, None)
        for qt in range(4):
            Q = 4 * ci + qt
            if Q < 3:
                continue
            ev = emit_attn_qtile(bld, l, ci, qt, qk_done, v_done)
            emit_wo_qtile(bld, l, ci, qt, ev)
        guard_set(bld, "ring", (bld.s_pe, bld.s_pe.v))
    guard_set(bld, "wqkv", (bld.s_pe, bld.s_pe.v))
    guard_set(bld, "wo", (bld.s_pe, bld.s_pe.v))
    if not skip_ffn:
        for ti in reversed(range(8)):
            emit_ffn_ttile(bld, l, ti)
    guard_set(bld, "lcols", (bld.s_ve, bld.s_ve.v))


def emit_final(bld: Build):
    d = bld.dram
    s = lambda n, sh, dt: bld.sbuf(n, sh, dt, scope=bld.es_fin)
    xld = [s(f"xld{i}", [128, D], F32) for i in range(2)]
    sld = [s(f"sld{i}", [128, D], BF16) for i in range(2)]
    sgb = s("sgb", [128, D], BF16)
    orow = s("orow", [128, D], F32)
    for ti in range(32):
        slot = ti % 2
        t0 = HALO + 128 * ti
        guard_wait(bld, "sync", f"fxld{slot}")
        E(bld, "sync", lambda e, ti=ti, t0=t0, slot=slot: e.dma_start(
            xld[slot][:], d["x_in"][t0:t0 + 128, :]), inc=bld.d_sy, n=16)
        E(bld, "sync", lambda e, ti=ti, slot=slot: e.dma_start(
            sld[slot][:], d["sc_s"][128 * ti:128 * (ti + 1), :]), inc=bld.d_sy, n=16)
        lf = fence(bld, "sync")
        guard_wait(bld, "scalar", "sgb")
        for kc in range(8):
            inc = bld.s_ac if kc == 7 else None
            E(bld, "scalar", lambda e, kc=kc, t0=t0: e.activation(
                sgb[:, 128 * kc:128 * (kc + 1)], bld.h[:, TLOC * kc + t0:TLOC * kc + t0 + 128],
                AF.Silu), inc=inc)
        sg_a = bld.s_ac.v
        guard_wait(bld, "tensor", "pbt")
        W8(bld, "tensor", bld.s_ac, sg_a)
        for kc in range(8):
            inc = bld.s_pe if kc == 7 else None
            E(bld, "tensor", lambda e, kc=kc: e.transpose(
                bld.pbt[:, 128 * kc:128 * (kc + 1)], sgb[:, 128 * kc:128 * (kc + 1)],
                bld.ident_bf[:]), inc=inc)
        tr_pe = bld.s_pe.v
        guard_set(bld, "sgb", (bld.s_pe, tr_pe))
        W8(bld, "vector", bld.s_pe, tr_pe)
        W8(bld, "vector", bld.s_sy, lf)
        guard_wait(bld, "vector", "orow")
        E(bld, "vector", lambda e, slot=slot: e.tensor_mul(orow[:], sld[slot][:], bld.pbt[:]))
        ov = E(bld, "vector", lambda e, slot=slot: e.tensor_add(orow[:], orow[:], xld[slot][:]))
        guard_set(bld, "pbt", (bld.s_ve, ov))
        guard_set(bld, f"fxld{slot}", (bld.s_ve, ov))
        W8(bld, "sync", bld.s_ve, ov)
        E(bld, "sync", lambda e, ti=ti: e.dma_start(
            d["out"][128 * ti:128 * (ti + 1), :], orow[:]), inc=bld.d_sy, n=16)
        of = fence(bld, "sync")
        guard_set(bld, "orow", (bld.s_sy, of))


def build_full(n_layers=NLAYER, dbg=False, skip_ffn=False):
    bld = Build()
    build_common(bld)
    if dbg:
        bld.dout("dbg_h", [128, 8 * TLOC], F16)
        bld.dout("dbg_s", [4096, D], BF16)
        bld.dout("dbg_p", [128, 16 * 384], F16)
        bld.dout("dbg_ofm", [128, 8 * 128], F16)
    emit_setup(bld)
    emit_dn_phase(bld)
    phase_barrier(bld)
    bld.es_dn.close()
    tf_alloc(bld)
    emit_tf_setup(bld)
    for l in range(n_layers):
        emit_layer(bld, l, skip_ffn=skip_ffn)
    phase_barrier(bld)
    if dbg:
        for j in range(8):
            E(bld, "sync", lambda e, j=j: e.dma_start(
                bld.dram["dbg_h"][:, TLOC * j:TLOC * (j + 1)],
                bld.h[:, TLOC * j:TLOC * (j + 1)]), inc=bld.d_sy, n=16)
        for i in range(4):
            E(bld, "sync", lambda e, i=i: e.dma_start(
                bld.dram["dbg_s"][1024 * i:1024 * (i + 1), :],
                bld.dram["sc_s"][1024 * i:1024 * (i + 1), :]), inc=bld.d_sy, n=16)
        E(bld, "sync", lambda e: e.dma_start(bld.dram["dbg_p"], bld.dram["sc_p"][10]),
          inc=bld.d_sy, n=16)
        fence(bld, "sync")
    bld.es_tf.close()
    bld.es_fin = contextlib.ExitStack()
    emit_final(bld)
    finish_program(bld)
    with bld.es, bld.es_fin, bld.nc.Block() as blk:
        bld.prog.play(blk)
    return bld


_BUILD_CACHE = {}


def kernel(**inputs):
    if "bld" not in _BUILD_CACHE:
        _BUILD_CACHE["bld"] = build_full()
    bld = _BUILD_CACHE["bld"]
    in_maps = [core_input_map(inputs, c) for c in range(8)]
    res = run_bass_kernel_spmd(bld.nc, in_maps, core_ids=list(range(8)))
    out = np.empty((B, L, D), np.float32)
    for c in range(8):
        b, half = c // 2, c % 2
        out[b, 4096 * half:4096 * (half + 1)] = res.results[c]["out"]
    return out


# version 06
# speedup vs baseline: 1.1195x; 1.1195x over previous
import math
import contextlib
import numpy as np
import ml_dtypes
import concourse.bass as bass
import concourse.mybir as mybir
from concourse.bass_utils import run_bass_kernel_spmd

F32 = mybir.dt.float32
BF16 = mybir.dt.bfloat16
F16 = mybir.dt.float16
AF = mybir.ActivationFunctionType
ALU = mybir.AluOpType
AX = mybir.AxisListType

B, L, D, NH, dh = 4, 8192, 1024, 16, 64
HALO = 512
TLOC = 4096 + HALO
NT = TLOC // 128
C = 128
NCH = 4096 // C
WINDOW = 256
QK_SCALE = 8.0
DIM_INNER = 2730
NLAYER = 6
LN_EPS = 1e-5

DEBUG = {}


def _host_consts(core_id):
    half = core_id % 2
    gstart = 4096 * half - HALO
    c = {}
    c["ident128"] = np.eye(128, dtype=np.float32)
    Rm = np.zeros((128, 128), np.float32)
    for hb in range(2):
        o = 64 * hb
        for p in range(32):
            Rm[o + p + 32, o + p] = -1.0
            Rm[o + p, o + p + 32] = 1.0
    c["rotmat"] = Rm
    halfv = np.arange(0, dh, 2, dtype=np.float64)
    inv_freq = 1.0 / 10000.0 ** (halfv / dh)
    t = np.arange(gstart, gstart + TLOC, dtype=np.float64)
    freqs = t[:, None] * inv_freq[None, :]
    freqs = np.concatenate([freqs, freqs], -1)
    base_scale = (halfv + 0.4 * dh) / (1.4 * dh)
    power = (t - L // 2) / (WINDOW // 2)
    scale = base_scale[None, :] ** power[:, None]
    scale = np.concatenate([scale, scale], -1)
    cos, sin = np.cos(freqs), np.sin(freqs)
    for nm, tab in [("rotAQ", cos * scale), ("rotBQ", sin * scale),
                    ("rotAK", cos / scale), ("rotBK", sin / scale)]:
        tabf = np.asarray(tab.T, np.float32)
        c[nm] = np.concatenate([tabf, tabf], axis=0)
    r = np.arange(128)[:, None]
    cc_ = np.arange(384)[None, :]
    def mkmask(lo):
        allowed = (cc_ >= r) & (cc_ <= r + 256)
        if lo is not None:
            allowed &= (cc_ >= lo)
            full_row_dead = (~allowed).all(axis=1, keepdims=True)
            allowed = np.where(full_row_dead, (cc_ >= r) & (cc_ <= r + 256), allowed)
        return np.where(allowed, 0.0, -1e30).astype(np.float32)
    masks = [mkmask(None)]
    for qt in (4, 5):
        lo = 256 - (gstart + 128 * qt)
        masks.append(mkmask(lo if lo > 0 else None))
    mm = np.stack(masks)
    half_cat = np.concatenate([mm[:, 0:64, :], mm[:, 64:128, :]], axis=2)
    c["attn_masks"] = np.concatenate([half_cat, half_cat], axis=1)
    c["sel"] = np.full((128, 1), 1.0 if half == 1 else 0.0, np.float32)
    tl = np.arange(C, dtype=np.float32)
    tim = tl[:, None] - tl[None, :]
    c["d_strict"] = np.where(tim > 0, tim, 0.0).astype(np.float32)
    c["m_strict"] = (tim > 0).astype(np.float32)
    c["d_strictT"] = np.ascontiguousarray(c["d_strict"].T)
    c["m_strictT"] = np.ascontiguousarray(c["m_strict"].T)
    c["d_inclT"] = np.where(tim >= 0, tim, 0.0).astype(np.float32).T.copy()
    c["m_inclT"] = (tim >= 0).astype(np.float32).T.copy()
    c["tcol"] = (tl + 1.0).reshape(128, 1).astype(np.float32)
    c["tbarcol"] = (C - (tl + 1.0)).reshape(128, 1).astype(np.float32)
    i64p = np.zeros((128, 64), np.float32)
    i64p[0:64] = np.eye(64)
    i64p[64:128] = np.eye(64)
    c["i64pair"] = i64p
    c["ones_row"] = np.ones((1, 128), np.float32)
    co = np.zeros((8, 128, 16), np.float32)
    for j in range(8):
        co[j, 0:64, 2 * j] = 1.0
        co[j, 64:128, 2 * j + 1] = 1.0
    c["chunk_ones"] = co
    c["chunk_onesT"] = np.ascontiguousarray(co.transpose(0, 2, 1))
    c["ident64h"] = np.concatenate([np.eye(64, dtype=np.float32)] * 2, axis=0)
    return c


ENGINES = ("sync", "gpsimd", "tensor", "scalar", "vector")


class Sem:
    __slots__ = ("h", "v", "name")

    def __init__(self, handle, name):
        self.h = handle
        self.v = 0
        self.name = name

    def inc(self, n=1):
        self.v += n
        return self.v


class Prog:
    def __init__(self, nc):
        self.nc = nc
        self.ops = {e: [] for e in ENGINES}

    def add(self, engine, fn):
        self.ops[engine].append(fn)

    def wait(self, engine, sem, val):
        self.add(engine, lambda e, s=sem.h, v=val: e.wait_ge(s, v))

    def play(self, blk):
        for eng in ENGINES:
            ops = self.ops[eng]
            if not ops:
                continue
            def body(e, ops=ops):
                for fn in ops:
                    fn(e)
            getattr(blk, eng)(body)


class Bank:

    def __init__(self, prog, tensor_handle, pe_sem, ev_sem):
        self.prog = prog
        self.t = tensor_handle
        self.pe = pe_sem
        self.ev = ev_sem
        self.n_cons_expected = 0

    def produce(self, mm_fns, n_consumers):
        p = self.prog
        if self.n_cons_expected:
            p.wait("tensor", self.ev, self.ev.v)
        for fn in mm_fns[:-1]:
            p.add("tensor", fn)
        tgt = self.pe.inc()
        p.add("tensor", lambda e, fn=mm_fns[-1], s=self.pe.h: fn(e).then_inc(s, 1))
        self.n_cons_expected = n_consumers
        return tgt

    def consume(self, engine, fn, pe_tgt):
        p = self.prog
        p.wait(engine, self.pe, pe_tgt)
        p.add(engine, lambda e, fn=fn, s=self.ev.h: fn(e).then_inc(s, 1))
        self.ev.inc()
        self.n_cons_expected -= 1


def ceil_div(a, b):
    return (a + b - 1) // b


class Build:

    def __init__(self, with_transformer=True, n_layers=NLAYER, dbg=()):
        self.nc = bass.Bass("TRN2", target_bir_lowering=False)
        self.prog = Prog(self.nc)
        self.es = contextlib.ExitStack()
        self.es_dn = contextlib.ExitStack()
        self.es_tf = contextlib.ExitStack()
        self.with_transformer = with_transformer
        self.n_layers = n_layers
        self.dbg_names = set(dbg)
        self.dram = {}
        self.sb = {}
        self.psum = {}
        self.sems = {}

    def din(self, name, shape, dtype=F32):
        t = self.nc.dram_tensor(name, list(shape), dtype, kind="ExternalInput").ap()
        self.dram[name] = t
        return t

    def dout(self, name, shape, dtype=F32):
        t = self.nc.dram_tensor(name, list(shape), dtype, kind="ExternalOutput").ap()
        self.dram[name] = t
        return t

    def dint(self, name, shape, dtype=F32):
        t = self.nc.dram_tensor(name, list(shape), dtype, kind="Internal").ap()
        self.dram[name] = t
        return t

    def sbuf(self, name, shape, dtype=F32, scope=None):
        es = scope if scope is not None else self.es
        t = es.enter_context(self.nc.sbuf_tensor(name, list(shape), dtype))
        self.sb[name] = t
        return t

    def psum_t(self, name, shape, dtype=F32):
        t = self.es.enter_context(self.nc.psum_tensor(name, list(shape), dtype))
        self.psum[name] = t
        return t

    def sem(self, name):
        s = Sem(self.es.enter_context(self.nc.semaphore(name)), name)
        self.sems[name] = s
        return s

    def dbg_store(self, name, src_fn, shape, dtype=F32):
        if name not in self.dbg_names:
            return
        o = self.dout("dbg_" + name, shape, dtype)
        d = self.d_sy
        tgt = d.inc(16)
        self.prog.add("sync", lambda e, o=o, f=src_fn, s=d.h: e.dma_start(o, f()).then_inc(s, 16))


def build_common(bld: "Build"):
    nc, p = bld.nc, bld.prog
    bld.din("x_in", [TLOC, D])
    bld.din("Wt_d", [D, D])
    bld.din("ln_gamma", [1, D]); bld.din("ln_beta", [1, D])
    bld.din("eta", [1, 1]); bld.din("lam_logit", [1, 1])
    bld.din("sel", [128, 1])
    for nm in ("ident128", "rotmat", "d_strict", "m_strict", "d_strictT", "m_strictT",
               "d_inclT", "m_inclT"):
        bld.din(nm, [128, 128])
    bld.din("i64pair", [128, 64])
    bld.din("tcol", [128, 1]); bld.din("tbarcol", [128, 1])
    bld.din("ones_row", [1, 128])
    bld.din("attn_masks", [3, 128, 768])
    bld.din("chunk_ones", [8, 128, 16])
    for nm in ("rotAQ", "rotBQ", "rotAK", "rotBK"):
        bld.din(nm, [128, TLOC])
    bld.din("Wqkv", [NLAYER, D, 3 * D]); bld.din("Wo", [NLAYER, D, D])
    bld.din("W1", [NLAYER, D, 2 * DIM_INNER]); bld.din("W2", [NLAYER, DIM_INNER, D])
    bld.din("b1cols", [NLAYER, 2, 22, 128, 1]); bld.din("b2cols", [NLAYER, 8, 128, 1])
    bld.din("ffn_g", [NLAYER, 8, 128, 1]); bld.din("ffn_b", [NLAYER, 8, 128, 1])
    bld.din("qk_scale", [NLAYER, 2, 128, 1])
    bld.din("chunk_onesT", [8, 16, 128]); bld.din("ident64h", [128, 64])
    bld.din("ones_col128", [128, 1])
    bld.dout("out", [4096, D])
    bld.dint("sc_static", [4096, D], BF16)
    bld.dint("sc_o0", [4096, D], BF16)
    bld.dint("sc_ofin", [4096, D], BF16)
    bld.dint("sc_T", [NCH, 128, 8 * 128], BF16)
    bld.dint("sc_P", [NCH, 128, 8 * 64])
    bld.dint("cc_m_in", [128, 1024])
    bld.dint("cc_m_out", [256, 1024])
    bld.dint("sc_s", [4096, D], BF16)
    bld.dint("sc_p", [36, 128, 16 * 384], F16)
    bld.dint("cc_h_in", [D, HALO], F16)
    bld.dint("cc_h_out", [2 * D, HALO], F16)
    bld.dint("sc_wq", [NLAYER, D, 3 * D], F16)
    bld.dint("sc_w1", [NLAYER, D, 2 * DIM_INNER], BF16)
    bld.dint("sc_w2", [NLAYER, DIM_INNER, D], BF16)

    bld.s_pe = bld.sem("s_pe")
    bld.s_ve = bld.sem("s_ve")
    bld.s_ac = bld.sem("s_ac")
    bld.s_gp = bld.sem("s_gp")
    bld.d_sy = bld.sem("d_sy")
    bld.d_sc = bld.sem("d_sc")
    bld.d_gp = bld.sem("d_gp")
    bld.s_cc = bld.sem("s_cc")
    bld.s_sy = bld.sem("s_sy")
    bld.s_sg = bld.sem("s_sg")
    bld.s_gq = bld.sem("s_gq")
    bld.d_w1p = [bld.sem("d_w10"), bld.sem("d_w11")]
    bld.d_w2p = [bld.sem("d_w20"), bld.sem("d_w21")]
    bld.d_qk = [bld.sem("d_qk0"), bld.sem("d_qk1")]
    bld.d_rt = bld.sem("d_rt")
    bld.d_wb = bld.sem("d_wb")
    bld.wb_pend = {}

    bld.pbt = bld.psum_t("pbt", [128, 1024], BF16)
    bld.pb = [bld.psum_t(f"pb{i}", [128, 512], F32) for i in range(7)]

    bld.h = bld.sbuf("h", [128, 8 * TLOC], F16)
    bld.cols = bld.sbuf("cols", [128, 32], F32)
    bld.x2 = bld.sbuf("x2", [128, D], F32)
    bld.ident_bf = bld.sbuf("ident_bf", [128, 128], BF16)
    bld.rot_bf = bld.sbuf("rot_bf", [128, 128], BF16)
    bld.dn_cols = bld.sbuf("dn_cols", [128, 16], F32)
    bld.ones_row_sb = bld.sbuf("ones_row_sb", [1, 128], BF16)
    bld.ones_row_f32 = bld.sbuf("ones_row_f32", [1, 128], F32)
    bld.xrow = [bld.sbuf(f"xrow{i}", [128, D], F32, scope=bld.es_dn) for i in range(2)]
    bld.xn_tok = [bld.sbuf(f"xn_tok{i}", [128, D], BF16, scope=bld.es_dn) for i in range(2)]
    bld.xa_tok = [bld.sbuf(f"xa_tok{i}", [128, D], BF16, scope=bld.es_dn) for i in range(2)]
    bld.xa_fm = [bld.sbuf(f"xa_fm{i}", [128, 8 * 128], BF16, scope=bld.es_dn) for i in range(2)]
    bld.st_tok = [bld.sbuf(f"st_tok{i}", [128, D], BF16, scope=bld.es_dn) for i in range(2)]
    bld.Wt_sb = bld.sbuf("Wt_sb", [128, 8 * D], BF16, scope=bld.es_dn)
    bld.Ls = bld.sbuf("Ls", [128, 128], BF16, scope=bld.es_dn)
    bld.LsT = bld.sbuf("LsT", [128, 128], BF16, scope=bld.es_dn)
    bld.LiT = bld.sbuf("LiT", [128, 128], BF16, scope=bld.es_dn)
    bld.i64_bf = bld.sbuf("i64_bf", [128, 64], F32, scope=bld.es_dn)
    bld.consts_f32 = bld.sbuf("consts_f32", [128, 8 * 128], F32, scope=bld.es_dn)
    bld.X_all = bld.sbuf("X_all", [128, 16 * 128], BF16, scope=bld.es_dn)
    bld.Na = bld.sbuf("Na", [128, 16 * 128], BF16, scope=bld.es_dn)
    bld.NTa = bld.sbuf("NTa", [128, 16 * 128], BF16, scope=bld.es_dn)
    bld.Nb = bld.sbuf("Nb", [128, 16 * 128], BF16, scope=bld.es_dn)
    bld.NTb = bld.sbuf("NTb", [128, 16 * 128], BF16, scope=bld.es_dn)
    bld.Bt_all = bld.sbuf("Bt_all", [128, 16 * 128], BF16, scope=bld.es_dn)
    bld.LbK = bld.sbuf("LbK", [128, D], BF16, scope=bld.es_dn)
    bld.U_all = bld.sbuf("U_all", [128, D], BF16, scope=bld.es_dn)
    bld.Wfm = bld.sbuf("Wfm", [128, 8 * 128], BF16, scope=bld.es_dn)
    bld.Ttok = bld.sbuf("Ttok", [128, D], BF16, scope=bld.es_dn)
    bld.Tfm = bld.sbuf("Tfm", [128, 8 * 128], BF16, scope=bld.es_dn)
    bld.M_all = bld.sbuf("M_all", [128, 8 * 64], F32, scope=bld.es_dn)
    bld.MT_all = bld.sbuf("MT_all", [128, 8 * 64], F32, scope=bld.es_dn)
    bld.MTbf = bld.sbuf("MTbf", [128, 8 * 64], BF16, scope=bld.es_dn)
    bld.P_all = bld.sbuf("P_all", [128, 8 * 64], F32, scope=bld.es_dn)
    bld.PT_all = bld.sbuf("PT_all", [128, 8 * 64], F32, scope=bld.es_dn)
    bld.G_all = bld.sbuf("G_all", [128, 8 * 64], F32, scope=bld.es_dn)
    bld.o0_all = bld.sbuf("o0_all", [128, D], BF16, scope=bld.es_dn)
    bld.otmp = bld.sbuf("otmp", [128, 512], F32, scope=bld.es_dn)
    bld.Minit = bld.sbuf("Minit", [128, 8 * 64], F32, scope=bld.es_dn)
    bld.MinitT = bld.sbuf("MinitT", [128, 8 * 64], F32, scope=bld.es_dn)
    bld.MinitTbf = bld.sbuf("MinitTbf", [128, 8 * 64], BF16, scope=bld.es_dn)
    bld.Pld = bld.sbuf("Pld", [128, 8 * 64], F32, scope=bld.es_dn)
    bld.Tld = bld.sbuf("Tld", [128, 8 * 128], BF16, scope=bld.es_dn)
    bld.o0ld = bld.sbuf("o0ld", [128, D], BF16, scope=bld.es_dn)
    bld.dMbf = bld.sbuf("dMbf", [128, 8 * 64], BF16, scope=bld.es_dn)
    bld.gamma_b = bld.sbuf("gamma_b", [128, D], BF16, scope=bld.es_dn)
    bld.beta_b = bld.sbuf("beta_b", [128, D], BF16, scope=bld.es_dn)
    bld.grow = bld.sbuf("grow", [1, D], F32, scope=bld.es_dn)
    bld.brow = bld.sbuf("brow", [1, D], F32, scope=bld.es_dn)
    bld.yrow = bld.sbuf("yrow", [128, D], F32, scope=bld.es_dn)
    bld.st_ld = bld.sbuf("st_ld", [128, D], BF16, scope=bld.es_dn)
    bld.of_ld = bld.sbuf("of_ld", [128, D], BF16, scope=bld.es_dn)
    bld.s_bf = bld.sbuf("s_bf", [128, D], BF16, scope=bld.es_dn)
_ENG_SELF_SEM = {"vector": "s_ve", "scalar": "s_ac", "gpsimd": "s_gp"}


def E(bld, eng, fn, inc=None, n=1):
    ssn = _ENG_SELF_SEM.get(eng)
    if ssn is not None:
        sem = getattr(bld, ssn)
        if inc is None or inc is sem:
            W8(bld, eng, sem, sem.v)
            tgt = sem.inc(1)
            bld.prog.add(eng, lambda e, fn=fn, s=sem.h: fn(e).then_inc(s, 1))
            return tgt
    if inc is None:
        bld.prog.add(eng, fn)
        return None
    tgt = inc.inc(n)
    bld.prog.add(eng, lambda e, fn=fn, s=inc.h, n=n: fn(e).then_inc(s, n))
    return tgt


def W8(bld, eng, sem, val):
    if val is None or val <= 0:
        return
    bld.prog.wait(eng, sem, val)


def fence(bld, queue):
    dsem, msem = {"sync": (bld.d_sy, bld.s_sy), "scalar": (bld.d_sc, bld.s_sg),
                  "gpsimd": (bld.d_gp, bld.s_gq)}[queue]
    W8(bld, queue, dsem, dsem.v)
    tgt = msem.inc()
    bld.prog.add(queue, lambda e, s=msem.h: e.sem_inc(s, 1))
    return tgt


CL_LNLAM, CL_LAM, CL_LBAR, CL_BETA, CL_BLAM, CL_LAMC, CL_SEL, CL_NBETA = range(8)


def emit_setup(bld: Build):
    nc, p = bld.nc, bld.prog
    d = bld.dram
    gp_loads = [
        (bld.ident_bf[:], d["ident128"]),
        (bld.rot_bf[:], d["rotmat"]),
        (bld.ones_row_sb[:], d["ones_row"]),
    ]
    for j in range(8):
        gp_loads.append((bld.Wt_sb[:, j * D:(j + 1) * D], d["Wt_d"][128 * j:128 * (j + 1), :]))
    for dst, src in gp_loads:
        E(bld, "gpsimd", lambda e, a=dst, b=src: e.dma_start(a, b), inc=bld.d_gp, n=16)
    for t in (bld.M_all, bld.MT_all, bld.MTbf):
        E(bld, "gpsimd", lambda e, a=t: e.memset(a[:], 0.0), inc=bld.s_gp)
    gp_done = fence(bld, "gpsimd")
    gp_ms = bld.s_gp.v

    cf = bld.consts_f32
    sy_loads = [
        (cf[:, 0:128], d["d_strict"]), (cf[:, 128:256], d["m_strict"]),
        (cf[:, 256:384], d["d_strictT"]), (cf[:, 384:512], d["m_strictT"]),
        (cf[:, 512:640], d["d_inclT"]), (cf[:, 640:768], d["m_inclT"]),
        (bld.i64_bf[:], d["i64pair"][:, 0:64]),
        (bld.cols[0:128, 0:1], d["tcol"]), (bld.cols[0:128, 1:2], d["tbarcol"]),
        (bld.cols[0:1, 2:3], d["eta"]), (bld.cols[0:1, 3:4], d["lam_logit"]),
        (bld.dn_cols[:, CL_SEL:CL_SEL + 1], d["sel"]),
        (bld.grow[:], d["ln_gamma"]), (bld.brow[:], d["ln_beta"]),
        (bld.ones_row_f32[:], d["ones_row"]),
    ]
    for dst, src in sy_loads:
        E(bld, "sync", lambda e, a=dst, b=src: e.dma_start(a, b), inc=bld.d_sy, n=16)
    sy_done = fence(bld, "sync")

    ones_r = bld.ones_row_f32

    W8(bld, "scalar", bld.s_sy, sy_done)
    E(bld, "scalar", lambda e: e.activation(bld.cols[0:1, 4:5], bld.cols[0:1, 3:4], AF.Sigmoid))
    E(bld, "scalar", lambda e: e.activation(bld.cols[0:1, 5:6], bld.cols[0:1, 4:5], AF.Ln),
      inc=bld.s_ac)
    a_ms = bld.s_ac.v
    W8(bld, "vector", bld.s_ac, a_ms)
    E(bld, "vector", lambda e: e.tensor_scalar_max(bld.cols[0:1, 5:6], bld.cols[0:1, 5:6], -85.0))
    E(bld, "vector", lambda e: e.tensor_scalar(bld.cols[0:1, 6:7], bld.cols[0:1, 4:5], -1.0, 1.0,
                                               ALU.mult, ALU.add))
    E(bld, "vector", lambda e: e.tensor_mul(bld.cols[0:1, 7:8], bld.cols[0:1, 6:7], bld.cols[0:1, 2:3]))
    E(bld, "vector", lambda e: e.tensor_scalar_mul(bld.cols[0:1, 7:8], bld.cols[0:1, 7:8], 1.0 / 64.0))
    E(bld, "vector", lambda e: e.tensor_scalar_mul(bld.cols[0:1, 8:9], bld.cols[0:1, 5:6], 128.0),
      inc=bld.s_ve)
    v_ms = bld.s_ve.v
    W8(bld, "scalar", bld.s_ve, v_ms)
    E(bld, "scalar", lambda e: e.activation(bld.cols[0:1, 9:10], bld.cols[0:1, 8:9], AF.Exp),
      inc=bld.s_ac)
    a_ms = bld.s_ac.v

    W8(bld, "tensor", bld.s_ac, a_ms)
    W8(bld, "tensor", bld.s_sy, sy_done)
    ps = bld.pb[6]
    def bcast_mm(src_ap, col):
        E(bld, "tensor", lambda e, s=src_ap, c=col: e.matmul(
            ps[:, c:c + 1], ones_r[:], s, start=True, stop=True))
    bcast_mm(bld.cols[0:1, 5:6], 0)
    bcast_mm(bld.cols[0:1, 7:8], 1)
    bcast_mm(bld.cols[0:1, 9:10], 2)
    E(bld, "tensor", lambda e: e.matmul(ps[:, 3:4], ones_r[:], bld.cols[0:1, 7:8],
                                        start=True, stop=True), inc=bld.s_pe)
    pe_ms = bld.s_pe.v
    W8(bld, "vector", bld.s_pe, pe_ms)
    E(bld, "vector", lambda e: e.tensor_copy(bld.dn_cols[:, CL_LNLAM:CL_LNLAM + 1], ps[:, 0:1]))
    E(bld, "vector", lambda e: e.tensor_copy(bld.dn_cols[:, CL_BETA:CL_BETA + 1], ps[:, 1:2]))
    E(bld, "vector", lambda e: e.tensor_copy(bld.dn_cols[:, CL_LAMC:CL_LAMC + 1], ps[:, 2:3]))
    E(bld, "vector", lambda e: e.tensor_scalar_mul(bld.dn_cols[:, CL_NBETA:CL_NBETA + 1],
                                                   ps[:, 3:4], -1.0))
    E(bld, "vector", lambda e: e.tensor_mul(bld.cols[:, 10:11], bld.cols[:, 0:1],
                                            bld.dn_cols[:, CL_LNLAM:CL_LNLAM + 1]))
    E(bld, "vector", lambda e: e.tensor_mul(bld.cols[:, 11:12], bld.cols[:, 1:2],
                                            bld.dn_cols[:, CL_LNLAM:CL_LNLAM + 1]), inc=bld.s_ve)
    v_ms = bld.s_ve.v
    W8(bld, "scalar", bld.s_ve, v_ms)
    E(bld, "scalar", lambda e: e.activation(bld.dn_cols[:, CL_LAM:CL_LAM + 1], bld.cols[:, 10:11], AF.Exp))
    E(bld, "scalar", lambda e: e.activation(bld.dn_cols[:, CL_LBAR:CL_LBAR + 1], bld.cols[:, 11:12], AF.Exp),
      inc=bld.s_ac)
    a_ms = bld.s_ac.v
    W8(bld, "vector", bld.s_ac, a_ms)
    E(bld, "vector", lambda e: e.tensor_mul(bld.dn_cols[:, CL_BLAM:CL_BLAM + 1],
                                            bld.dn_cols[:, CL_BETA:CL_BETA + 1],
                                            bld.dn_cols[:, CL_LAM:CL_LAM + 1]))

    def mk_decay(dst, d_off, m_off, use_nbeta):
        E(bld, "vector", lambda e: e.tensor_scalar_mul(
            bld.x2[:, 0:128], cf[:, d_off:d_off + 128], bld.dn_cols[:, CL_LNLAM:CL_LNLAM + 1]),
            inc=bld.s_ve)
        W8(bld, "scalar", bld.s_ve, bld.s_ve.v)
        E(bld, "scalar", lambda e: e.activation(bld.x2[:, 128:256], bld.x2[:, 0:128], AF.Exp),
          inc=bld.s_ac)
        W8(bld, "vector", bld.s_ac, bld.s_ac.v)
        if use_nbeta:
            E(bld, "vector", lambda e: e.tensor_mul(bld.x2[:, 0:128], bld.x2[:, 128:256],
                                                    cf[:, m_off:m_off + 128]))
            E(bld, "vector", lambda e: e.tensor_scalar_mul(
                dst[:], bld.x2[:, 0:128], bld.dn_cols[:, CL_NBETA:CL_NBETA + 1]), inc=bld.s_ve)
        else:
            E(bld, "vector", lambda e: e.tensor_mul(dst[:], bld.x2[:, 128:256],
                                                    cf[:, m_off:m_off + 128]), inc=bld.s_ve)

    mk_decay(bld.Ls, 0, 128, True)
    mk_decay(bld.LsT, 256, 384, True)
    mk_decay(bld.LiT, 512, 640, False)
    bld.guards = getattr(bld, "guards", {})
    guard_set(bld, "x2", (bld.s_ve, bld.s_ve.v))

    W8(bld, "vector", bld.s_sy, sy_done)
    for pj in range(8):
        E(bld, "vector", lambda e, pj=pj: e.tensor_copy(bld.P_all[:, 64 * pj:64 * (pj + 1)],
                                                        bld.i64_bf[:, :]))
        E(bld, "vector", lambda e, pj=pj: e.tensor_copy(bld.PT_all[:, 64 * pj:64 * (pj + 1)],
                                                        bld.i64_bf[:, :]))
    E(bld, "vector", lambda e: e.tensor_copy(bld.PT_all[:, 0:1], bld.PT_all[:, 0:1]), inc=bld.s_ve)

    W8(bld, "tensor", bld.s_sy, sy_done)
    for half in range(2):
        W8(bld, "tensor", bld.s_ve, bld.s_ve.v)
        E(bld, "tensor", lambda e, hf=half: e.matmul(
            bld.pb[5][:, 0:512], ones_r[:], bld.grow[:, 512 * hf:512 * (hf + 1)],
            start=True, stop=True), inc=bld.s_pe)
        W8(bld, "vector", bld.s_pe, bld.s_pe.v)
        E(bld, "vector", lambda e, hf=half: e.tensor_copy(
            bld.gamma_b[:, 512 * hf:512 * (hf + 1)], bld.pb[5][:, 0:512]), inc=bld.s_ve)
        W8(bld, "tensor", bld.s_ve, bld.s_ve.v)
        E(bld, "tensor", lambda e, hf=half: e.matmul(
            bld.pb[5][:, 0:512], ones_r[:], bld.brow[:, 512 * hf:512 * (hf + 1)],
            start=True, stop=True), inc=bld.s_pe)
        W8(bld, "vector", bld.s_pe, bld.s_pe.v)
        E(bld, "vector", lambda e, hf=half: e.tensor_copy(
            bld.beta_b[:, 512 * hf:512 * (hf + 1)], bld.pb[5][:, 0:512]), inc=bld.s_ve)

    for eng in ("tensor", "vector", "scalar"):
        W8(bld, eng, bld.s_gq, gp_done)
        W8(bld, eng, bld.s_gp, gp_ms)


def guard_wait(bld, eng, key):
    for sem, val in bld.guards.get(key, ()):
        if sem.name.startswith("d_"):
            val = sem.v
        W8(bld, eng, sem, val)


def guard_set(bld, key, *pairs):
    bld.guards[key] = list(pairs)


def guard_add(bld, key, sem, val):
    bld.guards.setdefault(key, []).append((sem, val))


def emit_tile_prep(bld: Build, tile):
    p = bld.prog
    slot = tile % 2
    own = tile >= 4
    if not hasattr(bld, "guards"):
        bld.guards = {}
    cb = 12 + 10 * slot

    guard_wait(bld, "sync", f"xrow{slot}")
    E(bld, "sync", lambda e, t=tile, s=slot: e.dma_start(
        bld.xrow[s][:], bld.dram["x_in"][128 * t:128 * (t + 1), :]), inc=bld.d_sy, n=16)
    xv = fence(bld, "sync")

    W8(bld, "vector", bld.s_sy, xv)
    X = bld.xrow[slot]
    c = bld.cols
    E(bld, "vector", lambda e: e.reduce_sum(c[:, cb + 0:cb + 1], X[:], axis=AX.X), inc=bld.s_ve)
    W8(bld, "scalar", bld.s_sy, xv)
    guard_wait(bld, "scalar", "x2")
    sq_a = E(bld, "scalar", lambda e: e.activation(bld.x2[:], X[:], AF.Square), inc=bld.s_ac)
    W8(bld, "vector", bld.s_ac, sq_a)
    x2r = E(bld, "vector", lambda e: e.reduce_sum(c[:, cb + 1:cb + 2], bld.x2[:], axis=AX.X))
    guard_set(bld, "x2", (bld.s_ve, x2r))
    E(bld, "vector", lambda e: e.tensor_scalar_mul(c[:, cb + 2:cb + 3], c[:, cb + 0:cb + 1], 1.0 / D))
    E(bld, "vector", lambda e: e.tensor_scalar_mul(c[:, cb + 3:cb + 4], c[:, cb + 2:cb + 3], -1.0))
    E(bld, "vector", lambda e: e.tensor_scalar_mul(c[:, cb + 4:cb + 5], c[:, cb + 1:cb + 2], 1.0 / D))
    E(bld, "vector", lambda e: e.tensor_mul(c[:, cb + 5:cb + 6], c[:, cb + 2:cb + 3], c[:, cb + 2:cb + 3]))
    E(bld, "vector", lambda e: e.tensor_sub(c[:, cb + 6:cb + 7], c[:, cb + 4:cb + 5], c[:, cb + 5:cb + 6]))
    E(bld, "vector", lambda e: e.tensor_scalar_add(c[:, cb + 6:cb + 7], c[:, cb + 6:cb + 7], LN_EPS),
      inc=bld.s_ve)
    W8(bld, "scalar", bld.s_ve, bld.s_ve.v)
    sqrt_a = E(bld, "scalar", lambda e: e.activation(c[:, cb + 7:cb + 8], c[:, cb + 6:cb + 7], AF.Sqrt),
               inc=bld.s_ac)
    W8(bld, "vector", bld.s_ac, sqrt_a)
    E(bld, "vector", lambda e: e.reciprocal(c[:, cb + 8:cb + 9], c[:, cb + 7:cb + 8]))
    guard_wait(bld, "vector", f"xn{slot}")
    xn_v = E(bld, "vector", lambda e: e.tensor_scalar(
        bld.xn_tok[slot][:], X[:], c[:, cb + 3:cb + 4], c[:, cb + 8:cb + 9],
        ALU.add, ALU.mult), inc=bld.s_ve)
    guard_set(bld, f"xrow{slot}", (bld.s_ve, xn_v), (bld.s_ac, sq_a))
    tanh_a = None
    if own:
        W8(bld, "scalar", bld.s_ve, xn_v)
        guard_wait(bld, "scalar", f"xa{slot}")
        tanh_a = E(bld, "scalar", lambda e: e.activation(
            bld.xa_tok[slot][:], bld.xn_tok[slot][:], AF.Tanh), inc=bld.s_ac)

    guard_wait(bld, "tensor", "pbt")
    W8(bld, "tensor", bld.s_ve, xn_v)
    for j in range(8):
        inc = bld.s_pe if j == 7 else None
        E(bld, "tensor", lambda e, j=j: e.transpose(
            bld.pbt[:, 128 * j:128 * (j + 1)], bld.xn_tok[slot][:, 128 * j:128 * (j + 1)],
            bld.ident_bf[:]), inc=inc)
    trn_pe = bld.s_pe.v
    W8(bld, "scalar", bld.s_pe, trn_pe)
    hdst = bld.h[:].rearrange("p (c t) -> p c t", t=TLOC)[:, :, 128 * tile:128 * (tile + 1)]
    psrc = bld.pbt[:].rearrange("p (c t) -> p c t", t=128)
    h_a = E(bld, "scalar", lambda e, d2=hdst, s2=psrc: e.activation(d2, s2, AF.Copy), inc=bld.s_ac)
    guard_set(bld, "pbt", (bld.s_ac, h_a))
    guard_set(bld, f"xn{slot}", (bld.s_pe, trn_pe), (bld.s_ac, tanh_a if own else h_a))

    if not own:
        return

    guard_wait(bld, "tensor", "pbt")
    W8(bld, "tensor", bld.s_ac, tanh_a)
    guard_wait(bld, "tensor", f"xafm{slot}")
    for j in range(8):
        inc = bld.s_pe if j == 7 else None
        E(bld, "tensor", lambda e, j=j: e.transpose(
            bld.pbt[:, 128 * j:128 * (j + 1)], bld.xa_tok[slot][:, 128 * j:128 * (j + 1)],
            bld.ident_bf[:]), inc=inc)
    tra_pe = bld.s_pe.v
    W8(bld, "vector", bld.s_pe, tra_pe)
    xa_v = E(bld, "vector", lambda e: e.tensor_copy(bld.xa_fm[slot][:], bld.pbt[:]), inc=bld.s_ve)
    guard_set(bld, "pbt", (bld.s_ve, xa_v))
    guard_set(bld, f"xa{slot}", (bld.s_pe, tra_pe))

    for half in range(2):
        guard_wait(bld, "tensor", f"pb{half}")
        W8(bld, "tensor", bld.s_ve, xa_v)
        for j in range(8):
            inc = bld.s_pe if j == 7 else None
            E(bld, "tensor", lambda e, j=j, hf=half: e.matmul(
                bld.pb[hf][:, 0:512], bld.xa_fm[slot][:, 128 * j:128 * (j + 1)],
                bld.Wt_sb[:, 1024 * j + 512 * hf:1024 * j + 512 * hf + 512],
                start=(j == 0), stop=(j == 7)), inc=inc)
        st_pe = bld.s_pe.v
        W8(bld, "vector", bld.s_pe, st_pe)
        if half == 0:
            guard_wait(bld, "vector", f"st{slot}")
        ev = E(bld, "vector", lambda e, hf=half: e.tensor_copy(
            bld.st_tok[slot][:, 512 * hf:512 * (hf + 1)], bld.pb[hf][:, 0:512]), inc=bld.s_ve)
        guard_set(bld, f"pb{half}", (bld.s_ve, ev))
    st_v = bld.s_ve.v
    guard_set(bld, f"xafm{slot}", (bld.s_pe, bld.s_pe.v))
    W8(bld, "sync", bld.s_ve, st_v)
    r0 = (tile - 4) * 128
    E(bld, "sync", lambda e, r0=r0, s=slot: e.dma_start(
        bld.dram["sc_static"][r0:r0 + 128, :], bld.st_tok[s][:]), inc=bld.d_sy, n=16)
    stv = fence(bld, "sync")
    guard_set(bld, f"st{slot}", (bld.s_sy, stv))


def emit_dn_chunk(bld: Build, ch):
    p = bld.prog
    slot = (4 + ch) % 2
    xa_t = bld.xa_tok[slot]
    xa_f = bld.xa_fm[slot]
    st_t = bld.st_tok[slot]
    dcl = bld.dn_cols

    def lam_c(i):
        return dcl[:, i:i + 1]

    def hblk(h):
        return 128 * (h // 2), 64 * (h % 2)

    for grp in range(2):
        for bank, par in ((2, 0), (3, 1)):
            guard_wait(bld, "tensor", f"pb{bank}")
            if grp == 0 and bank == 2 and par == 0:
                pass
            heads = [8 * grp + 2 * i + par for i in range(4)]
            for ii, h in enumerate(heads):
                co, pb_ = hblk(h)
                inc = bld.s_pe if ii == 3 else None
                E(bld, "tensor", lambda e, co=co, pb_=pb_, ii=ii, bank=bank: e.matmul(
                    bld.pb[bank][:, 128 * ii:128 * (ii + 1)],
                    xa_f[pb_:pb_ + 64, co:co + 128], xa_f[pb_:pb_ + 64, co:co + 128],
                    start=True, stop=True, tile_position=(pb_, 0)), inc=inc)
            kk_pe = bld.s_pe.v
            W8(bld, "vector", bld.s_pe, kk_pe)
            psv = bld.pb[bank][:].rearrange("p (i x) -> p i x", x=128)
            if grp == 0 and bank == 2:
                guard_wait(bld, "vector", "NBt")
            for dst, msk in ((bld.Na, bld.Ls), (bld.NTa, bld.LsT), (bld.Bt_all, bld.LiT)):
                dv = dst[:].rearrange("p (hh pp x) -> p hh pp x", pp=2, x=128)[:, 4 * grp:4 * grp + 4, par, :]
                inc2 = bld.s_ve if msk is bld.LiT else None
                E(bld, "vector", lambda e, dv=dv, psv=psv, msk=msk: e.tensor_tensor(
                    dv, psv, msk[:].unsqueeze(1).broadcast_to([128, 4, 128]), ALU.mult),
                  inc=inc2)
            guard_set(bld, f"pb{bank}", (bld.s_ve, bld.s_ve.v))
    masks_v = bld.s_ve.v

    xv3 = bld.X_all[:].rearrange("p (h x) -> p h x", x=128)
    guard_wait(bld, "vector", "X_all")
    guard_wait(bld, "vector", "LbK")
    E(bld, "vector", lambda e: e.tensor_scalar_mul(
        xv3[:, :, 0:64], st_t[:].rearrange("p (h x) -> p h x", x=64), lam_c(CL_BETA)))
    E(bld, "vector", lambda e: e.tensor_scalar_mul(
        xv3[:, :, 64:128], xa_t[:].rearrange("p (h x) -> p h x", x=64), lam_c(CL_BLAM)))
    E(bld, "vector", lambda e: e.tensor_scalar_mul(bld.LbK[:], xa_t[:], lam_c(CL_LBAR)),
      inc=bld.s_ve)
    rhs_v = bld.s_ve.v
    guard_add(bld, f"xa{slot}", bld.s_ve, rhs_v)
    guard_add(bld, f"st{slot}", bld.s_ve, rhs_v)

    Ncur, NTcur, Nnxt, NTnxt = bld.Na, bld.NTa, bld.Nb, bld.NTb
    for j in range(7):
        for grp in range(4):
            bank = 2 + (grp % 2)
            guard_wait(bld, "tensor", f"pb{bank}")
            if j == 0:
                W8(bld, "tensor", bld.s_ve, rhs_v)
            for ii in range(4):
                h = 4 * grp + ii
                inc = bld.s_pe if ii == 3 else None
                E(bld, "tensor", lambda e, h=h, ii=ii, bank=bank, NT=NTcur: e.matmul(
                    bld.pb[bank][:, 128 * ii:128 * (ii + 1)],
                    NT[:, 128 * h:128 * (h + 1)], bld.X_all[:, 128 * h:128 * (h + 1)],
                    start=True, stop=True), inc=inc)
            app_pe = bld.s_pe.v
            W8(bld, "vector", bld.s_pe, app_pe)
            xslice = bld.X_all[:, 512 * grp:512 * (grp + 1)]
            E(bld, "vector", lambda e, xs=xslice, bank=bank: e.tensor_tensor(
                xs, xs, bld.pb[bank][:, 0:512], ALU.add), inc=bld.s_ve)
            guard_set(bld, f"pb{bank}", (bld.s_ve, bld.s_ve.v))
        if j < 6:
            for grp in range(4):
                guard_wait(bld, "tensor", "pb4")
                guard_wait(bld, "tensor", "pb5")
                for ii in range(4):
                    h = 4 * grp + ii
                    E(bld, "tensor", lambda e, h=h, ii=ii, N=Ncur, NT=NTcur: e.matmul(
                        bld.pb[4][:, 128 * ii:128 * (ii + 1)],
                        NT[:, 128 * h:128 * (h + 1)], N[:, 128 * h:128 * (h + 1)],
                        start=True, stop=True))
                for ii in range(4):
                    h = 4 * grp + ii
                    inc = bld.s_pe if ii == 3 else None
                    E(bld, "tensor", lambda e, h=h, ii=ii, N=Ncur, NT=NTcur: e.matmul(
                        bld.pb[5][:, 128 * ii:128 * (ii + 1)],
                        N[:, 128 * h:128 * (h + 1)], NT[:, 128 * h:128 * (h + 1)],
                        start=True, stop=True), inc=inc)
                sq_pe = bld.s_pe.v
                W8(bld, "vector", bld.s_pe, sq_pe)
                E(bld, "vector", lambda e, grp=grp, Nn=Nnxt: e.tensor_copy(
                    Nn[:, 512 * grp:512 * (grp + 1)], bld.pb[4][:, 0:512]))
                E(bld, "vector", lambda e, grp=grp, NTn=NTnxt: e.tensor_copy(
                    NTn[:, 512 * grp:512 * (grp + 1)], bld.pb[5][:, 0:512]), inc=bld.s_ve)
                guard_set(bld, "pb4", (bld.s_ve, bld.s_ve.v))
                guard_set(bld, "pb5", (bld.s_ve, bld.s_ve.v))
            Ncur, NTcur, Nnxt, NTnxt = Nnxt, NTnxt, Ncur, NTcur
    solve_v = bld.s_ve.v

    for rnd in range(2):
        guard_wait(bld, "tensor", "pbt")
        W8(bld, "tensor", bld.s_ve, solve_v)
        for pi in range(4):
            pj = 4 * rnd + pi
            for par in range(2):
                h = 2 * pj + par
                inc = bld.s_pe if (pi == 3 and par == 1) else None
                E(bld, "tensor", lambda e, h=h, pi=pi, par=par: e.transpose(
                    bld.pbt[64 * par:64 * (par + 1), 128 * pi:128 * (pi + 1)],
                    bld.X_all[:, 128 * h + 64:128 * (h + 1)], bld.ident_bf[:],
                    tile_position=(0, 64 * par)), inc=inc)
        wt_pe = bld.s_pe.v
        W8(bld, "vector", bld.s_pe, wt_pe)
        if rnd == 0:
            guard_wait(bld, "vector", "Wfm")
        E(bld, "vector", lambda e, rnd=rnd: e.tensor_scalar_mul(
            bld.Wfm[:, 512 * rnd:512 * (rnd + 1)], bld.pbt[:, 0:512], -1.0), inc=bld.s_ve)
        guard_set(bld, "pbt", (bld.s_ve, bld.s_ve.v))

    for par in range(2):
        bank = 6 if par == 0 else 5
        guard_wait(bld, "tensor", f"pb{bank}")
        W8(bld, "tensor", bld.s_ve, bld.s_ve.v)
        for pj in range(8):
            h = 2 * pj + par
            inc = bld.s_pe if pj == 7 else None
            E(bld, "tensor", lambda e, pj=pj, par=par, bank=bank: e.matmul(
                bld.pb[bank][:, 64 * pj:64 * (pj + 1)],
                bld.Wfm[64 * par:64 * (par + 1), 128 * pj:128 * (pj + 1)],
                bld.MTbf[64 * par:64 * (par + 1), 64 * pj:64 * (pj + 1)],
                start=True, stop=True, tile_position=(64 * par, 0)), inc=inc)
        u_pe = bld.s_pe.v
        W8(bld, "vector", bld.s_pe, u_pe)
        if par == 0:
            guard_wait(bld, "vector", "U_all")
        uv = bld.U_all[:].rearrange("p (hh pp x) -> p hh pp x", pp=2, x=64)[:, :, par, :]
        ub = bld.X_all[:].rearrange("p (hh pp x) -> p hh pp x", pp=2, x=128)[:, :, par, 0:64]
        pv = bld.pb[bank][:].rearrange("p (h x) -> p h x", x=64)
        E(bld, "vector", lambda e, uv=uv, ub=ub, pv=pv: e.tensor_tensor(uv, ub, pv, ALU.add),
          inc=bld.s_ve)
        guard_set(bld, f"pb{bank}", (bld.s_ve, bld.s_ve.v))
    u_v = bld.s_ve.v

    mtbf_readers_pe = []

    guard_wait(bld, "vector", "o0_all")
    for par in range(2):
        bank = 0 if par == 0 else 1
        guard_wait(bld, "tensor", f"pb{bank}")
        for pj in range(8):
            h = 2 * pj + par
            co, pb_ = hblk(h)
            inc = bld.s_pe if pj == 7 else None
            E(bld, "tensor", lambda e, pj=pj, par=par, bank=bank, co=co, pb_=pb_: e.matmul(
                bld.pb[bank][:, 64 * pj:64 * (pj + 1)],
                xa_f[pb_:pb_ + 64, co:co + 128],
                bld.MTbf[64 * par:64 * (par + 1), 64 * pj:64 * (pj + 1)],
                start=True, stop=True, tile_position=(pb_, 0)), inc=inc)
        mtbf_readers_pe.append(bld.s_pe.v)
    o1_pe = bld.s_pe.v
    o0v = bld.o0_all[:].rearrange("p (h x) -> p h x", x=64)
    for rnd in range(2):
        guard_wait(bld, "tensor", "pb4")
        W8(bld, "tensor", bld.s_ve, bld.s_ve.v)
        for i8 in range(8):
            h = 8 * rnd + i8
            inc = bld.s_pe if i8 == 7 else None
            E(bld, "tensor", lambda e, h=h, i8=i8: e.matmul(
                bld.pb[4][:, 64 * i8:64 * (i8 + 1)],
                bld.Bt_all[:, 128 * h:128 * (h + 1)], bld.U_all[:, 64 * h:64 * (h + 1)],
                start=True, stop=True), inc=inc)
        o2_pe = bld.s_pe.v
        W8(bld, "scalar", bld.s_pe, o2_pe)
        otv = bld.otmp[:].rearrange("p (hh pp x) -> p hh pp x", pp=2, x=64)
        for par in range(2):
            bank = 0 if par == 0 else 1
            pv = bld.pb[bank][:].rearrange("p (h x) -> p h x", x=64)[:, 4 * rnd:4 * rnd + 4]
            E(bld, "scalar", lambda e, par=par, pv=pv, otv=otv: e.activation(
                otv[:, :, par, :], pv, AF.Copy), inc=(bld.s_ac if par == 1 else None))
        ocp_a = bld.s_ac.v
        W8(bld, "vector", bld.s_ac, ocp_a)
        W8(bld, "vector", bld.s_pe, o2_pe)
        pv2 = bld.pb[4][:].rearrange("p (h x) -> p h x", x=64)
        E(bld, "vector", lambda e, rnd=rnd, pv2=pv2, o0v=o0v: e.scalar_tensor_tensor(
            o0v[:, 8 * rnd:8 * rnd + 8], bld.otmp[:].rearrange("p (h x) -> p h x", x=64),
            lam_c(CL_LAM), pv2, ALU.mult, ALU.add), inc=bld.s_ve)
        guard_set(bld, "pb4", (bld.s_ve, bld.s_ve.v))
        if rnd == 1:
            guard_set(bld, "pb0", (bld.s_ac, ocp_a))
            guard_set(bld, "pb1", (bld.s_ac, ocp_a))
    o0_v = bld.s_ve.v
    W8(bld, "sync", bld.s_ve, o0_v)
    E(bld, "sync", lambda e, ch=ch: e.dma_start(
        bld.dram["sc_o0"][128 * ch:128 * (ch + 1), :], bld.o0_all[:]), inc=bld.d_sy, n=16)
    guard_set(bld, "o0_all", (bld.s_sy, fence(bld, "sync")))

    for rnd in range(2):
        guard_wait(bld, "tensor", "pb4")
        for i8 in range(8):
            h = 8 * rnd + i8
            inc = bld.s_pe if i8 == 7 else None
            E(bld, "tensor", lambda e, h=h, i8=i8: e.matmul(
                bld.pb[4][:, 64 * i8:64 * (i8 + 1)],
                bld.Bt_all[:, 128 * h:128 * (h + 1)], bld.X_all[:, 128 * h + 64:128 * (h + 1)],
                start=True, stop=True), inc=inc)
        bw_pe = bld.s_pe.v
        W8(bld, "vector", bld.s_pe, bw_pe)
        if rnd == 0:
            guard_wait(bld, "vector", "Ttok")
            guard_wait(bld, "vector", "Tfm")
        ttv = bld.Ttok[:].rearrange("p (h x) -> p h x", x=64)[:, 8 * rnd:8 * rnd + 8]
        qv = xa_t[:].rearrange("p (h x) -> p h x", x=64)[:, 8 * rnd:8 * rnd + 8]
        pv2 = bld.pb[4][:].rearrange("p (h x) -> p h x", x=64)
        E(bld, "vector", lambda e, ttv=ttv, qv=qv, pv2=pv2: e.scalar_tensor_tensor(
            ttv, qv, lam_c(CL_LAM), pv2, ALU.mult, ALU.subtract), inc=bld.s_ve)
        guard_set(bld, "pb4", (bld.s_ve, bld.s_ve.v))
    tt_v = bld.s_ve.v
    guard_add(bld, f"xa{slot}", bld.s_ve, tt_v)
    for rnd in range(2):
        guard_wait(bld, "tensor", "pbt")
        W8(bld, "tensor", bld.s_ve, tt_v)
        for pi in range(4):
            pj = 4 * rnd + pi
            for par in range(2):
                h = 2 * pj + par
                inc = bld.s_pe if (pi == 3 and par == 1) else None
                E(bld, "tensor", lambda e, h=h, pi=pi, par=par: e.transpose(
                    bld.pbt[64 * par:64 * (par + 1), 128 * pi:128 * (pi + 1)],
                    bld.Ttok[:, 64 * h:64 * (h + 1)], bld.ident_bf[:],
                    tile_position=(0, 64 * par)), inc=inc)
        tf_pe = bld.s_pe.v
        W8(bld, "vector", bld.s_pe, tf_pe)
        E(bld, "vector", lambda e, rnd=rnd: e.tensor_copy(
            bld.Tfm[:, 512 * rnd:512 * (rnd + 1)], bld.pbt[:, 0:512]), inc=bld.s_ve)
        guard_set(bld, "pbt", (bld.s_ve, bld.s_ve.v))
    guard_set(bld, "Ttok", (bld.s_pe, bld.s_pe.v))
    tfm_v = bld.s_ve.v
    W8(bld, "sync", bld.s_ve, tfm_v)
    E(bld, "sync", lambda e, ch=ch: e.dma_start(bld.dram["sc_T"][ch], bld.Tfm[:]),
      inc=bld.d_sy, n=16)

    guard_wait(bld, "sync", "P_write")
    E(bld, "sync", lambda e, ch=ch: e.dma_start(bld.dram["sc_P"][ch], bld.P_all[:]),
      inc=bld.d_sy, n=16)
    psnap = fence(bld, "sync")

    guard_wait(bld, "tensor", "pb6")
    W8(bld, "tensor", bld.s_ve, tfm_v)
    for pj in range(8):
        for par in range(2):
            h = 2 * pj + par
            inc = bld.s_pe if (pj == 7 and par == 1) else None
            E(bld, "tensor", lambda e, h=h, pj=pj, par=par: e.matmul(
                bld.pb[6][64 * par:64 * (par + 1), 64 * pj:64 * (pj + 1)],
                bld.X_all[:, 128 * h + 64:128 * (h + 1)], bld.LbK[:, 64 * h:64 * (h + 1)],
                start=True, stop=True, tile_position=(0, 64 * par)), inc=inc)
    g_pe = bld.s_pe.v
    W8(bld, "vector", bld.s_pe, g_pe)
    E(bld, "vector", lambda e: e.scalar_tensor_tensor(
        bld.G_all[:].rearrange("p (h x) -> p h x", x=64),
        bld.i64_bf[:].unsqueeze(1).broadcast_to([128, 8, 64]),
        lam_c(CL_LAMC), bld.pb[6][:].rearrange("p (h x) -> p h x", x=64),
        ALU.mult, ALU.subtract), inc=bld.s_ve)
    g_v = bld.s_ve.v
    guard_set(bld, "pb6", (bld.s_ve, g_v))

    guard_wait(bld, "tensor", "pb6")
    for pj in range(8):
        for par in range(2):
            h = 2 * pj + par
            E(bld, "tensor", lambda e, h=h, pj=pj, par=par: e.matmul(
                bld.pb[6][64 * par:64 * (par + 1), 64 * pj:64 * (pj + 1)],
                bld.U_all[:, 64 * h:64 * (h + 1)], bld.LbK[:, 64 * h:64 * (h + 1)],
                start=True, stop=True, tile_position=(0, 64 * par)))
    guard_wait(bld, "tensor", "pb5")
    for pj in range(8):
        for par in range(2):
            h = 2 * pj + par
            inc = bld.s_pe if (pj == 7 and par == 1) else None
            E(bld, "tensor", lambda e, h=h, pj=pj, par=par: e.matmul(
                bld.pb[5][64 * par:64 * (par + 1), 64 * pj:64 * (pj + 1)],
                bld.LbK[:, 64 * h:64 * (h + 1)], bld.U_all[:, 64 * h:64 * (h + 1)],
                start=True, stop=True, tile_position=(0, 64 * par)), inc=inc)
    mm_pe = bld.s_pe.v
    for r in mtbf_readers_pe:
        pass
    W8(bld, "vector", bld.s_pe, mm_pe)
    E(bld, "vector", lambda e: e.scalar_tensor_tensor(
        bld.M_all[:], bld.M_all[:], lam_c(CL_LAMC), bld.pb[6][:], ALU.mult, ALU.add))
    E(bld, "vector", lambda e: e.scalar_tensor_tensor(
        bld.MT_all[:], bld.MT_all[:], lam_c(CL_LAMC), bld.pb[5][:], ALU.mult, ALU.add),
      inc=bld.s_ve)
    m_v = bld.s_ve.v
    guard_wait(bld, "vector", "MTbf_w")
    E(bld, "vector", lambda e: e.tensor_copy(bld.MTbf[:], bld.MT_all[:]), inc=bld.s_ve)
    guard_set(bld, "pb5", (bld.s_ve, bld.s_ve.v))
    guard_set(bld, "pb6", (bld.s_ve, bld.s_ve.v))

    W8(bld, "tensor", bld.s_ve, g_v)
    W8(bld, "tensor", bld.s_sy, psnap)
    guard_wait(bld, "tensor", "pb3")
    for pj in range(8):
        for par in range(2):
            E(bld, "tensor", lambda e, pj=pj, par=par: e.matmul(
                bld.pb[3][64 * par:64 * (par + 1), 64 * pj:64 * (pj + 1)],
                bld.PT_all[64 * par:64 * (par + 1), 64 * pj:64 * (pj + 1)],
                bld.G_all[64 * par:64 * (par + 1), 64 * pj:64 * (pj + 1)],
                start=True, stop=True, tile_position=(64 * par, 64 * par)))
    guard_wait(bld, "tensor", "pb2")
    for pj in range(8):
        for par in range(2):
            inc = bld.s_pe if (pj == 7 and par == 1) else None
            E(bld, "tensor", lambda e, pj=pj, par=par: e.matmul(
                bld.pb[2][64 * par:64 * (par + 1), 64 * pj:64 * (pj + 1)],
                bld.G_all[64 * par:64 * (par + 1), 64 * pj:64 * (pj + 1)],
                bld.PT_all[64 * par:64 * (par + 1), 64 * pj:64 * (pj + 1)],
                start=True, stop=True, tile_position=(64 * par, 64 * par)), inc=inc)
    p_pe = bld.s_pe.v
    W8(bld, "vector", bld.s_pe, p_pe)
    E(bld, "vector", lambda e: e.tensor_copy(bld.P_all[:], bld.pb[3][:]))
    E(bld, "vector", lambda e: e.tensor_copy(bld.PT_all[:], bld.pb[2][:]), inc=bld.s_ve)
    p_v = bld.s_ve.v
    guard_set(bld, "pb2", (bld.s_ve, p_v))
    guard_set(bld, "pb3", (bld.s_ve, p_v))
    guard_set(bld, "P_write", (bld.s_ve, p_v))
    guard_set(bld, "Wfm", (bld.s_pe, mm_pe))
    guard_set(bld, "U_all", (bld.s_pe, mm_pe))
    guard_set(bld, "LbK", (bld.s_pe, mm_pe))
    guard_set(bld, "X_all", (bld.s_pe, g_pe))
    guard_set(bld, "NBt", (bld.s_pe, g_pe))
    guard_set(bld, "MTbf_w", (bld.s_pe, mm_pe))
    guard_set(bld, "Tfm", (bld.s_sy, psnap))


def emit_dn_ag(bld: Build):
    p = bld.prog
    W8(bld, "sync", bld.s_ve, bld.s_ve.v)
    E(bld, "sync", lambda e: e.dma_start(bld.dram["cc_m_in"][:, 0:512], bld.M_all[:]),
      inc=bld.d_sy, n=16)
    E(bld, "sync", lambda e: e.dma_start(bld.dram["cc_m_in"][:, 512:1024], bld.MT_all[:]),
      inc=bld.d_sy, n=16)
    ag_in = fence(bld, "sync")
    W8(bld, "gpsimd", bld.s_sy, ag_in)
    cc = E(bld, "gpsimd", lambda e: e.collective_compute(
        "AllGather", ALU.bypass, replica_groups=[[0, 1], [2, 3], [4, 5], [6, 7]],
        ins=[bld.dram["cc_m_in"].opt()], outs=[bld.dram["cc_m_out"].opt()]), inc=bld.s_cc)
    W8(bld, "sync", bld.s_cc, cc)
    guard_wait(bld, "sync", "xrow0")
    E(bld, "sync", lambda e: e.dma_start(bld.xrow[0][:], bld.dram["cc_m_out"][0:128, :]),
      inc=bld.d_sy, n=16)
    ld = fence(bld, "sync")
    W8(bld, "vector", bld.s_sy, ld)
    E(bld, "vector", lambda e: e.tensor_scalar_mul(bld.Minit[:], bld.xrow[0][:, 0:512],
                                                   bld.dn_cols[:, CL_SEL:CL_SEL + 1]))
    E(bld, "vector", lambda e: e.tensor_scalar_mul(bld.MinitT[:], bld.xrow[0][:, 512:1024],
                                                   bld.dn_cols[:, CL_SEL:CL_SEL + 1]))
    E(bld, "vector", lambda e: e.tensor_copy(bld.MinitTbf[:], bld.MinitT[:]), inc=bld.s_ve)


def emit_dn_corr(bld: Build, ch):
    p = bld.prog
    guard_wait(bld, "sync", "corr_ld")
    E(bld, "sync", lambda e, ch=ch: e.dma_start(bld.Pld[:], bld.dram["sc_P"][ch]),
      inc=bld.d_sy, n=16)
    E(bld, "sync", lambda e, ch=ch: e.dma_start(bld.Tld[:], bld.dram["sc_T"][ch]),
      inc=bld.d_sy, n=16)
    E(bld, "sync", lambda e, ch=ch: e.dma_start(
        bld.o0ld[:], bld.dram["sc_o0"][128 * ch:128 * (ch + 1), :]), inc=bld.d_sy, n=16)
    l3 = fence(bld, "sync")
    guard_wait(bld, "tensor", "pb6")
    W8(bld, "tensor", bld.s_sy, l3)
    W8(bld, "tensor", bld.s_ve, bld.s_ve.v)
    for pj in range(8):
        for par in range(2):
            inc = bld.s_pe if (pj == 7 and par == 1) else None
            E(bld, "tensor", lambda e, pj=pj, par=par: e.matmul(
                bld.pb[6][64 * par:64 * (par + 1), 64 * pj:64 * (pj + 1)],
                bld.Pld[64 * par:64 * (par + 1), 64 * pj:64 * (pj + 1)],
                bld.MinitT[64 * par:64 * (par + 1), 64 * pj:64 * (pj + 1)],
                start=True, stop=True, tile_position=(64 * par, 64 * par)), inc=inc)
    dm_pe = bld.s_pe.v
    W8(bld, "vector", bld.s_pe, dm_pe)
    guard_wait(bld, "vector", "dMbf")
    dm_v = E(bld, "vector", lambda e: e.tensor_copy(bld.dMbf[:], bld.pb[6][:]), inc=bld.s_ve)
    guard_set(bld, "pb6", (bld.s_ve, dm_v))
    for par in range(2):
        bank = 2 if par == 0 else 3
        guard_wait(bld, "tensor", f"pb{bank}")
        W8(bld, "tensor", bld.s_ve, dm_v)
        W8(bld, "tensor", bld.s_sy, l3)
        for pj in range(8):
            inc = bld.s_pe if pj == 7 else None
            E(bld, "tensor", lambda e, pj=pj, par=par, bank=bank: e.matmul(
                bld.pb[bank][:, 64 * pj:64 * (pj + 1)],
                bld.Tld[64 * par:64 * (par + 1), 128 * pj:128 * (pj + 1)],
                bld.dMbf[64 * par:64 * (par + 1), 64 * pj:64 * (pj + 1)],
                start=True, stop=True, tile_position=(64 * par, 0)), inc=inc)
    do_pe = bld.s_pe.v
    guard_set(bld, "dMbf", (bld.s_pe, do_pe))
    W8(bld, "vector", bld.s_pe, do_pe)
    W8(bld, "vector", bld.s_sy, l3)
    guard_wait(bld, "vector", "o0_all")
    for par in range(2):
        bank = 2 if par == 0 else 3
        ov = bld.o0_all[:].rearrange("p (hh pp x) -> p hh pp x", pp=2, x=64)[:, :, par, :]
        iv = bld.o0ld[:].rearrange("p (hh pp x) -> p hh pp x", pp=2, x=64)[:, :, par, :]
        pv = bld.pb[bank][:].rearrange("p (h x) -> p h x", x=64)
        inc = bld.s_ve if par == 1 else None
        E(bld, "vector", lambda e, ov=ov, iv=iv, pv=pv: e.tensor_tensor(ov, iv, pv, ALU.add),
          inc=inc)
    of_v = bld.s_ve.v
    guard_set(bld, "pb2", (bld.s_ve, of_v))
    guard_set(bld, "pb3", (bld.s_ve, of_v))
    W8(bld, "sync", bld.s_ve, of_v)
    E(bld, "sync", lambda e, ch=ch: e.dma_start(
        bld.dram["sc_ofin"][128 * ch:128 * (ch + 1), :], bld.o0_all[:]), inc=bld.d_sy, n=16)
    st = fence(bld, "sync")
    guard_set(bld, "o0_all", (bld.s_sy, st))
    guard_set(bld, "corr_ld", (bld.s_pe, do_pe), (bld.s_ve, of_v))


def emit_s_tile(bld: Build, ti):
    p = bld.prog
    guard_wait(bld, "sync", "s_ld")
    E(bld, "sync", lambda e, ti=ti: e.dma_start(
        bld.st_ld[:], bld.dram["sc_static"][128 * ti:128 * (ti + 1), :]), inc=bld.d_sy, n=16)
    E(bld, "sync", lambda e, ti=ti: e.dma_start(
        bld.of_ld[:], bld.dram["sc_ofin"][128 * ti:128 * (ti + 1), :]), inc=bld.d_sy, n=16)
    l2 = fence(bld, "sync")
    W8(bld, "vector", bld.s_sy, l2)
    guard_wait(bld, "vector", "yrow")
    c = bld.cols
    cb = 12
    E(bld, "vector", lambda e: e.tensor_add(bld.yrow[:], bld.st_ld[:], bld.of_ld[:]))
    E(bld, "vector", lambda e: e.reduce_sum(c[:, cb + 0:cb + 1], bld.yrow[:], axis=AX.X),
      inc=bld.s_ve)
    y_v = bld.s_ve.v
    W8(bld, "scalar", bld.s_ve, y_v)
    guard_wait(bld, "scalar", "x2")
    sq_a = E(bld, "scalar", lambda e: e.activation(bld.x2[:], bld.yrow[:], AF.Square), inc=bld.s_ac)
    W8(bld, "vector", bld.s_ac, sq_a)
    x2r = E(bld, "vector", lambda e: e.reduce_sum(c[:, cb + 1:cb + 2], bld.x2[:], axis=AX.X))
    guard_set(bld, "x2", (bld.s_ve, x2r))
    E(bld, "vector", lambda e: e.tensor_scalar_mul(c[:, cb + 2:cb + 3], c[:, cb + 0:cb + 1], 1.0 / D))
    E(bld, "vector", lambda e: e.tensor_scalar_mul(c[:, cb + 3:cb + 4], c[:, cb + 2:cb + 3], -1.0))
    E(bld, "vector", lambda e: e.tensor_scalar_mul(c[:, cb + 4:cb + 5], c[:, cb + 1:cb + 2], 1.0 / D))
    E(bld, "vector", lambda e: e.tensor_mul(c[:, cb + 5:cb + 6], c[:, cb + 2:cb + 3], c[:, cb + 2:cb + 3]))
    E(bld, "vector", lambda e: e.tensor_sub(c[:, cb + 6:cb + 7], c[:, cb + 4:cb + 5], c[:, cb + 5:cb + 6]))
    E(bld, "vector", lambda e: e.tensor_scalar_add(c[:, cb + 6:cb + 7], c[:, cb + 6:cb + 7], LN_EPS),
      inc=bld.s_ve)
    W8(bld, "scalar", bld.s_ve, bld.s_ve.v)
    sr_a = E(bld, "scalar", lambda e: e.activation(c[:, cb + 7:cb + 8], c[:, cb + 6:cb + 7], AF.Sqrt),
             inc=bld.s_ac)
    W8(bld, "vector", bld.s_ac, sr_a)
    E(bld, "vector", lambda e: e.reciprocal(c[:, cb + 8:cb + 9], c[:, cb + 7:cb + 8]))
    E(bld, "vector", lambda e: e.tensor_scalar(bld.yrow[:], bld.yrow[:], c[:, cb + 3:cb + 4],
                                               c[:, cb + 8:cb + 9], ALU.add, ALU.mult))
    E(bld, "vector", lambda e: e.tensor_mul(bld.yrow[:], bld.yrow[:], bld.gamma_b[:]))
    guard_wait(bld, "vector", "s_bf")
    sv = E(bld, "vector", lambda e: e.tensor_add(bld.s_bf[:], bld.yrow[:], bld.beta_b[:]),
           inc=bld.s_ve)
    guard_set(bld, "yrow", (bld.s_ve, sv))
    guard_set(bld, "s_ld", (bld.s_ve, sv))
    W8(bld, "sync", bld.s_ve, sv)
    E(bld, "sync", lambda e, ti=ti: e.dma_start(
        bld.dram["sc_s"][128 * ti:128 * (ti + 1), :], bld.s_bf[:]), inc=bld.d_sy, n=16)
    st = fence(bld, "sync")
    guard_set(bld, "s_bf", (bld.s_sy, st))


def emit_dn_phase(bld: Build, n_tiles=NT, n_chunks=NCH):
    for tile in range(n_tiles):
        emit_tile_prep(bld, tile)
        if 4 <= tile < 4 + n_chunks:
            emit_dn_chunk(bld, tile - 4)
    emit_dn_ag(bld)
    for ch in range(n_chunks):
        emit_dn_corr(bld, ch)
    for ti in range(n_chunks):
        emit_s_tile(bld, ti)


def finish_program(bld: Build):
    for q, s in (("sync", bld.d_sy), ("gpsimd", bld.d_gp), ("scalar", bld.d_sc)):
        if s.v:
            W8(bld, q, s, s.v)


def build_dn_standalone(n_tiles=NT, n_chunks=NCH):
    bld = Build(with_transformer=False)
    build_common(bld)
    emit_setup(bld)
    emit_dn_phase(bld, n_tiles, n_chunks)
    sdone = fence(bld, "sync")
    W8(bld, "gpsimd", bld.s_sy, sdone)
    nrows = 128 * n_chunks
    step = 128
    for i in range(0, nrows, step):
        E(bld, "gpsimd", lambda e, i=i: e.dma_start(
            bld.dram["out"][i:i + step, :],
            bld.dram["sc_s"][i:i + step, :]), inc=bld.d_gp, n=16)
    finish_program(bld)
    with bld.es, bld.nc.Block() as blk:
        bld.prog.play(blk)
    return bld


def core_input_map(inputs, core_id):
    b, half = core_id // 2, core_id % 2
    gstart = 4096 * half - HALO
    x = np.asarray(inputs["L0_attn_x"][b], np.float32)
    xloc = np.zeros((TLOC, D), np.float32)
    lo = max(0, gstart)
    xloc[lo - gstart:] = x[lo:gstart + TLOC]
    m = dict(_host_consts(core_id))
    m["x_in"] = xloc
    m["Wt_d"] = np.ascontiguousarray(np.asarray(inputs["L0_attn_W"], np.float32).T)
    m["ln_gamma"] = np.asarray(inputs["L0_attn_ln_gamma"], np.float32).reshape(1, D)
    m["ln_beta"] = np.asarray(inputs["L0_attn_ln_beta"], np.float32).reshape(1, D)
    m["eta"] = np.asarray(inputs["L0_attn_eta"], np.float32).reshape(1, 1)
    m["lam_logit"] = np.asarray(inputs["L0_attn_lam_logit"], np.float32).reshape(1, 1)
    m["Wqkv"] = np.asarray(inputs["L0_ffn_Wqkv"], np.float32)
    m["Wo"] = np.asarray(inputs["L0_ffn_Wo"], np.float32)
    m["W1"] = np.asarray(inputs["L0_ffn_W1"], np.float32)
    m["W2"] = np.asarray(inputs["L0_ffn_W2"], np.float32)
    b1 = np.asarray(inputs["L0_ffn_b1"], np.float32)
    b1a = np.pad(b1[:, :DIM_INNER], ((0, 0), (0, 22 * 128 - DIM_INNER))).reshape(NLAYER, 22, 128)
    b1g = np.pad(b1[:, DIM_INNER:], ((0, 0), (0, 22 * 128 - DIM_INNER))).reshape(NLAYER, 22, 128)
    m["b1cols"] = np.ascontiguousarray(np.stack([b1a, b1g], axis=1).reshape(NLAYER, 2, 22, 128, 1))
    m["b2cols"] = np.ascontiguousarray(
        np.asarray(inputs["L0_ffn_b2"], np.float32).reshape(NLAYER, 8, 128, 1))
    m["ffn_g"] = np.ascontiguousarray(
        np.asarray(inputs["L0_ffn_ln_g"], np.float32).reshape(NLAYER, 8, 128, 1))
    m["ffn_b"] = np.ascontiguousarray(
        np.asarray(inputs["L0_ffn_ln_b"], np.float32).reshape(NLAYER, 8, 128, 1))
    qs2 = np.tile(np.asarray(inputs["L0_ffn_q_scale"], np.float32), (1, 2))
    ks2 = np.tile(np.asarray(inputs["L0_ffn_k_scale"], np.float32), (1, 2))
    m["qk_scale"] = np.ascontiguousarray(
        np.stack([qs2, ks2], axis=1).reshape(NLAYER, 2, 128, 1))
    m["ones_col128"] = np.ones((128, 1), np.float32)
    return m


def phase_barrier(bld: Build):
    for q in ("sync", "scalar", "gpsimd"):
        fence(bld, q)
    sems = (bld.s_pe, bld.s_ve, bld.s_ac, bld.s_gp, bld.s_sy, bld.s_sg, bld.s_gq, bld.s_cc)
    for eng in ENGINES:
        for s in sems:
            if s.v:
                W8(bld, eng, s, s.v)


def tf_alloc(bld: Build):
    s = lambda n, sh, dt: bld.sbuf(n, sh, dt, scope=bld.es_tf)
    bld.wqk_buf = s("wqk_buf", [128, 8 * 512], F16)
    bld.wv_buf = s("wv_buf", [128, 8 * 512], F16)
    bld.wo_sb = s("wo_sb", [128, 8 * 1024], F16)
    bld.masks_sb = s("masks_sb", [128, 3 * 768], BF16)
    bld.i64h = s("i64h", [128, 64], BF16)
    bld.co_sb = s("co_sb", [128, 8 * 16], BF16)
    bld.coT_sb = s("coT_sb", [16, 8 * 128], BF16)
    bld.lcols = s("lcols", [128, 64], F32)
    bld.b1c = s("b1c", [128, 44], F32)
    bld.qbuf = s("qbuf", [128, 8 * 512], BF16)
    bld.kbuf = s("kbuf", [128, 8 * 768], BF16)
    bld.vbuf = s("vbuf", [128, 6 * 1024], F16)
    bld.rotA = s("rotA", [128, 512], BF16)
    bld.rotB = s("rotB", [128, 512], BF16)

    bld.q2t = s("q2t", [128, 512], BF16)
    bld.q2t2 = s("q2t2", [128, 512], BF16)
    bld.nrm = s("nrm", [16, 512], BF16)
    bld.Pbuf = s("Pbuf", [128, 4 * 384], BF16)
    bld.PTb = [s(f"PTb{i}", [128, 128], F16) for i in range(12)]
    bld.ofm = s("ofm", [128, 8 * 128], F16)
    bld.m_sb = s("m_sb", [128, 22 * 512], BF16)
    bld.xn_t = bld.qbuf
    bld.w1b = [s(f"w1b{i}", [128, 8 * 256], BF16) for i in range(2)]
    bld.w2b = [s(f"w2b{i}", [128, 1024], BF16) for i in range(2)]
    bld.ones16 = s("ones16", [128, 1], F16)
    bld.hsave = s("hsave", [128, 8 * 9], F16)
    bld.hdelta = s("hdelta", [128, 8], F16)


def emit_tf_setup(bld: Build):
    d = bld.dram
    E(bld, "gpsimd", lambda e: e.dma_start(
        bld.masks_sb[:].rearrange("p (s n) -> p s n", n=768),
        d["attn_masks"].rearrange("s p n -> p s n")), inc=bld.d_gp, n=16)
    E(bld, "gpsimd", lambda e: e.dma_start(bld.i64h[:], d["ident64h"]), inc=bld.d_gp, n=16)
    E(bld, "gpsimd", lambda e: e.dma_start(
        bld.co_sb[:].rearrange("p (j n) -> p j n", n=16),
        d["chunk_ones"].rearrange("j p n -> p j n")), inc=bld.d_gp, n=16)
    E(bld, "gpsimd", lambda e: e.dma_start(bld.ones16[:], d["ones_col128"]), inc=bld.d_gp, n=16)
    E(bld, "gpsimd", lambda e: e.memset(bld.kbuf[:], 0.0), inc=bld.s_gp)
    E(bld, "gpsimd", lambda e: e.memset(bld.vbuf[:], 0.0), inc=bld.s_gp)
    E(bld, "gpsimd", lambda e: e.dma_start(
        bld.coT_sb[:].rearrange("p (j n) -> p j n", n=128),
        d["chunk_onesT"].rearrange("j p n -> p j n")), inc=bld.d_gp, n=16)
    gq = fence(bld, "gpsimd")
    sy = fence(bld, "sync")
    for eng in ("tensor", "vector", "scalar"):
        W8(bld, eng, bld.s_gq, gq)
        W8(bld, eng, bld.s_sy, sy)
        W8(bld, eng, bld.s_gp, bld.s_gp.v)


LQ_QS, LQ_KS = 0, 1
LQ_G0 = 2
LQ_B0 = 10
LQ_B2 = 18


def emit_layer_loads(bld: Build, l):
    d = bld.dram

    guard_wait(bld, "gpsimd", "wo")
    for vc in range(8):
        bld.wo_tot = E(bld, "gpsimd", lambda e, vc=vc, l=l: e.dma_start(
            bld.wo_sb[:, 1024 * vc:1024 * (vc + 1)],
            d["Wo"][l][128 * vc:128 * (vc + 1), :]), inc=bld.d_gp, n=16)
    guard_wait(bld, "sync", "lcols")
    E(bld, "sync", lambda e, l=l: e.dma_start(bld.lcols[:, 0:1], d["qk_scale"][l][0]), inc=bld.d_sy, n=16)
    E(bld, "sync", lambda e, l=l: e.dma_start(bld.lcols[:, 1:2], d["qk_scale"][l][1]), inc=bld.d_sy, n=16)
    E(bld, "sync", lambda e, l=l: e.dma_start(
        bld.lcols[:, LQ_G0:LQ_G0 + 8].rearrange("p (c o) -> p c o", o=1),
        d["ffn_g"][l].rearrange("c p o -> p c o"), allow_slow_non_contiguous=True), inc=bld.d_sy, n=16)
    E(bld, "sync", lambda e, l=l: e.dma_start(
        bld.lcols[:, LQ_B0:LQ_B0 + 8].rearrange("p (c o) -> p c o", o=1),
        d["ffn_b"][l].rearrange("c p o -> p c o"), allow_slow_non_contiguous=True), inc=bld.d_sy, n=16)
    E(bld, "sync", lambda e, l=l: e.dma_start(
        bld.lcols[:, LQ_B2:LQ_B2 + 8].rearrange("p (c o) -> p c o", o=1),
        d["b2cols"][l].rearrange("c p o -> p c o"), allow_slow_non_contiguous=True), inc=bld.d_sy, n=16)
    E(bld, "sync", lambda e, l=l: e.dma_start(
        bld.b1c[:, 0:44].rearrange("p (s c o) -> p s c o", s=2, o=1),
        d["b1cols"][l].rearrange("s c p o -> p s c o"), allow_slow_non_contiguous=True), inc=bld.d_sy, n=16)
    return fence(bld, "sync")


def emit_halo_ag(bld: Build, l):
    d = bld.dram
    hv = bld.h[:].rearrange("p (c t) -> p c t", t=TLOC)
    W8(bld, "sync", bld.s_ve, bld.s_ve.v)
    E(bld, "sync", lambda e, hv=hv: e.dma_start(
        d["cc_h_in"].rearrange("(c p) t -> p c t", p=128), hv[:, :, TLOC - HALO:TLOC]),
      inc=bld.d_sy, n=16)
    agi = fence(bld, "sync")
    W8(bld, "gpsimd", bld.s_sy, agi)
    cc = E(bld, "gpsimd", lambda e: e.collective_compute(
        "AllGather", ALU.bypass, replica_groups=[[0, 1], [2, 3], [4, 5], [6, 7]],
        ins=[d["cc_h_in"].opt()], outs=[d["cc_h_out"].opt()]), inc=bld.s_cc)
    W8(bld, "sync", bld.s_cc, cc)
    guard_wait(bld, "sync", "vbuf")
    W8(bld, "sync", bld.s_pe, bld.s_pe.v)
    E(bld, "sync", lambda e: e.dma_start(
        bld.vbuf[:, 0:4096].rearrange("p (c t) -> p c t", t=512),
        d["cc_h_out"][0:1024, :].rearrange("(c p) t -> p c t", p=128)), inc=bld.d_sy, n=16)
    ld = fence(bld, "sync")
    W8(bld, "vector", bld.s_sy, ld)
    W8(bld, "vector", bld.s_pe, bld.s_pe.v)
    for kc in range(8):
        E(bld, "vector", lambda e, kc=kc, hv=hv: e.tensor_scalar_mul(
            hv[:, kc, 0:HALO], bld.vbuf[:, 512 * kc:512 * kc + HALO],
            bld.dn_cols[:, CL_SEL:CL_SEL + 1]))
    hf = E(bld, "vector", lambda e, hv=hv: e.tensor_copy(hv[:, 0, 0:1], hv[:, 0, 0:1]))
    guard_set(bld, "vbuf", (bld.s_ve, hf))
    return hf


def emit_attn_chunk(bld: Build, l, ci, pv_carry):
    d = bld.dram
    t0 = 512 * ci
    width = min(512, TLOC - t0)
    hv = bld.h[:].rearrange("p (c t) -> p c t", t=TLOC)


    guard_wait(bld, "vector", "ring")
    for kc in range(8):
        E(bld, "vector", lambda e, kc=kc: e.tensor_copy(
            bld.kbuf[:, 768 * kc:768 * kc + 256], bld.kbuf[:, 768 * kc + 512:768 * kc + 768]))
    for i in range(2):
        E(bld, "vector", lambda e, i=i: e.tensor_copy(
            bld.vbuf[:, 1024 * i:1024 * (i + 1)], bld.vbuf[:, 1024 * (i + 4):1024 * (i + 5)]))
    ring_v = bld.s_ve.v

    if ci >= 1:
        E(bld, "vector", lambda e, ci=ci, t0=t0: e.tensor_sub(
            bld.hdelta[:], bld.hsave[:].rearrange("p (c i) -> p c i", i=9)[:, :, ci - 1],
            bld.h[:].rearrange("p (c t) -> p c t", t=TLOC)[:, :, t0 - 1]), inc=bld.s_ve)
    delta_v = bld.s_ve.v

    nrm_bank = bld.pb[6]
    qkv_bufs = (bld.wqk_buf, bld.wv_buf)
    sq_v = {}
    tc_q = (ci == 0)

    def iss_qkv(idx, col0):
        p = idx % 2
        guard_wait(bld, "gpsimd", f"qkv{p}")
        pend = bld.wb_pend.pop(("qkv", p), None)
        if pend:
            W8(bld, "gpsimd", bld.d_wb, pend)
        src = d["Wqkv"] if tc_q else d["sc_wq"]
        tot = E(bld, "gpsimd", lambda e, l=l, col0=col0, p=p, src=src: e.dma_start(
            qkv_bufs[p][:], src[l].rearrange("(c p) n -> p c n", p=128)
            [:, :, col0:col0 + 512]), inc=bld.d_qk[p], n=16)
        if tc_q:
            W8(bld, "scalar", bld.d_qk[p], tot)
            wt = E(bld, "scalar", lambda e, l=l, col0=col0, p=p: e.dma_start(
                d["sc_wq"][l].rearrange("(c p) n -> p c n", p=128)[:, :, col0:col0 + 512],
                qkv_bufs[p][:].rearrange("p (c n) -> p c n", n=512)), inc=bld.d_wb, n=16)
            bld.wb_pend[("qkv", p)] = wt
        return tot

    for side, base_col in (("q", 0), ("k", 1024)):
        for mh in range(2):
          idx = (0 if side == "q" else 2) + mh
          qtot = iss_qkv(idx, base_col + 512 * mh)
          wbuf = qkv_bufs[idx % 2]
          W8(bld, "tensor", bld.d_qk[idx % 2], qtot)
          for m4 in range(4):
            m = 4 * mh + m4
            bank = bld.pb[m % 2]
            guard_wait(bld, "tensor", f"pb{m % 2}")
            if m == 0 and mh == 0 and side == "q":
                W8(bld, "tensor", bld.s_ve, ring_v)
            corr = ci >= 1
            if corr and m == 0:
                W8(bld, "tensor", bld.s_ve, delta_v)
            for kc in range(8):
                off = (1 if kc < 4 else 0)
                last = (kc == 7) and not corr
                inc = bld.s_pe if last else None
                E(bld, "tensor", lambda e, kc=kc, m4=m4, off=off, t0=t0, w=width,
                    bank=bank, last=last, wbuf=wbuf: e.matmul(
                    bank[:, 0:w], wbuf[:, 512 * kc + 128 * m4:512 * kc + 128 * (m4 + 1)],
                    bld.h[:, TLOC * kc + t0 - 1 + off: TLOC * kc + t0 - 1 + off + w],
                    start=(kc == 0), stop=last), inc=inc)
            if corr:
                for kc in range(4, 8):
                    last = kc == 7
                    inc = bld.s_pe if last else None
                    E(bld, "tensor", lambda e, kc=kc, m4=m4, bank=bank, last=last, wbuf=wbuf: e.matmul(
                        bank[:, 0:1], wbuf[:, 512 * kc + 128 * m4:512 * kc + 128 * (m4 + 1)],
                        bld.hdelta[:, kc:kc + 1], start=False, stop=last), inc=inc)
            mm_pe = bld.s_pe.v
            W8(bld, "vector", bld.s_pe, mm_pe)
            if side == "q":
                dst = bld.qbuf[:, 512 * m:512 * m + width]
            else:
                dst = bld.kbuf[:, 768 * m + 256:768 * m + 256 + width]
            ev = E(bld, "vector", lambda e, dst=dst, bank=bank, w=width: e.tensor_copy(
                dst, bank[:, 0:w]))
            W8(bld, "scalar", bld.s_ve, ev)
            slot = bld.q2t if m % 2 == 0 else bld.q2t2
            guard_wait(bld, "scalar", f"q2t{m % 2}")
            sq_v[m] = E(bld, "scalar", lambda e, dst=dst, w=width, slot=slot: e.activation(
                slot[:, 0:w], dst, AF.Square), inc=bld.s_ac)

            def nmm(mm, last):
                slot2 = bld.q2t if mm % 2 == 0 else bld.q2t2
                W8(bld, "tensor", bld.s_ac, sq_v[mm])
                if mm == 0:
                    guard_wait(bld, "tensor", "pb6")
                E(bld, "tensor", lambda e, mm=mm, w=width, slot2=slot2: e.matmul(
                    nrm_bank[0:16, 0:w], bld.co_sb[:, 16 * mm:16 * (mm + 1)], slot2[:, 0:w],
                    start=(mm == 0), stop=last), inc=(bld.s_pe if last else None))
                guard_set(bld, f"q2t{mm % 2}", (bld.s_pe, bld.s_pe.v))
            if m >= 1:
                nmm(m - 1, False)
            if m == 7:
                nmm(7, True)
            guard_set(bld, f"pb{m % 2}", (bld.s_ve, bld.s_ve.v))
          guard_set(bld, f"qkv{idx % 2}", (bld.s_pe, bld.s_pe.v))
        nq_pe = bld.s_pe.v
        W8(bld, "scalar", bld.s_pe, nq_pe)
        guard_wait(bld, "scalar", "nrm")
        sq2 = E(bld, "scalar", lambda e, w=width: e.activation(
            bld.nrm[:, 0:w], nrm_bank[0:16, 0:w], AF.Sqrt), inc=bld.s_ac)
        W8(bld, "vector", bld.s_ac, sq2)
        E(bld, "vector", lambda e, w=width: e.tensor_scalar_max(bld.nrm[:, 0:w], bld.nrm[:, 0:w], 1e-12))
        def _recip(e, w=width, bld=bld):
            with bld.nc.allow_low_precision(reason="l2norm recip in bf16 (bf16-matmul noise class)"):
                return e.reciprocal(bld.nrm[:, 0:w], bld.nrm[:, 0:w])
        rc = E(bld, "vector", _recip)
        guard_set(bld, "pb6", (bld.s_ac, sq2))
        scol = bld.lcols[:, LQ_QS:LQ_QS + 1] if side == "q" else bld.lcols[:, LQ_KS:LQ_KS + 1]
        rA, rB = bld.rotA, bld.rotB
        guard_wait(bld, "gpsimd", "rot")
        for dst, nm in ((rA, "rotAQ" if side == "q" else "rotAK"),
                        (rB, "rotBQ" if side == "q" else "rotBK")):
            rt_tot = E(bld, "gpsimd", lambda e, dst=dst, nm=nm, t0=t0, w=width: e.dma_start(
                dst[:, 0:w], d[nm][:, t0:t0 + w]), inc=bld.d_rt, n=16)
        for m in range(8):
            if side == "q":
                sl = bld.qbuf[:, 512 * m:512 * m + width]
            else:
                sl = bld.kbuf[:, 768 * m + 256:768 * m + 256 + width]
            guard_wait(bld, "tensor", "pb2")
            W8(bld, "tensor", bld.s_ve, rc)
            bc = E(bld, "tensor", lambda e, m=m, w=width: e.matmul(
                bld.pb[2][:, 0:w], bld.coT_sb[:, 128 * m:128 * (m + 1)], bld.nrm[:, 0:w],
                start=True, stop=True), inc=bld.s_pe)
            W8(bld, "vector", bld.s_pe, bc)
            E(bld, "vector", lambda e, sl=sl, scol=scol, w=width: e.scalar_tensor_tensor(
                sl, sl, scol, bld.pb[2][:, 0:w], ALU.mult, ALU.mult))
            guard_wait(bld, "tensor", "pb3")
            W8(bld, "tensor", bld.s_ve, bld.s_ve.v)
            rt = E(bld, "tensor", lambda e, sl=sl, w=width: e.matmul(
                bld.pb[3][:, 0:w], bld.rot_bf[:], sl, start=True, stop=True), inc=bld.s_pe)
            W8(bld, "vector", bld.s_pe, rt)
            W8(bld, "vector", bld.d_rt, rt_tot)
            guard_wait(bld, "vector", "q2t0")
            E(bld, "vector", lambda e, sl=sl, rA=rA, w=width: e.tensor_mul(
                bld.q2t[:, 0:w], sl, rA[:, 0:w]))
            E(bld, "vector", lambda e, sl=sl, rB=rB, w=width: e.tensor_tensor(
                sl, bld.pb[3][:, 0:w], rB[:, 0:w], ALU.mult))
            qr = E(bld, "vector", lambda e, sl=sl, w=width: e.tensor_add(sl, sl, bld.q2t[:, 0:w]))
            guard_set(bld, "pb2", (bld.s_ve, bld.s_ve.v))
            guard_set(bld, "pb3", (bld.s_ve, qr))
            guard_set(bld, "q2t0", (bld.s_ve, qr))
        guard_set(bld, "rot", (bld.s_ve, bld.s_ve.v))
    qk_done = bld.s_ve.v

    for half in range(2):
        idx = 4 + half
        vtot = iss_qkv(idx, 2048 + 512 * half)
        wbuf = qkv_bufs[idx % 2]
        W8(bld, "tensor", bld.d_qk[idx % 2], vtot)
        for tt in range(ceil_div(width, 128)):
            vt = 2 + tt
            bank = bld.pb[4 + tt % 2]
            guard_wait(bld, "tensor", f"pb{4 + tt % 2}")
            vcorr = (ci >= 1) and (tt == 0)
            for kc in range(8):
                off = (1 if kc < 4 else 0)
                last = (kc == 7) and not vcorr
                inc = bld.s_pe if last else None
                E(bld, "tensor", lambda e, kc=kc, tt=tt, off=off, t0=t0, bank=bank,
                    last=last, wbuf=wbuf: e.matmul(
                    bank[:, 0:512],
                    bld.h[:, TLOC * kc + t0 - 1 + off + 128 * tt: TLOC * kc + t0 - 1 + off + 128 * (tt + 1)],
                    wbuf[:, 512 * kc:512 * (kc + 1)],
                    start=(kc == 0), stop=last), inc=inc)
            if vcorr:
                for kc in range(4, 8):
                    last = kc == 7
                    inc = bld.s_pe if last else None
                    E(bld, "tensor", lambda e, kc=kc, bank=bank, last=last, wbuf=wbuf: e.matmul(
                        bank[0:1, 0:512], bld.hdelta[:, kc:kc + 1],
                        wbuf[:, 512 * kc:512 * (kc + 1)],
                        start=False, stop=last), inc=inc)
            v_pe = bld.s_pe.v
            W8(bld, "vector", bld.s_pe, v_pe)
            ev = E(bld, "vector", lambda e, vt=vt, hf=half, bank=bank: e.tensor_copy(
                bld.vbuf[:, 1024 * vt + 512 * hf:1024 * vt + 512 * (hf + 1)], bank[:, 0:512]))
            guard_set(bld, f"pb{4 + tt % 2}", (bld.s_ve, ev))
        guard_set(bld, f"qkv{idx % 2}", (bld.s_pe, bld.s_pe.v))
    v_done = bld.s_ve.v
    return qk_done, v_done


def emit_attn_qtile(bld: Build, l, ci, qt, qk_done, v_done):
    Q = 4 * ci + qt
    qoff = 128 * qt
    slot = {4: 1, 5: 2}.get(Q, 0)
    c = bld.cols
    pn_v, tr_v, ev_v = {}, {}, {}
    state = {"first_pv": True}

    def s_phase(h):
        par = h % 2
        pj = h // 2
        bank = bld.pb[2 + par]
        guard_wait(bld, "tensor", f"pb{2 + par}")
        if h < 2:
            W8(bld, "tensor", bld.s_ve, qk_done)
        E(bld, "tensor", lambda e, par=par, slot=slot: e.matmul(
            bld.pb[2 + par][0:64, 0:384], bld.i64h[64 * par:64 * (par + 1), :],
            bld.masks_sb[64 * par:64 * (par + 1), 768 * slot:768 * slot + 384],
            start=True, stop=False, tile_position=(64 * par, 0)))
        E(bld, "tensor", lambda e, par=par, slot=slot: e.matmul(
            bld.pb[2 + par][64:128, 0:384], bld.i64h[64 * par:64 * (par + 1), :],
            bld.masks_sb[64 * par:64 * (par + 1), 768 * slot + 384:768 * slot + 768],
            start=True, stop=False, tile_position=(64 * par, 64)))
        smm = E(bld, "tensor", lambda e, par=par, pj=pj, qoff=qoff: e.matmul(
            bld.pb[2 + par][:, 0:384],
            bld.qbuf[64 * par:64 * (par + 1), 512 * pj + qoff:512 * pj + qoff + 128],
            bld.kbuf[64 * par:64 * (par + 1), 768 * pj + qoff:768 * pj + qoff + 384],
            start=False, stop=True, tile_position=(64 * par, 0)), inc=bld.s_pe)
        W8(bld, "vector", bld.s_pe, smm)
        E(bld, "vector", lambda e, bank=bank: e.reduce_max(c[:, 12:13], bank[:, 0:384],
                                                           axis=AX.X, negate=True))
        nm = E(bld, "vector", lambda e: e.tensor_scalar_mul(c[:, 13:14], c[:, 12:13], 8.0))
        W8(bld, "scalar", bld.s_ve, nm)
        guard_wait(bld, "scalar", f"P{h % 4}")
        ex = E(bld, "scalar", lambda e, h=h, bank=bank: e.activation(
            bld.Pbuf[:, 384 * (h % 4):384 * (h % 4) + 384], bank[:, 0:384], AF.Exp,
            bias=c[:, 13:14], scale=8.0, accum_out=c[:, 14:15]), inc=bld.s_ac)
        W8(bld, "vector", bld.s_ac, ex)
        rc = E(bld, "vector", lambda e: e.reciprocal(c[:, 15:16], c[:, 14:15]))
        W8(bld, "scalar", bld.s_ve, rc)
        pn_v[h] = E(bld, "scalar", lambda e, h=h: e.mul(
            bld.Pbuf[:, 384 * (h % 4):384 * (h % 4) + 384],
            bld.Pbuf[:, 384 * (h % 4):384 * (h % 4) + 384], c[:, 15:16]), inc=bld.s_ac)
        guard_set(bld, f"pb{2 + par}", (bld.s_ac, ex))

    def tr_phase(h):
        if h >= 14:
            bank_bf = bld.pbt[:]
            base = (h - 14) * 3
            gkey = "pbtr"
        else:
            bank_bf = bld.pb[h % 2][:].bitcast(BF16)
            base = 0
            gkey = f"pb{h % 2}"
        guard_wait(bld, "tensor", gkey)
        W8(bld, "tensor", bld.s_ac, pn_v[h])
        for j in range(3):
            inc = bld.s_pe if j == 2 else None
            E(bld, "tensor", lambda e, h=h, j=j, bank_bf=bank_bf, base=base: e.transpose(
                bank_bf[:, 128 * (base + j):128 * (base + j + 1)],
                bld.Pbuf[:, 384 * (h % 4) + 128 * j:384 * (h % 4) + 128 * (j + 1)],
                bld.ident_bf[:]), inc=inc)
        tr_v[h] = bld.s_pe.v
        guard_set(bld, f"P{h % 4}", (bld.s_pe, tr_v[h]))
        W8(bld, "scalar", bld.s_pe, tr_v[h])
        for j in range(3):
            ev_v[h] = E(bld, "scalar", lambda e, h=h, j=j, bank_bf=bank_bf, base=base:
                e.activation(bld.PTb[(h % 4) * 3 + j][:],
                             bank_bf[:, 128 * (base + j):128 * (base + j + 1)], AF.Copy),
                inc=bld.s_ac)
        guard_set(bld, gkey, (bld.s_ac, ev_v[h]))

    def pv_phase(h):
        par = h % 2
        pj = h // 2
        bnk = 4 + (pj % 2)
        cslot = pj // 2
        if state["first_pv"]:
            guard_wait(bld, "tensor", "pb4")
            guard_wait(bld, "tensor", "pb5")
            W8(bld, "tensor", bld.s_ve, v_done)
            state["first_pv"] = False
        W8(bld, "tensor", bld.s_ac, ev_v[h])
        for kc in range(3):
            inc = bld.s_pe if kc == 2 else None
            E(bld, "tensor", lambda e, h=h, kc=kc, par=par, bnk=bnk, cslot=cslot: e.matmul(
                bld.pb[bnk][64 * par:64 * (par + 1), 128 * cslot:128 * (cslot + 1)],
                bld.vbuf[:, 1024 * (qt + kc) + 64 * h:1024 * (qt + kc) + 64 * (h + 1)],
                bld.PTb[(h % 4) * 3 + kc][:], start=(kc == 0), stop=(kc == 2),
                tile_position=(0, 64 * par)), inc=inc)

    for i in range(NH + 2):
        if i >= 2:
            tr_phase(i - 2)
        if i < NH:
            s_phase(i)
        if i >= 2:
            pv_phase(i - 2)
    pv_pe = bld.s_pe.v
    W8(bld, "scalar", bld.s_pe, pv_pe)
    guard_wait(bld, "scalar", "ofm")
    for b2 in range(2):
        dv = bld.ofm[:].rearrange("p (c b t) -> p c b t", b=2, t=128)[:, :, b2, :]
        ev = E(bld, "scalar", lambda e, dv=dv, b2=b2: e.activation(
            dv, bld.pb[4 + b2][:].rearrange("p (c t) -> p c t", t=128), AF.Copy),
            inc=bld.s_ac)
    guard_set(bld, "pb4", (bld.s_ac, ev))
    guard_set(bld, "pb5", (bld.s_ac, ev))
    return ev


def emit_wo_qtile(bld: Build, l, ci, qt, ofm_v):
    Q = 4 * ci + qt
    t0 = 128 * Q
    for dm in range(8):
        bank = bld.pb[2 + dm % 2]
        guard_wait(bld, "tensor", f"pb{2 + dm % 2}")
        if dm == 0:
            W8(bld, "tensor", bld.s_ac, ofm_v)
            W8(bld, "tensor", bld.d_gp, bld.wo_tot)
        for vc in range(8):
            inc = bld.s_pe if vc == 7 else None
            E(bld, "tensor", lambda e, dm=dm, vc=vc, bank=bank: e.matmul(
                bank[:, 0:128], bld.wo_sb[:, 1024 * vc + 128 * dm:1024 * vc + 128 * (dm + 1)],
                bld.ofm[:, 128 * vc:128 * (vc + 1)],
                start=(vc == 0), stop=(vc == 7)), inc=inc)
        wo_pe = bld.s_pe.v
        W8(bld, "vector", bld.s_pe, wo_pe)
        rs = E(bld, "vector", lambda e, dm=dm, t0=t0, bank=bank: e.tensor_tensor(
            bld.h[:, TLOC * dm + t0:TLOC * dm + t0 + 128],
            bld.h[:, TLOC * dm + t0:TLOC * dm + t0 + 128], bank[:, 0:128], ALU.add))
        guard_set(bld, f"pb{2 + dm % 2}", (bld.s_ve, rs))
    guard_set(bld, "ofm", (bld.s_pe, bld.s_pe.v))
    return bld.s_ve.v



def emit_ffn_ttile(bld: Build, l, ti):
    d = bld.dram
    t0 = HALO + 512 * ti
    c2 = bld.x2
    NJ = 22

    w1tot, w2tot = {}, {}
    tc = (ti == 7)

    def iss_w1(jp):
        p = jp % 2
        jw = 128 if jp < 21 else DIM_INNER - 128 * 21
        buf = bld.w1b[p]
        guard_wait(bld, "gpsimd", f"w1b{p}")
        pend = bld.wb_pend.pop(("w1", p), None)
        if pend:
            W8(bld, "gpsimd", bld.d_wb, pend)
        srcw = d["W1"] if tc else d["sc_w1"]
        E(bld, "gpsimd", lambda e, l=l, jp=jp, jw=jw, buf=buf, srcw=srcw: e.dma_start(
            buf[:].rearrange("p (c s j) -> p c s j", s=2, j=128)[:, :, 0, 0:jw],
            srcw[l].rearrange("(c p) n -> p c n", p=128)[:, :, 128 * jp:128 * jp + jw]),
            inc=bld.d_w1p[p], n=16)
        w1tot[jp] = E(bld, "gpsimd", lambda e, l=l, jp=jp, jw=jw, buf=buf, srcw=srcw: e.dma_start(
            buf[:].rearrange("p (c s j) -> p c s j", s=2, j=128)[:, :, 1, 0:jw],
            srcw[l].rearrange("(c p) n -> p c n", p=128)
            [:, :, DIM_INNER + 128 * jp:DIM_INNER + 128 * jp + jw]),
            inc=bld.d_w1p[p], n=16)
        if tc:
            W8(bld, "scalar", bld.d_w1p[p], w1tot[jp])
            for s2 in range(2):
                wt = E(bld, "scalar", lambda e, l=l, jp=jp, jw=jw, buf=buf, s2=s2: e.dma_start(
                    d["sc_w1"][l].rearrange("(c p) n -> p c n", p=128)
                    [:, :, DIM_INNER * s2 + 128 * jp:DIM_INNER * s2 + 128 * jp + jw],
                    buf[:].rearrange("p (c s j) -> p c s j", s=2, j=128)[:, :, s2, 0:jw]),
                    inc=bld.d_wb, n=16)
            bld.wb_pend[("w1", p)] = wt

    def iss_w2(jp):
        p = jp % 2
        jw = 128 if jp < 21 else DIM_INNER - 128 * 21
        buf = bld.w2b[p]
        qn = "gpsimd" if tc else "scalar"
        guard_wait(bld, qn, f"w2b{p}")
        pend = bld.wb_pend.pop(("w2", p), None)
        if pend:
            W8(bld, qn, bld.d_wb, pend)
        if tc:
            w2tot[jp] = E(bld, "gpsimd", lambda e, l=l, jp=jp, jw=jw, buf=buf: e.dma_start(
                buf[0:jw, :], d["W2"][l][128 * jp:128 * jp + jw, :]), inc=bld.d_w2p[p], n=16)
            W8(bld, "scalar", bld.d_w2p[p], w2tot[jp])
            wt = E(bld, "scalar", lambda e, l=l, jp=jp, jw=jw, buf=buf: e.dma_start(
                d["sc_w2"][l][128 * jp:128 * jp + jw, :], buf[0:jw, :]), inc=bld.d_wb, n=16)
            bld.wb_pend[("w2", p)] = wt
        else:
            if jp < 2:
                W8(bld, "scalar", bld.d_wb, bld.d_wb.v)
            w2tot[jp] = E(bld, "scalar", lambda e, l=l, jp=jp, jw=jw, buf=buf: e.dma_start(
                buf[0:jw, :], d["sc_w2"][l][128 * jp:128 * jp + jw, :]), inc=bld.d_w2p[p], n=16)

    iss_w1(0); iss_w1(1); iss_w2(0); iss_w2(1)

    guard_wait(bld, "scalar", "xn_t")
    for kc in range(8):
        off = (1 if kc < 4 else 0)
        inc = bld.s_ac if kc == 7 else None
        E(bld, "scalar", lambda e, kc=kc, off=off, t0=t0: e.activation(
            bld.xn_t[:, 512 * kc:512 * (kc + 1)],
            bld.h[:, TLOC * kc + t0 - 1 + off:TLOC * kc + t0 - 1 + off + 512], AF.Square),
            inc=inc)
    sq_a = bld.s_ac.v
    guard_wait(bld, "tensor", "pb6")
    guard_wait(bld, "tensor", "pb2")
    W8(bld, "tensor", bld.s_ac, sq_a)
    for kc in range(8):
        off = (1 if kc < 4 else 0)
        E(bld, "tensor", lambda e, kc=kc, off=off, t0=t0: e.matmul(
            bld.pb[6][0:1, 0:512], bld.ones16[:],
            bld.h[:, TLOC * kc + t0 - 1 + off:TLOC * kc + t0 - 1 + off + 512],
            start=(kc == 0), stop=(kc == 7)))
    for kc in range(8):
        inc = bld.s_pe if kc == 7 else None
        E(bld, "tensor", lambda e, kc=kc: e.matmul(
            bld.pb[2][0:1, 0:512], bld.ones16[:], bld.xn_t[:, 512 * kc:512 * (kc + 1)],
            start=(kc == 0), stop=(kc == 7)), inc=inc)
    st_pe = bld.s_pe.v
    W8(bld, "vector", bld.s_pe, st_pe)
    guard_wait(bld, "vector", "x2rows")
    A = c2[0:1, 0:512]
    Brow = c2[0:1, 512:1024]
    E(bld, "vector", lambda e: e.tensor_scalar_mul(A, bld.pb[6][0:1, 0:512], 1.0 / D))
    E(bld, "vector", lambda e: e.tensor_scalar_mul(Brow, bld.pb[2][0:1, 0:512], 1.0 / D))
    E(bld, "vector", lambda e: e.tensor_mul(bld.pb[6][0:1, 0:512], A, A))
    E(bld, "vector", lambda e: e.tensor_sub(Brow, Brow, bld.pb[6][0:1, 0:512]))
    sv = E(bld, "vector", lambda e: e.tensor_scalar_add(Brow, Brow, LN_EPS))
    W8(bld, "scalar", bld.s_ve, sv)
    sa = E(bld, "scalar", lambda e: e.activation(bld.pb[6][0:1, 0:512], Brow, AF.Sqrt),
           inc=bld.s_ac)
    W8(bld, "vector", bld.s_ac, sa)
    E(bld, "vector", lambda e: e.reciprocal(Brow, bld.pb[6][0:1, 0:512]))
    E(bld, "vector", lambda e: e.tensor_mul(A, A, Brow))
    nr = E(bld, "vector", lambda e: e.tensor_scalar_mul(A, A, -1.0))
    guard_wait(bld, "tensor", "pb3")
    W8(bld, "tensor", bld.s_ve, nr)
    E(bld, "tensor", lambda e: e.matmul(bld.pb[2][:, 0:512], bld.ones_row_f32[:],
                                        Brow, start=True, stop=True))
    bc = E(bld, "tensor", lambda e: e.matmul(bld.pb[3][:, 0:512], bld.ones_row_f32[:],
                                             A, start=True, stop=True), inc=bld.s_pe)
    guard_set(bld, "x2rows", (bld.s_pe, bc))
    W8(bld, "vector", bld.s_pe, bc)
    lnt = bld.Pbuf[:, 0:512]
    for kc in range(8):
        off = (1 if kc < 4 else 0)
        E(bld, "vector", lambda e, kc=kc, off=off, t0=t0: e.tensor_tensor(
            lnt, bld.h[:, TLOC * kc + t0 - 1 + off:TLOC * kc + t0 - 1 + off + 512],
            bld.pb[2][:, 0:512], ALU.mult))
        E(bld, "vector", lambda e: e.tensor_tensor(lnt, lnt, bld.pb[3][:, 0:512], ALU.add))
        E(bld, "vector", lambda e, kc=kc: e.tensor_scalar(
            bld.xn_t[:, 512 * kc:512 * (kc + 1)], lnt,
            bld.lcols[:, LQ_G0 + kc:LQ_G0 + kc + 1], bld.lcols[:, LQ_B0 + kc:LQ_B0 + kc + 1],
            ALU.mult, ALU.add))
    xn_v = bld.s_ve.v
    guard_set(bld, "pb2", (bld.s_ve, xn_v))
    guard_set(bld, "pb3", (bld.s_ve, xn_v))
    guard_set(bld, "xn_t", (bld.s_ve, xn_v))

    if True:
      for jp in range(NJ):
        jw = 128 if jp < 21 else DIM_INNER - 128 * 21
        buf = bld.w1b[jp % 2]
        for s2, bank in ((0, 0), (1, 1)):
            guard_wait(bld, "tensor", f"pb{bank}")
            if s2 == 0:
                W8(bld, "tensor", bld.d_w1p[jp % 2], w1tot[jp])
                if jp == 0:
                    W8(bld, "tensor", bld.s_ve, xn_v)
            for kc in range(8):
                inc = bld.s_pe if kc == 7 else None
                E(bld, "tensor", lambda e, kc=kc, s2=s2, jw=jw, bank=bank, buf=buf: e.matmul(
                    bld.pb[bank][0:jw, 0:512], buf[:, 256 * kc + 128 * s2:256 * kc + 128 * s2 + jw],
                    bld.xn_t[:, 512 * kc:512 * (kc + 1)], start=(kc == 0), stop=(kc == 7)), inc=inc)
        hg_pe = bld.s_pe.v
        guard_set(bld, f"w1b{jp % 2}", (bld.s_pe, hg_pe))
        if jp + 2 < NJ:
            iss_w1(jp + 2)
        W8(bld, "scalar", bld.s_pe, hg_pe)
        guard_wait(bld, "scalar", "Pb2")
        g1 = E(bld, "scalar", lambda e, jp=jp, jw=jw: e.activation(
            bld.Pbuf[:, 512:512 + 512][0:jw, :], bld.pb[1][0:jw, 0:512], AF.Gelu,
            bias=bld.b1c[0:jw, 22 + jp:23 + jp]), inc=bld.s_ac)
        W8(bld, "vector", bld.s_ac, g1)
        guard_wait(bld, "vector", f"m{jp % 2}")
        mv = E(bld, "vector", lambda e, jp=jp, jw=jw: e.scalar_tensor_tensor(
            bld.m_sb[0:jw, 512 * jp:512 * (jp + 1)], bld.pb[0][0:jw, 0:512],
            bld.b1c[0:jw, jp:jp + 1], bld.Pbuf[:, 512:1024][0:jw, :], ALU.add, ALU.mult))
        guard_set(bld, "pb0", (bld.s_ve, mv))
        guard_set(bld, "pb1", (bld.s_ac, g1))
        guard_set(bld, "Pb2", (bld.s_ve, mv))
    m_v = bld.s_ve.v

    pbanks = [bld.pb[i][:] for i in range(7)] + [bld.pbt[:].bitcast(F32)]
    for b in range(7):
        guard_wait(bld, "tensor", f"pb{b}")
    guard_wait(bld, "tensor", "pbt")
    W8(bld, "tensor", bld.s_ve, m_v)
    for jp in range(NJ):
        jw = 128 if jp < 21 else DIM_INNER - 128 * 21
        buf = bld.w2b[jp % 2]
        W8(bld, "tensor", bld.d_w2p[jp % 2], w2tot[jp])
        for dm in range(8):
            inc = bld.s_pe if dm == 7 else None
            E(bld, "tensor", lambda e, jp=jp, jw=jw, dm=dm, buf=buf, pbanks=pbanks: e.matmul(
                pbanks[dm][:, 0:512], buf[0:jw, 128 * dm:128 * (dm + 1)],
                bld.m_sb[0:jw, 512 * jp:512 * (jp + 1)],
                start=(jp == 0), stop=(jp == NJ - 1)), inc=inc)
        guard_set(bld, f"w2b{jp % 2}", (bld.s_pe, bld.s_pe.v))
        if jp + 2 < NJ:
            iss_w2(jp + 2)
    w2_pe = bld.s_pe.v
    guard_set(bld, "w2b0", (bld.s_pe, w2_pe))
    guard_set(bld, "w2b1", (bld.s_pe, w2_pe))
    guard_set(bld, "m0", (bld.s_pe, w2_pe))
    guard_set(bld, "m1", (bld.s_pe, w2_pe))
    W8(bld, "vector", bld.s_pe, w2_pe)
    for dm in range(8):
        E(bld, "vector", lambda e, dm=dm, t0=t0, pbanks=pbanks: e.scalar_tensor_tensor(
            bld.h[:, TLOC * dm + t0:TLOC * dm + t0 + 512], pbanks[dm][:, 0:512],
            bld.lcols[:, LQ_B2 + dm:LQ_B2 + dm + 1],
            bld.h[:, TLOC * dm + t0:TLOC * dm + t0 + 512], ALU.add, ALU.add))
    res_v = bld.s_ve.v
    for b in range(7):
        guard_set(bld, f"pb{b}", (bld.s_ve, res_v))
    guard_set(bld, "pbt", (bld.s_ve, res_v))
    guard_set(bld, "pb4", (bld.s_ve, res_v))
    guard_set(bld, "pb5", (bld.s_ve, res_v))
    return res_v


def emit_layer(bld: Build, l, skip_ffn=False):
    sy = emit_layer_loads(bld, l)
    if l > 0:
        emit_halo_ag(bld, l)
    for eng in ("tensor", "vector", "scalar"):
        W8(bld, eng, bld.s_sy, sy)
    hsv = bld.h[:].rearrange("p (c t) -> p c t", t=TLOC)
    E(bld, "vector", lambda e, hsv=hsv: e.tensor_copy(
        bld.hsave[:].rearrange("p (c i) -> p c i", i=9), hsv[:, :, 511::512]), inc=bld.s_ve)
    for ci in range(9):
        qk_done, v_done = emit_attn_chunk(bld, l, ci, None)
        for qt in range(4):
            Q = 4 * ci + qt
            if Q < 3:
                continue
            ev = emit_attn_qtile(bld, l, ci, qt, qk_done, v_done)
            emit_wo_qtile(bld, l, ci, qt, ev)
        guard_set(bld, "ring", (bld.s_pe, bld.s_pe.v))
    guard_set(bld, "wqkv", (bld.s_pe, bld.s_pe.v))
    guard_set(bld, "wo", (bld.s_pe, bld.s_pe.v))
    if not skip_ffn:
        for ti in reversed(range(8)):
            emit_ffn_ttile(bld, l, ti)
    guard_set(bld, "lcols", (bld.s_ve, bld.s_ve.v))


def emit_final(bld: Build):
    d = bld.dram
    s = lambda n, sh, dt: bld.sbuf(n, sh, dt, scope=bld.es_fin)
    xld = [s(f"xld{i}", [128, D], F32) for i in range(2)]
    sld = [s(f"sld{i}", [128, D], BF16) for i in range(2)]
    sgb = s("sgb", [128, D], BF16)
    orow = s("orow", [128, D], F32)
    for ti in range(32):
        slot = ti % 2
        t0 = HALO + 128 * ti
        guard_wait(bld, "sync", f"fxld{slot}")
        E(bld, "sync", lambda e, ti=ti, t0=t0, slot=slot: e.dma_start(
            xld[slot][:], d["x_in"][t0:t0 + 128, :]), inc=bld.d_sy, n=16)
        E(bld, "sync", lambda e, ti=ti, slot=slot: e.dma_start(
            sld[slot][:], d["sc_s"][128 * ti:128 * (ti + 1), :]), inc=bld.d_sy, n=16)
        lf = fence(bld, "sync")
        guard_wait(bld, "scalar", "sgb")
        for kc in range(8):
            inc = bld.s_ac if kc == 7 else None
            E(bld, "scalar", lambda e, kc=kc, t0=t0: e.activation(
                sgb[:, 128 * kc:128 * (kc + 1)], bld.h[:, TLOC * kc + t0:TLOC * kc + t0 + 128],
                AF.Silu), inc=inc)
        sg_a = bld.s_ac.v
        guard_wait(bld, "tensor", "pbt")
        W8(bld, "tensor", bld.s_ac, sg_a)
        for kc in range(8):
            inc = bld.s_pe if kc == 7 else None
            E(bld, "tensor", lambda e, kc=kc: e.transpose(
                bld.pbt[:, 128 * kc:128 * (kc + 1)], sgb[:, 128 * kc:128 * (kc + 1)],
                bld.ident_bf[:]), inc=inc)
        tr_pe = bld.s_pe.v
        guard_set(bld, "sgb", (bld.s_pe, tr_pe))
        W8(bld, "vector", bld.s_pe, tr_pe)
        W8(bld, "vector", bld.s_sy, lf)
        guard_wait(bld, "vector", "orow")
        E(bld, "vector", lambda e, slot=slot: e.tensor_mul(orow[:], sld[slot][:], bld.pbt[:]))
        ov = E(bld, "vector", lambda e, slot=slot: e.tensor_add(orow[:], orow[:], xld[slot][:]))
        guard_set(bld, "pbt", (bld.s_ve, ov))
        guard_set(bld, f"fxld{slot}", (bld.s_ve, ov))
        W8(bld, "sync", bld.s_ve, ov)
        E(bld, "sync", lambda e, ti=ti: e.dma_start(
            d["out"][128 * ti:128 * (ti + 1), :], orow[:]), inc=bld.d_sy, n=16)
        of = fence(bld, "sync")
        guard_set(bld, "orow", (bld.s_sy, of))


def build_full(n_layers=NLAYER, dbg=False, skip_ffn=False):
    bld = Build()
    build_common(bld)
    if dbg:
        bld.dout("dbg_h", [128, 8 * TLOC], F16)
        bld.dout("dbg_s", [4096, D], BF16)
        bld.dout("dbg_p", [128, 16 * 384], F16)
        bld.dout("dbg_ofm", [128, 8 * 128], F16)
    emit_setup(bld)
    emit_dn_phase(bld)
    phase_barrier(bld)
    bld.es_dn.close()
    tf_alloc(bld)
    emit_tf_setup(bld)
    for l in range(n_layers):
        emit_layer(bld, l, skip_ffn=skip_ffn)
    phase_barrier(bld)
    if dbg:
        for j in range(8):
            E(bld, "sync", lambda e, j=j: e.dma_start(
                bld.dram["dbg_h"][:, TLOC * j:TLOC * (j + 1)],
                bld.h[:, TLOC * j:TLOC * (j + 1)]), inc=bld.d_sy, n=16)
        for i in range(4):
            E(bld, "sync", lambda e, i=i: e.dma_start(
                bld.dram["dbg_s"][1024 * i:1024 * (i + 1), :],
                bld.dram["sc_s"][1024 * i:1024 * (i + 1), :]), inc=bld.d_sy, n=16)
        E(bld, "sync", lambda e: e.dma_start(bld.dram["dbg_p"], bld.dram["sc_p"][10]),
          inc=bld.d_sy, n=16)
        fence(bld, "sync")
    bld.es_tf.close()
    bld.es_fin = contextlib.ExitStack()
    emit_final(bld)
    finish_program(bld)
    with bld.es, bld.es_fin, bld.nc.Block() as blk:
        bld.prog.play(blk)
    return bld


_BUILD_CACHE = {}


def kernel(**inputs):
    if "bld" not in _BUILD_CACHE:
        _BUILD_CACHE["bld"] = build_full()
    bld = _BUILD_CACHE["bld"]
    in_maps = [core_input_map(inputs, c) for c in range(8)]
    res = run_bass_kernel_spmd(bld.nc, in_maps, core_ids=list(range(8)))
    out = np.empty((B, L, D), np.float32)
    for c in range(8):
        b, half = c // 2, c % 2
        out[b, 4096 * half:4096 * (half + 1)] = res.results[c]["out"]
    return out


# version 07
# speedup vs baseline: 1.1679x; 1.0432x over previous
import math
import contextlib
import numpy as np
import ml_dtypes
import concourse.bass as bass
import concourse.mybir as mybir
from concourse.bass_utils import run_bass_kernel_spmd

F32 = mybir.dt.float32
BF16 = mybir.dt.bfloat16
F16 = mybir.dt.float16
AF = mybir.ActivationFunctionType
ALU = mybir.AluOpType
AX = mybir.AxisListType

B, L, D, NH, dh = 4, 8192, 1024, 16, 64
HALO = 512
TLOC = 4096 + HALO
NT = TLOC // 128
C = 128
NCH = 4096 // C
WINDOW = 256
QK_SCALE = 8.0
DIM_INNER = 2730
NLAYER = 6
LN_EPS = 1e-5

DEBUG = {}


def _host_consts(core_id):
    half = core_id % 2
    gstart = 4096 * half - HALO
    c = {}
    c["ident128"] = np.eye(128, dtype=np.float32)
    Rm = np.zeros((128, 128), np.float32)
    for hb in range(2):
        o = 64 * hb
        for p in range(32):
            Rm[o + p + 32, o + p] = -1.0
            Rm[o + p, o + p + 32] = 1.0
    c["rotmat"] = Rm
    halfv = np.arange(0, dh, 2, dtype=np.float64)
    inv_freq = 1.0 / 10000.0 ** (halfv / dh)
    t = np.arange(gstart, gstart + TLOC, dtype=np.float64)
    freqs = t[:, None] * inv_freq[None, :]
    freqs = np.concatenate([freqs, freqs], -1)
    base_scale = (halfv + 0.4 * dh) / (1.4 * dh)
    power = (t - L // 2) / (WINDOW // 2)
    scale = base_scale[None, :] ** power[:, None]
    scale = np.concatenate([scale, scale], -1)
    cos, sin = np.cos(freqs), np.sin(freqs)
    for nm, tab in [("rotAQ", cos * scale), ("rotBQ", sin * scale),
                    ("rotAK", cos / scale), ("rotBK", sin / scale)]:
        tabf = np.asarray(tab.T, np.float32)
        c[nm] = np.concatenate([tabf, tabf], axis=0)
    r = np.arange(128)[:, None]
    cc_ = np.arange(384)[None, :]
    def mkmask(lo):
        allowed = (cc_ >= r) & (cc_ <= r + 256)
        if lo is not None:
            allowed &= (cc_ >= lo)
            full_row_dead = (~allowed).all(axis=1, keepdims=True)
            allowed = np.where(full_row_dead, (cc_ >= r) & (cc_ <= r + 256), allowed)
        return np.where(allowed, 0.0, -1e30).astype(np.float32)
    masks = [mkmask(None)]
    for qt in (4, 5):
        lo = 256 - (gstart + 128 * qt)
        masks.append(mkmask(lo if lo > 0 else None))
    mm = np.stack(masks)
    half_cat = np.concatenate([mm[:, 0:64, :], mm[:, 64:128, :]], axis=2)
    c["attn_masks"] = np.concatenate([half_cat, half_cat], axis=1)
    c["sel"] = np.full((128, 1), 1.0 if half == 1 else 0.0, np.float32)
    tl = np.arange(C, dtype=np.float32)
    tim = tl[:, None] - tl[None, :]
    c["d_strict"] = np.where(tim > 0, tim, 0.0).astype(np.float32)
    c["m_strict"] = (tim > 0).astype(np.float32)
    c["d_strictT"] = np.ascontiguousarray(c["d_strict"].T)
    c["m_strictT"] = np.ascontiguousarray(c["m_strict"].T)
    c["d_inclT"] = np.where(tim >= 0, tim, 0.0).astype(np.float32).T.copy()
    c["m_inclT"] = (tim >= 0).astype(np.float32).T.copy()
    c["tcol"] = (tl + 1.0).reshape(128, 1).astype(np.float32)
    c["tbarcol"] = (C - (tl + 1.0)).reshape(128, 1).astype(np.float32)
    i64p = np.zeros((128, 64), np.float32)
    i64p[0:64] = np.eye(64)
    i64p[64:128] = np.eye(64)
    c["i64pair"] = i64p
    c["ones_row"] = np.ones((1, 128), np.float32)
    co = np.zeros((8, 128, 16), np.float32)
    for j in range(8):
        co[j, 0:64, 2 * j] = 1.0
        co[j, 64:128, 2 * j + 1] = 1.0
    c["chunk_ones"] = co
    c["chunk_onesT"] = np.ascontiguousarray(co.transpose(0, 2, 1))
    c["ident64h"] = np.concatenate([np.eye(64, dtype=np.float32)] * 2, axis=0)
    return c


ENGINES = ("sync", "gpsimd", "tensor", "scalar", "vector")


class Sem:
    __slots__ = ("h", "v", "name")

    def __init__(self, handle, name):
        self.h = handle
        self.v = 0
        self.name = name

    def inc(self, n=1):
        self.v += n
        return self.v


class Prog:
    def __init__(self, nc):
        self.nc = nc
        self.ops = {e: [] for e in ENGINES}

    def add(self, engine, fn):
        self.ops[engine].append(fn)

    def wait(self, engine, sem, val):
        self.add(engine, lambda e, s=sem.h, v=val: e.wait_ge(s, v))

    def play(self, blk):
        for eng in ENGINES:
            ops = self.ops[eng]
            if not ops:
                continue
            def body(e, ops=ops):
                for fn in ops:
                    fn(e)
            getattr(blk, eng)(body)


class Bank:

    def __init__(self, prog, tensor_handle, pe_sem, ev_sem):
        self.prog = prog
        self.t = tensor_handle
        self.pe = pe_sem
        self.ev = ev_sem
        self.n_cons_expected = 0

    def produce(self, mm_fns, n_consumers):
        p = self.prog
        if self.n_cons_expected:
            p.wait("tensor", self.ev, self.ev.v)
        for fn in mm_fns[:-1]:
            p.add("tensor", fn)
        tgt = self.pe.inc()
        p.add("tensor", lambda e, fn=mm_fns[-1], s=self.pe.h: fn(e).then_inc(s, 1))
        self.n_cons_expected = n_consumers
        return tgt

    def consume(self, engine, fn, pe_tgt):
        p = self.prog
        p.wait(engine, self.pe, pe_tgt)
        p.add(engine, lambda e, fn=fn, s=self.ev.h: fn(e).then_inc(s, 1))
        self.ev.inc()
        self.n_cons_expected -= 1


def ceil_div(a, b):
    return (a + b - 1) // b


class Build:

    def __init__(self, with_transformer=True, n_layers=NLAYER, dbg=()):
        self.nc = bass.Bass("TRN2", target_bir_lowering=False)
        self.prog = Prog(self.nc)
        self.es = contextlib.ExitStack()
        self.es_dn = contextlib.ExitStack()
        self.es_tf = contextlib.ExitStack()
        self.with_transformer = with_transformer
        self.n_layers = n_layers
        self.dbg_names = set(dbg)
        self.dram = {}
        self.sb = {}
        self.psum = {}
        self.sems = {}

    def din(self, name, shape, dtype=F32):
        t = self.nc.dram_tensor(name, list(shape), dtype, kind="ExternalInput").ap()
        self.dram[name] = t
        return t

    def dout(self, name, shape, dtype=F32):
        t = self.nc.dram_tensor(name, list(shape), dtype, kind="ExternalOutput").ap()
        self.dram[name] = t
        return t

    def dint(self, name, shape, dtype=F32):
        t = self.nc.dram_tensor(name, list(shape), dtype, kind="Internal").ap()
        self.dram[name] = t
        return t

    def sbuf(self, name, shape, dtype=F32, scope=None):
        es = scope if scope is not None else self.es
        t = es.enter_context(self.nc.sbuf_tensor(name, list(shape), dtype))
        self.sb[name] = t
        return t

    def psum_t(self, name, shape, dtype=F32):
        t = self.es.enter_context(self.nc.psum_tensor(name, list(shape), dtype))
        self.psum[name] = t
        return t

    def sem(self, name):
        s = Sem(self.es.enter_context(self.nc.semaphore(name)), name)
        self.sems[name] = s
        return s

    def dbg_store(self, name, src_fn, shape, dtype=F32):
        if name not in self.dbg_names:
            return
        o = self.dout("dbg_" + name, shape, dtype)
        d = self.d_sy
        tgt = d.inc(16)
        self.prog.add("sync", lambda e, o=o, f=src_fn, s=d.h: e.dma_start(o, f()).then_inc(s, 16))


def build_common(bld: "Build"):
    nc, p = bld.nc, bld.prog
    bld.din("x_in", [TLOC, D])
    bld.din("Wt_d", [D, D])
    bld.din("ln_gamma", [1, D]); bld.din("ln_beta", [1, D])
    bld.din("eta", [1, 1]); bld.din("lam_logit", [1, 1])
    bld.din("sel", [128, 1])
    for nm in ("ident128", "rotmat", "d_strict", "m_strict", "d_strictT", "m_strictT",
               "d_inclT", "m_inclT"):
        bld.din(nm, [128, 128])
    bld.din("i64pair", [128, 64])
    bld.din("tcol", [128, 1]); bld.din("tbarcol", [128, 1])
    bld.din("ones_row", [1, 128])
    bld.din("attn_masks", [3, 128, 768])
    bld.din("chunk_ones", [8, 128, 16])
    for nm in ("rotAQ", "rotBQ", "rotAK", "rotBK"):
        bld.din(nm, [128, TLOC])
    bld.din("Wqkv", [NLAYER, D, 3 * D]); bld.din("Wo", [NLAYER, D, D])
    bld.din("W1", [NLAYER, D, 2 * DIM_INNER]); bld.din("W2", [NLAYER, DIM_INNER, D])
    bld.din("b1cols", [NLAYER, 2, 22, 128, 1]); bld.din("b2cols", [NLAYER, 8, 128, 1])
    bld.din("ffn_g", [NLAYER, 8, 128, 1]); bld.din("ffn_b", [NLAYER, 8, 128, 1])
    bld.din("qk_scale", [NLAYER, 2, 128, 1])
    bld.din("chunk_onesT", [8, 16, 128]); bld.din("ident64h", [128, 64])
    bld.din("ones_col128", [128, 1])
    bld.dout("out", [4096, D])
    bld.dint("sc_static", [4096, D], BF16)
    bld.dint("sc_o0", [4096, D], BF16)
    bld.dint("sc_ofin", [4096, D], BF16)
    bld.dint("sc_T", [NCH, 128, 8 * 128], BF16)
    bld.dint("sc_P", [NCH, 128, 8 * 64])
    bld.dint("cc_m_in", [128, 1024])
    bld.dint("cc_m_out", [256, 1024])
    bld.dint("sc_s", [4096, D], BF16)
    bld.dint("sc_p", [36, 128, 16 * 384], F16)
    bld.dint("cc_h_in", [D, HALO], F16)
    bld.dint("cc_h_out", [2 * D, HALO], F16)
    bld.dint("sc_wq", [NLAYER, D, 3 * D], F16)
    bld.dint("sc_w1", [NLAYER, D, 2 * DIM_INNER], BF16)
    bld.dint("sc_w2", [NLAYER, DIM_INNER, D], BF16)

    bld.s_pe = bld.sem("s_pe")
    bld.s_ve = bld.sem("s_ve")
    bld.s_ac = bld.sem("s_ac")
    bld.s_gp = bld.sem("s_gp")
    bld.d_sy = bld.sem("d_sy")
    bld.d_sc = bld.sem("d_sc")
    bld.d_gp = bld.sem("d_gp")
    bld.s_cc = bld.sem("s_cc")
    bld.s_sy = bld.sem("s_sy")
    bld.s_sg = bld.sem("s_sg")
    bld.s_gq = bld.sem("s_gq")
    bld.d_w1p = [bld.sem("d_w10"), bld.sem("d_w11")]
    bld.d_w2p = [bld.sem("d_w20"), bld.sem("d_w21")]
    bld.d_qk = [bld.sem("d_qk0"), bld.sem("d_qk1")]
    bld.d_rt = bld.sem("d_rt")
    bld.d_wb = bld.sem("d_wb")
    bld.wb_pend = {}

    bld.pbt = bld.psum_t("pbt", [128, 1024], BF16)
    bld.pb = [bld.psum_t(f"pb{i}", [128, 512], F32) for i in range(7)]

    bld.h = bld.sbuf("h", [128, 8 * TLOC], F16)
    bld.cols = bld.sbuf("cols", [128, 32], F32)
    bld.x2 = bld.sbuf("x2", [128, D], F32)
    bld.ident_bf = bld.sbuf("ident_bf", [128, 128], BF16)
    bld.rot_bf = bld.sbuf("rot_bf", [128, 128], BF16)
    bld.dn_cols = bld.sbuf("dn_cols", [128, 16], F32)
    bld.ones_row_sb = bld.sbuf("ones_row_sb", [1, 128], BF16)
    bld.ones_row_f32 = bld.sbuf("ones_row_f32", [1, 128], F32)
    bld.xrow = [bld.sbuf(f"xrow{i}", [128, D], F32, scope=bld.es_dn) for i in range(2)]
    bld.xn_tok = [bld.sbuf(f"xn_tok{i}", [128, D], BF16, scope=bld.es_dn) for i in range(2)]
    bld.xa_tok = [bld.sbuf(f"xa_tok{i}", [128, D], BF16, scope=bld.es_dn) for i in range(2)]
    bld.xa_fm = [bld.sbuf(f"xa_fm{i}", [128, 8 * 128], BF16, scope=bld.es_dn) for i in range(2)]
    bld.st_tok = [bld.sbuf(f"st_tok{i}", [128, D], BF16, scope=bld.es_dn) for i in range(2)]
    bld.Wt_sb = bld.sbuf("Wt_sb", [128, 8 * D], BF16, scope=bld.es_dn)
    bld.Ls = bld.sbuf("Ls", [128, 128], BF16, scope=bld.es_dn)
    bld.LsT = bld.sbuf("LsT", [128, 128], BF16, scope=bld.es_dn)
    bld.LiT = bld.sbuf("LiT", [128, 128], BF16, scope=bld.es_dn)
    bld.i64_bf = bld.sbuf("i64_bf", [128, 64], F32, scope=bld.es_dn)
    bld.consts_f32 = bld.sbuf("consts_f32", [128, 8 * 128], F32, scope=bld.es_dn)
    bld.X_all = bld.sbuf("X_all", [128, 16 * 128], BF16, scope=bld.es_dn)
    bld.Na = bld.sbuf("Na", [128, 16 * 128], BF16, scope=bld.es_dn)
    bld.NTa = bld.sbuf("NTa", [128, 16 * 128], BF16, scope=bld.es_dn)
    bld.Nb = bld.sbuf("Nb", [128, 16 * 128], BF16, scope=bld.es_dn)
    bld.NTb = bld.sbuf("NTb", [128, 16 * 128], BF16, scope=bld.es_dn)
    bld.Bt_all = bld.sbuf("Bt_all", [128, 16 * 128], BF16, scope=bld.es_dn)
    bld.LbK = bld.sbuf("LbK", [128, D], BF16, scope=bld.es_dn)
    bld.U_all = bld.sbuf("U_all", [128, D], BF16, scope=bld.es_dn)
    bld.Wfm = bld.sbuf("Wfm", [128, 8 * 128], BF16, scope=bld.es_dn)
    bld.Ttok = bld.sbuf("Ttok", [128, D], BF16, scope=bld.es_dn)
    bld.Tfm = bld.sbuf("Tfm", [128, 8 * 128], BF16, scope=bld.es_dn)
    bld.M_all = bld.sbuf("M_all", [128, 8 * 64], F32, scope=bld.es_dn)
    bld.MT_all = bld.sbuf("MT_all", [128, 8 * 64], F32, scope=bld.es_dn)
    bld.MTbf = bld.sbuf("MTbf", [128, 8 * 64], BF16, scope=bld.es_dn)
    bld.P_all = bld.sbuf("P_all", [128, 8 * 64], F32, scope=bld.es_dn)
    bld.PT_all = bld.sbuf("PT_all", [128, 8 * 64], F32, scope=bld.es_dn)
    bld.G_all = bld.sbuf("G_all", [128, 8 * 64], F32, scope=bld.es_dn)
    bld.o0_all = bld.sbuf("o0_all", [128, D], BF16, scope=bld.es_dn)
    bld.otmp = bld.sbuf("otmp", [128, 512], F32, scope=bld.es_dn)
    bld.Minit = bld.sbuf("Minit", [128, 8 * 64], F32, scope=bld.es_dn)
    bld.MinitT = bld.sbuf("MinitT", [128, 8 * 64], F32, scope=bld.es_dn)
    bld.MinitTbf = bld.sbuf("MinitTbf", [128, 8 * 64], BF16, scope=bld.es_dn)
    bld.Pld = bld.sbuf("Pld", [128, 8 * 64], F32, scope=bld.es_dn)
    bld.Tld = bld.sbuf("Tld", [128, 8 * 128], BF16, scope=bld.es_dn)
    bld.o0ld = bld.sbuf("o0ld", [128, D], BF16, scope=bld.es_dn)
    bld.dMbf = bld.sbuf("dMbf", [128, 8 * 64], BF16, scope=bld.es_dn)
    bld.gamma_b = bld.sbuf("gamma_b", [128, D], BF16, scope=bld.es_dn)
    bld.beta_b = bld.sbuf("beta_b", [128, D], BF16, scope=bld.es_dn)
    bld.grow = bld.sbuf("grow", [1, D], F32, scope=bld.es_dn)
    bld.brow = bld.sbuf("brow", [1, D], F32, scope=bld.es_dn)
    bld.yrow = bld.sbuf("yrow", [128, D], F32, scope=bld.es_dn)
    bld.st_ld = bld.sbuf("st_ld", [128, D], BF16, scope=bld.es_dn)
    bld.of_ld = bld.sbuf("of_ld", [128, D], BF16, scope=bld.es_dn)
    bld.s_bf = bld.sbuf("s_bf", [128, D], BF16, scope=bld.es_dn)
_ENG_SELF_SEM = {"vector": "s_ve", "scalar": "s_ac", "gpsimd": "s_gp"}


def E(bld, eng, fn, inc=None, n=1):
    ssn = _ENG_SELF_SEM.get(eng)
    if ssn is not None:
        sem = getattr(bld, ssn)
        if inc is None or inc is sem:
            W8(bld, eng, sem, sem.v)
            tgt = sem.inc(1)
            bld.prog.add(eng, lambda e, fn=fn, s=sem.h: fn(e).then_inc(s, 1))
            return tgt
    if inc is None:
        bld.prog.add(eng, fn)
        return None
    tgt = inc.inc(n)
    bld.prog.add(eng, lambda e, fn=fn, s=inc.h, n=n: fn(e).then_inc(s, n))
    return tgt


def W8(bld, eng, sem, val):
    if val is None or val <= 0:
        return
    bld.prog.wait(eng, sem, val)


def fence(bld, queue):
    dsem, msem = {"sync": (bld.d_sy, bld.s_sy), "scalar": (bld.d_sc, bld.s_sg),
                  "gpsimd": (bld.d_gp, bld.s_gq)}[queue]
    W8(bld, queue, dsem, dsem.v)
    tgt = msem.inc()
    bld.prog.add(queue, lambda e, s=msem.h: e.sem_inc(s, 1))
    return tgt


CL_LNLAM, CL_LAM, CL_LBAR, CL_BETA, CL_BLAM, CL_LAMC, CL_SEL, CL_NBETA = range(8)


def emit_setup(bld: Build):
    nc, p = bld.nc, bld.prog
    d = bld.dram
    gp_loads = [
        (bld.ident_bf[:], d["ident128"]),
        (bld.rot_bf[:], d["rotmat"]),
        (bld.ones_row_sb[:], d["ones_row"]),
    ]
    for j in range(8):
        gp_loads.append((bld.Wt_sb[:, j * D:(j + 1) * D], d["Wt_d"][128 * j:128 * (j + 1), :]))
    for dst, src in gp_loads:
        E(bld, "gpsimd", lambda e, a=dst, b=src: e.dma_start(a, b), inc=bld.d_gp, n=16)
    for t in (bld.M_all, bld.MT_all, bld.MTbf):
        E(bld, "gpsimd", lambda e, a=t: e.memset(a[:], 0.0), inc=bld.s_gp)
    gp_done = fence(bld, "gpsimd")
    gp_ms = bld.s_gp.v

    cf = bld.consts_f32
    sy_loads = [
        (cf[:, 0:128], d["d_strict"]), (cf[:, 128:256], d["m_strict"]),
        (cf[:, 256:384], d["d_strictT"]), (cf[:, 384:512], d["m_strictT"]),
        (cf[:, 512:640], d["d_inclT"]), (cf[:, 640:768], d["m_inclT"]),
        (bld.i64_bf[:], d["i64pair"][:, 0:64]),
        (bld.cols[0:128, 0:1], d["tcol"]), (bld.cols[0:128, 1:2], d["tbarcol"]),
        (bld.cols[0:1, 2:3], d["eta"]), (bld.cols[0:1, 3:4], d["lam_logit"]),
        (bld.dn_cols[:, CL_SEL:CL_SEL + 1], d["sel"]),
        (bld.grow[:], d["ln_gamma"]), (bld.brow[:], d["ln_beta"]),
        (bld.ones_row_f32[:], d["ones_row"]),
    ]
    for dst, src in sy_loads:
        E(bld, "sync", lambda e, a=dst, b=src: e.dma_start(a, b), inc=bld.d_sy, n=16)
    sy_done = fence(bld, "sync")

    ones_r = bld.ones_row_f32

    W8(bld, "scalar", bld.s_sy, sy_done)
    E(bld, "scalar", lambda e: e.activation(bld.cols[0:1, 4:5], bld.cols[0:1, 3:4], AF.Sigmoid))
    E(bld, "scalar", lambda e: e.activation(bld.cols[0:1, 5:6], bld.cols[0:1, 4:5], AF.Ln),
      inc=bld.s_ac)
    a_ms = bld.s_ac.v
    W8(bld, "vector", bld.s_ac, a_ms)
    E(bld, "vector", lambda e: e.tensor_scalar_max(bld.cols[0:1, 5:6], bld.cols[0:1, 5:6], -85.0))
    E(bld, "vector", lambda e: e.tensor_scalar(bld.cols[0:1, 6:7], bld.cols[0:1, 4:5], -1.0, 1.0,
                                               ALU.mult, ALU.add))
    E(bld, "vector", lambda e: e.tensor_mul(bld.cols[0:1, 7:8], bld.cols[0:1, 6:7], bld.cols[0:1, 2:3]))
    E(bld, "vector", lambda e: e.tensor_scalar_mul(bld.cols[0:1, 7:8], bld.cols[0:1, 7:8], 1.0 / 64.0))
    E(bld, "vector", lambda e: e.tensor_scalar_mul(bld.cols[0:1, 8:9], bld.cols[0:1, 5:6], 128.0),
      inc=bld.s_ve)
    v_ms = bld.s_ve.v
    W8(bld, "scalar", bld.s_ve, v_ms)
    E(bld, "scalar", lambda e: e.activation(bld.cols[0:1, 9:10], bld.cols[0:1, 8:9], AF.Exp),
      inc=bld.s_ac)
    a_ms = bld.s_ac.v

    W8(bld, "tensor", bld.s_ac, a_ms)
    W8(bld, "tensor", bld.s_sy, sy_done)
    ps = bld.pb[6]
    def bcast_mm(src_ap, col):
        E(bld, "tensor", lambda e, s=src_ap, c=col: e.matmul(
            ps[:, c:c + 1], ones_r[:], s, start=True, stop=True))
    bcast_mm(bld.cols[0:1, 5:6], 0)
    bcast_mm(bld.cols[0:1, 7:8], 1)
    bcast_mm(bld.cols[0:1, 9:10], 2)
    E(bld, "tensor", lambda e: e.matmul(ps[:, 3:4], ones_r[:], bld.cols[0:1, 7:8],
                                        start=True, stop=True), inc=bld.s_pe)
    pe_ms = bld.s_pe.v
    W8(bld, "vector", bld.s_pe, pe_ms)
    E(bld, "vector", lambda e: e.tensor_copy(bld.dn_cols[:, CL_LNLAM:CL_LNLAM + 1], ps[:, 0:1]))
    E(bld, "vector", lambda e: e.tensor_copy(bld.dn_cols[:, CL_BETA:CL_BETA + 1], ps[:, 1:2]))
    E(bld, "vector", lambda e: e.tensor_copy(bld.dn_cols[:, CL_LAMC:CL_LAMC + 1], ps[:, 2:3]))
    E(bld, "vector", lambda e: e.tensor_scalar_mul(bld.dn_cols[:, CL_NBETA:CL_NBETA + 1],
                                                   ps[:, 3:4], -1.0))
    E(bld, "vector", lambda e: e.tensor_mul(bld.cols[:, 10:11], bld.cols[:, 0:1],
                                            bld.dn_cols[:, CL_LNLAM:CL_LNLAM + 1]))
    E(bld, "vector", lambda e: e.tensor_mul(bld.cols[:, 11:12], bld.cols[:, 1:2],
                                            bld.dn_cols[:, CL_LNLAM:CL_LNLAM + 1]), inc=bld.s_ve)
    v_ms = bld.s_ve.v
    W8(bld, "scalar", bld.s_ve, v_ms)
    E(bld, "scalar", lambda e: e.activation(bld.dn_cols[:, CL_LAM:CL_LAM + 1], bld.cols[:, 10:11], AF.Exp))
    E(bld, "scalar", lambda e: e.activation(bld.dn_cols[:, CL_LBAR:CL_LBAR + 1], bld.cols[:, 11:12], AF.Exp),
      inc=bld.s_ac)
    a_ms = bld.s_ac.v
    W8(bld, "vector", bld.s_ac, a_ms)
    E(bld, "vector", lambda e: e.tensor_mul(bld.dn_cols[:, CL_BLAM:CL_BLAM + 1],
                                            bld.dn_cols[:, CL_BETA:CL_BETA + 1],
                                            bld.dn_cols[:, CL_LAM:CL_LAM + 1]))

    def mk_decay(dst, d_off, m_off, use_nbeta):
        E(bld, "vector", lambda e: e.tensor_scalar_mul(
            bld.x2[:, 0:128], cf[:, d_off:d_off + 128], bld.dn_cols[:, CL_LNLAM:CL_LNLAM + 1]),
            inc=bld.s_ve)
        W8(bld, "scalar", bld.s_ve, bld.s_ve.v)
        E(bld, "scalar", lambda e: e.activation(bld.x2[:, 128:256], bld.x2[:, 0:128], AF.Exp),
          inc=bld.s_ac)
        W8(bld, "vector", bld.s_ac, bld.s_ac.v)
        if use_nbeta:
            E(bld, "vector", lambda e: e.tensor_mul(bld.x2[:, 0:128], bld.x2[:, 128:256],
                                                    cf[:, m_off:m_off + 128]))
            E(bld, "vector", lambda e: e.tensor_scalar_mul(
                dst[:], bld.x2[:, 0:128], bld.dn_cols[:, CL_NBETA:CL_NBETA + 1]), inc=bld.s_ve)
        else:
            E(bld, "vector", lambda e: e.tensor_mul(dst[:], bld.x2[:, 128:256],
                                                    cf[:, m_off:m_off + 128]), inc=bld.s_ve)

    mk_decay(bld.Ls, 0, 128, True)
    mk_decay(bld.LsT, 256, 384, True)
    mk_decay(bld.LiT, 512, 640, False)
    bld.guards = getattr(bld, "guards", {})
    guard_set(bld, "x2", (bld.s_ve, bld.s_ve.v))

    W8(bld, "vector", bld.s_sy, sy_done)
    for pj in range(8):
        E(bld, "vector", lambda e, pj=pj: e.tensor_copy(bld.P_all[:, 64 * pj:64 * (pj + 1)],
                                                        bld.i64_bf[:, :]))
        E(bld, "vector", lambda e, pj=pj: e.tensor_copy(bld.PT_all[:, 64 * pj:64 * (pj + 1)],
                                                        bld.i64_bf[:, :]))
    E(bld, "vector", lambda e: e.tensor_copy(bld.PT_all[:, 0:1], bld.PT_all[:, 0:1]), inc=bld.s_ve)

    W8(bld, "tensor", bld.s_sy, sy_done)
    for half in range(2):
        W8(bld, "tensor", bld.s_ve, bld.s_ve.v)
        E(bld, "tensor", lambda e, hf=half: e.matmul(
            bld.pb[5][:, 0:512], ones_r[:], bld.grow[:, 512 * hf:512 * (hf + 1)],
            start=True, stop=True), inc=bld.s_pe)
        W8(bld, "vector", bld.s_pe, bld.s_pe.v)
        E(bld, "vector", lambda e, hf=half: e.tensor_copy(
            bld.gamma_b[:, 512 * hf:512 * (hf + 1)], bld.pb[5][:, 0:512]), inc=bld.s_ve)
        W8(bld, "tensor", bld.s_ve, bld.s_ve.v)
        E(bld, "tensor", lambda e, hf=half: e.matmul(
            bld.pb[5][:, 0:512], ones_r[:], bld.brow[:, 512 * hf:512 * (hf + 1)],
            start=True, stop=True), inc=bld.s_pe)
        W8(bld, "vector", bld.s_pe, bld.s_pe.v)
        E(bld, "vector", lambda e, hf=half: e.tensor_copy(
            bld.beta_b[:, 512 * hf:512 * (hf + 1)], bld.pb[5][:, 0:512]), inc=bld.s_ve)

    for eng in ("tensor", "vector", "scalar"):
        W8(bld, eng, bld.s_gq, gp_done)
        W8(bld, eng, bld.s_gp, gp_ms)


def guard_wait(bld, eng, key):
    for sem, val in bld.guards.get(key, ()):
        if sem.name.startswith("d_"):
            val = sem.v
        W8(bld, eng, sem, val)


def guard_set(bld, key, *pairs):
    bld.guards[key] = list(pairs)


def guard_add(bld, key, sem, val):
    bld.guards.setdefault(key, []).append((sem, val))


def emit_tile_prep(bld: Build, tile):
    p = bld.prog
    slot = tile % 2
    own = tile >= 4
    if not hasattr(bld, "guards"):
        bld.guards = {}
    cb = 12 + 10 * slot

    guard_wait(bld, "sync", f"xrow{slot}")
    E(bld, "sync", lambda e, t=tile, s=slot: e.dma_start(
        bld.xrow[s][:], bld.dram["x_in"][128 * t:128 * (t + 1), :]), inc=bld.d_sy, n=16)
    xv = fence(bld, "sync")

    W8(bld, "vector", bld.s_sy, xv)
    X = bld.xrow[slot]
    c = bld.cols
    E(bld, "vector", lambda e: e.reduce_sum(c[:, cb + 0:cb + 1], X[:], axis=AX.X), inc=bld.s_ve)
    W8(bld, "scalar", bld.s_sy, xv)
    guard_wait(bld, "scalar", "x2")
    sq_a = E(bld, "scalar", lambda e: e.activation(bld.x2[:], X[:], AF.Square), inc=bld.s_ac)
    W8(bld, "vector", bld.s_ac, sq_a)
    x2r = E(bld, "vector", lambda e: e.reduce_sum(c[:, cb + 1:cb + 2], bld.x2[:], axis=AX.X))
    guard_set(bld, "x2", (bld.s_ve, x2r))
    E(bld, "vector", lambda e: e.tensor_scalar_mul(c[:, cb + 2:cb + 3], c[:, cb + 0:cb + 1], 1.0 / D))
    E(bld, "vector", lambda e: e.tensor_scalar_mul(c[:, cb + 3:cb + 4], c[:, cb + 2:cb + 3], -1.0))
    E(bld, "vector", lambda e: e.tensor_scalar_mul(c[:, cb + 4:cb + 5], c[:, cb + 1:cb + 2], 1.0 / D))
    E(bld, "vector", lambda e: e.tensor_mul(c[:, cb + 5:cb + 6], c[:, cb + 2:cb + 3], c[:, cb + 2:cb + 3]))
    E(bld, "vector", lambda e: e.tensor_sub(c[:, cb + 6:cb + 7], c[:, cb + 4:cb + 5], c[:, cb + 5:cb + 6]))
    E(bld, "vector", lambda e: e.tensor_scalar_add(c[:, cb + 6:cb + 7], c[:, cb + 6:cb + 7], LN_EPS),
      inc=bld.s_ve)
    W8(bld, "scalar", bld.s_ve, bld.s_ve.v)
    sqrt_a = E(bld, "scalar", lambda e: e.activation(c[:, cb + 7:cb + 8], c[:, cb + 6:cb + 7], AF.Sqrt),
               inc=bld.s_ac)
    W8(bld, "vector", bld.s_ac, sqrt_a)
    E(bld, "vector", lambda e: e.reciprocal(c[:, cb + 8:cb + 9], c[:, cb + 7:cb + 8]))
    guard_wait(bld, "vector", f"xn{slot}")
    xn_v = E(bld, "vector", lambda e: e.tensor_scalar(
        bld.xn_tok[slot][:], X[:], c[:, cb + 3:cb + 4], c[:, cb + 8:cb + 9],
        ALU.add, ALU.mult), inc=bld.s_ve)
    guard_set(bld, f"xrow{slot}", (bld.s_ve, xn_v), (bld.s_ac, sq_a))
    tanh_a = None
    if own:
        W8(bld, "scalar", bld.s_ve, xn_v)
        guard_wait(bld, "scalar", f"xa{slot}")
        tanh_a = E(bld, "scalar", lambda e: e.activation(
            bld.xa_tok[slot][:], bld.xn_tok[slot][:], AF.Tanh), inc=bld.s_ac)

    guard_wait(bld, "tensor", "pbt")
    W8(bld, "tensor", bld.s_ve, xn_v)
    for j in range(8):
        inc = bld.s_pe if j == 7 else None
        E(bld, "tensor", lambda e, j=j: e.transpose(
            bld.pbt[:, 128 * j:128 * (j + 1)], bld.xn_tok[slot][:, 128 * j:128 * (j + 1)],
            bld.ident_bf[:]), inc=inc)
    trn_pe = bld.s_pe.v
    W8(bld, "scalar", bld.s_pe, trn_pe)
    hdst = bld.h[:].rearrange("p (c t) -> p c t", t=TLOC)[:, :, 128 * tile:128 * (tile + 1)]
    psrc = bld.pbt[:].rearrange("p (c t) -> p c t", t=128)
    h_a = E(bld, "scalar", lambda e, d2=hdst, s2=psrc: e.activation(d2, s2, AF.Copy), inc=bld.s_ac)
    guard_set(bld, "pbt", (bld.s_ac, h_a))
    guard_set(bld, f"xn{slot}", (bld.s_pe, trn_pe), (bld.s_ac, tanh_a if own else h_a))

    if not own:
        return

    guard_wait(bld, "tensor", "pbt")
    W8(bld, "tensor", bld.s_ac, tanh_a)
    guard_wait(bld, "tensor", f"xafm{slot}")
    for j in range(8):
        inc = bld.s_pe if j == 7 else None
        E(bld, "tensor", lambda e, j=j: e.transpose(
            bld.pbt[:, 128 * j:128 * (j + 1)], bld.xa_tok[slot][:, 128 * j:128 * (j + 1)],
            bld.ident_bf[:]), inc=inc)
    tra_pe = bld.s_pe.v
    W8(bld, "vector", bld.s_pe, tra_pe)
    xa_v = E(bld, "vector", lambda e: e.tensor_copy(bld.xa_fm[slot][:], bld.pbt[:]), inc=bld.s_ve)
    guard_set(bld, "pbt", (bld.s_ve, xa_v))
    guard_set(bld, f"xa{slot}", (bld.s_pe, tra_pe))

    for half in range(2):
        guard_wait(bld, "tensor", f"pb{half}")
        W8(bld, "tensor", bld.s_ve, xa_v)
        for j in range(8):
            inc = bld.s_pe if j == 7 else None
            E(bld, "tensor", lambda e, j=j, hf=half: e.matmul(
                bld.pb[hf][:, 0:512], bld.xa_fm[slot][:, 128 * j:128 * (j + 1)],
                bld.Wt_sb[:, 1024 * j + 512 * hf:1024 * j + 512 * hf + 512],
                start=(j == 0), stop=(j == 7)), inc=inc)
        st_pe = bld.s_pe.v
        W8(bld, "vector", bld.s_pe, st_pe)
        if half == 0:
            guard_wait(bld, "vector", f"st{slot}")
        ev = E(bld, "vector", lambda e, hf=half: e.tensor_copy(
            bld.st_tok[slot][:, 512 * hf:512 * (hf + 1)], bld.pb[hf][:, 0:512]), inc=bld.s_ve)
        guard_set(bld, f"pb{half}", (bld.s_ve, ev))
    st_v = bld.s_ve.v
    guard_set(bld, f"xafm{slot}", (bld.s_pe, bld.s_pe.v))
    W8(bld, "sync", bld.s_ve, st_v)
    r0 = (tile - 4) * 128
    E(bld, "sync", lambda e, r0=r0, s=slot: e.dma_start(
        bld.dram["sc_static"][r0:r0 + 128, :], bld.st_tok[s][:]), inc=bld.d_sy, n=16)
    stv = fence(bld, "sync")
    guard_set(bld, f"st{slot}", (bld.s_sy, stv))


def emit_dn_chunk(bld: Build, ch):
    p = bld.prog
    slot = (4 + ch) % 2
    xa_t = bld.xa_tok[slot]
    xa_f = bld.xa_fm[slot]
    st_t = bld.st_tok[slot]
    dcl = bld.dn_cols

    def lam_c(i):
        return dcl[:, i:i + 1]

    def hblk(h):
        return 128 * (h // 2), 64 * (h % 2)

    for grp in range(2):
        for bank, par in ((2, 0), (3, 1)):
            guard_wait(bld, "tensor", f"pb{bank}")
            if grp == 0 and bank == 2 and par == 0:
                pass
            heads = [8 * grp + 2 * i + par for i in range(4)]
            for ii, h in enumerate(heads):
                co, pb_ = hblk(h)
                inc = bld.s_pe if ii == 3 else None
                E(bld, "tensor", lambda e, co=co, pb_=pb_, ii=ii, bank=bank: e.matmul(
                    bld.pb[bank][:, 128 * ii:128 * (ii + 1)],
                    xa_f[pb_:pb_ + 64, co:co + 128], xa_f[pb_:pb_ + 64, co:co + 128],
                    start=True, stop=True, tile_position=(pb_, 0)), inc=inc)
            kk_pe = bld.s_pe.v
            W8(bld, "vector", bld.s_pe, kk_pe)
            psv = bld.pb[bank][:].rearrange("p (i x) -> p i x", x=128)
            if grp == 0 and bank == 2:
                guard_wait(bld, "vector", "NBt")
            for dst, msk in ((bld.Na, bld.Ls), (bld.NTa, bld.LsT), (bld.Bt_all, bld.LiT)):
                dv = dst[:].rearrange("p (hh pp x) -> p hh pp x", pp=2, x=128)[:, 4 * grp:4 * grp + 4, par, :]
                inc2 = bld.s_ve if msk is bld.LiT else None
                E(bld, "vector", lambda e, dv=dv, psv=psv, msk=msk: e.tensor_tensor(
                    dv, psv, msk[:].unsqueeze(1).broadcast_to([128, 4, 128]), ALU.mult),
                  inc=inc2)
            guard_set(bld, f"pb{bank}", (bld.s_ve, bld.s_ve.v))
    masks_v = bld.s_ve.v

    xv3 = bld.X_all[:].rearrange("p (h x) -> p h x", x=128)
    guard_wait(bld, "vector", "X_all")
    guard_wait(bld, "vector", "LbK")
    E(bld, "vector", lambda e: e.tensor_scalar_mul(
        xv3[:, :, 0:64], st_t[:].rearrange("p (h x) -> p h x", x=64), lam_c(CL_BETA)))
    E(bld, "vector", lambda e: e.tensor_scalar_mul(
        xv3[:, :, 64:128], xa_t[:].rearrange("p (h x) -> p h x", x=64), lam_c(CL_BLAM)))
    E(bld, "vector", lambda e: e.tensor_scalar_mul(bld.LbK[:], xa_t[:], lam_c(CL_LBAR)),
      inc=bld.s_ve)
    rhs_v = bld.s_ve.v
    guard_add(bld, f"xa{slot}", bld.s_ve, rhs_v)
    guard_add(bld, f"st{slot}", bld.s_ve, rhs_v)

    Ncur, NTcur, Nnxt, NTnxt = bld.Na, bld.NTa, bld.Nb, bld.NTb
    for j in range(7):
        for grp in range(4):
            bank = 2 + (grp % 2)
            guard_wait(bld, "tensor", f"pb{bank}")
            if j == 0:
                W8(bld, "tensor", bld.s_ve, rhs_v)
            for ii in range(4):
                h = 4 * grp + ii
                inc = bld.s_pe if ii == 3 else None
                E(bld, "tensor", lambda e, h=h, ii=ii, bank=bank, NT=NTcur: e.matmul(
                    bld.pb[bank][:, 128 * ii:128 * (ii + 1)],
                    NT[:, 128 * h:128 * (h + 1)], bld.X_all[:, 128 * h:128 * (h + 1)],
                    start=True, stop=True), inc=inc)
            app_pe = bld.s_pe.v
            W8(bld, "vector", bld.s_pe, app_pe)
            xslice = bld.X_all[:, 512 * grp:512 * (grp + 1)]
            E(bld, "vector", lambda e, xs=xslice, bank=bank: e.tensor_tensor(
                xs, xs, bld.pb[bank][:, 0:512], ALU.add), inc=bld.s_ve)
            guard_set(bld, f"pb{bank}", (bld.s_ve, bld.s_ve.v))
        if j < 6:
            for grp in range(4):
                guard_wait(bld, "tensor", "pb4")
                guard_wait(bld, "tensor", "pb5")
                for ii in range(4):
                    h = 4 * grp + ii
                    E(bld, "tensor", lambda e, h=h, ii=ii, N=Ncur, NT=NTcur: e.matmul(
                        bld.pb[4][:, 128 * ii:128 * (ii + 1)],
                        NT[:, 128 * h:128 * (h + 1)], N[:, 128 * h:128 * (h + 1)],
                        start=True, stop=True))
                for ii in range(4):
                    h = 4 * grp + ii
                    inc = bld.s_pe if ii == 3 else None
                    E(bld, "tensor", lambda e, h=h, ii=ii, N=Ncur, NT=NTcur: e.matmul(
                        bld.pb[5][:, 128 * ii:128 * (ii + 1)],
                        N[:, 128 * h:128 * (h + 1)], NT[:, 128 * h:128 * (h + 1)],
                        start=True, stop=True), inc=inc)
                sq_pe = bld.s_pe.v
                W8(bld, "vector", bld.s_pe, sq_pe)
                E(bld, "vector", lambda e, grp=grp, Nn=Nnxt: e.tensor_copy(
                    Nn[:, 512 * grp:512 * (grp + 1)], bld.pb[4][:, 0:512]))
                E(bld, "vector", lambda e, grp=grp, NTn=NTnxt: e.tensor_copy(
                    NTn[:, 512 * grp:512 * (grp + 1)], bld.pb[5][:, 0:512]), inc=bld.s_ve)
                guard_set(bld, "pb4", (bld.s_ve, bld.s_ve.v))
                guard_set(bld, "pb5", (bld.s_ve, bld.s_ve.v))
            Ncur, NTcur, Nnxt, NTnxt = Nnxt, NTnxt, Ncur, NTcur
    solve_v = bld.s_ve.v

    for rnd in range(2):
        guard_wait(bld, "tensor", "pbt")
        W8(bld, "tensor", bld.s_ve, solve_v)
        for pi in range(4):
            pj = 4 * rnd + pi
            for par in range(2):
                h = 2 * pj + par
                inc = bld.s_pe if (pi == 3 and par == 1) else None
                E(bld, "tensor", lambda e, h=h, pi=pi, par=par: e.transpose(
                    bld.pbt[64 * par:64 * (par + 1), 128 * pi:128 * (pi + 1)],
                    bld.X_all[:, 128 * h + 64:128 * (h + 1)], bld.ident_bf[:],
                    tile_position=(0, 64 * par)), inc=inc)
        wt_pe = bld.s_pe.v
        W8(bld, "vector", bld.s_pe, wt_pe)
        if rnd == 0:
            guard_wait(bld, "vector", "Wfm")
        E(bld, "vector", lambda e, rnd=rnd: e.tensor_scalar_mul(
            bld.Wfm[:, 512 * rnd:512 * (rnd + 1)], bld.pbt[:, 0:512], -1.0), inc=bld.s_ve)
        guard_set(bld, "pbt", (bld.s_ve, bld.s_ve.v))

    for par in range(2):
        bank = 6 if par == 0 else 5
        guard_wait(bld, "tensor", f"pb{bank}")
        W8(bld, "tensor", bld.s_ve, bld.s_ve.v)
        for pj in range(8):
            h = 2 * pj + par
            inc = bld.s_pe if pj == 7 else None
            E(bld, "tensor", lambda e, pj=pj, par=par, bank=bank: e.matmul(
                bld.pb[bank][:, 64 * pj:64 * (pj + 1)],
                bld.Wfm[64 * par:64 * (par + 1), 128 * pj:128 * (pj + 1)],
                bld.MTbf[64 * par:64 * (par + 1), 64 * pj:64 * (pj + 1)],
                start=True, stop=True, tile_position=(64 * par, 0)), inc=inc)
        u_pe = bld.s_pe.v
        W8(bld, "vector", bld.s_pe, u_pe)
        if par == 0:
            guard_wait(bld, "vector", "U_all")
        uv = bld.U_all[:].rearrange("p (hh pp x) -> p hh pp x", pp=2, x=64)[:, :, par, :]
        ub = bld.X_all[:].rearrange("p (hh pp x) -> p hh pp x", pp=2, x=128)[:, :, par, 0:64]
        pv = bld.pb[bank][:].rearrange("p (h x) -> p h x", x=64)
        E(bld, "vector", lambda e, uv=uv, ub=ub, pv=pv: e.tensor_tensor(uv, ub, pv, ALU.add),
          inc=bld.s_ve)
        guard_set(bld, f"pb{bank}", (bld.s_ve, bld.s_ve.v))
    u_v = bld.s_ve.v

    mtbf_readers_pe = []

    guard_wait(bld, "vector", "o0_all")
    for par in range(2):
        bank = 0 if par == 0 else 1
        guard_wait(bld, "tensor", f"pb{bank}")
        for pj in range(8):
            h = 2 * pj + par
            co, pb_ = hblk(h)
            inc = bld.s_pe if pj == 7 else None
            E(bld, "tensor", lambda e, pj=pj, par=par, bank=bank, co=co, pb_=pb_: e.matmul(
                bld.pb[bank][:, 64 * pj:64 * (pj + 1)],
                xa_f[pb_:pb_ + 64, co:co + 128],
                bld.MTbf[64 * par:64 * (par + 1), 64 * pj:64 * (pj + 1)],
                start=True, stop=True, tile_position=(pb_, 0)), inc=inc)
        mtbf_readers_pe.append(bld.s_pe.v)
    o1_pe = bld.s_pe.v
    o0v = bld.o0_all[:].rearrange("p (h x) -> p h x", x=64)
    for rnd in range(2):
        guard_wait(bld, "tensor", "pb4")
        W8(bld, "tensor", bld.s_ve, bld.s_ve.v)
        for i8 in range(8):
            h = 8 * rnd + i8
            inc = bld.s_pe if i8 == 7 else None
            E(bld, "tensor", lambda e, h=h, i8=i8: e.matmul(
                bld.pb[4][:, 64 * i8:64 * (i8 + 1)],
                bld.Bt_all[:, 128 * h:128 * (h + 1)], bld.U_all[:, 64 * h:64 * (h + 1)],
                start=True, stop=True), inc=inc)
        o2_pe = bld.s_pe.v
        W8(bld, "scalar", bld.s_pe, o2_pe)
        otv = bld.otmp[:].rearrange("p (hh pp x) -> p hh pp x", pp=2, x=64)
        for par in range(2):
            bank = 0 if par == 0 else 1
            pv = bld.pb[bank][:].rearrange("p (h x) -> p h x", x=64)[:, 4 * rnd:4 * rnd + 4]
            E(bld, "scalar", lambda e, par=par, pv=pv, otv=otv: e.activation(
                otv[:, :, par, :], pv, AF.Copy), inc=(bld.s_ac if par == 1 else None))
        ocp_a = bld.s_ac.v
        W8(bld, "vector", bld.s_ac, ocp_a)
        W8(bld, "vector", bld.s_pe, o2_pe)
        pv2 = bld.pb[4][:].rearrange("p (h x) -> p h x", x=64)
        E(bld, "vector", lambda e, rnd=rnd, pv2=pv2, o0v=o0v: e.scalar_tensor_tensor(
            o0v[:, 8 * rnd:8 * rnd + 8], bld.otmp[:].rearrange("p (h x) -> p h x", x=64),
            lam_c(CL_LAM), pv2, ALU.mult, ALU.add), inc=bld.s_ve)
        guard_set(bld, "pb4", (bld.s_ve, bld.s_ve.v))
        if rnd == 1:
            guard_set(bld, "pb0", (bld.s_ac, ocp_a))
            guard_set(bld, "pb1", (bld.s_ac, ocp_a))
    o0_v = bld.s_ve.v
    W8(bld, "sync", bld.s_ve, o0_v)
    E(bld, "sync", lambda e, ch=ch: e.dma_start(
        bld.dram["sc_o0"][128 * ch:128 * (ch + 1), :], bld.o0_all[:]), inc=bld.d_sy, n=16)
    guard_set(bld, "o0_all", (bld.s_sy, fence(bld, "sync")))

    for rnd in range(2):
        guard_wait(bld, "tensor", "pb4")
        for i8 in range(8):
            h = 8 * rnd + i8
            inc = bld.s_pe if i8 == 7 else None
            E(bld, "tensor", lambda e, h=h, i8=i8: e.matmul(
                bld.pb[4][:, 64 * i8:64 * (i8 + 1)],
                bld.Bt_all[:, 128 * h:128 * (h + 1)], bld.X_all[:, 128 * h + 64:128 * (h + 1)],
                start=True, stop=True), inc=inc)
        bw_pe = bld.s_pe.v
        W8(bld, "vector", bld.s_pe, bw_pe)
        if rnd == 0:
            guard_wait(bld, "vector", "Ttok")
            guard_wait(bld, "vector", "Tfm")
        ttv = bld.Ttok[:].rearrange("p (h x) -> p h x", x=64)[:, 8 * rnd:8 * rnd + 8]
        qv = xa_t[:].rearrange("p (h x) -> p h x", x=64)[:, 8 * rnd:8 * rnd + 8]
        pv2 = bld.pb[4][:].rearrange("p (h x) -> p h x", x=64)
        E(bld, "vector", lambda e, ttv=ttv, qv=qv, pv2=pv2: e.scalar_tensor_tensor(
            ttv, qv, lam_c(CL_LAM), pv2, ALU.mult, ALU.subtract), inc=bld.s_ve)
        guard_set(bld, "pb4", (bld.s_ve, bld.s_ve.v))
    tt_v = bld.s_ve.v
    guard_add(bld, f"xa{slot}", bld.s_ve, tt_v)
    for rnd in range(2):
        guard_wait(bld, "tensor", "pbt")
        W8(bld, "tensor", bld.s_ve, tt_v)
        for pi in range(4):
            pj = 4 * rnd + pi
            for par in range(2):
                h = 2 * pj + par
                inc = bld.s_pe if (pi == 3 and par == 1) else None
                E(bld, "tensor", lambda e, h=h, pi=pi, par=par: e.transpose(
                    bld.pbt[64 * par:64 * (par + 1), 128 * pi:128 * (pi + 1)],
                    bld.Ttok[:, 64 * h:64 * (h + 1)], bld.ident_bf[:],
                    tile_position=(0, 64 * par)), inc=inc)
        tf_pe = bld.s_pe.v
        W8(bld, "vector", bld.s_pe, tf_pe)
        E(bld, "vector", lambda e, rnd=rnd: e.tensor_copy(
            bld.Tfm[:, 512 * rnd:512 * (rnd + 1)], bld.pbt[:, 0:512]), inc=bld.s_ve)
        guard_set(bld, "pbt", (bld.s_ve, bld.s_ve.v))
    guard_set(bld, "Ttok", (bld.s_pe, bld.s_pe.v))
    tfm_v = bld.s_ve.v
    W8(bld, "sync", bld.s_ve, tfm_v)
    E(bld, "sync", lambda e, ch=ch: e.dma_start(bld.dram["sc_T"][ch], bld.Tfm[:]),
      inc=bld.d_sy, n=16)

    guard_wait(bld, "sync", "P_write")
    E(bld, "sync", lambda e, ch=ch: e.dma_start(bld.dram["sc_P"][ch], bld.P_all[:]),
      inc=bld.d_sy, n=16)
    psnap = fence(bld, "sync")

    guard_wait(bld, "tensor", "pb6")
    W8(bld, "tensor", bld.s_ve, tfm_v)
    for pj in range(8):
        for par in range(2):
            h = 2 * pj + par
            inc = bld.s_pe if (pj == 7 and par == 1) else None
            E(bld, "tensor", lambda e, h=h, pj=pj, par=par: e.matmul(
                bld.pb[6][64 * par:64 * (par + 1), 64 * pj:64 * (pj + 1)],
                bld.X_all[:, 128 * h + 64:128 * (h + 1)], bld.LbK[:, 64 * h:64 * (h + 1)],
                start=True, stop=True, tile_position=(0, 64 * par)), inc=inc)
    g_pe = bld.s_pe.v
    W8(bld, "vector", bld.s_pe, g_pe)
    E(bld, "vector", lambda e: e.scalar_tensor_tensor(
        bld.G_all[:].rearrange("p (h x) -> p h x", x=64),
        bld.i64_bf[:].unsqueeze(1).broadcast_to([128, 8, 64]),
        lam_c(CL_LAMC), bld.pb[6][:].rearrange("p (h x) -> p h x", x=64),
        ALU.mult, ALU.subtract), inc=bld.s_ve)
    g_v = bld.s_ve.v
    guard_set(bld, "pb6", (bld.s_ve, g_v))

    guard_wait(bld, "tensor", "pb6")
    for pj in range(8):
        for par in range(2):
            h = 2 * pj + par
            E(bld, "tensor", lambda e, h=h, pj=pj, par=par: e.matmul(
                bld.pb[6][64 * par:64 * (par + 1), 64 * pj:64 * (pj + 1)],
                bld.U_all[:, 64 * h:64 * (h + 1)], bld.LbK[:, 64 * h:64 * (h + 1)],
                start=True, stop=True, tile_position=(0, 64 * par)))
    guard_wait(bld, "tensor", "pb5")
    for pj in range(8):
        for par in range(2):
            h = 2 * pj + par
            inc = bld.s_pe if (pj == 7 and par == 1) else None
            E(bld, "tensor", lambda e, h=h, pj=pj, par=par: e.matmul(
                bld.pb[5][64 * par:64 * (par + 1), 64 * pj:64 * (pj + 1)],
                bld.LbK[:, 64 * h:64 * (h + 1)], bld.U_all[:, 64 * h:64 * (h + 1)],
                start=True, stop=True, tile_position=(0, 64 * par)), inc=inc)
    mm_pe = bld.s_pe.v
    for r in mtbf_readers_pe:
        pass
    W8(bld, "vector", bld.s_pe, mm_pe)
    E(bld, "vector", lambda e: e.scalar_tensor_tensor(
        bld.M_all[:], bld.M_all[:], lam_c(CL_LAMC), bld.pb[6][:], ALU.mult, ALU.add))
    E(bld, "vector", lambda e: e.scalar_tensor_tensor(
        bld.MT_all[:], bld.MT_all[:], lam_c(CL_LAMC), bld.pb[5][:], ALU.mult, ALU.add),
      inc=bld.s_ve)
    m_v = bld.s_ve.v
    guard_wait(bld, "vector", "MTbf_w")
    E(bld, "vector", lambda e: e.tensor_copy(bld.MTbf[:], bld.MT_all[:]), inc=bld.s_ve)
    guard_set(bld, "pb5", (bld.s_ve, bld.s_ve.v))
    guard_set(bld, "pb6", (bld.s_ve, bld.s_ve.v))

    W8(bld, "tensor", bld.s_ve, g_v)
    W8(bld, "tensor", bld.s_sy, psnap)
    guard_wait(bld, "tensor", "pb3")
    for pj in range(8):
        for par in range(2):
            E(bld, "tensor", lambda e, pj=pj, par=par: e.matmul(
                bld.pb[3][64 * par:64 * (par + 1), 64 * pj:64 * (pj + 1)],
                bld.PT_all[64 * par:64 * (par + 1), 64 * pj:64 * (pj + 1)],
                bld.G_all[64 * par:64 * (par + 1), 64 * pj:64 * (pj + 1)],
                start=True, stop=True, tile_position=(64 * par, 64 * par)))
    guard_wait(bld, "tensor", "pb2")
    for pj in range(8):
        for par in range(2):
            inc = bld.s_pe if (pj == 7 and par == 1) else None
            E(bld, "tensor", lambda e, pj=pj, par=par: e.matmul(
                bld.pb[2][64 * par:64 * (par + 1), 64 * pj:64 * (pj + 1)],
                bld.G_all[64 * par:64 * (par + 1), 64 * pj:64 * (pj + 1)],
                bld.PT_all[64 * par:64 * (par + 1), 64 * pj:64 * (pj + 1)],
                start=True, stop=True, tile_position=(64 * par, 64 * par)), inc=inc)
    p_pe = bld.s_pe.v
    W8(bld, "vector", bld.s_pe, p_pe)
    E(bld, "vector", lambda e: e.tensor_copy(bld.P_all[:], bld.pb[3][:]))
    E(bld, "vector", lambda e: e.tensor_copy(bld.PT_all[:], bld.pb[2][:]), inc=bld.s_ve)
    p_v = bld.s_ve.v
    guard_set(bld, "pb2", (bld.s_ve, p_v))
    guard_set(bld, "pb3", (bld.s_ve, p_v))
    guard_set(bld, "P_write", (bld.s_ve, p_v))
    guard_set(bld, "Wfm", (bld.s_pe, mm_pe))
    guard_set(bld, "U_all", (bld.s_pe, mm_pe))
    guard_set(bld, "LbK", (bld.s_pe, mm_pe))
    guard_set(bld, "X_all", (bld.s_pe, g_pe))
    guard_set(bld, "NBt", (bld.s_pe, g_pe))
    guard_set(bld, "MTbf_w", (bld.s_pe, mm_pe))
    guard_set(bld, "Tfm", (bld.s_sy, psnap))


def emit_dn_ag(bld: Build):
    p = bld.prog
    W8(bld, "sync", bld.s_ve, bld.s_ve.v)
    E(bld, "sync", lambda e: e.dma_start(bld.dram["cc_m_in"][:, 0:512], bld.M_all[:]),
      inc=bld.d_sy, n=16)
    E(bld, "sync", lambda e: e.dma_start(bld.dram["cc_m_in"][:, 512:1024], bld.MT_all[:]),
      inc=bld.d_sy, n=16)
    ag_in = fence(bld, "sync")
    W8(bld, "gpsimd", bld.s_sy, ag_in)
    cc = E(bld, "gpsimd", lambda e: e.collective_compute(
        "AllGather", ALU.bypass, replica_groups=[[0, 1], [2, 3], [4, 5], [6, 7]],
        ins=[bld.dram["cc_m_in"].opt()], outs=[bld.dram["cc_m_out"].opt()]), inc=bld.s_cc)
    W8(bld, "sync", bld.s_cc, cc)
    guard_wait(bld, "sync", "xrow0")
    E(bld, "sync", lambda e: e.dma_start(bld.xrow[0][:], bld.dram["cc_m_out"][0:128, :]),
      inc=bld.d_sy, n=16)
    ld = fence(bld, "sync")
    W8(bld, "vector", bld.s_sy, ld)
    E(bld, "vector", lambda e: e.tensor_scalar_mul(bld.Minit[:], bld.xrow[0][:, 0:512],
                                                   bld.dn_cols[:, CL_SEL:CL_SEL + 1]))
    E(bld, "vector", lambda e: e.tensor_scalar_mul(bld.MinitT[:], bld.xrow[0][:, 512:1024],
                                                   bld.dn_cols[:, CL_SEL:CL_SEL + 1]))
    E(bld, "vector", lambda e: e.tensor_copy(bld.MinitTbf[:], bld.MinitT[:]), inc=bld.s_ve)


def emit_dn_corr(bld: Build, ch):
    p = bld.prog
    guard_wait(bld, "sync", "corr_ld")
    E(bld, "sync", lambda e, ch=ch: e.dma_start(bld.Pld[:], bld.dram["sc_P"][ch]),
      inc=bld.d_sy, n=16)
    E(bld, "sync", lambda e, ch=ch: e.dma_start(bld.Tld[:], bld.dram["sc_T"][ch]),
      inc=bld.d_sy, n=16)
    E(bld, "sync", lambda e, ch=ch: e.dma_start(
        bld.o0ld[:], bld.dram["sc_o0"][128 * ch:128 * (ch + 1), :]), inc=bld.d_sy, n=16)
    l3 = fence(bld, "sync")
    guard_wait(bld, "tensor", "pb6")
    W8(bld, "tensor", bld.s_sy, l3)
    W8(bld, "tensor", bld.s_ve, bld.s_ve.v)
    for pj in range(8):
        for par in range(2):
            inc = bld.s_pe if (pj == 7 and par == 1) else None
            E(bld, "tensor", lambda e, pj=pj, par=par: e.matmul(
                bld.pb[6][64 * par:64 * (par + 1), 64 * pj:64 * (pj + 1)],
                bld.Pld[64 * par:64 * (par + 1), 64 * pj:64 * (pj + 1)],
                bld.MinitT[64 * par:64 * (par + 1), 64 * pj:64 * (pj + 1)],
                start=True, stop=True, tile_position=(64 * par, 64 * par)), inc=inc)
    dm_pe = bld.s_pe.v
    W8(bld, "vector", bld.s_pe, dm_pe)
    guard_wait(bld, "vector", "dMbf")
    dm_v = E(bld, "vector", lambda e: e.tensor_copy(bld.dMbf[:], bld.pb[6][:]), inc=bld.s_ve)
    guard_set(bld, "pb6", (bld.s_ve, dm_v))
    for par in range(2):
        bank = 2 if par == 0 else 3
        guard_wait(bld, "tensor", f"pb{bank}")
        W8(bld, "tensor", bld.s_ve, dm_v)
        W8(bld, "tensor", bld.s_sy, l3)
        for pj in range(8):
            inc = bld.s_pe if pj == 7 else None
            E(bld, "tensor", lambda e, pj=pj, par=par, bank=bank: e.matmul(
                bld.pb[bank][:, 64 * pj:64 * (pj + 1)],
                bld.Tld[64 * par:64 * (par + 1), 128 * pj:128 * (pj + 1)],
                bld.dMbf[64 * par:64 * (par + 1), 64 * pj:64 * (pj + 1)],
                start=True, stop=True, tile_position=(64 * par, 0)), inc=inc)
    do_pe = bld.s_pe.v
    guard_set(bld, "dMbf", (bld.s_pe, do_pe))
    W8(bld, "vector", bld.s_pe, do_pe)
    W8(bld, "vector", bld.s_sy, l3)
    guard_wait(bld, "vector", "o0_all")
    for par in range(2):
        bank = 2 if par == 0 else 3
        ov = bld.o0_all[:].rearrange("p (hh pp x) -> p hh pp x", pp=2, x=64)[:, :, par, :]
        iv = bld.o0ld[:].rearrange("p (hh pp x) -> p hh pp x", pp=2, x=64)[:, :, par, :]
        pv = bld.pb[bank][:].rearrange("p (h x) -> p h x", x=64)
        inc = bld.s_ve if par == 1 else None
        E(bld, "vector", lambda e, ov=ov, iv=iv, pv=pv: e.tensor_tensor(ov, iv, pv, ALU.add),
          inc=inc)
    of_v = bld.s_ve.v
    guard_set(bld, "pb2", (bld.s_ve, of_v))
    guard_set(bld, "pb3", (bld.s_ve, of_v))
    W8(bld, "sync", bld.s_ve, of_v)
    E(bld, "sync", lambda e, ch=ch: e.dma_start(
        bld.dram["sc_ofin"][128 * ch:128 * (ch + 1), :], bld.o0_all[:]), inc=bld.d_sy, n=16)
    st = fence(bld, "sync")
    guard_set(bld, "o0_all", (bld.s_sy, st))
    guard_set(bld, "corr_ld", (bld.s_pe, do_pe), (bld.s_ve, of_v))


def emit_s_tile(bld: Build, ti):
    p = bld.prog
    guard_wait(bld, "sync", "s_ld")
    E(bld, "sync", lambda e, ti=ti: e.dma_start(
        bld.st_ld[:], bld.dram["sc_static"][128 * ti:128 * (ti + 1), :]), inc=bld.d_sy, n=16)
    E(bld, "sync", lambda e, ti=ti: e.dma_start(
        bld.of_ld[:], bld.dram["sc_ofin"][128 * ti:128 * (ti + 1), :]), inc=bld.d_sy, n=16)
    l2 = fence(bld, "sync")
    W8(bld, "vector", bld.s_sy, l2)
    guard_wait(bld, "vector", "yrow")
    c = bld.cols
    cb = 12
    E(bld, "vector", lambda e: e.tensor_add(bld.yrow[:], bld.st_ld[:], bld.of_ld[:]))
    E(bld, "vector", lambda e: e.reduce_sum(c[:, cb + 0:cb + 1], bld.yrow[:], axis=AX.X),
      inc=bld.s_ve)
    y_v = bld.s_ve.v
    W8(bld, "scalar", bld.s_ve, y_v)
    guard_wait(bld, "scalar", "x2")
    sq_a = E(bld, "scalar", lambda e: e.activation(bld.x2[:], bld.yrow[:], AF.Square), inc=bld.s_ac)
    W8(bld, "vector", bld.s_ac, sq_a)
    x2r = E(bld, "vector", lambda e: e.reduce_sum(c[:, cb + 1:cb + 2], bld.x2[:], axis=AX.X))
    guard_set(bld, "x2", (bld.s_ve, x2r))
    E(bld, "vector", lambda e: e.tensor_scalar_mul(c[:, cb + 2:cb + 3], c[:, cb + 0:cb + 1], 1.0 / D))
    E(bld, "vector", lambda e: e.tensor_scalar_mul(c[:, cb + 3:cb + 4], c[:, cb + 2:cb + 3], -1.0))
    E(bld, "vector", lambda e: e.tensor_scalar_mul(c[:, cb + 4:cb + 5], c[:, cb + 1:cb + 2], 1.0 / D))
    E(bld, "vector", lambda e: e.tensor_mul(c[:, cb + 5:cb + 6], c[:, cb + 2:cb + 3], c[:, cb + 2:cb + 3]))
    E(bld, "vector", lambda e: e.tensor_sub(c[:, cb + 6:cb + 7], c[:, cb + 4:cb + 5], c[:, cb + 5:cb + 6]))
    E(bld, "vector", lambda e: e.tensor_scalar_add(c[:, cb + 6:cb + 7], c[:, cb + 6:cb + 7], LN_EPS),
      inc=bld.s_ve)
    W8(bld, "scalar", bld.s_ve, bld.s_ve.v)
    sr_a = E(bld, "scalar", lambda e: e.activation(c[:, cb + 7:cb + 8], c[:, cb + 6:cb + 7], AF.Sqrt),
             inc=bld.s_ac)
    W8(bld, "vector", bld.s_ac, sr_a)
    E(bld, "vector", lambda e: e.reciprocal(c[:, cb + 8:cb + 9], c[:, cb + 7:cb + 8]))
    E(bld, "vector", lambda e: e.tensor_scalar(bld.yrow[:], bld.yrow[:], c[:, cb + 3:cb + 4],
                                               c[:, cb + 8:cb + 9], ALU.add, ALU.mult))
    E(bld, "vector", lambda e: e.tensor_mul(bld.yrow[:], bld.yrow[:], bld.gamma_b[:]))
    guard_wait(bld, "vector", "s_bf")
    sv = E(bld, "vector", lambda e: e.tensor_add(bld.s_bf[:], bld.yrow[:], bld.beta_b[:]),
           inc=bld.s_ve)
    guard_set(bld, "yrow", (bld.s_ve, sv))
    guard_set(bld, "s_ld", (bld.s_ve, sv))
    W8(bld, "sync", bld.s_ve, sv)
    E(bld, "sync", lambda e, ti=ti: e.dma_start(
        bld.dram["sc_s"][128 * ti:128 * (ti + 1), :], bld.s_bf[:]), inc=bld.d_sy, n=16)
    st = fence(bld, "sync")
    guard_set(bld, "s_bf", (bld.s_sy, st))


def emit_dn_phase(bld: Build, n_tiles=NT, n_chunks=NCH):
    for tile in range(n_tiles):
        emit_tile_prep(bld, tile)
        if 4 <= tile < 4 + n_chunks:
            emit_dn_chunk(bld, tile - 4)
    emit_dn_ag(bld)
    for ch in range(n_chunks):
        emit_dn_corr(bld, ch)
    for ti in range(n_chunks):
        emit_s_tile(bld, ti)


def finish_program(bld: Build):
    for q, s in (("sync", bld.d_sy), ("gpsimd", bld.d_gp), ("scalar", bld.d_sc)):
        if s.v:
            W8(bld, q, s, s.v)


def build_dn_standalone(n_tiles=NT, n_chunks=NCH):
    bld = Build(with_transformer=False)
    build_common(bld)
    emit_setup(bld)
    emit_dn_phase(bld, n_tiles, n_chunks)
    sdone = fence(bld, "sync")
    W8(bld, "gpsimd", bld.s_sy, sdone)
    nrows = 128 * n_chunks
    step = 128
    for i in range(0, nrows, step):
        E(bld, "gpsimd", lambda e, i=i: e.dma_start(
            bld.dram["out"][i:i + step, :],
            bld.dram["sc_s"][i:i + step, :]), inc=bld.d_gp, n=16)
    finish_program(bld)
    with bld.es, bld.nc.Block() as blk:
        bld.prog.play(blk)
    return bld


def core_input_map(inputs, core_id):
    b, half = core_id // 2, core_id % 2
    gstart = 4096 * half - HALO
    x = np.asarray(inputs["L0_attn_x"][b], np.float32)
    xloc = np.zeros((TLOC, D), np.float32)
    lo = max(0, gstart)
    xloc[lo - gstart:] = x[lo:gstart + TLOC]
    m = dict(_host_consts(core_id))
    m["x_in"] = xloc
    m["Wt_d"] = np.ascontiguousarray(np.asarray(inputs["L0_attn_W"], np.float32).T)
    m["ln_gamma"] = np.asarray(inputs["L0_attn_ln_gamma"], np.float32).reshape(1, D)
    m["ln_beta"] = np.asarray(inputs["L0_attn_ln_beta"], np.float32).reshape(1, D)
    m["eta"] = np.asarray(inputs["L0_attn_eta"], np.float32).reshape(1, 1)
    m["lam_logit"] = np.asarray(inputs["L0_attn_lam_logit"], np.float32).reshape(1, 1)
    m["Wqkv"] = np.asarray(inputs["L0_ffn_Wqkv"], np.float32)
    m["Wo"] = np.asarray(inputs["L0_ffn_Wo"], np.float32)
    m["W1"] = np.asarray(inputs["L0_ffn_W1"], np.float32)
    m["W2"] = np.asarray(inputs["L0_ffn_W2"], np.float32)
    b1 = np.asarray(inputs["L0_ffn_b1"], np.float32)
    b1a = np.pad(b1[:, :DIM_INNER], ((0, 0), (0, 22 * 128 - DIM_INNER))).reshape(NLAYER, 22, 128)
    b1g = np.pad(b1[:, DIM_INNER:], ((0, 0), (0, 22 * 128 - DIM_INNER))).reshape(NLAYER, 22, 128)
    m["b1cols"] = np.ascontiguousarray(np.stack([b1a, b1g], axis=1).reshape(NLAYER, 2, 22, 128, 1))
    m["b2cols"] = np.ascontiguousarray(
        np.asarray(inputs["L0_ffn_b2"], np.float32).reshape(NLAYER, 8, 128, 1))
    m["ffn_g"] = np.ascontiguousarray(
        np.asarray(inputs["L0_ffn_ln_g"], np.float32).reshape(NLAYER, 8, 128, 1))
    m["ffn_b"] = np.ascontiguousarray(
        np.asarray(inputs["L0_ffn_ln_b"], np.float32).reshape(NLAYER, 8, 128, 1))
    qs2 = np.tile(np.asarray(inputs["L0_ffn_q_scale"], np.float32), (1, 2))
    ks2 = np.tile(np.asarray(inputs["L0_ffn_k_scale"], np.float32), (1, 2))
    m["qk_scale"] = np.ascontiguousarray(
        np.stack([qs2, ks2], axis=1).reshape(NLAYER, 2, 128, 1))
    m["ones_col128"] = np.ones((128, 1), np.float32)
    return m


def phase_barrier(bld: Build):
    for q in ("sync", "scalar", "gpsimd"):
        fence(bld, q)
    sems = (bld.s_pe, bld.s_ve, bld.s_ac, bld.s_gp, bld.s_sy, bld.s_sg, bld.s_gq, bld.s_cc)
    for eng in ENGINES:
        for s in sems:
            if s.v:
                W8(bld, eng, s, s.v)


def tf_alloc(bld: Build):
    s = lambda n, sh, dt: bld.sbuf(n, sh, dt, scope=bld.es_tf)
    bld.wqk_buf = s("wqk_buf", [128, 8 * 512], F16)
    bld.wv_buf = s("wv_buf", [128, 8 * 512], F16)
    bld.wo_sb = s("wo_sb", [128, 8 * 1024], F16)
    bld.masks_sb = s("masks_sb", [128, 3 * 768], BF16)
    bld.i64h = s("i64h", [128, 64], BF16)
    bld.co_sb = s("co_sb", [128, 8 * 16], BF16)
    bld.coT_sb = s("coT_sb", [16, 8 * 128], BF16)
    bld.lcols = s("lcols", [128, 64], F32)
    bld.b1c = s("b1c", [128, 44], F32)
    bld.qbuf = s("qbuf", [128, 8 * 512], BF16)
    bld.kbuf = s("kbuf", [128, 8 * 768], BF16)
    bld.vbuf = s("vbuf", [128, 6 * 1024], F16)
    bld.rotA = s("rotA", [128, 512], BF16)
    bld.rotB = s("rotB", [128, 512], BF16)

    bld.q2t = s("q2t", [128, 512], BF16)
    bld.q2t2 = s("q2t2", [128, 512], BF16)
    bld.nrm = s("nrm", [16, 512], BF16)
    bld.Pbuf = s("Pbuf", [128, 4 * 384], BF16)
    bld.PTb = [s(f"PTb{i}", [128, 128], F16) for i in range(12)]
    bld.ofm = s("ofm", [128, 8 * 128], F16)
    bld.m_sb = s("m_sb", [128, 22 * 512], BF16)
    bld.xn_t = bld.qbuf
    bld.w1b = [s(f"w1b{i}", [128, 8 * 256], BF16) for i in range(2)]
    bld.w2b = [s(f"w2b{i}", [128, 1024], BF16) for i in range(2)]
    bld.ones16 = s("ones16", [128, 1], F16)
    bld.hsave = s("hsave", [128, 8 * 9], F16)
    bld.hdelta = s("hdelta", [128, 8], F16)


def emit_tf_setup(bld: Build):
    d = bld.dram
    E(bld, "gpsimd", lambda e: e.dma_start(
        bld.masks_sb[:].rearrange("p (s n) -> p s n", n=768),
        d["attn_masks"].rearrange("s p n -> p s n")), inc=bld.d_gp, n=16)
    E(bld, "gpsimd", lambda e: e.dma_start(bld.i64h[:], d["ident64h"]), inc=bld.d_gp, n=16)
    E(bld, "gpsimd", lambda e: e.dma_start(
        bld.co_sb[:].rearrange("p (j n) -> p j n", n=16),
        d["chunk_ones"].rearrange("j p n -> p j n")), inc=bld.d_gp, n=16)
    E(bld, "gpsimd", lambda e: e.dma_start(bld.ones16[:], d["ones_col128"]), inc=bld.d_gp, n=16)
    E(bld, "gpsimd", lambda e: e.memset(bld.kbuf[:], 0.0), inc=bld.s_gp)
    E(bld, "gpsimd", lambda e: e.memset(bld.vbuf[:], 0.0), inc=bld.s_gp)
    E(bld, "gpsimd", lambda e: e.dma_start(
        bld.coT_sb[:].rearrange("p (j n) -> p j n", n=128),
        d["chunk_onesT"].rearrange("j p n -> p j n")), inc=bld.d_gp, n=16)
    gq = fence(bld, "gpsimd")
    sy = fence(bld, "sync")
    for eng in ("tensor", "vector", "scalar"):
        W8(bld, eng, bld.s_gq, gq)
        W8(bld, eng, bld.s_sy, sy)
        W8(bld, eng, bld.s_gp, bld.s_gp.v)


LQ_QS, LQ_KS = 0, 1
LQ_G0 = 2
LQ_B0 = 10
LQ_B2 = 18


def emit_layer_loads(bld: Build, l):
    d = bld.dram

    guard_wait(bld, "gpsimd", "wo")
    for vc in range(8):
        bld.wo_tot = E(bld, "gpsimd", lambda e, vc=vc, l=l: e.dma_start(
            bld.wo_sb[:, 1024 * vc:1024 * (vc + 1)],
            d["Wo"][l][128 * vc:128 * (vc + 1), :]), inc=bld.d_gp, n=16)
    guard_wait(bld, "sync", "lcols")
    E(bld, "sync", lambda e, l=l: e.dma_start(bld.lcols[:, 0:1], d["qk_scale"][l][0]), inc=bld.d_sy, n=16)
    E(bld, "sync", lambda e, l=l: e.dma_start(bld.lcols[:, 1:2], d["qk_scale"][l][1]), inc=bld.d_sy, n=16)
    E(bld, "sync", lambda e, l=l: e.dma_start(
        bld.lcols[:, LQ_G0:LQ_G0 + 8].rearrange("p (c o) -> p c o", o=1),
        d["ffn_g"][l].rearrange("c p o -> p c o"), allow_slow_non_contiguous=True), inc=bld.d_sy, n=16)
    E(bld, "sync", lambda e, l=l: e.dma_start(
        bld.lcols[:, LQ_B0:LQ_B0 + 8].rearrange("p (c o) -> p c o", o=1),
        d["ffn_b"][l].rearrange("c p o -> p c o"), allow_slow_non_contiguous=True), inc=bld.d_sy, n=16)
    E(bld, "sync", lambda e, l=l: e.dma_start(
        bld.lcols[:, LQ_B2:LQ_B2 + 8].rearrange("p (c o) -> p c o", o=1),
        d["b2cols"][l].rearrange("c p o -> p c o"), allow_slow_non_contiguous=True), inc=bld.d_sy, n=16)
    E(bld, "sync", lambda e, l=l: e.dma_start(
        bld.b1c[:, 0:44].rearrange("p (s c o) -> p s c o", s=2, o=1),
        d["b1cols"][l].rearrange("s c p o -> p s c o"), allow_slow_non_contiguous=True), inc=bld.d_sy, n=16)
    return fence(bld, "sync")


def emit_halo_ag(bld: Build, l):
    d = bld.dram
    hv = bld.h[:].rearrange("p (c t) -> p c t", t=TLOC)
    W8(bld, "sync", bld.s_ve, bld.s_ve.v)
    E(bld, "sync", lambda e, hv=hv: e.dma_start(
        d["cc_h_in"].rearrange("(c p) t -> p c t", p=128), hv[:, :, TLOC - HALO:TLOC]),
      inc=bld.d_sy, n=16)
    agi = fence(bld, "sync")
    W8(bld, "gpsimd", bld.s_sy, agi)
    cc = E(bld, "gpsimd", lambda e: e.collective_compute(
        "AllGather", ALU.bypass, replica_groups=[[0, 1], [2, 3], [4, 5], [6, 7]],
        ins=[d["cc_h_in"].opt()], outs=[d["cc_h_out"].opt()]), inc=bld.s_cc)
    W8(bld, "sync", bld.s_cc, cc)
    guard_wait(bld, "sync", "vbuf")
    W8(bld, "sync", bld.s_pe, bld.s_pe.v)
    E(bld, "sync", lambda e: e.dma_start(
        bld.vbuf[:, 0:4096].rearrange("p (c t) -> p c t", t=512),
        d["cc_h_out"][0:1024, :].rearrange("(c p) t -> p c t", p=128)), inc=bld.d_sy, n=16)
    ld = fence(bld, "sync")
    W8(bld, "vector", bld.s_sy, ld)
    W8(bld, "vector", bld.s_pe, bld.s_pe.v)
    for kc in range(8):
        E(bld, "vector", lambda e, kc=kc, hv=hv: e.tensor_scalar_mul(
            hv[:, kc, 0:HALO], bld.vbuf[:, 512 * kc:512 * kc + HALO],
            bld.dn_cols[:, CL_SEL:CL_SEL + 1]))
    hf = E(bld, "vector", lambda e, hv=hv: e.tensor_copy(hv[:, 0, 0:1], hv[:, 0, 0:1]))
    guard_set(bld, "vbuf", (bld.s_ve, hf))
    return hf


def emit_attn_chunk(bld: Build, l, ci, pv_carry):
    d = bld.dram
    t0 = 512 * ci
    width = min(512, TLOC - t0)
    hv = bld.h[:].rearrange("p (c t) -> p c t", t=TLOC)


    guard_wait(bld, "vector", "ring")
    for kc in range(8):
        E(bld, "vector", lambda e, kc=kc: e.tensor_copy(
            bld.kbuf[:, 768 * kc:768 * kc + 256], bld.kbuf[:, 768 * kc + 512:768 * kc + 768]))
    for i in range(2):
        E(bld, "vector", lambda e, i=i: e.tensor_copy(
            bld.vbuf[:, 1024 * i:1024 * (i + 1)], bld.vbuf[:, 1024 * (i + 4):1024 * (i + 5)]))
    ring_v = bld.s_ve.v

    if ci >= 1:
        E(bld, "vector", lambda e, ci=ci, t0=t0: e.tensor_sub(
            bld.hdelta[:], bld.hsave[:].rearrange("p (c i) -> p c i", i=9)[:, :, ci - 1],
            bld.h[:].rearrange("p (c t) -> p c t", t=TLOC)[:, :, t0 - 1]), inc=bld.s_ve)
    delta_v = bld.s_ve.v

    nrm_bank = bld.pb[6]
    qkv_bufs = (bld.wqk_buf, bld.wv_buf)
    sq_v = {}
    tc_q = (ci == 0)

    def iss_qkv(idx, col0):
        p = idx % 2
        guard_wait(bld, "gpsimd", f"qkv{p}")
        pend = bld.wb_pend.pop(("qkv", p), None)
        if pend:
            W8(bld, "gpsimd", bld.d_wb, pend)
        src = d["Wqkv"] if tc_q else d["sc_wq"]
        tot = E(bld, "gpsimd", lambda e, l=l, col0=col0, p=p, src=src: e.dma_start(
            qkv_bufs[p][:], src[l].rearrange("(c p) n -> p c n", p=128)
            [:, :, col0:col0 + 512]), inc=bld.d_qk[p], n=16)
        if tc_q:
            W8(bld, "scalar", bld.d_qk[p], tot)
            wt = E(bld, "scalar", lambda e, l=l, col0=col0, p=p: e.dma_start(
                d["sc_wq"][l].rearrange("(c p) n -> p c n", p=128)[:, :, col0:col0 + 512],
                qkv_bufs[p][:].rearrange("p (c n) -> p c n", n=512)), inc=bld.d_wb, n=16)
            bld.wb_pend[("qkv", p)] = wt
        return tot

    for side, base_col in (("q", 0), ("k", 1024)):
        for mh in range(2):
          idx = (0 if side == "q" else 2) + mh
          qtot = iss_qkv(idx, base_col + 512 * mh)
          wbuf = qkv_bufs[idx % 2]
          W8(bld, "tensor", bld.d_qk[idx % 2], qtot)
          for m4 in range(4):
            m = 4 * mh + m4
            bank = bld.pb[m % 2]
            guard_wait(bld, "tensor", f"pb{m % 2}")
            if m == 0 and mh == 0 and side == "q":
                W8(bld, "tensor", bld.s_ve, ring_v)
            corr = ci >= 1
            if corr and m == 0:
                W8(bld, "tensor", bld.s_ve, delta_v)
            for kc in range(8):
                off = (1 if kc < 4 else 0)
                last = (kc == 7) and not corr
                inc = bld.s_pe if last else None
                E(bld, "tensor", lambda e, kc=kc, m4=m4, off=off, t0=t0, w=width,
                    bank=bank, last=last, wbuf=wbuf: e.matmul(
                    bank[:, 0:w], wbuf[:, 512 * kc + 128 * m4:512 * kc + 128 * (m4 + 1)],
                    bld.h[:, TLOC * kc + t0 - 1 + off: TLOC * kc + t0 - 1 + off + w],
                    start=(kc == 0), stop=last), inc=inc)
            if corr:
                for kc in range(4, 8):
                    last = kc == 7
                    inc = bld.s_pe if last else None
                    E(bld, "tensor", lambda e, kc=kc, m4=m4, bank=bank, last=last, wbuf=wbuf: e.matmul(
                        bank[:, 0:1], wbuf[:, 512 * kc + 128 * m4:512 * kc + 128 * (m4 + 1)],
                        bld.hdelta[:, kc:kc + 1], start=False, stop=last), inc=inc)
            mm_pe = bld.s_pe.v
            W8(bld, "vector", bld.s_pe, mm_pe)
            if side == "q":
                dst = bld.qbuf[:, 512 * m:512 * m + width]
            else:
                dst = bld.kbuf[:, 768 * m + 256:768 * m + 256 + width]
            ev = E(bld, "vector", lambda e, dst=dst, bank=bank, w=width: e.tensor_copy(
                dst, bank[:, 0:w]))
            W8(bld, "scalar", bld.s_ve, ev)
            slot = bld.q2t if m % 2 == 0 else bld.q2t2
            guard_wait(bld, "scalar", f"q2t{m % 2}")
            sq_v[m] = E(bld, "scalar", lambda e, dst=dst, w=width, slot=slot: e.activation(
                slot[:, 0:w], dst, AF.Square), inc=bld.s_ac)

            def nmm(mm, last):
                slot2 = bld.q2t if mm % 2 == 0 else bld.q2t2
                W8(bld, "tensor", bld.s_ac, sq_v[mm])
                if mm == 0:
                    guard_wait(bld, "tensor", "pb6")
                E(bld, "tensor", lambda e, mm=mm, w=width, slot2=slot2: e.matmul(
                    nrm_bank[0:16, 0:w], bld.co_sb[:, 16 * mm:16 * (mm + 1)], slot2[:, 0:w],
                    start=(mm == 0), stop=last), inc=(bld.s_pe if last else None))
                guard_set(bld, f"q2t{mm % 2}", (bld.s_pe, bld.s_pe.v))
            if m >= 1:
                nmm(m - 1, False)
            if m == 7:
                nmm(7, True)
            guard_set(bld, f"pb{m % 2}", (bld.s_ve, bld.s_ve.v))
          guard_set(bld, f"qkv{idx % 2}", (bld.s_pe, bld.s_pe.v))
        nq_pe = bld.s_pe.v
        W8(bld, "scalar", bld.s_pe, nq_pe)
        guard_wait(bld, "scalar", "nrm")
        sq2 = E(bld, "scalar", lambda e, w=width: e.activation(
            bld.nrm[:, 0:w], nrm_bank[0:16, 0:w], AF.Sqrt), inc=bld.s_ac)
        W8(bld, "vector", bld.s_ac, sq2)
        E(bld, "vector", lambda e, w=width: e.tensor_scalar_max(bld.nrm[:, 0:w], bld.nrm[:, 0:w], 1e-12))
        def _recip(e, w=width, bld=bld):
            with bld.nc.allow_low_precision(reason="l2norm recip in bf16 (bf16-matmul noise class)"):
                return e.reciprocal(bld.nrm[:, 0:w], bld.nrm[:, 0:w])
        rc = E(bld, "vector", _recip)
        guard_set(bld, "pb6", (bld.s_ac, sq2))
        scol = bld.lcols[:, LQ_QS:LQ_QS + 1] if side == "q" else bld.lcols[:, LQ_KS:LQ_KS + 1]
        rA, rB = bld.rotA, bld.rotB
        guard_wait(bld, "gpsimd", "rot")
        for dst, nm in ((rA, "rotAQ" if side == "q" else "rotAK"),
                        (rB, "rotBQ" if side == "q" else "rotBK")):
            rt_tot = E(bld, "gpsimd", lambda e, dst=dst, nm=nm, t0=t0, w=width: e.dma_start(
                dst[:, 0:w], d[nm][:, t0:t0 + w]), inc=bld.d_rt, n=16)
        for m in range(8):
            if side == "q":
                sl = bld.qbuf[:, 512 * m:512 * m + width]
            else:
                sl = bld.kbuf[:, 768 * m + 256:768 * m + 256 + width]
            guard_wait(bld, "tensor", "pb2")
            W8(bld, "tensor", bld.s_ve, rc)
            bc = E(bld, "tensor", lambda e, m=m, w=width: e.matmul(
                bld.pb[2][:, 0:w], bld.coT_sb[:, 128 * m:128 * (m + 1)], bld.nrm[:, 0:w],
                start=True, stop=True), inc=bld.s_pe)
            W8(bld, "vector", bld.s_pe, bc)
            E(bld, "vector", lambda e, sl=sl, scol=scol, w=width: e.scalar_tensor_tensor(
                sl, sl, scol, bld.pb[2][:, 0:w], ALU.mult, ALU.mult))
            guard_wait(bld, "tensor", "pb3")
            W8(bld, "tensor", bld.s_ve, bld.s_ve.v)
            rt = E(bld, "tensor", lambda e, sl=sl, w=width: e.matmul(
                bld.pb[3][:, 0:w], bld.rot_bf[:], sl, start=True, stop=True), inc=bld.s_pe)
            W8(bld, "vector", bld.s_pe, rt)
            W8(bld, "vector", bld.d_rt, rt_tot)
            guard_wait(bld, "vector", "q2t0")
            E(bld, "vector", lambda e, sl=sl, rA=rA, w=width: e.tensor_mul(
                bld.q2t[:, 0:w], sl, rA[:, 0:w]))
            E(bld, "vector", lambda e, sl=sl, rB=rB, w=width: e.tensor_tensor(
                sl, bld.pb[3][:, 0:w], rB[:, 0:w], ALU.mult))
            qr = E(bld, "vector", lambda e, sl=sl, w=width: e.tensor_add(sl, sl, bld.q2t[:, 0:w]))
            guard_set(bld, "pb2", (bld.s_ve, bld.s_ve.v))
            guard_set(bld, "pb3", (bld.s_ve, qr))
            guard_set(bld, "q2t0", (bld.s_ve, qr))
        guard_set(bld, "rot", (bld.s_ve, bld.s_ve.v))
    qk_done = bld.s_ve.v

    for half in range(2):
        idx = 4 + half
        vtot = iss_qkv(idx, 2048 + 512 * half)
        wbuf = qkv_bufs[idx % 2]
        W8(bld, "tensor", bld.d_qk[idx % 2], vtot)
        for tt in range(ceil_div(width, 128)):
            vt = 2 + tt
            bank = bld.pb[4 + tt % 2]
            guard_wait(bld, "tensor", f"pb{4 + tt % 2}")
            vcorr = (ci >= 1) and (tt == 0)
            for kc in range(8):
                off = (1 if kc < 4 else 0)
                last = (kc == 7) and not vcorr
                inc = bld.s_pe if last else None
                E(bld, "tensor", lambda e, kc=kc, tt=tt, off=off, t0=t0, bank=bank,
                    last=last, wbuf=wbuf: e.matmul(
                    bank[:, 0:512],
                    bld.h[:, TLOC * kc + t0 - 1 + off + 128 * tt: TLOC * kc + t0 - 1 + off + 128 * (tt + 1)],
                    wbuf[:, 512 * kc:512 * (kc + 1)],
                    start=(kc == 0), stop=last), inc=inc)
            if vcorr:
                for kc in range(4, 8):
                    last = kc == 7
                    inc = bld.s_pe if last else None
                    E(bld, "tensor", lambda e, kc=kc, bank=bank, last=last, wbuf=wbuf: e.matmul(
                        bank[0:1, 0:512], bld.hdelta[:, kc:kc + 1],
                        wbuf[:, 512 * kc:512 * (kc + 1)],
                        start=False, stop=last), inc=inc)
            v_pe = bld.s_pe.v
            W8(bld, "vector", bld.s_pe, v_pe)
            ev = E(bld, "vector", lambda e, vt=vt, hf=half, bank=bank: e.tensor_copy(
                bld.vbuf[:, 1024 * vt + 512 * hf:1024 * vt + 512 * (hf + 1)], bank[:, 0:512]))
            guard_set(bld, f"pb{4 + tt % 2}", (bld.s_ve, ev))
        guard_set(bld, f"qkv{idx % 2}", (bld.s_pe, bld.s_pe.v))
    v_done = bld.s_ve.v
    return qk_done, v_done


def emit_attn_qtile(bld: Build, l, ci, qt, qk_done, v_done):
    Q = 4 * ci + qt
    qoff = 128 * qt
    slot = {4: 1, 5: 2}.get(Q, 0)
    c = bld.cols
    pn_v, tr_v, ev_v = {}, {}, {}
    state = {"first_pv": True}

    def s_phase(h):
        par = h % 2
        pj = h // 2
        bank = bld.pb[2 + par]
        guard_wait(bld, "tensor", f"pb{2 + par}")
        if h < 2:
            W8(bld, "tensor", bld.s_ve, qk_done)
        E(bld, "tensor", lambda e, par=par, slot=slot: e.matmul(
            bld.pb[2 + par][0:64, 0:384], bld.i64h[64 * par:64 * (par + 1), :],
            bld.masks_sb[64 * par:64 * (par + 1), 768 * slot:768 * slot + 384],
            start=True, stop=False, tile_position=(64 * par, 0)))
        E(bld, "tensor", lambda e, par=par, slot=slot: e.matmul(
            bld.pb[2 + par][64:128, 0:384], bld.i64h[64 * par:64 * (par + 1), :],
            bld.masks_sb[64 * par:64 * (par + 1), 768 * slot + 384:768 * slot + 768],
            start=True, stop=False, tile_position=(64 * par, 64)))
        smm = E(bld, "tensor", lambda e, par=par, pj=pj, qoff=qoff: e.matmul(
            bld.pb[2 + par][:, 0:384],
            bld.qbuf[64 * par:64 * (par + 1), 512 * pj + qoff:512 * pj + qoff + 128],
            bld.kbuf[64 * par:64 * (par + 1), 768 * pj + qoff:768 * pj + qoff + 384],
            start=False, stop=True, tile_position=(64 * par, 0)), inc=bld.s_pe)
        W8(bld, "vector", bld.s_pe, smm)
        E(bld, "vector", lambda e, bank=bank: e.reduce_max(c[:, 12:13], bank[:, 0:384],
                                                           axis=AX.X, negate=True))
        nm = E(bld, "vector", lambda e: e.tensor_scalar_mul(c[:, 13:14], c[:, 12:13], 8.0))
        W8(bld, "scalar", bld.s_ve, nm)
        guard_wait(bld, "scalar", f"P{h % 4}")
        ex = E(bld, "scalar", lambda e, h=h, bank=bank: e.activation(
            bld.Pbuf[:, 384 * (h % 4):384 * (h % 4) + 384], bank[:, 0:384], AF.Exp,
            bias=c[:, 13:14], scale=8.0, accum_out=c[:, 14:15]), inc=bld.s_ac)
        W8(bld, "vector", bld.s_ac, ex)
        E(bld, "vector", lambda e: e.reciprocal(c[:, 15:16], c[:, 14:15]))
        pn_v[h] = E(bld, "vector", lambda e, h=h: e.tensor_scalar_mul(
            bld.Pbuf[:, 384 * (h % 4):384 * (h % 4) + 384],
            bld.Pbuf[:, 384 * (h % 4):384 * (h % 4) + 384], c[:, 15:16]))
        guard_set(bld, f"pb{2 + par}", (bld.s_ac, ex))

    def tr_phase(h):
        if h >= 14:
            bank_bf = bld.pbt[:]
            base = (h - 14) * 3
            gkey = "pbtr"
        else:
            bank_bf = bld.pb[h % 2][:].bitcast(BF16)
            base = 0
            gkey = f"pb{h % 2}"
        guard_wait(bld, "tensor", gkey)
        W8(bld, "tensor", bld.s_ve, pn_v[h])
        for j in range(3):
            inc = bld.s_pe if j == 2 else None
            E(bld, "tensor", lambda e, h=h, j=j, bank_bf=bank_bf, base=base: e.transpose(
                bank_bf[:, 128 * (base + j):128 * (base + j + 1)],
                bld.Pbuf[:, 384 * (h % 4) + 128 * j:384 * (h % 4) + 128 * (j + 1)],
                bld.ident_bf[:]), inc=inc)
        tr_v[h] = bld.s_pe.v
        guard_set(bld, f"P{h % 4}", (bld.s_pe, tr_v[h]))
        W8(bld, "scalar", bld.s_pe, tr_v[h])
        for j in range(3):
            ev_v[h] = E(bld, "scalar", lambda e, h=h, j=j, bank_bf=bank_bf, base=base:
                e.activation(bld.PTb[(h % 4) * 3 + j][:],
                             bank_bf[:, 128 * (base + j):128 * (base + j + 1)], AF.Copy),
                inc=bld.s_ac)
        guard_set(bld, gkey, (bld.s_ac, ev_v[h]))

    def pv_phase(h):
        par = h % 2
        pj = h // 2
        bnk = 4 + (pj % 2)
        cslot = pj // 2
        if state["first_pv"]:
            guard_wait(bld, "tensor", "pb4")
            guard_wait(bld, "tensor", "pb5")
            W8(bld, "tensor", bld.s_ve, v_done)
            state["first_pv"] = False
        W8(bld, "tensor", bld.s_ac, ev_v[h])
        for kc in range(3):
            inc = bld.s_pe if kc == 2 else None
            E(bld, "tensor", lambda e, h=h, kc=kc, par=par, bnk=bnk, cslot=cslot: e.matmul(
                bld.pb[bnk][64 * par:64 * (par + 1), 128 * cslot:128 * (cslot + 1)],
                bld.vbuf[:, 1024 * (qt + kc) + 64 * h:1024 * (qt + kc) + 64 * (h + 1)],
                bld.PTb[(h % 4) * 3 + kc][:], start=(kc == 0), stop=(kc == 2),
                tile_position=(0, 64 * par)), inc=inc)

    for i in range(NH + 2):
        if i >= 2:
            tr_phase(i - 2)
        if i < NH:
            s_phase(i)
        if i >= 2:
            pv_phase(i - 2)
    pv_pe = bld.s_pe.v
    W8(bld, "vector", bld.s_pe, pv_pe)
    guard_wait(bld, "vector", "ofm")
    for b2 in range(2):
        dv = bld.ofm[:].rearrange("p (c b t) -> p c b t", b=2, t=128)[:, :, b2, :]
        E(bld, "vector", lambda e, dv=dv, b2=b2: e.tensor_copy(
            dv, bld.pb[4 + b2][:].rearrange("p (c t) -> p c t", t=128)))
    ev = bld.s_ve.v
    guard_set(bld, "pb4", (bld.s_ve, ev))
    guard_set(bld, "pb5", (bld.s_ve, ev))
    return ev


def emit_wo_qtile(bld: Build, l, ci, qt, ofm_v):
    Q = 4 * ci + qt
    t0 = 128 * Q
    for dm in range(8):
        bank = bld.pb[2 + dm % 2]
        guard_wait(bld, "tensor", f"pb{2 + dm % 2}")
        if dm == 0:
            W8(bld, "tensor", bld.s_ve, ofm_v)
            W8(bld, "tensor", bld.d_gp, bld.wo_tot)
        for vc in range(8):
            inc = bld.s_pe if vc == 7 else None
            E(bld, "tensor", lambda e, dm=dm, vc=vc, bank=bank: e.matmul(
                bank[:, 0:128], bld.wo_sb[:, 1024 * vc + 128 * dm:1024 * vc + 128 * (dm + 1)],
                bld.ofm[:, 128 * vc:128 * (vc + 1)],
                start=(vc == 0), stop=(vc == 7)), inc=inc)
        wo_pe = bld.s_pe.v
        W8(bld, "vector", bld.s_pe, wo_pe)
        rs = E(bld, "vector", lambda e, dm=dm, t0=t0, bank=bank: e.tensor_tensor(
            bld.h[:, TLOC * dm + t0:TLOC * dm + t0 + 128],
            bld.h[:, TLOC * dm + t0:TLOC * dm + t0 + 128], bank[:, 0:128], ALU.add))
        guard_set(bld, f"pb{2 + dm % 2}", (bld.s_ve, rs))
    guard_set(bld, "ofm", (bld.s_pe, bld.s_pe.v))
    return bld.s_ve.v



def emit_ffn_ttile(bld: Build, l, ti):
    d = bld.dram
    t0 = HALO + 512 * ti
    c2 = bld.x2
    NJ = 22

    w1tot, w2tot = {}, {}
    tc = (ti == 7)

    def iss_w1(jp):
        p = jp % 2
        jw = 128 if jp < 21 else DIM_INNER - 128 * 21
        buf = bld.w1b[p]
        guard_wait(bld, "gpsimd", f"w1b{p}")
        pend = bld.wb_pend.pop(("w1", p), None)
        if pend:
            W8(bld, "gpsimd", bld.d_wb, pend)
        srcw = d["W1"] if tc else d["sc_w1"]
        E(bld, "gpsimd", lambda e, l=l, jp=jp, jw=jw, buf=buf, srcw=srcw: e.dma_start(
            buf[:].rearrange("p (c s j) -> p c s j", s=2, j=128)[:, :, 0, 0:jw],
            srcw[l].rearrange("(c p) n -> p c n", p=128)[:, :, 128 * jp:128 * jp + jw]),
            inc=bld.d_w1p[p], n=16)
        w1tot[jp] = E(bld, "gpsimd", lambda e, l=l, jp=jp, jw=jw, buf=buf, srcw=srcw: e.dma_start(
            buf[:].rearrange("p (c s j) -> p c s j", s=2, j=128)[:, :, 1, 0:jw],
            srcw[l].rearrange("(c p) n -> p c n", p=128)
            [:, :, DIM_INNER + 128 * jp:DIM_INNER + 128 * jp + jw]),
            inc=bld.d_w1p[p], n=16)
        if tc:
            W8(bld, "scalar", bld.d_w1p[p], w1tot[jp])
            for s2 in range(2):
                wt = E(bld, "scalar", lambda e, l=l, jp=jp, jw=jw, buf=buf, s2=s2: e.dma_start(
                    d["sc_w1"][l].rearrange("(c p) n -> p c n", p=128)
                    [:, :, DIM_INNER * s2 + 128 * jp:DIM_INNER * s2 + 128 * jp + jw],
                    buf[:].rearrange("p (c s j) -> p c s j", s=2, j=128)[:, :, s2, 0:jw]),
                    inc=bld.d_wb, n=16)
            bld.wb_pend[("w1", p)] = wt

    def iss_w2(jp):
        p = jp % 2
        jw = 128 if jp < 21 else DIM_INNER - 128 * 21
        buf = bld.w2b[p]
        qn = "gpsimd" if tc else "scalar"
        guard_wait(bld, qn, f"w2b{p}")
        pend = bld.wb_pend.pop(("w2", p), None)
        if pend:
            W8(bld, qn, bld.d_wb, pend)
        if tc:
            w2tot[jp] = E(bld, "gpsimd", lambda e, l=l, jp=jp, jw=jw, buf=buf: e.dma_start(
                buf[0:jw, :], d["W2"][l][128 * jp:128 * jp + jw, :]), inc=bld.d_w2p[p], n=16)
            W8(bld, "scalar", bld.d_w2p[p], w2tot[jp])
            wt = E(bld, "scalar", lambda e, l=l, jp=jp, jw=jw, buf=buf: e.dma_start(
                d["sc_w2"][l][128 * jp:128 * jp + jw, :], buf[0:jw, :]), inc=bld.d_wb, n=16)
            bld.wb_pend[("w2", p)] = wt
        else:
            if jp < 2:
                W8(bld, "scalar", bld.d_wb, bld.d_wb.v)
            w2tot[jp] = E(bld, "scalar", lambda e, l=l, jp=jp, jw=jw, buf=buf: e.dma_start(
                buf[0:jw, :], d["sc_w2"][l][128 * jp:128 * jp + jw, :]), inc=bld.d_w2p[p], n=16)

    iss_w1(0); iss_w1(1); iss_w2(0); iss_w2(1)

    guard_wait(bld, "scalar", "xn_t")
    for kc in range(8):
        off = (1 if kc < 4 else 0)
        inc = bld.s_ac if kc == 7 else None
        E(bld, "scalar", lambda e, kc=kc, off=off, t0=t0: e.activation(
            bld.xn_t[:, 512 * kc:512 * (kc + 1)],
            bld.h[:, TLOC * kc + t0 - 1 + off:TLOC * kc + t0 - 1 + off + 512], AF.Square),
            inc=inc)
    sq_a = bld.s_ac.v
    guard_wait(bld, "tensor", "pb6")
    guard_wait(bld, "tensor", "pb2")
    W8(bld, "tensor", bld.s_ac, sq_a)
    for kc in range(8):
        off = (1 if kc < 4 else 0)
        E(bld, "tensor", lambda e, kc=kc, off=off, t0=t0: e.matmul(
            bld.pb[6][0:1, 0:512], bld.ones16[:],
            bld.h[:, TLOC * kc + t0 - 1 + off:TLOC * kc + t0 - 1 + off + 512],
            start=(kc == 0), stop=(kc == 7)))
    for kc in range(8):
        inc = bld.s_pe if kc == 7 else None
        E(bld, "tensor", lambda e, kc=kc: e.matmul(
            bld.pb[2][0:1, 0:512], bld.ones16[:], bld.xn_t[:, 512 * kc:512 * (kc + 1)],
            start=(kc == 0), stop=(kc == 7)), inc=inc)
    st_pe = bld.s_pe.v
    W8(bld, "vector", bld.s_pe, st_pe)
    guard_wait(bld, "vector", "x2rows")
    A = c2[0:1, 0:512]
    Brow = c2[0:1, 512:1024]
    E(bld, "vector", lambda e: e.tensor_scalar_mul(A, bld.pb[6][0:1, 0:512], 1.0 / D))
    E(bld, "vector", lambda e: e.tensor_scalar_mul(Brow, bld.pb[2][0:1, 0:512], 1.0 / D))
    E(bld, "vector", lambda e: e.tensor_mul(bld.pb[6][0:1, 0:512], A, A))
    E(bld, "vector", lambda e: e.tensor_sub(Brow, Brow, bld.pb[6][0:1, 0:512]))
    sv = E(bld, "vector", lambda e: e.tensor_scalar_add(Brow, Brow, LN_EPS))
    W8(bld, "scalar", bld.s_ve, sv)
    sa = E(bld, "scalar", lambda e: e.activation(bld.pb[6][0:1, 0:512], Brow, AF.Sqrt),
           inc=bld.s_ac)
    W8(bld, "vector", bld.s_ac, sa)
    E(bld, "vector", lambda e: e.reciprocal(Brow, bld.pb[6][0:1, 0:512]))
    E(bld, "vector", lambda e: e.tensor_mul(A, A, Brow))
    nr = E(bld, "vector", lambda e: e.tensor_scalar_mul(A, A, -1.0))
    guard_wait(bld, "tensor", "pb3")
    W8(bld, "tensor", bld.s_ve, nr)
    E(bld, "tensor", lambda e: e.matmul(bld.pb[2][:, 0:512], bld.ones_row_f32[:],
                                        Brow, start=True, stop=True))
    bc = E(bld, "tensor", lambda e: e.matmul(bld.pb[3][:, 0:512], bld.ones_row_f32[:],
                                             A, start=True, stop=True), inc=bld.s_pe)
    guard_set(bld, "x2rows", (bld.s_pe, bc))
    W8(bld, "vector", bld.s_pe, bc)
    lnt = bld.Pbuf[:, 0:512]
    for kc in range(8):
        off = (1 if kc < 4 else 0)
        E(bld, "vector", lambda e, kc=kc, off=off, t0=t0: e.tensor_tensor(
            lnt, bld.h[:, TLOC * kc + t0 - 1 + off:TLOC * kc + t0 - 1 + off + 512],
            bld.pb[2][:, 0:512], ALU.mult))
        E(bld, "vector", lambda e: e.tensor_tensor(lnt, lnt, bld.pb[3][:, 0:512], ALU.add))
        E(bld, "vector", lambda e, kc=kc: e.tensor_scalar(
            bld.xn_t[:, 512 * kc:512 * (kc + 1)], lnt,
            bld.lcols[:, LQ_G0 + kc:LQ_G0 + kc + 1], bld.lcols[:, LQ_B0 + kc:LQ_B0 + kc + 1],
            ALU.mult, ALU.add))
    xn_v = bld.s_ve.v
    guard_set(bld, "pb2", (bld.s_ve, xn_v))
    guard_set(bld, "pb3", (bld.s_ve, xn_v))
    guard_set(bld, "xn_t", (bld.s_ve, xn_v))

    if True:
      for jp in range(NJ):
        jw = 128 if jp < 21 else DIM_INNER - 128 * 21
        buf = bld.w1b[jp % 2]
        for s2, bank in ((0, 0), (1, 1)):
            guard_wait(bld, "tensor", f"pb{bank}")
            if s2 == 0:
                W8(bld, "tensor", bld.d_w1p[jp % 2], w1tot[jp])
                if jp == 0:
                    W8(bld, "tensor", bld.s_ve, xn_v)
            for kc in range(8):
                inc = bld.s_pe if kc == 7 else None
                E(bld, "tensor", lambda e, kc=kc, s2=s2, jw=jw, bank=bank, buf=buf: e.matmul(
                    bld.pb[bank][0:jw, 0:512], buf[:, 256 * kc + 128 * s2:256 * kc + 128 * s2 + jw],
                    bld.xn_t[:, 512 * kc:512 * (kc + 1)], start=(kc == 0), stop=(kc == 7)), inc=inc)
        hg_pe = bld.s_pe.v
        guard_set(bld, f"w1b{jp % 2}", (bld.s_pe, hg_pe))
        if jp + 2 < NJ:
            iss_w1(jp + 2)
        W8(bld, "scalar", bld.s_pe, hg_pe)
        guard_wait(bld, "scalar", "Pb2")
        g1 = E(bld, "scalar", lambda e, jp=jp, jw=jw: e.activation(
            bld.Pbuf[:, 512:512 + 512][0:jw, :], bld.pb[1][0:jw, 0:512], AF.Gelu,
            bias=bld.b1c[0:jw, 22 + jp:23 + jp]), inc=bld.s_ac)
        W8(bld, "vector", bld.s_ac, g1)
        guard_wait(bld, "vector", f"m{jp % 2}")
        mv = E(bld, "vector", lambda e, jp=jp, jw=jw: e.scalar_tensor_tensor(
            bld.m_sb[0:jw, 512 * jp:512 * (jp + 1)], bld.pb[0][0:jw, 0:512],
            bld.b1c[0:jw, jp:jp + 1], bld.Pbuf[:, 512:1024][0:jw, :], ALU.add, ALU.mult))
        guard_set(bld, "pb0", (bld.s_ve, mv))
        guard_set(bld, "pb1", (bld.s_ac, g1))
        guard_set(bld, "Pb2", (bld.s_ve, mv))
    m_v = bld.s_ve.v

    pbanks = [bld.pb[i][:] for i in range(7)] + [bld.pbt[:].bitcast(F32)]
    for b in range(7):
        guard_wait(bld, "tensor", f"pb{b}")
    guard_wait(bld, "tensor", "pbt")
    W8(bld, "tensor", bld.s_ve, m_v)
    for jp in range(NJ):
        jw = 128 if jp < 21 else DIM_INNER - 128 * 21
        buf = bld.w2b[jp % 2]
        W8(bld, "tensor", bld.d_w2p[jp % 2], w2tot[jp])
        for dm in range(8):
            inc = bld.s_pe if dm == 7 else None
            E(bld, "tensor", lambda e, jp=jp, jw=jw, dm=dm, buf=buf, pbanks=pbanks: e.matmul(
                pbanks[dm][:, 0:512], buf[0:jw, 128 * dm:128 * (dm + 1)],
                bld.m_sb[0:jw, 512 * jp:512 * (jp + 1)],
                start=(jp == 0), stop=(jp == NJ - 1)), inc=inc)
        guard_set(bld, f"w2b{jp % 2}", (bld.s_pe, bld.s_pe.v))
        if jp + 2 < NJ:
            iss_w2(jp + 2)
    w2_pe = bld.s_pe.v
    guard_set(bld, "w2b0", (bld.s_pe, w2_pe))
    guard_set(bld, "w2b1", (bld.s_pe, w2_pe))
    guard_set(bld, "m0", (bld.s_pe, w2_pe))
    guard_set(bld, "m1", (bld.s_pe, w2_pe))
    W8(bld, "vector", bld.s_pe, w2_pe)
    for dm in range(8):
        E(bld, "vector", lambda e, dm=dm, t0=t0, pbanks=pbanks: e.scalar_tensor_tensor(
            bld.h[:, TLOC * dm + t0:TLOC * dm + t0 + 512], pbanks[dm][:, 0:512],
            bld.lcols[:, LQ_B2 + dm:LQ_B2 + dm + 1],
            bld.h[:, TLOC * dm + t0:TLOC * dm + t0 + 512], ALU.add, ALU.add))
    res_v = bld.s_ve.v
    for b in range(7):
        guard_set(bld, f"pb{b}", (bld.s_ve, res_v))
    guard_set(bld, "pbt", (bld.s_ve, res_v))
    guard_set(bld, "pb4", (bld.s_ve, res_v))
    guard_set(bld, "pb5", (bld.s_ve, res_v))
    return res_v


def emit_layer(bld: Build, l, skip_ffn=False):
    sy = emit_layer_loads(bld, l)
    if l > 0:
        emit_halo_ag(bld, l)
    for eng in ("tensor", "vector", "scalar"):
        W8(bld, eng, bld.s_sy, sy)
    hsv = bld.h[:].rearrange("p (c t) -> p c t", t=TLOC)
    E(bld, "vector", lambda e, hsv=hsv: e.tensor_copy(
        bld.hsave[:].rearrange("p (c i) -> p c i", i=9), hsv[:, :, 511::512]), inc=bld.s_ve)
    for ci in range(9):
        qk_done, v_done = emit_attn_chunk(bld, l, ci, None)
        for qt in range(4):
            Q = 4 * ci + qt
            if Q < 3:
                continue
            ev = emit_attn_qtile(bld, l, ci, qt, qk_done, v_done)
            emit_wo_qtile(bld, l, ci, qt, ev)
        guard_set(bld, "ring", (bld.s_pe, bld.s_pe.v))
    guard_set(bld, "wqkv", (bld.s_pe, bld.s_pe.v))
    guard_set(bld, "wo", (bld.s_pe, bld.s_pe.v))
    if not skip_ffn:
        for ti in reversed(range(8)):
            emit_ffn_ttile(bld, l, ti)
    guard_set(bld, "lcols", (bld.s_ve, bld.s_ve.v))


def emit_final(bld: Build):
    d = bld.dram
    s = lambda n, sh, dt: bld.sbuf(n, sh, dt, scope=bld.es_fin)
    xld = [s(f"xld{i}", [128, D], F32) for i in range(2)]
    sld = [s(f"sld{i}", [128, D], BF16) for i in range(2)]
    sgb = s("sgb", [128, D], BF16)
    orow = s("orow", [128, D], F32)
    for ti in range(32):
        slot = ti % 2
        t0 = HALO + 128 * ti
        guard_wait(bld, "sync", f"fxld{slot}")
        E(bld, "sync", lambda e, ti=ti, t0=t0, slot=slot: e.dma_start(
            xld[slot][:], d["x_in"][t0:t0 + 128, :]), inc=bld.d_sy, n=16)
        E(bld, "sync", lambda e, ti=ti, slot=slot: e.dma_start(
            sld[slot][:], d["sc_s"][128 * ti:128 * (ti + 1), :]), inc=bld.d_sy, n=16)
        lf = fence(bld, "sync")
        guard_wait(bld, "scalar", "sgb")
        for kc in range(8):
            inc = bld.s_ac if kc == 7 else None
            E(bld, "scalar", lambda e, kc=kc, t0=t0: e.activation(
                sgb[:, 128 * kc:128 * (kc + 1)], bld.h[:, TLOC * kc + t0:TLOC * kc + t0 + 128],
                AF.Silu), inc=inc)
        sg_a = bld.s_ac.v
        guard_wait(bld, "tensor", "pbt")
        W8(bld, "tensor", bld.s_ac, sg_a)
        for kc in range(8):
            inc = bld.s_pe if kc == 7 else None
            E(bld, "tensor", lambda e, kc=kc: e.transpose(
                bld.pbt[:, 128 * kc:128 * (kc + 1)], sgb[:, 128 * kc:128 * (kc + 1)],
                bld.ident_bf[:]), inc=inc)
        tr_pe = bld.s_pe.v
        guard_set(bld, "sgb", (bld.s_pe, tr_pe))
        W8(bld, "vector", bld.s_pe, tr_pe)
        W8(bld, "vector", bld.s_sy, lf)
        guard_wait(bld, "vector", "orow")
        E(bld, "vector", lambda e, slot=slot: e.tensor_mul(orow[:], sld[slot][:], bld.pbt[:]))
        ov = E(bld, "vector", lambda e, slot=slot: e.tensor_add(orow[:], orow[:], xld[slot][:]))
        guard_set(bld, "pbt", (bld.s_ve, ov))
        guard_set(bld, f"fxld{slot}", (bld.s_ve, ov))
        W8(bld, "sync", bld.s_ve, ov)
        E(bld, "sync", lambda e, ti=ti: e.dma_start(
            d["out"][128 * ti:128 * (ti + 1), :], orow[:]), inc=bld.d_sy, n=16)
        of = fence(bld, "sync")
        guard_set(bld, "orow", (bld.s_sy, of))


def build_full(n_layers=NLAYER, dbg=False, skip_ffn=False):
    bld = Build()
    build_common(bld)
    if dbg:
        bld.dout("dbg_h", [128, 8 * TLOC], F16)
        bld.dout("dbg_s", [4096, D], BF16)
        bld.dout("dbg_p", [128, 16 * 384], F16)
        bld.dout("dbg_ofm", [128, 8 * 128], F16)
    emit_setup(bld)
    emit_dn_phase(bld)
    phase_barrier(bld)
    bld.es_dn.close()
    tf_alloc(bld)
    emit_tf_setup(bld)
    for l in range(n_layers):
        emit_layer(bld, l, skip_ffn=skip_ffn)
    phase_barrier(bld)
    if dbg:
        for j in range(8):
            E(bld, "sync", lambda e, j=j: e.dma_start(
                bld.dram["dbg_h"][:, TLOC * j:TLOC * (j + 1)],
                bld.h[:, TLOC * j:TLOC * (j + 1)]), inc=bld.d_sy, n=16)
        for i in range(4):
            E(bld, "sync", lambda e, i=i: e.dma_start(
                bld.dram["dbg_s"][1024 * i:1024 * (i + 1), :],
                bld.dram["sc_s"][1024 * i:1024 * (i + 1), :]), inc=bld.d_sy, n=16)
        E(bld, "sync", lambda e: e.dma_start(bld.dram["dbg_p"], bld.dram["sc_p"][10]),
          inc=bld.d_sy, n=16)
        fence(bld, "sync")
    bld.es_tf.close()
    bld.es_fin = contextlib.ExitStack()
    emit_final(bld)
    finish_program(bld)
    with bld.es, bld.es_fin, bld.nc.Block() as blk:
        bld.prog.play(blk)
    return bld


_BUILD_CACHE = {}


def kernel(**inputs):
    if "bld" not in _BUILD_CACHE:
        _BUILD_CACHE["bld"] = build_full()
    bld = _BUILD_CACHE["bld"]
    in_maps = [core_input_map(inputs, c) for c in range(8)]
    res = run_bass_kernel_spmd(bld.nc, in_maps, core_ids=list(range(8)))
    out = np.empty((B, L, D), np.float32)
    for c in range(8):
        b, half = c // 2, c % 2
        out[b, 4096 * half:4096 * (half + 1)] = res.results[c]["out"]
    return out


# version 08 (final)
# speedup vs baseline: 1.1717x; 1.0033x over previous
import math
import contextlib
import numpy as np
import ml_dtypes
import concourse.bass as bass
import concourse.mybir as mybir
from concourse.bass_utils import run_bass_kernel_spmd

F32 = mybir.dt.float32
BF16 = mybir.dt.bfloat16
F16 = mybir.dt.float16
AF = mybir.ActivationFunctionType
ALU = mybir.AluOpType
AX = mybir.AxisListType

B, L, D, NH, dh = 4, 8192, 1024, 16, 64
HALO = 512
TLOC = 4096 + HALO
NT = TLOC // 128
C = 128
NCH = 4096 // C
WINDOW = 256
QK_SCALE = 8.0
DIM_INNER = 2730
NLAYER = 6
LN_EPS = 1e-5

DEBUG = {}


def _host_consts(core_id):
    half = core_id % 2
    gstart = 4096 * half - HALO
    c = {}
    c["ident128"] = np.eye(128, dtype=np.float32)
    Rm = np.zeros((128, 128), np.float32)
    for hb in range(2):
        o = 64 * hb
        for p in range(32):
            Rm[o + p + 32, o + p] = -1.0
            Rm[o + p, o + p + 32] = 1.0
    c["rotmat"] = Rm
    halfv = np.arange(0, dh, 2, dtype=np.float64)
    inv_freq = 1.0 / 10000.0 ** (halfv / dh)
    t = np.arange(gstart, gstart + TLOC, dtype=np.float64)
    freqs = t[:, None] * inv_freq[None, :]
    freqs = np.concatenate([freqs, freqs], -1)
    base_scale = (halfv + 0.4 * dh) / (1.4 * dh)
    power = (t - L // 2) / (WINDOW // 2)
    scale = base_scale[None, :] ** power[:, None]
    scale = np.concatenate([scale, scale], -1)
    cos, sin = np.cos(freqs), np.sin(freqs)
    for nm, tab in [("rotAQ", cos * scale), ("rotBQ", sin * scale),
                    ("rotAK", cos / scale), ("rotBK", sin / scale)]:
        tabf = np.asarray(tab.T, np.float32)
        c[nm] = np.concatenate([tabf, tabf], axis=0)
    r = np.arange(128)[:, None]
    cc_ = np.arange(384)[None, :]
    def mkmask(lo):
        allowed = (cc_ >= r) & (cc_ <= r + 256)
        if lo is not None:
            allowed &= (cc_ >= lo)
            full_row_dead = (~allowed).all(axis=1, keepdims=True)
            allowed = np.where(full_row_dead, (cc_ >= r) & (cc_ <= r + 256), allowed)
        return np.where(allowed, 0.0, -1e30).astype(np.float32)
    masks = [mkmask(None)]
    for qt in (4, 5):
        lo = 256 - (gstart + 128 * qt)
        masks.append(mkmask(lo if lo > 0 else None))
    mm = np.stack(masks)
    half_cat = np.concatenate([mm[:, 0:64, :], mm[:, 64:128, :]], axis=2)
    c["attn_masks"] = np.concatenate([half_cat, half_cat], axis=1)
    c["sel"] = np.full((128, 1), 1.0 if half == 1 else 0.0, np.float32)
    tl = np.arange(C, dtype=np.float32)
    tim = tl[:, None] - tl[None, :]
    c["d_strict"] = np.where(tim > 0, tim, 0.0).astype(np.float32)
    c["m_strict"] = (tim > 0).astype(np.float32)
    c["d_strictT"] = np.ascontiguousarray(c["d_strict"].T)
    c["m_strictT"] = np.ascontiguousarray(c["m_strict"].T)
    c["d_inclT"] = np.where(tim >= 0, tim, 0.0).astype(np.float32).T.copy()
    c["m_inclT"] = (tim >= 0).astype(np.float32).T.copy()
    c["tcol"] = (tl + 1.0).reshape(128, 1).astype(np.float32)
    c["tbarcol"] = (C - (tl + 1.0)).reshape(128, 1).astype(np.float32)
    i64p = np.zeros((128, 64), np.float32)
    i64p[0:64] = np.eye(64)
    i64p[64:128] = np.eye(64)
    c["i64pair"] = i64p
    c["ones_row"] = np.ones((1, 128), np.float32)
    co = np.zeros((8, 128, 16), np.float32)
    for j in range(8):
        co[j, 0:64, 2 * j] = 1.0
        co[j, 64:128, 2 * j + 1] = 1.0
    c["chunk_ones"] = co
    c["chunk_onesT"] = np.ascontiguousarray(co.transpose(0, 2, 1))
    c["ident64h"] = np.concatenate([np.eye(64, dtype=np.float32)] * 2, axis=0)
    return c


ENGINES = ("sync", "gpsimd", "tensor", "scalar", "vector")


class Sem:
    __slots__ = ("h", "v", "name")

    def __init__(self, handle, name):
        self.h = handle
        self.v = 0
        self.name = name

    def inc(self, n=1):
        self.v += n
        return self.v


class Prog:
    def __init__(self, nc):
        self.nc = nc
        self.ops = {e: [] for e in ENGINES}

    def add(self, engine, fn):
        self.ops[engine].append(fn)

    def wait(self, engine, sem, val):
        self.add(engine, lambda e, s=sem.h, v=val: e.wait_ge(s, v))

    def play(self, blk):
        for eng in ENGINES:
            ops = self.ops[eng]
            if not ops:
                continue
            def body(e, ops=ops):
                for fn in ops:
                    fn(e)
            getattr(blk, eng)(body)


class Bank:

    def __init__(self, prog, tensor_handle, pe_sem, ev_sem):
        self.prog = prog
        self.t = tensor_handle
        self.pe = pe_sem
        self.ev = ev_sem
        self.n_cons_expected = 0

    def produce(self, mm_fns, n_consumers):
        p = self.prog
        if self.n_cons_expected:
            p.wait("tensor", self.ev, self.ev.v)
        for fn in mm_fns[:-1]:
            p.add("tensor", fn)
        tgt = self.pe.inc()
        p.add("tensor", lambda e, fn=mm_fns[-1], s=self.pe.h: fn(e).then_inc(s, 1))
        self.n_cons_expected = n_consumers
        return tgt

    def consume(self, engine, fn, pe_tgt):
        p = self.prog
        p.wait(engine, self.pe, pe_tgt)
        p.add(engine, lambda e, fn=fn, s=self.ev.h: fn(e).then_inc(s, 1))
        self.ev.inc()
        self.n_cons_expected -= 1


def ceil_div(a, b):
    return (a + b - 1) // b


class Build:

    def __init__(self, with_transformer=True, n_layers=NLAYER, dbg=()):
        self.nc = bass.Bass("TRN2", target_bir_lowering=False)
        self.prog = Prog(self.nc)
        self.es = contextlib.ExitStack()
        self.es_dn = contextlib.ExitStack()
        self.es_tf = contextlib.ExitStack()
        self.with_transformer = with_transformer
        self.n_layers = n_layers
        self.dbg_names = set(dbg)
        self.dram = {}
        self.sb = {}
        self.psum = {}
        self.sems = {}

    def din(self, name, shape, dtype=F32):
        t = self.nc.dram_tensor(name, list(shape), dtype, kind="ExternalInput").ap()
        self.dram[name] = t
        return t

    def dout(self, name, shape, dtype=F32):
        t = self.nc.dram_tensor(name, list(shape), dtype, kind="ExternalOutput").ap()
        self.dram[name] = t
        return t

    def dint(self, name, shape, dtype=F32):
        t = self.nc.dram_tensor(name, list(shape), dtype, kind="Internal").ap()
        self.dram[name] = t
        return t

    def sbuf(self, name, shape, dtype=F32, scope=None):
        es = scope if scope is not None else self.es
        t = es.enter_context(self.nc.sbuf_tensor(name, list(shape), dtype))
        self.sb[name] = t
        return t

    def psum_t(self, name, shape, dtype=F32):
        t = self.es.enter_context(self.nc.psum_tensor(name, list(shape), dtype))
        self.psum[name] = t
        return t

    def sem(self, name):
        s = Sem(self.es.enter_context(self.nc.semaphore(name)), name)
        self.sems[name] = s
        return s

    def dbg_store(self, name, src_fn, shape, dtype=F32):
        if name not in self.dbg_names:
            return
        o = self.dout("dbg_" + name, shape, dtype)
        d = self.d_sy
        tgt = d.inc(16)
        self.prog.add("sync", lambda e, o=o, f=src_fn, s=d.h: e.dma_start(o, f()).then_inc(s, 16))


def build_common(bld: "Build"):
    nc, p = bld.nc, bld.prog
    bld.din("x_in", [TLOC, D])
    bld.din("Wt_d", [D, D])
    bld.din("ln_gamma", [1, D]); bld.din("ln_beta", [1, D])
    bld.din("eta", [1, 1]); bld.din("lam_logit", [1, 1])
    bld.din("sel", [128, 1])
    for nm in ("ident128", "rotmat", "d_strict", "m_strict", "d_strictT", "m_strictT",
               "d_inclT", "m_inclT"):
        bld.din(nm, [128, 128])
    bld.din("i64pair", [128, 64])
    bld.din("tcol", [128, 1]); bld.din("tbarcol", [128, 1])
    bld.din("ones_row", [1, 128])
    bld.din("attn_masks", [3, 128, 768])
    bld.din("chunk_ones", [8, 128, 16])
    for nm in ("rotAQ", "rotBQ", "rotAK", "rotBK"):
        bld.din(nm, [128, TLOC])
    bld.din("Wqkv", [NLAYER, D, 3 * D]); bld.din("Wo", [NLAYER, D, D])
    bld.din("W1", [NLAYER, D, 2 * DIM_INNER]); bld.din("W2", [NLAYER, DIM_INNER, D])
    bld.din("b1cols", [NLAYER, 2, 22, 128, 1]); bld.din("b2cols", [NLAYER, 8, 128, 1])
    bld.din("ffn_g", [NLAYER, 8, 128, 1]); bld.din("ffn_b", [NLAYER, 8, 128, 1])
    bld.din("qk_scale", [NLAYER, 2, 128, 1])
    bld.din("chunk_onesT", [8, 16, 128]); bld.din("ident64h", [128, 64])
    bld.din("ones_col128", [128, 1])
    bld.dout("out", [4096, D])
    bld.dint("sc_static", [4096, D], BF16)
    bld.dint("sc_o0", [4096, D], BF16)
    bld.dint("sc_ofin", [4096, D], BF16)
    bld.dint("sc_T", [NCH, 128, 8 * 128], BF16)
    bld.dint("sc_P", [NCH, 128, 8 * 64])
    bld.dint("cc_m_in", [128, 1024])
    bld.dint("cc_m_out", [256, 1024])
    bld.dint("sc_s", [4096, D], BF16)
    bld.dint("sc_p", [36, 128, 16 * 384], F16)
    bld.dint("cc_h_in", [D, HALO], F16)
    bld.dint("cc_h_out", [2 * D, HALO], F16)
    bld.dint("sc_wq", [NLAYER, D, 3 * D], F16)
    bld.dint("sc_w1", [NLAYER, D, 2 * DIM_INNER], BF16)
    bld.dint("sc_w2", [NLAYER, DIM_INNER, D], BF16)

    bld.s_pe = bld.sem("s_pe")
    bld.s_ve = bld.sem("s_ve")
    bld.s_ac = bld.sem("s_ac")
    bld.s_gp = bld.sem("s_gp")
    bld.d_sy = bld.sem("d_sy")
    bld.d_sc = bld.sem("d_sc")
    bld.d_gp = bld.sem("d_gp")
    bld.s_cc = bld.sem("s_cc")
    bld.s_sy = bld.sem("s_sy")
    bld.s_sg = bld.sem("s_sg")
    bld.s_gq = bld.sem("s_gq")
    bld.d_w1p = [bld.sem("d_w10"), bld.sem("d_w11")]
    bld.d_w2p = [bld.sem("d_w20"), bld.sem("d_w21")]
    bld.d_qk = [bld.sem("d_qk0"), bld.sem("d_qk1")]
    bld.d_rt = bld.sem("d_rt")
    bld.d_wbq = [bld.sem("d_wbq0"), bld.sem("d_wbq1")]
    bld.d_wb1 = [bld.sem("d_wb10"), bld.sem("d_wb11")]
    bld.d_wb2 = [bld.sem("d_wb20"), bld.sem("d_wb21")]
    bld.wb_pend = {}

    bld.pbt = bld.psum_t("pbt", [128, 1024], BF16)
    bld.pb = [bld.psum_t(f"pb{i}", [128, 512], F32) for i in range(7)]

    bld.h = bld.sbuf("h", [128, 8 * TLOC], F16)
    bld.cols = bld.sbuf("cols", [128, 32], F32)
    bld.x2 = bld.sbuf("x2", [128, D], F32)
    bld.ident_bf = bld.sbuf("ident_bf", [128, 128], BF16)
    bld.rot_bf = bld.sbuf("rot_bf", [128, 128], BF16)
    bld.dn_cols = bld.sbuf("dn_cols", [128, 16], F32)
    bld.ones_row_sb = bld.sbuf("ones_row_sb", [1, 128], BF16)
    bld.ones_row_f32 = bld.sbuf("ones_row_f32", [1, 128], F32)
    bld.xrow = [bld.sbuf(f"xrow{i}", [128, D], F32, scope=bld.es_dn) for i in range(2)]
    bld.xn_tok = [bld.sbuf(f"xn_tok{i}", [128, D], BF16, scope=bld.es_dn) for i in range(2)]
    bld.xa_tok = [bld.sbuf(f"xa_tok{i}", [128, D], BF16, scope=bld.es_dn) for i in range(2)]
    bld.xa_fm = [bld.sbuf(f"xa_fm{i}", [128, 8 * 128], BF16, scope=bld.es_dn) for i in range(2)]
    bld.st_tok = [bld.sbuf(f"st_tok{i}", [128, D], BF16, scope=bld.es_dn) for i in range(2)]
    bld.Wt_sb = bld.sbuf("Wt_sb", [128, 8 * D], BF16, scope=bld.es_dn)
    bld.Ls = bld.sbuf("Ls", [128, 128], BF16, scope=bld.es_dn)
    bld.LsT = bld.sbuf("LsT", [128, 128], BF16, scope=bld.es_dn)
    bld.LiT = bld.sbuf("LiT", [128, 128], BF16, scope=bld.es_dn)
    bld.i64_bf = bld.sbuf("i64_bf", [128, 64], F32, scope=bld.es_dn)
    bld.consts_f32 = bld.sbuf("consts_f32", [128, 8 * 128], F32, scope=bld.es_dn)
    bld.X_all = bld.sbuf("X_all", [128, 16 * 128], BF16, scope=bld.es_dn)
    bld.Na = bld.sbuf("Na", [128, 16 * 128], BF16, scope=bld.es_dn)
    bld.NTa = bld.sbuf("NTa", [128, 16 * 128], BF16, scope=bld.es_dn)
    bld.Nb = bld.sbuf("Nb", [128, 16 * 128], BF16, scope=bld.es_dn)
    bld.NTb = bld.sbuf("NTb", [128, 16 * 128], BF16, scope=bld.es_dn)
    bld.Bt_all = bld.sbuf("Bt_all", [128, 16 * 128], BF16, scope=bld.es_dn)
    bld.LbK = bld.sbuf("LbK", [128, D], BF16, scope=bld.es_dn)
    bld.U_all = bld.sbuf("U_all", [128, D], BF16, scope=bld.es_dn)
    bld.Wfm = bld.sbuf("Wfm", [128, 8 * 128], BF16, scope=bld.es_dn)
    bld.Ttok = bld.sbuf("Ttok", [128, D], BF16, scope=bld.es_dn)
    bld.Tfm = bld.sbuf("Tfm", [128, 8 * 128], BF16, scope=bld.es_dn)
    bld.M_all = bld.sbuf("M_all", [128, 8 * 64], F32, scope=bld.es_dn)
    bld.MT_all = bld.sbuf("MT_all", [128, 8 * 64], F32, scope=bld.es_dn)
    bld.MTbf = bld.sbuf("MTbf", [128, 8 * 64], BF16, scope=bld.es_dn)
    bld.P_all = bld.sbuf("P_all", [128, 8 * 64], F32, scope=bld.es_dn)
    bld.PT_all = bld.sbuf("PT_all", [128, 8 * 64], F32, scope=bld.es_dn)
    bld.G_all = bld.sbuf("G_all", [128, 8 * 64], F32, scope=bld.es_dn)
    bld.o0_all = bld.sbuf("o0_all", [128, D], BF16, scope=bld.es_dn)
    bld.otmp = bld.sbuf("otmp", [128, 512], F32, scope=bld.es_dn)
    bld.Minit = bld.sbuf("Minit", [128, 8 * 64], F32, scope=bld.es_dn)
    bld.MinitT = bld.sbuf("MinitT", [128, 8 * 64], F32, scope=bld.es_dn)
    bld.MinitTbf = bld.sbuf("MinitTbf", [128, 8 * 64], BF16, scope=bld.es_dn)
    bld.Pld = bld.sbuf("Pld", [128, 8 * 64], F32, scope=bld.es_dn)
    bld.Tld = bld.sbuf("Tld", [128, 8 * 128], BF16, scope=bld.es_dn)
    bld.o0ld = bld.sbuf("o0ld", [128, D], BF16, scope=bld.es_dn)
    bld.dMbf = bld.sbuf("dMbf", [128, 8 * 64], BF16, scope=bld.es_dn)
    bld.gamma_b = bld.sbuf("gamma_b", [128, D], BF16, scope=bld.es_dn)
    bld.beta_b = bld.sbuf("beta_b", [128, D], BF16, scope=bld.es_dn)
    bld.grow = bld.sbuf("grow", [1, D], F32, scope=bld.es_dn)
    bld.brow = bld.sbuf("brow", [1, D], F32, scope=bld.es_dn)
    bld.yrow = bld.sbuf("yrow", [128, D], F32, scope=bld.es_dn)
    bld.st_ld = bld.sbuf("st_ld", [128, D], BF16, scope=bld.es_dn)
    bld.of_ld = bld.sbuf("of_ld", [128, D], BF16, scope=bld.es_dn)
    bld.s_bf = bld.sbuf("s_bf", [128, D], BF16, scope=bld.es_dn)
_ENG_SELF_SEM = {"vector": "s_ve", "scalar": "s_ac", "gpsimd": "s_gp"}


def E(bld, eng, fn, inc=None, n=1):
    ssn = _ENG_SELF_SEM.get(eng)
    if ssn is not None:
        sem = getattr(bld, ssn)
        if inc is None or inc is sem:
            W8(bld, eng, sem, sem.v)
            tgt = sem.inc(1)
            bld.prog.add(eng, lambda e, fn=fn, s=sem.h: fn(e).then_inc(s, 1))
            return tgt
    if inc is None:
        bld.prog.add(eng, fn)
        return None
    tgt = inc.inc(n)
    bld.prog.add(eng, lambda e, fn=fn, s=inc.h, n=n: fn(e).then_inc(s, n))
    return tgt


def W8(bld, eng, sem, val):
    if val is None or val <= 0:
        return
    bld.prog.wait(eng, sem, val)


def fence(bld, queue):
    dsem, msem = {"sync": (bld.d_sy, bld.s_sy), "scalar": (bld.d_sc, bld.s_sg),
                  "gpsimd": (bld.d_gp, bld.s_gq)}[queue]
    W8(bld, queue, dsem, dsem.v)
    tgt = msem.inc()
    bld.prog.add(queue, lambda e, s=msem.h: e.sem_inc(s, 1))
    return tgt


CL_LNLAM, CL_LAM, CL_LBAR, CL_BETA, CL_BLAM, CL_LAMC, CL_SEL, CL_NBETA = range(8)


def emit_setup(bld: Build):
    nc, p = bld.nc, bld.prog
    d = bld.dram
    gp_loads = [
        (bld.ident_bf[:], d["ident128"]),
        (bld.rot_bf[:], d["rotmat"]),
        (bld.ones_row_sb[:], d["ones_row"]),
    ]
    for j in range(8):
        gp_loads.append((bld.Wt_sb[:, j * D:(j + 1) * D], d["Wt_d"][128 * j:128 * (j + 1), :]))
    for dst, src in gp_loads:
        E(bld, "gpsimd", lambda e, a=dst, b=src: e.dma_start(a, b), inc=bld.d_gp, n=16)
    for t in (bld.M_all, bld.MT_all, bld.MTbf):
        E(bld, "gpsimd", lambda e, a=t: e.memset(a[:], 0.0), inc=bld.s_gp)
    gp_done = fence(bld, "gpsimd")
    gp_ms = bld.s_gp.v

    cf = bld.consts_f32
    sy_loads = [
        (cf[:, 0:128], d["d_strict"]), (cf[:, 128:256], d["m_strict"]),
        (cf[:, 256:384], d["d_strictT"]), (cf[:, 384:512], d["m_strictT"]),
        (cf[:, 512:640], d["d_inclT"]), (cf[:, 640:768], d["m_inclT"]),
        (bld.i64_bf[:], d["i64pair"][:, 0:64]),
        (bld.cols[0:128, 0:1], d["tcol"]), (bld.cols[0:128, 1:2], d["tbarcol"]),
        (bld.cols[0:1, 2:3], d["eta"]), (bld.cols[0:1, 3:4], d["lam_logit"]),
        (bld.dn_cols[:, CL_SEL:CL_SEL + 1], d["sel"]),
        (bld.grow[:], d["ln_gamma"]), (bld.brow[:], d["ln_beta"]),
        (bld.ones_row_f32[:], d["ones_row"]),
    ]
    for dst, src in sy_loads:
        E(bld, "sync", lambda e, a=dst, b=src: e.dma_start(a, b), inc=bld.d_sy, n=16)
    sy_done = fence(bld, "sync")

    ones_r = bld.ones_row_f32

    W8(bld, "scalar", bld.s_sy, sy_done)
    E(bld, "scalar", lambda e: e.activation(bld.cols[0:1, 4:5], bld.cols[0:1, 3:4], AF.Sigmoid))
    E(bld, "scalar", lambda e: e.activation(bld.cols[0:1, 5:6], bld.cols[0:1, 4:5], AF.Ln),
      inc=bld.s_ac)
    a_ms = bld.s_ac.v
    W8(bld, "vector", bld.s_ac, a_ms)
    E(bld, "vector", lambda e: e.tensor_scalar_max(bld.cols[0:1, 5:6], bld.cols[0:1, 5:6], -85.0))
    E(bld, "vector", lambda e: e.tensor_scalar(bld.cols[0:1, 6:7], bld.cols[0:1, 4:5], -1.0, 1.0,
                                               ALU.mult, ALU.add))
    E(bld, "vector", lambda e: e.tensor_mul(bld.cols[0:1, 7:8], bld.cols[0:1, 6:7], bld.cols[0:1, 2:3]))
    E(bld, "vector", lambda e: e.tensor_scalar_mul(bld.cols[0:1, 7:8], bld.cols[0:1, 7:8], 1.0 / 64.0))
    E(bld, "vector", lambda e: e.tensor_scalar_mul(bld.cols[0:1, 8:9], bld.cols[0:1, 5:6], 128.0),
      inc=bld.s_ve)
    v_ms = bld.s_ve.v
    W8(bld, "scalar", bld.s_ve, v_ms)
    E(bld, "scalar", lambda e: e.activation(bld.cols[0:1, 9:10], bld.cols[0:1, 8:9], AF.Exp),
      inc=bld.s_ac)
    a_ms = bld.s_ac.v

    W8(bld, "tensor", bld.s_ac, a_ms)
    W8(bld, "tensor", bld.s_sy, sy_done)
    ps = bld.pb[6]
    def bcast_mm(src_ap, col):
        E(bld, "tensor", lambda e, s=src_ap, c=col: e.matmul(
            ps[:, c:c + 1], ones_r[:], s, start=True, stop=True))
    bcast_mm(bld.cols[0:1, 5:6], 0)
    bcast_mm(bld.cols[0:1, 7:8], 1)
    bcast_mm(bld.cols[0:1, 9:10], 2)
    E(bld, "tensor", lambda e: e.matmul(ps[:, 3:4], ones_r[:], bld.cols[0:1, 7:8],
                                        start=True, stop=True), inc=bld.s_pe)
    pe_ms = bld.s_pe.v
    W8(bld, "vector", bld.s_pe, pe_ms)
    E(bld, "vector", lambda e: e.tensor_copy(bld.dn_cols[:, CL_LNLAM:CL_LNLAM + 1], ps[:, 0:1]))
    E(bld, "vector", lambda e: e.tensor_copy(bld.dn_cols[:, CL_BETA:CL_BETA + 1], ps[:, 1:2]))
    E(bld, "vector", lambda e: e.tensor_copy(bld.dn_cols[:, CL_LAMC:CL_LAMC + 1], ps[:, 2:3]))
    E(bld, "vector", lambda e: e.tensor_scalar_mul(bld.dn_cols[:, CL_NBETA:CL_NBETA + 1],
                                                   ps[:, 3:4], -1.0))
    E(bld, "vector", lambda e: e.tensor_mul(bld.cols[:, 10:11], bld.cols[:, 0:1],
                                            bld.dn_cols[:, CL_LNLAM:CL_LNLAM + 1]))
    E(bld, "vector", lambda e: e.tensor_mul(bld.cols[:, 11:12], bld.cols[:, 1:2],
                                            bld.dn_cols[:, CL_LNLAM:CL_LNLAM + 1]), inc=bld.s_ve)
    v_ms = bld.s_ve.v
    W8(bld, "scalar", bld.s_ve, v_ms)
    E(bld, "scalar", lambda e: e.activation(bld.dn_cols[:, CL_LAM:CL_LAM + 1], bld.cols[:, 10:11], AF.Exp))
    E(bld, "scalar", lambda e: e.activation(bld.dn_cols[:, CL_LBAR:CL_LBAR + 1], bld.cols[:, 11:12], AF.Exp),
      inc=bld.s_ac)
    a_ms = bld.s_ac.v
    W8(bld, "vector", bld.s_ac, a_ms)
    E(bld, "vector", lambda e: e.tensor_mul(bld.dn_cols[:, CL_BLAM:CL_BLAM + 1],
                                            bld.dn_cols[:, CL_BETA:CL_BETA + 1],
                                            bld.dn_cols[:, CL_LAM:CL_LAM + 1]))

    def mk_decay(dst, d_off, m_off, use_nbeta):
        E(bld, "vector", lambda e: e.tensor_scalar_mul(
            bld.x2[:, 0:128], cf[:, d_off:d_off + 128], bld.dn_cols[:, CL_LNLAM:CL_LNLAM + 1]),
            inc=bld.s_ve)
        W8(bld, "scalar", bld.s_ve, bld.s_ve.v)
        E(bld, "scalar", lambda e: e.activation(bld.x2[:, 128:256], bld.x2[:, 0:128], AF.Exp),
          inc=bld.s_ac)
        W8(bld, "vector", bld.s_ac, bld.s_ac.v)
        if use_nbeta:
            E(bld, "vector", lambda e: e.tensor_mul(bld.x2[:, 0:128], bld.x2[:, 128:256],
                                                    cf[:, m_off:m_off + 128]))
            E(bld, "vector", lambda e: e.tensor_scalar_mul(
                dst[:], bld.x2[:, 0:128], bld.dn_cols[:, CL_NBETA:CL_NBETA + 1]), inc=bld.s_ve)
        else:
            E(bld, "vector", lambda e: e.tensor_mul(dst[:], bld.x2[:, 128:256],
                                                    cf[:, m_off:m_off + 128]), inc=bld.s_ve)

    mk_decay(bld.Ls, 0, 128, True)
    mk_decay(bld.LsT, 256, 384, True)
    mk_decay(bld.LiT, 512, 640, False)
    bld.guards = getattr(bld, "guards", {})
    guard_set(bld, "x2", (bld.s_ve, bld.s_ve.v))

    W8(bld, "vector", bld.s_sy, sy_done)
    for pj in range(8):
        E(bld, "vector", lambda e, pj=pj: e.tensor_copy(bld.P_all[:, 64 * pj:64 * (pj + 1)],
                                                        bld.i64_bf[:, :]))
        E(bld, "vector", lambda e, pj=pj: e.tensor_copy(bld.PT_all[:, 64 * pj:64 * (pj + 1)],
                                                        bld.i64_bf[:, :]))
    E(bld, "vector", lambda e: e.tensor_copy(bld.PT_all[:, 0:1], bld.PT_all[:, 0:1]), inc=bld.s_ve)

    W8(bld, "tensor", bld.s_sy, sy_done)
    for half in range(2):
        W8(bld, "tensor", bld.s_ve, bld.s_ve.v)
        E(bld, "tensor", lambda e, hf=half: e.matmul(
            bld.pb[5][:, 0:512], ones_r[:], bld.grow[:, 512 * hf:512 * (hf + 1)],
            start=True, stop=True), inc=bld.s_pe)
        W8(bld, "vector", bld.s_pe, bld.s_pe.v)
        E(bld, "vector", lambda e, hf=half: e.tensor_copy(
            bld.gamma_b[:, 512 * hf:512 * (hf + 1)], bld.pb[5][:, 0:512]), inc=bld.s_ve)
        W8(bld, "tensor", bld.s_ve, bld.s_ve.v)
        E(bld, "tensor", lambda e, hf=half: e.matmul(
            bld.pb[5][:, 0:512], ones_r[:], bld.brow[:, 512 * hf:512 * (hf + 1)],
            start=True, stop=True), inc=bld.s_pe)
        W8(bld, "vector", bld.s_pe, bld.s_pe.v)
        E(bld, "vector", lambda e, hf=half: e.tensor_copy(
            bld.beta_b[:, 512 * hf:512 * (hf + 1)], bld.pb[5][:, 0:512]), inc=bld.s_ve)

    for eng in ("tensor", "vector", "scalar"):
        W8(bld, eng, bld.s_gq, gp_done)
        W8(bld, eng, bld.s_gp, gp_ms)


def guard_wait(bld, eng, key):
    for sem, val in bld.guards.get(key, ()):
        if sem.name.startswith("d_"):
            val = sem.v
        W8(bld, eng, sem, val)


def guard_set(bld, key, *pairs):
    bld.guards[key] = list(pairs)


def guard_add(bld, key, sem, val):
    bld.guards.setdefault(key, []).append((sem, val))


def emit_tile_prep(bld: Build, tile):
    p = bld.prog
    slot = tile % 2
    own = tile >= 4
    if not hasattr(bld, "guards"):
        bld.guards = {}
    cb = 12 + 10 * slot

    guard_wait(bld, "sync", f"xrow{slot}")
    E(bld, "sync", lambda e, t=tile, s=slot: e.dma_start(
        bld.xrow[s][:], bld.dram["x_in"][128 * t:128 * (t + 1), :]), inc=bld.d_sy, n=16)
    xv = fence(bld, "sync")

    W8(bld, "vector", bld.s_sy, xv)
    X = bld.xrow[slot]
    c = bld.cols
    E(bld, "vector", lambda e: e.reduce_sum(c[:, cb + 0:cb + 1], X[:], axis=AX.X), inc=bld.s_ve)
    W8(bld, "scalar", bld.s_sy, xv)
    guard_wait(bld, "scalar", "x2")
    sq_a = E(bld, "scalar", lambda e: e.activation(bld.x2[:], X[:], AF.Square), inc=bld.s_ac)
    W8(bld, "vector", bld.s_ac, sq_a)
    x2r = E(bld, "vector", lambda e: e.reduce_sum(c[:, cb + 1:cb + 2], bld.x2[:], axis=AX.X))
    guard_set(bld, "x2", (bld.s_ve, x2r))
    E(bld, "vector", lambda e: e.tensor_scalar_mul(c[:, cb + 2:cb + 3], c[:, cb + 0:cb + 1], 1.0 / D))
    E(bld, "vector", lambda e: e.tensor_scalar_mul(c[:, cb + 3:cb + 4], c[:, cb + 2:cb + 3], -1.0))
    E(bld, "vector", lambda e: e.tensor_scalar_mul(c[:, cb + 4:cb + 5], c[:, cb + 1:cb + 2], 1.0 / D))
    E(bld, "vector", lambda e: e.tensor_mul(c[:, cb + 5:cb + 6], c[:, cb + 2:cb + 3], c[:, cb + 2:cb + 3]))
    E(bld, "vector", lambda e: e.tensor_sub(c[:, cb + 6:cb + 7], c[:, cb + 4:cb + 5], c[:, cb + 5:cb + 6]))
    E(bld, "vector", lambda e: e.tensor_scalar_add(c[:, cb + 6:cb + 7], c[:, cb + 6:cb + 7], LN_EPS),
      inc=bld.s_ve)
    W8(bld, "scalar", bld.s_ve, bld.s_ve.v)
    sqrt_a = E(bld, "scalar", lambda e: e.activation(c[:, cb + 7:cb + 8], c[:, cb + 6:cb + 7], AF.Sqrt),
               inc=bld.s_ac)
    W8(bld, "vector", bld.s_ac, sqrt_a)
    E(bld, "vector", lambda e: e.reciprocal(c[:, cb + 8:cb + 9], c[:, cb + 7:cb + 8]))
    guard_wait(bld, "vector", f"xn{slot}")
    xn_v = E(bld, "vector", lambda e: e.tensor_scalar(
        bld.xn_tok[slot][:], X[:], c[:, cb + 3:cb + 4], c[:, cb + 8:cb + 9],
        ALU.add, ALU.mult), inc=bld.s_ve)
    guard_set(bld, f"xrow{slot}", (bld.s_ve, xn_v), (bld.s_ac, sq_a))
    tanh_a = None
    if own:
        W8(bld, "scalar", bld.s_ve, xn_v)
        guard_wait(bld, "scalar", f"xa{slot}")
        tanh_a = E(bld, "scalar", lambda e: e.activation(
            bld.xa_tok[slot][:], bld.xn_tok[slot][:], AF.Tanh), inc=bld.s_ac)

    guard_wait(bld, "tensor", "pbt")
    W8(bld, "tensor", bld.s_ve, xn_v)
    for j in range(8):
        inc = bld.s_pe if j == 7 else None
        E(bld, "tensor", lambda e, j=j: e.transpose(
            bld.pbt[:, 128 * j:128 * (j + 1)], bld.xn_tok[slot][:, 128 * j:128 * (j + 1)],
            bld.ident_bf[:]), inc=inc)
    trn_pe = bld.s_pe.v
    W8(bld, "scalar", bld.s_pe, trn_pe)
    hdst = bld.h[:].rearrange("p (c t) -> p c t", t=TLOC)[:, :, 128 * tile:128 * (tile + 1)]
    psrc = bld.pbt[:].rearrange("p (c t) -> p c t", t=128)
    h_a = E(bld, "scalar", lambda e, d2=hdst, s2=psrc: e.activation(d2, s2, AF.Copy), inc=bld.s_ac)
    guard_set(bld, "pbt", (bld.s_ac, h_a))
    guard_set(bld, f"xn{slot}", (bld.s_pe, trn_pe), (bld.s_ac, tanh_a if own else h_a))

    if not own:
        return

    guard_wait(bld, "tensor", "pbt")
    W8(bld, "tensor", bld.s_ac, tanh_a)
    guard_wait(bld, "tensor", f"xafm{slot}")
    for j in range(8):
        inc = bld.s_pe if j == 7 else None
        E(bld, "tensor", lambda e, j=j: e.transpose(
            bld.pbt[:, 128 * j:128 * (j + 1)], bld.xa_tok[slot][:, 128 * j:128 * (j + 1)],
            bld.ident_bf[:]), inc=inc)
    tra_pe = bld.s_pe.v
    W8(bld, "vector", bld.s_pe, tra_pe)
    xa_v = E(bld, "vector", lambda e: e.tensor_copy(bld.xa_fm[slot][:], bld.pbt[:]), inc=bld.s_ve)
    guard_set(bld, "pbt", (bld.s_ve, xa_v))
    guard_set(bld, f"xa{slot}", (bld.s_pe, tra_pe))

    for half in range(2):
        guard_wait(bld, "tensor", f"pb{half}")
        W8(bld, "tensor", bld.s_ve, xa_v)
        for j in range(8):
            inc = bld.s_pe if j == 7 else None
            E(bld, "tensor", lambda e, j=j, hf=half: e.matmul(
                bld.pb[hf][:, 0:512], bld.xa_fm[slot][:, 128 * j:128 * (j + 1)],
                bld.Wt_sb[:, 1024 * j + 512 * hf:1024 * j + 512 * hf + 512],
                start=(j == 0), stop=(j == 7)), inc=inc)
        st_pe = bld.s_pe.v
        W8(bld, "vector", bld.s_pe, st_pe)
        if half == 0:
            guard_wait(bld, "vector", f"st{slot}")
        ev = E(bld, "vector", lambda e, hf=half: e.tensor_copy(
            bld.st_tok[slot][:, 512 * hf:512 * (hf + 1)], bld.pb[hf][:, 0:512]), inc=bld.s_ve)
        guard_set(bld, f"pb{half}", (bld.s_ve, ev))
    st_v = bld.s_ve.v
    guard_set(bld, f"xafm{slot}", (bld.s_pe, bld.s_pe.v))
    W8(bld, "sync", bld.s_ve, st_v)
    r0 = (tile - 4) * 128
    E(bld, "sync", lambda e, r0=r0, s=slot: e.dma_start(
        bld.dram["sc_static"][r0:r0 + 128, :], bld.st_tok[s][:]), inc=bld.d_sy, n=16)
    stv = fence(bld, "sync")
    guard_set(bld, f"st{slot}", (bld.s_sy, stv))


def emit_dn_chunk(bld: Build, ch):
    p = bld.prog
    slot = (4 + ch) % 2
    xa_t = bld.xa_tok[slot]
    xa_f = bld.xa_fm[slot]
    st_t = bld.st_tok[slot]
    dcl = bld.dn_cols

    def lam_c(i):
        return dcl[:, i:i + 1]

    def hblk(h):
        return 128 * (h // 2), 64 * (h % 2)

    for grp in range(2):
        for bank, par in ((2, 0), (3, 1)):
            guard_wait(bld, "tensor", f"pb{bank}")
            if grp == 0 and bank == 2 and par == 0:
                pass
            heads = [8 * grp + 2 * i + par for i in range(4)]
            for ii, h in enumerate(heads):
                co, pb_ = hblk(h)
                inc = bld.s_pe if ii == 3 else None
                E(bld, "tensor", lambda e, co=co, pb_=pb_, ii=ii, bank=bank: e.matmul(
                    bld.pb[bank][:, 128 * ii:128 * (ii + 1)],
                    xa_f[pb_:pb_ + 64, co:co + 128], xa_f[pb_:pb_ + 64, co:co + 128],
                    start=True, stop=True, tile_position=(pb_, 0)), inc=inc)
            kk_pe = bld.s_pe.v
            W8(bld, "vector", bld.s_pe, kk_pe)
            psv = bld.pb[bank][:].rearrange("p (i x) -> p i x", x=128)
            if grp == 0 and bank == 2:
                guard_wait(bld, "vector", "NBt")
            for dst, msk in ((bld.Na, bld.Ls), (bld.NTa, bld.LsT), (bld.Bt_all, bld.LiT)):
                dv = dst[:].rearrange("p (hh pp x) -> p hh pp x", pp=2, x=128)[:, 4 * grp:4 * grp + 4, par, :]
                inc2 = bld.s_ve if msk is bld.LiT else None
                E(bld, "vector", lambda e, dv=dv, psv=psv, msk=msk: e.tensor_tensor(
                    dv, psv, msk[:].unsqueeze(1).broadcast_to([128, 4, 128]), ALU.mult),
                  inc=inc2)
            guard_set(bld, f"pb{bank}", (bld.s_ve, bld.s_ve.v))
    masks_v = bld.s_ve.v

    xv3 = bld.X_all[:].rearrange("p (h x) -> p h x", x=128)
    guard_wait(bld, "vector", "X_all")
    guard_wait(bld, "vector", "LbK")
    E(bld, "vector", lambda e: e.tensor_scalar_mul(
        xv3[:, :, 0:64], st_t[:].rearrange("p (h x) -> p h x", x=64), lam_c(CL_BETA)))
    E(bld, "vector", lambda e: e.tensor_scalar_mul(
        xv3[:, :, 64:128], xa_t[:].rearrange("p (h x) -> p h x", x=64), lam_c(CL_BLAM)))
    E(bld, "vector", lambda e: e.tensor_scalar_mul(bld.LbK[:], xa_t[:], lam_c(CL_LBAR)),
      inc=bld.s_ve)
    rhs_v = bld.s_ve.v
    guard_add(bld, f"xa{slot}", bld.s_ve, rhs_v)
    guard_add(bld, f"st{slot}", bld.s_ve, rhs_v)

    Ncur, NTcur, Nnxt, NTnxt = bld.Na, bld.NTa, bld.Nb, bld.NTb
    for j in range(7):
        for grp in range(4):
            bank = 2 + (grp % 2)
            guard_wait(bld, "tensor", f"pb{bank}")
            if j == 0:
                W8(bld, "tensor", bld.s_ve, rhs_v)
            for ii in range(4):
                h = 4 * grp + ii
                inc = bld.s_pe if ii == 3 else None
                E(bld, "tensor", lambda e, h=h, ii=ii, bank=bank, NT=NTcur: e.matmul(
                    bld.pb[bank][:, 128 * ii:128 * (ii + 1)],
                    NT[:, 128 * h:128 * (h + 1)], bld.X_all[:, 128 * h:128 * (h + 1)],
                    start=True, stop=True), inc=inc)
            app_pe = bld.s_pe.v
            W8(bld, "vector", bld.s_pe, app_pe)
            xslice = bld.X_all[:, 512 * grp:512 * (grp + 1)]
            E(bld, "vector", lambda e, xs=xslice, bank=bank: e.tensor_tensor(
                xs, xs, bld.pb[bank][:, 0:512], ALU.add), inc=bld.s_ve)
            guard_set(bld, f"pb{bank}", (bld.s_ve, bld.s_ve.v))
        if j < 6:
            for grp in range(4):
                guard_wait(bld, "tensor", "pb4")
                guard_wait(bld, "tensor", "pb5")
                for ii in range(4):
                    h = 4 * grp + ii
                    E(bld, "tensor", lambda e, h=h, ii=ii, N=Ncur, NT=NTcur: e.matmul(
                        bld.pb[4][:, 128 * ii:128 * (ii + 1)],
                        NT[:, 128 * h:128 * (h + 1)], N[:, 128 * h:128 * (h + 1)],
                        start=True, stop=True))
                for ii in range(4):
                    h = 4 * grp + ii
                    inc = bld.s_pe if ii == 3 else None
                    E(bld, "tensor", lambda e, h=h, ii=ii, N=Ncur, NT=NTcur: e.matmul(
                        bld.pb[5][:, 128 * ii:128 * (ii + 1)],
                        N[:, 128 * h:128 * (h + 1)], NT[:, 128 * h:128 * (h + 1)],
                        start=True, stop=True), inc=inc)
                sq_pe = bld.s_pe.v
                W8(bld, "vector", bld.s_pe, sq_pe)
                E(bld, "vector", lambda e, grp=grp, Nn=Nnxt: e.tensor_copy(
                    Nn[:, 512 * grp:512 * (grp + 1)], bld.pb[4][:, 0:512]))
                E(bld, "vector", lambda e, grp=grp, NTn=NTnxt: e.tensor_copy(
                    NTn[:, 512 * grp:512 * (grp + 1)], bld.pb[5][:, 0:512]), inc=bld.s_ve)
                guard_set(bld, "pb4", (bld.s_ve, bld.s_ve.v))
                guard_set(bld, "pb5", (bld.s_ve, bld.s_ve.v))
            Ncur, NTcur, Nnxt, NTnxt = Nnxt, NTnxt, Ncur, NTcur
    solve_v = bld.s_ve.v

    for rnd in range(2):
        guard_wait(bld, "tensor", "pbt")
        W8(bld, "tensor", bld.s_ve, solve_v)
        for pi in range(4):
            pj = 4 * rnd + pi
            for par in range(2):
                h = 2 * pj + par
                inc = bld.s_pe if (pi == 3 and par == 1) else None
                E(bld, "tensor", lambda e, h=h, pi=pi, par=par: e.transpose(
                    bld.pbt[64 * par:64 * (par + 1), 128 * pi:128 * (pi + 1)],
                    bld.X_all[:, 128 * h + 64:128 * (h + 1)], bld.ident_bf[:],
                    tile_position=(0, 64 * par)), inc=inc)
        wt_pe = bld.s_pe.v
        W8(bld, "vector", bld.s_pe, wt_pe)
        if rnd == 0:
            guard_wait(bld, "vector", "Wfm")
        E(bld, "vector", lambda e, rnd=rnd: e.tensor_scalar_mul(
            bld.Wfm[:, 512 * rnd:512 * (rnd + 1)], bld.pbt[:, 0:512], -1.0), inc=bld.s_ve)
        guard_set(bld, "pbt", (bld.s_ve, bld.s_ve.v))

    for par in range(2):
        bank = 6 if par == 0 else 5
        guard_wait(bld, "tensor", f"pb{bank}")
        W8(bld, "tensor", bld.s_ve, bld.s_ve.v)
        for pj in range(8):
            h = 2 * pj + par
            inc = bld.s_pe if pj == 7 else None
            E(bld, "tensor", lambda e, pj=pj, par=par, bank=bank: e.matmul(
                bld.pb[bank][:, 64 * pj:64 * (pj + 1)],
                bld.Wfm[64 * par:64 * (par + 1), 128 * pj:128 * (pj + 1)],
                bld.MTbf[64 * par:64 * (par + 1), 64 * pj:64 * (pj + 1)],
                start=True, stop=True, tile_position=(64 * par, 0)), inc=inc)
        u_pe = bld.s_pe.v
        W8(bld, "vector", bld.s_pe, u_pe)
        if par == 0:
            guard_wait(bld, "vector", "U_all")
        uv = bld.U_all[:].rearrange("p (hh pp x) -> p hh pp x", pp=2, x=64)[:, :, par, :]
        ub = bld.X_all[:].rearrange("p (hh pp x) -> p hh pp x", pp=2, x=128)[:, :, par, 0:64]
        pv = bld.pb[bank][:].rearrange("p (h x) -> p h x", x=64)
        E(bld, "vector", lambda e, uv=uv, ub=ub, pv=pv: e.tensor_tensor(uv, ub, pv, ALU.add),
          inc=bld.s_ve)
        guard_set(bld, f"pb{bank}", (bld.s_ve, bld.s_ve.v))
    u_v = bld.s_ve.v

    mtbf_readers_pe = []

    guard_wait(bld, "vector", "o0_all")
    for par in range(2):
        bank = 0 if par == 0 else 1
        guard_wait(bld, "tensor", f"pb{bank}")
        for pj in range(8):
            h = 2 * pj + par
            co, pb_ = hblk(h)
            inc = bld.s_pe if pj == 7 else None
            E(bld, "tensor", lambda e, pj=pj, par=par, bank=bank, co=co, pb_=pb_: e.matmul(
                bld.pb[bank][:, 64 * pj:64 * (pj + 1)],
                xa_f[pb_:pb_ + 64, co:co + 128],
                bld.MTbf[64 * par:64 * (par + 1), 64 * pj:64 * (pj + 1)],
                start=True, stop=True, tile_position=(pb_, 0)), inc=inc)
        mtbf_readers_pe.append(bld.s_pe.v)
    o1_pe = bld.s_pe.v
    o0v = bld.o0_all[:].rearrange("p (h x) -> p h x", x=64)
    for rnd in range(2):
        guard_wait(bld, "tensor", "pb4")
        W8(bld, "tensor", bld.s_ve, bld.s_ve.v)
        for i8 in range(8):
            h = 8 * rnd + i8
            inc = bld.s_pe if i8 == 7 else None
            E(bld, "tensor", lambda e, h=h, i8=i8: e.matmul(
                bld.pb[4][:, 64 * i8:64 * (i8 + 1)],
                bld.Bt_all[:, 128 * h:128 * (h + 1)], bld.U_all[:, 64 * h:64 * (h + 1)],
                start=True, stop=True), inc=inc)
        o2_pe = bld.s_pe.v
        W8(bld, "scalar", bld.s_pe, o2_pe)
        otv = bld.otmp[:].rearrange("p (hh pp x) -> p hh pp x", pp=2, x=64)
        for par in range(2):
            bank = 0 if par == 0 else 1
            pv = bld.pb[bank][:].rearrange("p (h x) -> p h x", x=64)[:, 4 * rnd:4 * rnd + 4]
            E(bld, "scalar", lambda e, par=par, pv=pv, otv=otv: e.activation(
                otv[:, :, par, :], pv, AF.Copy), inc=(bld.s_ac if par == 1 else None))
        ocp_a = bld.s_ac.v
        W8(bld, "vector", bld.s_ac, ocp_a)
        W8(bld, "vector", bld.s_pe, o2_pe)
        pv2 = bld.pb[4][:].rearrange("p (h x) -> p h x", x=64)
        E(bld, "vector", lambda e, rnd=rnd, pv2=pv2, o0v=o0v: e.scalar_tensor_tensor(
            o0v[:, 8 * rnd:8 * rnd + 8], bld.otmp[:].rearrange("p (h x) -> p h x", x=64),
            lam_c(CL_LAM), pv2, ALU.mult, ALU.add), inc=bld.s_ve)
        guard_set(bld, "pb4", (bld.s_ve, bld.s_ve.v))
        if rnd == 1:
            guard_set(bld, "pb0", (bld.s_ac, ocp_a))
            guard_set(bld, "pb1", (bld.s_ac, ocp_a))
    o0_v = bld.s_ve.v
    W8(bld, "sync", bld.s_ve, o0_v)
    E(bld, "sync", lambda e, ch=ch: e.dma_start(
        bld.dram["sc_o0"][128 * ch:128 * (ch + 1), :], bld.o0_all[:]), inc=bld.d_sy, n=16)
    guard_set(bld, "o0_all", (bld.s_sy, fence(bld, "sync")))

    for rnd in range(2):
        guard_wait(bld, "tensor", "pb4")
        for i8 in range(8):
            h = 8 * rnd + i8
            inc = bld.s_pe if i8 == 7 else None
            E(bld, "tensor", lambda e, h=h, i8=i8: e.matmul(
                bld.pb[4][:, 64 * i8:64 * (i8 + 1)],
                bld.Bt_all[:, 128 * h:128 * (h + 1)], bld.X_all[:, 128 * h + 64:128 * (h + 1)],
                start=True, stop=True), inc=inc)
        bw_pe = bld.s_pe.v
        W8(bld, "vector", bld.s_pe, bw_pe)
        if rnd == 0:
            guard_wait(bld, "vector", "Ttok")
            guard_wait(bld, "vector", "Tfm")
        ttv = bld.Ttok[:].rearrange("p (h x) -> p h x", x=64)[:, 8 * rnd:8 * rnd + 8]
        qv = xa_t[:].rearrange("p (h x) -> p h x", x=64)[:, 8 * rnd:8 * rnd + 8]
        pv2 = bld.pb[4][:].rearrange("p (h x) -> p h x", x=64)
        E(bld, "vector", lambda e, ttv=ttv, qv=qv, pv2=pv2: e.scalar_tensor_tensor(
            ttv, qv, lam_c(CL_LAM), pv2, ALU.mult, ALU.subtract), inc=bld.s_ve)
        guard_set(bld, "pb4", (bld.s_ve, bld.s_ve.v))
    tt_v = bld.s_ve.v
    guard_add(bld, f"xa{slot}", bld.s_ve, tt_v)
    for rnd in range(2):
        guard_wait(bld, "tensor", "pbt")
        W8(bld, "tensor", bld.s_ve, tt_v)
        for pi in range(4):
            pj = 4 * rnd + pi
            for par in range(2):
                h = 2 * pj + par
                inc = bld.s_pe if (pi == 3 and par == 1) else None
                E(bld, "tensor", lambda e, h=h, pi=pi, par=par: e.transpose(
                    bld.pbt[64 * par:64 * (par + 1), 128 * pi:128 * (pi + 1)],
                    bld.Ttok[:, 64 * h:64 * (h + 1)], bld.ident_bf[:],
                    tile_position=(0, 64 * par)), inc=inc)
        tf_pe = bld.s_pe.v
        W8(bld, "vector", bld.s_pe, tf_pe)
        E(bld, "vector", lambda e, rnd=rnd: e.tensor_copy(
            bld.Tfm[:, 512 * rnd:512 * (rnd + 1)], bld.pbt[:, 0:512]), inc=bld.s_ve)
        guard_set(bld, "pbt", (bld.s_ve, bld.s_ve.v))
    guard_set(bld, "Ttok", (bld.s_pe, bld.s_pe.v))
    tfm_v = bld.s_ve.v
    W8(bld, "sync", bld.s_ve, tfm_v)
    E(bld, "sync", lambda e, ch=ch: e.dma_start(bld.dram["sc_T"][ch], bld.Tfm[:]),
      inc=bld.d_sy, n=16)

    guard_wait(bld, "sync", "P_write")
    E(bld, "sync", lambda e, ch=ch: e.dma_start(bld.dram["sc_P"][ch], bld.P_all[:]),
      inc=bld.d_sy, n=16)
    psnap = fence(bld, "sync")

    guard_wait(bld, "tensor", "pb6")
    W8(bld, "tensor", bld.s_ve, tfm_v)
    for pj in range(8):
        for par in range(2):
            h = 2 * pj + par
            inc = bld.s_pe if (pj == 7 and par == 1) else None
            E(bld, "tensor", lambda e, h=h, pj=pj, par=par: e.matmul(
                bld.pb[6][64 * par:64 * (par + 1), 64 * pj:64 * (pj + 1)],
                bld.X_all[:, 128 * h + 64:128 * (h + 1)], bld.LbK[:, 64 * h:64 * (h + 1)],
                start=True, stop=True, tile_position=(0, 64 * par)), inc=inc)
    g_pe = bld.s_pe.v
    W8(bld, "vector", bld.s_pe, g_pe)
    E(bld, "vector", lambda e: e.scalar_tensor_tensor(
        bld.G_all[:].rearrange("p (h x) -> p h x", x=64),
        bld.i64_bf[:].unsqueeze(1).broadcast_to([128, 8, 64]),
        lam_c(CL_LAMC), bld.pb[6][:].rearrange("p (h x) -> p h x", x=64),
        ALU.mult, ALU.subtract), inc=bld.s_ve)
    g_v = bld.s_ve.v
    guard_set(bld, "pb6", (bld.s_ve, g_v))

    guard_wait(bld, "tensor", "pb6")
    for pj in range(8):
        for par in range(2):
            h = 2 * pj + par
            E(bld, "tensor", lambda e, h=h, pj=pj, par=par: e.matmul(
                bld.pb[6][64 * par:64 * (par + 1), 64 * pj:64 * (pj + 1)],
                bld.U_all[:, 64 * h:64 * (h + 1)], bld.LbK[:, 64 * h:64 * (h + 1)],
                start=True, stop=True, tile_position=(0, 64 * par)))
    guard_wait(bld, "tensor", "pb5")
    for pj in range(8):
        for par in range(2):
            h = 2 * pj + par
            inc = bld.s_pe if (pj == 7 and par == 1) else None
            E(bld, "tensor", lambda e, h=h, pj=pj, par=par: e.matmul(
                bld.pb[5][64 * par:64 * (par + 1), 64 * pj:64 * (pj + 1)],
                bld.LbK[:, 64 * h:64 * (h + 1)], bld.U_all[:, 64 * h:64 * (h + 1)],
                start=True, stop=True, tile_position=(0, 64 * par)), inc=inc)
    mm_pe = bld.s_pe.v
    for r in mtbf_readers_pe:
        pass
    W8(bld, "vector", bld.s_pe, mm_pe)
    E(bld, "vector", lambda e: e.scalar_tensor_tensor(
        bld.M_all[:], bld.M_all[:], lam_c(CL_LAMC), bld.pb[6][:], ALU.mult, ALU.add))
    E(bld, "vector", lambda e: e.scalar_tensor_tensor(
        bld.MT_all[:], bld.MT_all[:], lam_c(CL_LAMC), bld.pb[5][:], ALU.mult, ALU.add),
      inc=bld.s_ve)
    m_v = bld.s_ve.v
    guard_wait(bld, "vector", "MTbf_w")
    E(bld, "vector", lambda e: e.tensor_copy(bld.MTbf[:], bld.MT_all[:]), inc=bld.s_ve)
    guard_set(bld, "pb5", (bld.s_ve, bld.s_ve.v))
    guard_set(bld, "pb6", (bld.s_ve, bld.s_ve.v))

    W8(bld, "tensor", bld.s_ve, g_v)
    W8(bld, "tensor", bld.s_sy, psnap)
    guard_wait(bld, "tensor", "pb3")
    for pj in range(8):
        for par in range(2):
            E(bld, "tensor", lambda e, pj=pj, par=par: e.matmul(
                bld.pb[3][64 * par:64 * (par + 1), 64 * pj:64 * (pj + 1)],
                bld.PT_all[64 * par:64 * (par + 1), 64 * pj:64 * (pj + 1)],
                bld.G_all[64 * par:64 * (par + 1), 64 * pj:64 * (pj + 1)],
                start=True, stop=True, tile_position=(64 * par, 64 * par)))
    guard_wait(bld, "tensor", "pb2")
    for pj in range(8):
        for par in range(2):
            inc = bld.s_pe if (pj == 7 and par == 1) else None
            E(bld, "tensor", lambda e, pj=pj, par=par: e.matmul(
                bld.pb[2][64 * par:64 * (par + 1), 64 * pj:64 * (pj + 1)],
                bld.G_all[64 * par:64 * (par + 1), 64 * pj:64 * (pj + 1)],
                bld.PT_all[64 * par:64 * (par + 1), 64 * pj:64 * (pj + 1)],
                start=True, stop=True, tile_position=(64 * par, 64 * par)), inc=inc)
    p_pe = bld.s_pe.v
    W8(bld, "vector", bld.s_pe, p_pe)
    E(bld, "vector", lambda e: e.tensor_copy(bld.P_all[:], bld.pb[3][:]))
    E(bld, "vector", lambda e: e.tensor_copy(bld.PT_all[:], bld.pb[2][:]), inc=bld.s_ve)
    p_v = bld.s_ve.v
    guard_set(bld, "pb2", (bld.s_ve, p_v))
    guard_set(bld, "pb3", (bld.s_ve, p_v))
    guard_set(bld, "P_write", (bld.s_ve, p_v))
    guard_set(bld, "Wfm", (bld.s_pe, mm_pe))
    guard_set(bld, "U_all", (bld.s_pe, mm_pe))
    guard_set(bld, "LbK", (bld.s_pe, mm_pe))
    guard_set(bld, "X_all", (bld.s_pe, g_pe))
    guard_set(bld, "NBt", (bld.s_pe, g_pe))
    guard_set(bld, "MTbf_w", (bld.s_pe, mm_pe))
    guard_set(bld, "Tfm", (bld.s_sy, psnap))


def emit_dn_ag(bld: Build):
    p = bld.prog
    W8(bld, "sync", bld.s_ve, bld.s_ve.v)
    E(bld, "sync", lambda e: e.dma_start(bld.dram["cc_m_in"][:, 0:512], bld.M_all[:]),
      inc=bld.d_sy, n=16)
    E(bld, "sync", lambda e: e.dma_start(bld.dram["cc_m_in"][:, 512:1024], bld.MT_all[:]),
      inc=bld.d_sy, n=16)
    ag_in = fence(bld, "sync")
    W8(bld, "gpsimd", bld.s_sy, ag_in)
    cc = E(bld, "gpsimd", lambda e: e.collective_compute(
        "AllGather", ALU.bypass, replica_groups=[[0, 1], [2, 3], [4, 5], [6, 7]],
        ins=[bld.dram["cc_m_in"].opt()], outs=[bld.dram["cc_m_out"].opt()]), inc=bld.s_cc)
    W8(bld, "sync", bld.s_cc, cc)
    guard_wait(bld, "sync", "xrow0")
    E(bld, "sync", lambda e: e.dma_start(bld.xrow[0][:], bld.dram["cc_m_out"][0:128, :]),
      inc=bld.d_sy, n=16)
    ld = fence(bld, "sync")
    W8(bld, "vector", bld.s_sy, ld)
    E(bld, "vector", lambda e: e.tensor_scalar_mul(bld.Minit[:], bld.xrow[0][:, 0:512],
                                                   bld.dn_cols[:, CL_SEL:CL_SEL + 1]))
    E(bld, "vector", lambda e: e.tensor_scalar_mul(bld.MinitT[:], bld.xrow[0][:, 512:1024],
                                                   bld.dn_cols[:, CL_SEL:CL_SEL + 1]))
    E(bld, "vector", lambda e: e.tensor_copy(bld.MinitTbf[:], bld.MinitT[:]), inc=bld.s_ve)


def emit_dn_corr(bld: Build, ch):
    p = bld.prog
    guard_wait(bld, "sync", "corr_ld")
    E(bld, "sync", lambda e, ch=ch: e.dma_start(bld.Pld[:], bld.dram["sc_P"][ch]),
      inc=bld.d_sy, n=16)
    E(bld, "sync", lambda e, ch=ch: e.dma_start(bld.Tld[:], bld.dram["sc_T"][ch]),
      inc=bld.d_sy, n=16)
    E(bld, "sync", lambda e, ch=ch: e.dma_start(
        bld.o0ld[:], bld.dram["sc_o0"][128 * ch:128 * (ch + 1), :]), inc=bld.d_sy, n=16)
    l3 = fence(bld, "sync")
    guard_wait(bld, "tensor", "pb6")
    W8(bld, "tensor", bld.s_sy, l3)
    W8(bld, "tensor", bld.s_ve, bld.s_ve.v)
    for pj in range(8):
        for par in range(2):
            inc = bld.s_pe if (pj == 7 and par == 1) else None
            E(bld, "tensor", lambda e, pj=pj, par=par: e.matmul(
                bld.pb[6][64 * par:64 * (par + 1), 64 * pj:64 * (pj + 1)],
                bld.Pld[64 * par:64 * (par + 1), 64 * pj:64 * (pj + 1)],
                bld.MinitT[64 * par:64 * (par + 1), 64 * pj:64 * (pj + 1)],
                start=True, stop=True, tile_position=(64 * par, 64 * par)), inc=inc)
    dm_pe = bld.s_pe.v
    W8(bld, "vector", bld.s_pe, dm_pe)
    guard_wait(bld, "vector", "dMbf")
    dm_v = E(bld, "vector", lambda e: e.tensor_copy(bld.dMbf[:], bld.pb[6][:]), inc=bld.s_ve)
    guard_set(bld, "pb6", (bld.s_ve, dm_v))
    for par in range(2):
        bank = 2 if par == 0 else 3
        guard_wait(bld, "tensor", f"pb{bank}")
        W8(bld, "tensor", bld.s_ve, dm_v)
        W8(bld, "tensor", bld.s_sy, l3)
        for pj in range(8):
            inc = bld.s_pe if pj == 7 else None
            E(bld, "tensor", lambda e, pj=pj, par=par, bank=bank: e.matmul(
                bld.pb[bank][:, 64 * pj:64 * (pj + 1)],
                bld.Tld[64 * par:64 * (par + 1), 128 * pj:128 * (pj + 1)],
                bld.dMbf[64 * par:64 * (par + 1), 64 * pj:64 * (pj + 1)],
                start=True, stop=True, tile_position=(64 * par, 0)), inc=inc)
    do_pe = bld.s_pe.v
    guard_set(bld, "dMbf", (bld.s_pe, do_pe))
    W8(bld, "vector", bld.s_pe, do_pe)
    W8(bld, "vector", bld.s_sy, l3)
    guard_wait(bld, "vector", "o0_all")
    for par in range(2):
        bank = 2 if par == 0 else 3
        ov = bld.o0_all[:].rearrange("p (hh pp x) -> p hh pp x", pp=2, x=64)[:, :, par, :]
        iv = bld.o0ld[:].rearrange("p (hh pp x) -> p hh pp x", pp=2, x=64)[:, :, par, :]
        pv = bld.pb[bank][:].rearrange("p (h x) -> p h x", x=64)
        inc = bld.s_ve if par == 1 else None
        E(bld, "vector", lambda e, ov=ov, iv=iv, pv=pv: e.tensor_tensor(ov, iv, pv, ALU.add),
          inc=inc)
    of_v = bld.s_ve.v
    guard_set(bld, "pb2", (bld.s_ve, of_v))
    guard_set(bld, "pb3", (bld.s_ve, of_v))
    W8(bld, "sync", bld.s_ve, of_v)
    E(bld, "sync", lambda e, ch=ch: e.dma_start(
        bld.dram["sc_ofin"][128 * ch:128 * (ch + 1), :], bld.o0_all[:]), inc=bld.d_sy, n=16)
    st = fence(bld, "sync")
    guard_set(bld, "o0_all", (bld.s_sy, st))
    guard_set(bld, "corr_ld", (bld.s_pe, do_pe), (bld.s_ve, of_v))


def emit_s_tile(bld: Build, ti):
    p = bld.prog
    guard_wait(bld, "sync", "s_ld")
    E(bld, "sync", lambda e, ti=ti: e.dma_start(
        bld.st_ld[:], bld.dram["sc_static"][128 * ti:128 * (ti + 1), :]), inc=bld.d_sy, n=16)
    E(bld, "sync", lambda e, ti=ti: e.dma_start(
        bld.of_ld[:], bld.dram["sc_ofin"][128 * ti:128 * (ti + 1), :]), inc=bld.d_sy, n=16)
    l2 = fence(bld, "sync")
    W8(bld, "vector", bld.s_sy, l2)
    guard_wait(bld, "vector", "yrow")
    c = bld.cols
    cb = 12
    E(bld, "vector", lambda e: e.tensor_add(bld.yrow[:], bld.st_ld[:], bld.of_ld[:]))
    E(bld, "vector", lambda e: e.reduce_sum(c[:, cb + 0:cb + 1], bld.yrow[:], axis=AX.X),
      inc=bld.s_ve)
    y_v = bld.s_ve.v
    W8(bld, "scalar", bld.s_ve, y_v)
    guard_wait(bld, "scalar", "x2")
    sq_a = E(bld, "scalar", lambda e: e.activation(bld.x2[:], bld.yrow[:], AF.Square), inc=bld.s_ac)
    W8(bld, "vector", bld.s_ac, sq_a)
    x2r = E(bld, "vector", lambda e: e.reduce_sum(c[:, cb + 1:cb + 2], bld.x2[:], axis=AX.X))
    guard_set(bld, "x2", (bld.s_ve, x2r))
    E(bld, "vector", lambda e: e.tensor_scalar_mul(c[:, cb + 2:cb + 3], c[:, cb + 0:cb + 1], 1.0 / D))
    E(bld, "vector", lambda e: e.tensor_scalar_mul(c[:, cb + 3:cb + 4], c[:, cb + 2:cb + 3], -1.0))
    E(bld, "vector", lambda e: e.tensor_scalar_mul(c[:, cb + 4:cb + 5], c[:, cb + 1:cb + 2], 1.0 / D))
    E(bld, "vector", lambda e: e.tensor_mul(c[:, cb + 5:cb + 6], c[:, cb + 2:cb + 3], c[:, cb + 2:cb + 3]))
    E(bld, "vector", lambda e: e.tensor_sub(c[:, cb + 6:cb + 7], c[:, cb + 4:cb + 5], c[:, cb + 5:cb + 6]))
    E(bld, "vector", lambda e: e.tensor_scalar_add(c[:, cb + 6:cb + 7], c[:, cb + 6:cb + 7], LN_EPS),
      inc=bld.s_ve)
    W8(bld, "scalar", bld.s_ve, bld.s_ve.v)
    sr_a = E(bld, "scalar", lambda e: e.activation(c[:, cb + 7:cb + 8], c[:, cb + 6:cb + 7], AF.Sqrt),
             inc=bld.s_ac)
    W8(bld, "vector", bld.s_ac, sr_a)
    E(bld, "vector", lambda e: e.reciprocal(c[:, cb + 8:cb + 9], c[:, cb + 7:cb + 8]))
    E(bld, "vector", lambda e: e.tensor_scalar(bld.yrow[:], bld.yrow[:], c[:, cb + 3:cb + 4],
                                               c[:, cb + 8:cb + 9], ALU.add, ALU.mult))
    E(bld, "vector", lambda e: e.tensor_mul(bld.yrow[:], bld.yrow[:], bld.gamma_b[:]))
    guard_wait(bld, "vector", "s_bf")
    sv = E(bld, "vector", lambda e: e.tensor_add(bld.s_bf[:], bld.yrow[:], bld.beta_b[:]),
           inc=bld.s_ve)
    guard_set(bld, "yrow", (bld.s_ve, sv))
    guard_set(bld, "s_ld", (bld.s_ve, sv))
    W8(bld, "sync", bld.s_ve, sv)
    E(bld, "sync", lambda e, ti=ti: e.dma_start(
        bld.dram["sc_s"][128 * ti:128 * (ti + 1), :], bld.s_bf[:]), inc=bld.d_sy, n=16)
    st = fence(bld, "sync")
    guard_set(bld, "s_bf", (bld.s_sy, st))


def emit_dn_phase(bld: Build, n_tiles=NT, n_chunks=NCH):
    for tile in range(n_tiles):
        emit_tile_prep(bld, tile)
        if 4 <= tile < 4 + n_chunks:
            emit_dn_chunk(bld, tile - 4)
    emit_dn_ag(bld)
    for ch in range(n_chunks):
        emit_dn_corr(bld, ch)
    for ti in range(n_chunks):
        emit_s_tile(bld, ti)


def finish_program(bld: Build):
    for q, s in (("sync", bld.d_sy), ("gpsimd", bld.d_gp), ("scalar", bld.d_sc)):
        if s.v:
            W8(bld, q, s, s.v)


def build_dn_standalone(n_tiles=NT, n_chunks=NCH):
    bld = Build(with_transformer=False)
    build_common(bld)
    emit_setup(bld)
    emit_dn_phase(bld, n_tiles, n_chunks)
    sdone = fence(bld, "sync")
    W8(bld, "gpsimd", bld.s_sy, sdone)
    nrows = 128 * n_chunks
    step = 128
    for i in range(0, nrows, step):
        E(bld, "gpsimd", lambda e, i=i: e.dma_start(
            bld.dram["out"][i:i + step, :],
            bld.dram["sc_s"][i:i + step, :]), inc=bld.d_gp, n=16)
    finish_program(bld)
    with bld.es, bld.nc.Block() as blk:
        bld.prog.play(blk)
    return bld


def core_input_map(inputs, core_id):
    b, half = core_id // 2, core_id % 2
    gstart = 4096 * half - HALO
    x = np.asarray(inputs["L0_attn_x"][b], np.float32)
    xloc = np.zeros((TLOC, D), np.float32)
    lo = max(0, gstart)
    xloc[lo - gstart:] = x[lo:gstart + TLOC]
    m = dict(_host_consts(core_id))
    m["x_in"] = xloc
    m["Wt_d"] = np.ascontiguousarray(np.asarray(inputs["L0_attn_W"], np.float32).T)
    m["ln_gamma"] = np.asarray(inputs["L0_attn_ln_gamma"], np.float32).reshape(1, D)
    m["ln_beta"] = np.asarray(inputs["L0_attn_ln_beta"], np.float32).reshape(1, D)
    m["eta"] = np.asarray(inputs["L0_attn_eta"], np.float32).reshape(1, 1)
    m["lam_logit"] = np.asarray(inputs["L0_attn_lam_logit"], np.float32).reshape(1, 1)
    m["Wqkv"] = np.asarray(inputs["L0_ffn_Wqkv"], np.float32)
    m["Wo"] = np.asarray(inputs["L0_ffn_Wo"], np.float32)
    m["W1"] = np.asarray(inputs["L0_ffn_W1"], np.float32)
    m["W2"] = np.asarray(inputs["L0_ffn_W2"], np.float32)
    b1 = np.asarray(inputs["L0_ffn_b1"], np.float32)
    b1a = np.pad(b1[:, :DIM_INNER], ((0, 0), (0, 22 * 128 - DIM_INNER))).reshape(NLAYER, 22, 128)
    b1g = np.pad(b1[:, DIM_INNER:], ((0, 0), (0, 22 * 128 - DIM_INNER))).reshape(NLAYER, 22, 128)
    m["b1cols"] = np.ascontiguousarray(np.stack([b1a, b1g], axis=1).reshape(NLAYER, 2, 22, 128, 1))
    m["b2cols"] = np.ascontiguousarray(
        np.asarray(inputs["L0_ffn_b2"], np.float32).reshape(NLAYER, 8, 128, 1))
    m["ffn_g"] = np.ascontiguousarray(
        np.asarray(inputs["L0_ffn_ln_g"], np.float32).reshape(NLAYER, 8, 128, 1))
    m["ffn_b"] = np.ascontiguousarray(
        np.asarray(inputs["L0_ffn_ln_b"], np.float32).reshape(NLAYER, 8, 128, 1))
    qs2 = np.tile(np.asarray(inputs["L0_ffn_q_scale"], np.float32), (1, 2))
    ks2 = np.tile(np.asarray(inputs["L0_ffn_k_scale"], np.float32), (1, 2))
    m["qk_scale"] = np.ascontiguousarray(
        np.stack([qs2, ks2], axis=1).reshape(NLAYER, 2, 128, 1))
    m["ones_col128"] = np.ones((128, 1), np.float32)
    return m


def phase_barrier(bld: Build):
    for q in ("sync", "scalar", "gpsimd"):
        fence(bld, q)
    sems = (bld.s_pe, bld.s_ve, bld.s_ac, bld.s_gp, bld.s_sy, bld.s_sg, bld.s_gq, bld.s_cc)
    for eng in ENGINES:
        for s in sems:
            if s.v:
                W8(bld, eng, s, s.v)


def tf_alloc(bld: Build):
    s = lambda n, sh, dt: bld.sbuf(n, sh, dt, scope=bld.es_tf)
    bld.wqk_buf = s("wqk_buf", [128, 8 * 512], F16)
    bld.wv_buf = s("wv_buf", [128, 8 * 512], F16)
    bld.wo_sb = s("wo_sb", [128, 8 * 1024], F16)
    bld.masks_sb = s("masks_sb", [128, 3 * 768], BF16)
    bld.i64h = s("i64h", [128, 64], BF16)
    bld.co_sb = s("co_sb", [128, 8 * 16], BF16)
    bld.coT_sb = s("coT_sb", [16, 8 * 128], BF16)
    bld.lcols = s("lcols", [128, 64], F32)
    bld.b1c = s("b1c", [128, 44], F32)
    bld.qbuf = s("qbuf", [128, 8 * 512], BF16)
    bld.kbuf = s("kbuf", [128, 8 * 768], BF16)
    bld.vbuf = s("vbuf", [128, 6 * 1024], F16)
    bld.rotA = s("rotA", [128, 512], BF16)
    bld.rotB = s("rotB", [128, 512], BF16)

    bld.q2t = s("q2t", [128, 512], BF16)
    bld.q2t2 = s("q2t2", [128, 512], BF16)
    bld.nrm = s("nrm", [16, 512], BF16)
    bld.Pbuf = s("Pbuf", [128, 4 * 384], BF16)
    bld.PTb = [s(f"PTb{i}", [128, 128], F16) for i in range(12)]
    bld.ofm = s("ofm", [128, 8 * 128], F16)
    bld.m_sb = s("m_sb", [128, 22 * 512], BF16)
    bld.xn_t = bld.qbuf
    bld.w1b = [s(f"w1b{i}", [128, 8 * 256], BF16) for i in range(2)]
    bld.w2b = [s(f"w2b{i}", [128, 1024], BF16) for i in range(2)]
    bld.ones16 = s("ones16", [128, 1], F16)
    bld.hsave = s("hsave", [128, 8 * 9], F16)
    bld.hdelta = s("hdelta", [128, 8], F16)


def emit_tf_setup(bld: Build):
    d = bld.dram
    E(bld, "gpsimd", lambda e: e.dma_start(
        bld.masks_sb[:].rearrange("p (s n) -> p s n", n=768),
        d["attn_masks"].rearrange("s p n -> p s n")), inc=bld.d_gp, n=16)
    E(bld, "gpsimd", lambda e: e.dma_start(bld.i64h[:], d["ident64h"]), inc=bld.d_gp, n=16)
    E(bld, "gpsimd", lambda e: e.dma_start(
        bld.co_sb[:].rearrange("p (j n) -> p j n", n=16),
        d["chunk_ones"].rearrange("j p n -> p j n")), inc=bld.d_gp, n=16)
    E(bld, "gpsimd", lambda e: e.dma_start(bld.ones16[:], d["ones_col128"]), inc=bld.d_gp, n=16)
    E(bld, "gpsimd", lambda e: e.memset(bld.kbuf[:], 0.0), inc=bld.s_gp)
    E(bld, "gpsimd", lambda e: e.memset(bld.vbuf[:], 0.0), inc=bld.s_gp)
    E(bld, "gpsimd", lambda e: e.dma_start(
        bld.coT_sb[:].rearrange("p (j n) -> p j n", n=128),
        d["chunk_onesT"].rearrange("j p n -> p j n")), inc=bld.d_gp, n=16)
    gq = fence(bld, "gpsimd")
    sy = fence(bld, "sync")
    for eng in ("tensor", "vector", "scalar"):
        W8(bld, eng, bld.s_gq, gq)
        W8(bld, eng, bld.s_sy, sy)
        W8(bld, eng, bld.s_gp, bld.s_gp.v)


LQ_QS, LQ_KS = 0, 1
LQ_G0 = 2
LQ_B0 = 10
LQ_B2 = 18


def emit_layer_loads(bld: Build, l):
    d = bld.dram

    guard_wait(bld, "gpsimd", "wo")
    for vc in range(8):
        bld.wo_tot = E(bld, "gpsimd", lambda e, vc=vc, l=l: e.dma_start(
            bld.wo_sb[:, 1024 * vc:1024 * (vc + 1)],
            d["Wo"][l][128 * vc:128 * (vc + 1), :]), inc=bld.d_gp, n=16)
    guard_wait(bld, "sync", "lcols")
    E(bld, "sync", lambda e, l=l: e.dma_start(bld.lcols[:, 0:1], d["qk_scale"][l][0]), inc=bld.d_sy, n=16)
    E(bld, "sync", lambda e, l=l: e.dma_start(bld.lcols[:, 1:2], d["qk_scale"][l][1]), inc=bld.d_sy, n=16)
    E(bld, "sync", lambda e, l=l: e.dma_start(
        bld.lcols[:, LQ_G0:LQ_G0 + 8].rearrange("p (c o) -> p c o", o=1),
        d["ffn_g"][l].rearrange("c p o -> p c o"), allow_slow_non_contiguous=True), inc=bld.d_sy, n=16)
    E(bld, "sync", lambda e, l=l: e.dma_start(
        bld.lcols[:, LQ_B0:LQ_B0 + 8].rearrange("p (c o) -> p c o", o=1),
        d["ffn_b"][l].rearrange("c p o -> p c o"), allow_slow_non_contiguous=True), inc=bld.d_sy, n=16)
    E(bld, "sync", lambda e, l=l: e.dma_start(
        bld.lcols[:, LQ_B2:LQ_B2 + 8].rearrange("p (c o) -> p c o", o=1),
        d["b2cols"][l].rearrange("c p o -> p c o"), allow_slow_non_contiguous=True), inc=bld.d_sy, n=16)
    E(bld, "sync", lambda e, l=l: e.dma_start(
        bld.b1c[:, 0:44].rearrange("p (s c o) -> p s c o", s=2, o=1),
        d["b1cols"][l].rearrange("s c p o -> p s c o"), allow_slow_non_contiguous=True), inc=bld.d_sy, n=16)
    return fence(bld, "sync")


def emit_halo_ag(bld: Build, l):
    d = bld.dram
    hv = bld.h[:].rearrange("p (c t) -> p c t", t=TLOC)
    W8(bld, "sync", bld.s_ve, bld.s_ve.v)
    E(bld, "sync", lambda e, hv=hv: e.dma_start(
        d["cc_h_in"].rearrange("(c p) t -> p c t", p=128), hv[:, :, TLOC - HALO:TLOC]),
      inc=bld.d_sy, n=16)
    agi = fence(bld, "sync")
    W8(bld, "gpsimd", bld.s_sy, agi)
    cc = E(bld, "gpsimd", lambda e: e.collective_compute(
        "AllGather", ALU.bypass, replica_groups=[[0, 1], [2, 3], [4, 5], [6, 7]],
        ins=[d["cc_h_in"].opt()], outs=[d["cc_h_out"].opt()]), inc=bld.s_cc)
    W8(bld, "sync", bld.s_cc, cc)
    guard_wait(bld, "sync", "vbuf")
    W8(bld, "sync", bld.s_pe, bld.s_pe.v)
    E(bld, "sync", lambda e: e.dma_start(
        bld.vbuf[:, 0:4096].rearrange("p (c t) -> p c t", t=512),
        d["cc_h_out"][0:1024, :].rearrange("(c p) t -> p c t", p=128)), inc=bld.d_sy, n=16)
    ld = fence(bld, "sync")
    W8(bld, "vector", bld.s_sy, ld)
    W8(bld, "vector", bld.s_pe, bld.s_pe.v)
    for kc in range(8):
        E(bld, "vector", lambda e, kc=kc, hv=hv: e.tensor_scalar_mul(
            hv[:, kc, 0:HALO], bld.vbuf[:, 512 * kc:512 * kc + HALO],
            bld.dn_cols[:, CL_SEL:CL_SEL + 1]))
    hf = E(bld, "vector", lambda e, hv=hv: e.tensor_copy(hv[:, 0, 0:1], hv[:, 0, 0:1]))
    guard_set(bld, "vbuf", (bld.s_ve, hf))
    return hf


def emit_attn_chunk(bld: Build, l, ci, pv_carry):
    d = bld.dram
    t0 = 512 * ci
    width = min(512, TLOC - t0)
    hv = bld.h[:].rearrange("p (c t) -> p c t", t=TLOC)


    guard_wait(bld, "vector", "ring")
    for kc in range(8):
        E(bld, "vector", lambda e, kc=kc: e.tensor_copy(
            bld.kbuf[:, 768 * kc:768 * kc + 256], bld.kbuf[:, 768 * kc + 512:768 * kc + 768]))
    for i in range(2):
        E(bld, "vector", lambda e, i=i: e.tensor_copy(
            bld.vbuf[:, 1024 * i:1024 * (i + 1)], bld.vbuf[:, 1024 * (i + 4):1024 * (i + 5)]))
    ring_v = bld.s_ve.v

    if ci >= 1:
        E(bld, "vector", lambda e, ci=ci, t0=t0: e.tensor_sub(
            bld.hdelta[:], bld.hsave[:].rearrange("p (c i) -> p c i", i=9)[:, :, ci - 1],
            bld.h[:].rearrange("p (c t) -> p c t", t=TLOC)[:, :, t0 - 1]), inc=bld.s_ve)
    delta_v = bld.s_ve.v

    nrm_bank = bld.pb[6]
    qkv_bufs = (bld.wqk_buf, bld.wv_buf)
    sq_v = {}
    tc_q = (ci == 0)

    def iss_qkv(idx, col0):
        p = idx % 2
        guard_wait(bld, "gpsimd", f"qkv{p}")
        pend = bld.wb_pend.pop(("qkv", p), None)
        if pend:
            W8(bld, "gpsimd", pend[0], pend[1])
        src = d["Wqkv"] if tc_q else d["sc_wq"]
        tot = E(bld, "gpsimd", lambda e, l=l, col0=col0, p=p, src=src: e.dma_start(
            qkv_bufs[p][:], src[l].rearrange("(c p) n -> p c n", p=128)
            [:, :, col0:col0 + 512]), inc=bld.d_qk[p], n=16)
        if tc_q:
            W8(bld, "scalar", bld.d_qk[p], tot)
            wt = E(bld, "scalar", lambda e, l=l, col0=col0, p=p: e.dma_start(
                d["sc_wq"][l].rearrange("(c p) n -> p c n", p=128)[:, :, col0:col0 + 512],
                qkv_bufs[p][:].rearrange("p (c n) -> p c n", n=512)), inc=bld.d_wbq[p], n=16)
            bld.wb_pend[("qkv", p)] = (bld.d_wbq[p], wt)
        return tot

    for side, base_col in (("q", 0), ("k", 1024)):
        for mh in range(2):
          idx = (0 if side == "q" else 2) + mh
          qtot = iss_qkv(idx, base_col + 512 * mh)
          wbuf = qkv_bufs[idx % 2]
          W8(bld, "tensor", bld.d_qk[idx % 2], qtot)
          for m4 in range(4):
            m = 4 * mh + m4
            bank = bld.pb[m % 2]
            guard_wait(bld, "tensor", f"pb{m % 2}")
            if m == 0 and mh == 0 and side == "q":
                W8(bld, "tensor", bld.s_ve, ring_v)
            corr = ci >= 1
            if corr and m == 0:
                W8(bld, "tensor", bld.s_ve, delta_v)
            for kc in range(8):
                off = (1 if kc < 4 else 0)
                last = (kc == 7) and not corr
                inc = bld.s_pe if last else None
                E(bld, "tensor", lambda e, kc=kc, m4=m4, off=off, t0=t0, w=width,
                    bank=bank, last=last, wbuf=wbuf: e.matmul(
                    bank[:, 0:w], wbuf[:, 512 * kc + 128 * m4:512 * kc + 128 * (m4 + 1)],
                    bld.h[:, TLOC * kc + t0 - 1 + off: TLOC * kc + t0 - 1 + off + w],
                    start=(kc == 0), stop=last), inc=inc)
            if corr:
                for kc in range(4, 8):
                    last = kc == 7
                    inc = bld.s_pe if last else None
                    E(bld, "tensor", lambda e, kc=kc, m4=m4, bank=bank, last=last, wbuf=wbuf: e.matmul(
                        bank[:, 0:1], wbuf[:, 512 * kc + 128 * m4:512 * kc + 128 * (m4 + 1)],
                        bld.hdelta[:, kc:kc + 1], start=False, stop=last), inc=inc)
            mm_pe = bld.s_pe.v
            W8(bld, "vector", bld.s_pe, mm_pe)
            if side == "q":
                dst = bld.qbuf[:, 512 * m:512 * m + width]
            else:
                dst = bld.kbuf[:, 768 * m + 256:768 * m + 256 + width]
            ev = E(bld, "vector", lambda e, dst=dst, bank=bank, w=width: e.tensor_copy(
                dst, bank[:, 0:w]))
            W8(bld, "scalar", bld.s_ve, ev)
            slot = bld.q2t if m % 2 == 0 else bld.q2t2
            guard_wait(bld, "scalar", f"q2t{m % 2}")
            sq_v[m] = E(bld, "scalar", lambda e, dst=dst, w=width, slot=slot: e.activation(
                slot[:, 0:w], dst, AF.Square), inc=bld.s_ac)

            def nmm(mm, last):
                slot2 = bld.q2t if mm % 2 == 0 else bld.q2t2
                W8(bld, "tensor", bld.s_ac, sq_v[mm])
                if mm == 0:
                    guard_wait(bld, "tensor", "pb6")
                E(bld, "tensor", lambda e, mm=mm, w=width, slot2=slot2: e.matmul(
                    nrm_bank[0:16, 0:w], bld.co_sb[:, 16 * mm:16 * (mm + 1)], slot2[:, 0:w],
                    start=(mm == 0), stop=last), inc=(bld.s_pe if last else None))
                guard_set(bld, f"q2t{mm % 2}", (bld.s_pe, bld.s_pe.v))
            if m >= 1:
                nmm(m - 1, False)
            if m == 7:
                nmm(7, True)
            guard_set(bld, f"pb{m % 2}", (bld.s_ve, bld.s_ve.v))
          guard_set(bld, f"qkv{idx % 2}", (bld.s_pe, bld.s_pe.v))
        nq_pe = bld.s_pe.v
        W8(bld, "scalar", bld.s_pe, nq_pe)
        guard_wait(bld, "scalar", "nrm")
        sq2 = E(bld, "scalar", lambda e, w=width: e.activation(
            bld.nrm[:, 0:w], nrm_bank[0:16, 0:w], AF.Sqrt), inc=bld.s_ac)
        W8(bld, "vector", bld.s_ac, sq2)
        E(bld, "vector", lambda e, w=width: e.tensor_scalar_max(bld.nrm[:, 0:w], bld.nrm[:, 0:w], 1e-12))
        def _recip(e, w=width, bld=bld):
            with bld.nc.allow_low_precision(reason="l2norm recip in bf16 (bf16-matmul noise class)"):
                return e.reciprocal(bld.nrm[:, 0:w], bld.nrm[:, 0:w])
        rc = E(bld, "vector", _recip)
        guard_set(bld, "pb6", (bld.s_ac, sq2))
        scol = bld.lcols[:, LQ_QS:LQ_QS + 1] if side == "q" else bld.lcols[:, LQ_KS:LQ_KS + 1]
        rA, rB = bld.rotA, bld.rotB
        guard_wait(bld, "gpsimd", "rot")
        for dst, nm in ((rA, "rotAQ" if side == "q" else "rotAK"),
                        (rB, "rotBQ" if side == "q" else "rotBK")):
            rt_tot = E(bld, "gpsimd", lambda e, dst=dst, nm=nm, t0=t0, w=width: e.dma_start(
                dst[:, 0:w], d[nm][:, t0:t0 + w]), inc=bld.d_rt, n=16)
        for m in range(8):
            if side == "q":
                sl = bld.qbuf[:, 512 * m:512 * m + width]
            else:
                sl = bld.kbuf[:, 768 * m + 256:768 * m + 256 + width]
            guard_wait(bld, "tensor", "pb2")
            W8(bld, "tensor", bld.s_ve, rc)
            bc = E(bld, "tensor", lambda e, m=m, w=width: e.matmul(
                bld.pb[2][:, 0:w], bld.coT_sb[:, 128 * m:128 * (m + 1)], bld.nrm[:, 0:w],
                start=True, stop=True), inc=bld.s_pe)
            W8(bld, "vector", bld.s_pe, bc)
            E(bld, "vector", lambda e, sl=sl, scol=scol, w=width: e.scalar_tensor_tensor(
                sl, sl, scol, bld.pb[2][:, 0:w], ALU.mult, ALU.mult))
            guard_wait(bld, "tensor", "pb3")
            W8(bld, "tensor", bld.s_ve, bld.s_ve.v)
            rt = E(bld, "tensor", lambda e, sl=sl, w=width: e.matmul(
                bld.pb[3][:, 0:w], bld.rot_bf[:], sl, start=True, stop=True), inc=bld.s_pe)
            W8(bld, "vector", bld.s_pe, rt)
            W8(bld, "vector", bld.d_rt, rt_tot)
            guard_wait(bld, "vector", "q2t0")
            E(bld, "vector", lambda e, sl=sl, rA=rA, w=width: e.tensor_mul(
                bld.q2t[:, 0:w], sl, rA[:, 0:w]))
            E(bld, "vector", lambda e, sl=sl, rB=rB, w=width: e.tensor_tensor(
                sl, bld.pb[3][:, 0:w], rB[:, 0:w], ALU.mult))
            qr = E(bld, "vector", lambda e, sl=sl, w=width: e.tensor_add(sl, sl, bld.q2t[:, 0:w]))
            guard_set(bld, "pb2", (bld.s_ve, bld.s_ve.v))
            guard_set(bld, "pb3", (bld.s_ve, qr))
            guard_set(bld, "q2t0", (bld.s_ve, qr))
        guard_set(bld, "rot", (bld.s_ve, bld.s_ve.v))
    qk_done = bld.s_ve.v

    for half in range(2):
        idx = 4 + half
        vtot = iss_qkv(idx, 2048 + 512 * half)
        wbuf = qkv_bufs[idx % 2]
        W8(bld, "tensor", bld.d_qk[idx % 2], vtot)
        for tt in range(ceil_div(width, 128)):
            vt = 2 + tt
            bank = bld.pb[4 + tt % 2]
            guard_wait(bld, "tensor", f"pb{4 + tt % 2}")
            vcorr = (ci >= 1) and (tt == 0)
            for kc in range(8):
                off = (1 if kc < 4 else 0)
                last = (kc == 7) and not vcorr
                inc = bld.s_pe if last else None
                E(bld, "tensor", lambda e, kc=kc, tt=tt, off=off, t0=t0, bank=bank,
                    last=last, wbuf=wbuf: e.matmul(
                    bank[:, 0:512],
                    bld.h[:, TLOC * kc + t0 - 1 + off + 128 * tt: TLOC * kc + t0 - 1 + off + 128 * (tt + 1)],
                    wbuf[:, 512 * kc:512 * (kc + 1)],
                    start=(kc == 0), stop=last), inc=inc)
            if vcorr:
                for kc in range(4, 8):
                    last = kc == 7
                    inc = bld.s_pe if last else None
                    E(bld, "tensor", lambda e, kc=kc, bank=bank, last=last, wbuf=wbuf: e.matmul(
                        bank[0:1, 0:512], bld.hdelta[:, kc:kc + 1],
                        wbuf[:, 512 * kc:512 * (kc + 1)],
                        start=False, stop=last), inc=inc)
            v_pe = bld.s_pe.v
            W8(bld, "vector", bld.s_pe, v_pe)
            ev = E(bld, "vector", lambda e, vt=vt, hf=half, bank=bank: e.tensor_copy(
                bld.vbuf[:, 1024 * vt + 512 * hf:1024 * vt + 512 * (hf + 1)], bank[:, 0:512]))
            guard_set(bld, f"pb{4 + tt % 2}", (bld.s_ve, ev))
        guard_set(bld, f"qkv{idx % 2}", (bld.s_pe, bld.s_pe.v))
    v_done = bld.s_ve.v
    return qk_done, v_done


def emit_attn_qtile(bld: Build, l, ci, qt, qk_done, v_done):
    Q = 4 * ci + qt
    qoff = 128 * qt
    slot = {4: 1, 5: 2}.get(Q, 0)
    c = bld.cols
    pn_v, tr_v, ev_v = {}, {}, {}
    state = {"first_pv": True}

    def s_phase(h):
        par = h % 2
        pj = h // 2
        bank = bld.pb[2 + par]
        guard_wait(bld, "tensor", f"pb{2 + par}")
        if h < 2:
            W8(bld, "tensor", bld.s_ve, qk_done)
        E(bld, "tensor", lambda e, par=par, slot=slot: e.matmul(
            bld.pb[2 + par][0:64, 0:384], bld.i64h[64 * par:64 * (par + 1), :],
            bld.masks_sb[64 * par:64 * (par + 1), 768 * slot:768 * slot + 384],
            start=True, stop=False, tile_position=(64 * par, 0)))
        E(bld, "tensor", lambda e, par=par, slot=slot: e.matmul(
            bld.pb[2 + par][64:128, 0:384], bld.i64h[64 * par:64 * (par + 1), :],
            bld.masks_sb[64 * par:64 * (par + 1), 768 * slot + 384:768 * slot + 768],
            start=True, stop=False, tile_position=(64 * par, 64)))
        smm = E(bld, "tensor", lambda e, par=par, pj=pj, qoff=qoff: e.matmul(
            bld.pb[2 + par][:, 0:384],
            bld.qbuf[64 * par:64 * (par + 1), 512 * pj + qoff:512 * pj + qoff + 128],
            bld.kbuf[64 * par:64 * (par + 1), 768 * pj + qoff:768 * pj + qoff + 384],
            start=False, stop=True, tile_position=(64 * par, 0)), inc=bld.s_pe)
        W8(bld, "vector", bld.s_pe, smm)
        E(bld, "vector", lambda e, bank=bank: e.reduce_max(c[:, 12:13], bank[:, 0:384],
                                                           axis=AX.X, negate=True))
        nm = E(bld, "vector", lambda e: e.tensor_scalar_mul(c[:, 13:14], c[:, 12:13], 8.0))
        W8(bld, "scalar", bld.s_ve, nm)
        guard_wait(bld, "scalar", f"P{h % 4}")
        ex = E(bld, "scalar", lambda e, h=h, bank=bank: e.activation(
            bld.Pbuf[:, 384 * (h % 4):384 * (h % 4) + 384], bank[:, 0:384], AF.Exp,
            bias=c[:, 13:14], scale=8.0, accum_out=c[:, 14:15]), inc=bld.s_ac)
        W8(bld, "vector", bld.s_ac, ex)
        E(bld, "vector", lambda e: e.reciprocal(c[:, 15:16], c[:, 14:15]))
        pn_v[h] = E(bld, "vector", lambda e, h=h: e.tensor_scalar_mul(
            bld.Pbuf[:, 384 * (h % 4):384 * (h % 4) + 384],
            bld.Pbuf[:, 384 * (h % 4):384 * (h % 4) + 384], c[:, 15:16]))
        guard_set(bld, f"pb{2 + par}", (bld.s_ac, ex))

    def tr_phase(h):
        guard_wait(bld, "tensor", f"pb{h % 2}")
        W8(bld, "tensor", bld.s_ve, pn_v[h])
        for j in range(3):
            inc = bld.s_pe if j == 2 else None
            E(bld, "tensor", lambda e, h=h, j=j: e.transpose(
                bld.pb[h % 2][:].bitcast(BF16)[:, 128 * j:128 * (j + 1)],
                bld.Pbuf[:, 384 * (h % 4) + 128 * j:384 * (h % 4) + 128 * (j + 1)],
                bld.ident_bf[:]), inc=inc)
        tr_v[h] = bld.s_pe.v
        guard_set(bld, f"P{h % 4}", (bld.s_pe, tr_v[h]))
        W8(bld, "scalar", bld.s_pe, tr_v[h])
        for j in range(3):
            ev_v[h] = E(bld, "scalar", lambda e, h=h, j=j: e.activation(
                bld.PTb[(h % 4) * 3 + j][:],
                bld.pb[h % 2][:].bitcast(BF16)[:, 128 * j:128 * (j + 1)], AF.Copy),
                inc=bld.s_ac)
        guard_set(bld, f"pb{h % 2}", (bld.s_ac, ev_v[h]))

    def pv_phase(h):
        par = h % 2
        pj = h // 2
        bnk = 4 + (pj % 2)
        cslot = pj // 2
        if state["first_pv"]:
            guard_wait(bld, "tensor", "pb4")
            guard_wait(bld, "tensor", "pb5")
            W8(bld, "tensor", bld.s_ve, v_done)
            state["first_pv"] = False
        W8(bld, "tensor", bld.s_ac, ev_v[h])
        for kc in range(3):
            inc = bld.s_pe if kc == 2 else None
            E(bld, "tensor", lambda e, h=h, kc=kc, par=par, bnk=bnk, cslot=cslot: e.matmul(
                bld.pb[bnk][64 * par:64 * (par + 1), 128 * cslot:128 * (cslot + 1)],
                bld.vbuf[:, 1024 * (qt + kc) + 64 * h:1024 * (qt + kc) + 64 * (h + 1)],
                bld.PTb[(h % 4) * 3 + kc][:], start=(kc == 0), stop=(kc == 2),
                tile_position=(0, 64 * par)), inc=inc)

    for i in range(NH + 2):
        if i >= 2:
            tr_phase(i - 2)
        if i < NH:
            s_phase(i)
        if i >= 2:
            pv_phase(i - 2)
    pv_pe = bld.s_pe.v
    W8(bld, "vector", bld.s_pe, pv_pe)
    guard_wait(bld, "vector", "ofm")
    for b2 in range(2):
        dv = bld.ofm[:].rearrange("p (c b t) -> p c b t", b=2, t=128)[:, :, b2, :]
        E(bld, "vector", lambda e, dv=dv, b2=b2: e.tensor_copy(
            dv, bld.pb[4 + b2][:].rearrange("p (c t) -> p c t", t=128)))
    ev = bld.s_ve.v
    guard_set(bld, "pb4", (bld.s_ve, ev))
    guard_set(bld, "pb5", (bld.s_ve, ev))
    return ev


def emit_wo_qtile(bld: Build, l, ci, qt, ofm_v):
    Q = 4 * ci + qt
    t0 = 128 * Q
    for dm in range(8):
        bank = bld.pb[2 + dm % 2]
        guard_wait(bld, "tensor", f"pb{2 + dm % 2}")
        if dm == 0:
            W8(bld, "tensor", bld.s_ve, ofm_v)
            W8(bld, "tensor", bld.d_gp, bld.wo_tot)
        for vc in range(8):
            inc = bld.s_pe if vc == 7 else None
            E(bld, "tensor", lambda e, dm=dm, vc=vc, bank=bank: e.matmul(
                bank[:, 0:128], bld.wo_sb[:, 1024 * vc + 128 * dm:1024 * vc + 128 * (dm + 1)],
                bld.ofm[:, 128 * vc:128 * (vc + 1)],
                start=(vc == 0), stop=(vc == 7)), inc=inc)
        wo_pe = bld.s_pe.v
        W8(bld, "vector", bld.s_pe, wo_pe)
        rs = E(bld, "vector", lambda e, dm=dm, t0=t0, bank=bank: e.tensor_tensor(
            bld.h[:, TLOC * dm + t0:TLOC * dm + t0 + 128],
            bld.h[:, TLOC * dm + t0:TLOC * dm + t0 + 128], bank[:, 0:128], ALU.add))
        guard_set(bld, f"pb{2 + dm % 2}", (bld.s_ve, rs))
    guard_set(bld, "ofm", (bld.s_pe, bld.s_pe.v))
    return bld.s_ve.v



def emit_ffn_ttile(bld: Build, l, ti):
    d = bld.dram
    t0 = HALO + 512 * ti
    c2 = bld.x2
    NJ = 22

    w1tot, w2tot = {}, {}
    tc = (ti == 7)

    def iss_w1(jp):
        p = jp % 2
        jw = 128 if jp < 21 else DIM_INNER - 128 * 21
        buf = bld.w1b[p]
        guard_wait(bld, "gpsimd", f"w1b{p}")
        pend = bld.wb_pend.pop(("w1", p), None)
        if pend:
            W8(bld, "gpsimd", pend[0], pend[1])
        srcw = d["W1"] if tc else d["sc_w1"]
        E(bld, "gpsimd", lambda e, l=l, jp=jp, jw=jw, buf=buf, srcw=srcw: e.dma_start(
            buf[:].rearrange("p (c s j) -> p c s j", s=2, j=128)[:, :, 0, 0:jw],
            srcw[l].rearrange("(c p) n -> p c n", p=128)[:, :, 128 * jp:128 * jp + jw]),
            inc=bld.d_w1p[p], n=16)
        w1tot[jp] = E(bld, "gpsimd", lambda e, l=l, jp=jp, jw=jw, buf=buf, srcw=srcw: e.dma_start(
            buf[:].rearrange("p (c s j) -> p c s j", s=2, j=128)[:, :, 1, 0:jw],
            srcw[l].rearrange("(c p) n -> p c n", p=128)
            [:, :, DIM_INNER + 128 * jp:DIM_INNER + 128 * jp + jw]),
            inc=bld.d_w1p[p], n=16)
        if tc:
            W8(bld, "scalar", bld.d_w1p[p], w1tot[jp])
            for s2 in range(2):
                wt = E(bld, "scalar", lambda e, l=l, jp=jp, jw=jw, buf=buf, s2=s2: e.dma_start(
                    d["sc_w1"][l].rearrange("(c p) n -> p c n", p=128)
                    [:, :, DIM_INNER * s2 + 128 * jp:DIM_INNER * s2 + 128 * jp + jw],
                    buf[:].rearrange("p (c s j) -> p c s j", s=2, j=128)[:, :, s2, 0:jw]),
                    inc=bld.d_wb1[p], n=16)
            bld.wb_pend[("w1", p)] = (bld.d_wb1[p], wt)

    def iss_w2(jp):
        p = jp % 2
        jw = 128 if jp < 21 else DIM_INNER - 128 * 21
        buf = bld.w2b[p]
        qn = "gpsimd" if tc else "scalar"
        guard_wait(bld, qn, f"w2b{p}")
        pend = bld.wb_pend.pop(("w2", p), None)
        if pend:
            W8(bld, qn, pend[0], pend[1])
        if tc:
            w2tot[jp] = E(bld, "gpsimd", lambda e, l=l, jp=jp, jw=jw, buf=buf: e.dma_start(
                buf[0:jw, :], d["W2"][l][128 * jp:128 * jp + jw, :]), inc=bld.d_w2p[p], n=16)
            W8(bld, "scalar", bld.d_w2p[p], w2tot[jp])
            wt = E(bld, "scalar", lambda e, l=l, jp=jp, jw=jw, buf=buf: e.dma_start(
                d["sc_w2"][l][128 * jp:128 * jp + jw, :], buf[0:jw, :]), inc=bld.d_wb2[p], n=16)
            bld.wb_pend[("w2", p)] = (bld.d_wb2[p], wt)
        else:
            if jp < 2:
                W8(bld, "scalar", bld.d_wb2[0], bld.d_wb2[0].v)
                W8(bld, "scalar", bld.d_wb2[1], bld.d_wb2[1].v)
            w2tot[jp] = E(bld, "scalar", lambda e, l=l, jp=jp, jw=jw, buf=buf: e.dma_start(
                buf[0:jw, :], d["sc_w2"][l][128 * jp:128 * jp + jw, :]), inc=bld.d_w2p[p], n=16)

    iss_w1(0); iss_w1(1); iss_w2(0); iss_w2(1)

    guard_wait(bld, "scalar", "xn_t")
    for kc in range(8):
        off = (1 if kc < 4 else 0)
        inc = bld.s_ac if kc == 7 else None
        E(bld, "scalar", lambda e, kc=kc, off=off, t0=t0: e.activation(
            bld.xn_t[:, 512 * kc:512 * (kc + 1)],
            bld.h[:, TLOC * kc + t0 - 1 + off:TLOC * kc + t0 - 1 + off + 512], AF.Square),
            inc=inc)
    sq_a = bld.s_ac.v
    guard_wait(bld, "tensor", "pb6")
    guard_wait(bld, "tensor", "pb2")
    W8(bld, "tensor", bld.s_ac, sq_a)
    for kc in range(8):
        off = (1 if kc < 4 else 0)
        E(bld, "tensor", lambda e, kc=kc, off=off, t0=t0: e.matmul(
            bld.pb[6][0:1, 0:512], bld.ones16[:],
            bld.h[:, TLOC * kc + t0 - 1 + off:TLOC * kc + t0 - 1 + off + 512],
            start=(kc == 0), stop=(kc == 7)))
    for kc in range(8):
        inc = bld.s_pe if kc == 7 else None
        E(bld, "tensor", lambda e, kc=kc: e.matmul(
            bld.pb[2][0:1, 0:512], bld.ones16[:], bld.xn_t[:, 512 * kc:512 * (kc + 1)],
            start=(kc == 0), stop=(kc == 7)), inc=inc)
    st_pe = bld.s_pe.v
    W8(bld, "vector", bld.s_pe, st_pe)
    guard_wait(bld, "vector", "x2rows")
    A = c2[0:1, 0:512]
    Brow = c2[0:1, 512:1024]
    E(bld, "vector", lambda e: e.tensor_scalar_mul(A, bld.pb[6][0:1, 0:512], 1.0 / D))
    E(bld, "vector", lambda e: e.tensor_scalar_mul(Brow, bld.pb[2][0:1, 0:512], 1.0 / D))
    E(bld, "vector", lambda e: e.tensor_mul(bld.pb[6][0:1, 0:512], A, A))
    E(bld, "vector", lambda e: e.tensor_sub(Brow, Brow, bld.pb[6][0:1, 0:512]))
    sv = E(bld, "vector", lambda e: e.tensor_scalar_add(Brow, Brow, LN_EPS))
    W8(bld, "scalar", bld.s_ve, sv)
    sa = E(bld, "scalar", lambda e: e.activation(bld.pb[6][0:1, 0:512], Brow, AF.Sqrt),
           inc=bld.s_ac)
    W8(bld, "vector", bld.s_ac, sa)
    E(bld, "vector", lambda e: e.reciprocal(Brow, bld.pb[6][0:1, 0:512]))
    E(bld, "vector", lambda e: e.tensor_mul(A, A, Brow))
    nr = E(bld, "vector", lambda e: e.tensor_scalar_mul(A, A, -1.0))
    guard_wait(bld, "tensor", "pb3")
    W8(bld, "tensor", bld.s_ve, nr)
    E(bld, "tensor", lambda e: e.matmul(bld.pb[2][:, 0:512], bld.ones_row_f32[:],
                                        Brow, start=True, stop=True))
    bc = E(bld, "tensor", lambda e: e.matmul(bld.pb[3][:, 0:512], bld.ones_row_f32[:],
                                             A, start=True, stop=True), inc=bld.s_pe)
    guard_set(bld, "x2rows", (bld.s_pe, bc))
    W8(bld, "vector", bld.s_pe, bc)
    lnt = bld.Pbuf[:, 0:512]
    for kc in range(8):
        off = (1 if kc < 4 else 0)
        E(bld, "vector", lambda e, kc=kc, off=off, t0=t0: e.tensor_tensor(
            lnt, bld.h[:, TLOC * kc + t0 - 1 + off:TLOC * kc + t0 - 1 + off + 512],
            bld.pb[2][:, 0:512], ALU.mult))
        E(bld, "vector", lambda e: e.tensor_tensor(lnt, lnt, bld.pb[3][:, 0:512], ALU.add))
        E(bld, "vector", lambda e, kc=kc: e.tensor_scalar(
            bld.xn_t[:, 512 * kc:512 * (kc + 1)], lnt,
            bld.lcols[:, LQ_G0 + kc:LQ_G0 + kc + 1], bld.lcols[:, LQ_B0 + kc:LQ_B0 + kc + 1],
            ALU.mult, ALU.add))
    xn_v = bld.s_ve.v
    guard_set(bld, "pb2", (bld.s_ve, xn_v))
    guard_set(bld, "pb3", (bld.s_ve, xn_v))
    guard_set(bld, "xn_t", (bld.s_ve, xn_v))

    if True:
      for jp in range(NJ):
        jw = 128 if jp < 21 else DIM_INNER - 128 * 21
        buf = bld.w1b[jp % 2]
        for s2, bank in ((0, 0), (1, 1)):
            guard_wait(bld, "tensor", f"pb{bank}")
            if s2 == 0:
                W8(bld, "tensor", bld.d_w1p[jp % 2], w1tot[jp])
                if jp == 0:
                    W8(bld, "tensor", bld.s_ve, xn_v)
            for kc in range(8):
                inc = bld.s_pe if kc == 7 else None
                E(bld, "tensor", lambda e, kc=kc, s2=s2, jw=jw, bank=bank, buf=buf: e.matmul(
                    bld.pb[bank][0:jw, 0:512], buf[:, 256 * kc + 128 * s2:256 * kc + 128 * s2 + jw],
                    bld.xn_t[:, 512 * kc:512 * (kc + 1)], start=(kc == 0), stop=(kc == 7)), inc=inc)
        hg_pe = bld.s_pe.v
        guard_set(bld, f"w1b{jp % 2}", (bld.s_pe, hg_pe))
        if jp + 2 < NJ:
            iss_w1(jp + 2)
        W8(bld, "scalar", bld.s_pe, hg_pe)
        guard_wait(bld, "scalar", "Pb2")
        g1 = E(bld, "scalar", lambda e, jp=jp, jw=jw: e.activation(
            bld.Pbuf[:, 512:512 + 512][0:jw, :], bld.pb[1][0:jw, 0:512], AF.Gelu,
            bias=bld.b1c[0:jw, 22 + jp:23 + jp]), inc=bld.s_ac)
        W8(bld, "vector", bld.s_ac, g1)
        guard_wait(bld, "vector", f"m{jp % 2}")
        mv = E(bld, "vector", lambda e, jp=jp, jw=jw: e.scalar_tensor_tensor(
            bld.m_sb[0:jw, 512 * jp:512 * (jp + 1)], bld.pb[0][0:jw, 0:512],
            bld.b1c[0:jw, jp:jp + 1], bld.Pbuf[:, 512:1024][0:jw, :], ALU.add, ALU.mult))
        guard_set(bld, "pb0", (bld.s_ve, mv))
        guard_set(bld, "pb1", (bld.s_ac, g1))
        guard_set(bld, "Pb2", (bld.s_ve, mv))
    m_v = bld.s_ve.v

    pbanks = [bld.pb[i][:] for i in range(7)] + [bld.pbt[:].bitcast(F32)]
    for b in range(7):
        guard_wait(bld, "tensor", f"pb{b}")
    guard_wait(bld, "tensor", "pbt")
    W8(bld, "tensor", bld.s_ve, m_v)
    for jp in range(NJ):
        jw = 128 if jp < 21 else DIM_INNER - 128 * 21
        buf = bld.w2b[jp % 2]
        W8(bld, "tensor", bld.d_w2p[jp % 2], w2tot[jp])
        for dm in range(8):
            inc = bld.s_pe if dm == 7 else None
            E(bld, "tensor", lambda e, jp=jp, jw=jw, dm=dm, buf=buf, pbanks=pbanks: e.matmul(
                pbanks[dm][:, 0:512], buf[0:jw, 128 * dm:128 * (dm + 1)],
                bld.m_sb[0:jw, 512 * jp:512 * (jp + 1)],
                start=(jp == 0), stop=(jp == NJ - 1)), inc=inc)
        guard_set(bld, f"w2b{jp % 2}", (bld.s_pe, bld.s_pe.v))
        if jp + 2 < NJ:
            iss_w2(jp + 2)
    w2_pe = bld.s_pe.v
    guard_set(bld, "w2b0", (bld.s_pe, w2_pe))
    guard_set(bld, "w2b1", (bld.s_pe, w2_pe))
    guard_set(bld, "m0", (bld.s_pe, w2_pe))
    guard_set(bld, "m1", (bld.s_pe, w2_pe))
    W8(bld, "vector", bld.s_pe, w2_pe)
    for dm in range(8):
        E(bld, "vector", lambda e, dm=dm, t0=t0, pbanks=pbanks: e.scalar_tensor_tensor(
            bld.h[:, TLOC * dm + t0:TLOC * dm + t0 + 512], pbanks[dm][:, 0:512],
            bld.lcols[:, LQ_B2 + dm:LQ_B2 + dm + 1],
            bld.h[:, TLOC * dm + t0:TLOC * dm + t0 + 512], ALU.add, ALU.add))
    res_v = bld.s_ve.v
    for b in range(7):
        guard_set(bld, f"pb{b}", (bld.s_ve, res_v))
    guard_set(bld, "pbt", (bld.s_ve, res_v))
    guard_set(bld, "pb4", (bld.s_ve, res_v))
    guard_set(bld, "pb5", (bld.s_ve, res_v))
    return res_v


def emit_layer(bld: Build, l, skip_ffn=False):
    sy = emit_layer_loads(bld, l)
    if l > 0:
        emit_halo_ag(bld, l)
    for eng in ("tensor", "vector", "scalar"):
        W8(bld, eng, bld.s_sy, sy)
    hsv = bld.h[:].rearrange("p (c t) -> p c t", t=TLOC)
    E(bld, "vector", lambda e, hsv=hsv: e.tensor_copy(
        bld.hsave[:].rearrange("p (c i) -> p c i", i=9), hsv[:, :, 511::512]), inc=bld.s_ve)
    for ci in range(9):
        qk_done, v_done = emit_attn_chunk(bld, l, ci, None)
        for qt in range(4):
            Q = 4 * ci + qt
            if Q < 3:
                continue
            ev = emit_attn_qtile(bld, l, ci, qt, qk_done, v_done)
            emit_wo_qtile(bld, l, ci, qt, ev)
        guard_set(bld, "ring", (bld.s_pe, bld.s_pe.v))
    guard_set(bld, "wqkv", (bld.s_pe, bld.s_pe.v))
    guard_set(bld, "wo", (bld.s_pe, bld.s_pe.v))
    if not skip_ffn:
        for ti in reversed(range(8)):
            emit_ffn_ttile(bld, l, ti)
    guard_set(bld, "lcols", (bld.s_ve, bld.s_ve.v))


def emit_final(bld: Build):
    d = bld.dram
    s = lambda n, sh, dt: bld.sbuf(n, sh, dt, scope=bld.es_fin)
    xld = [s(f"xld{i}", [128, D], F32) for i in range(2)]
    sld = [s(f"sld{i}", [128, D], BF16) for i in range(2)]
    sgb = s("sgb", [128, D], BF16)
    orow = s("orow", [128, D], F32)
    for ti in range(32):
        slot = ti % 2
        t0 = HALO + 128 * ti
        guard_wait(bld, "sync", f"fxld{slot}")
        E(bld, "sync", lambda e, ti=ti, t0=t0, slot=slot: e.dma_start(
            xld[slot][:], d["x_in"][t0:t0 + 128, :]), inc=bld.d_sy, n=16)
        E(bld, "sync", lambda e, ti=ti, slot=slot: e.dma_start(
            sld[slot][:], d["sc_s"][128 * ti:128 * (ti + 1), :]), inc=bld.d_sy, n=16)
        lf = fence(bld, "sync")
        guard_wait(bld, "scalar", "sgb")
        for kc in range(8):
            inc = bld.s_ac if kc == 7 else None
            E(bld, "scalar", lambda e, kc=kc, t0=t0: e.activation(
                sgb[:, 128 * kc:128 * (kc + 1)], bld.h[:, TLOC * kc + t0:TLOC * kc + t0 + 128],
                AF.Silu), inc=inc)
        sg_a = bld.s_ac.v
        guard_wait(bld, "tensor", "pbt")
        W8(bld, "tensor", bld.s_ac, sg_a)
        for kc in range(8):
            inc = bld.s_pe if kc == 7 else None
            E(bld, "tensor", lambda e, kc=kc: e.transpose(
                bld.pbt[:, 128 * kc:128 * (kc + 1)], sgb[:, 128 * kc:128 * (kc + 1)],
                bld.ident_bf[:]), inc=inc)
        tr_pe = bld.s_pe.v
        guard_set(bld, "sgb", (bld.s_pe, tr_pe))
        W8(bld, "vector", bld.s_pe, tr_pe)
        W8(bld, "vector", bld.s_sy, lf)
        guard_wait(bld, "vector", "orow")
        E(bld, "vector", lambda e, slot=slot: e.tensor_mul(orow[:], sld[slot][:], bld.pbt[:]))
        ov = E(bld, "vector", lambda e, slot=slot: e.tensor_add(orow[:], orow[:], xld[slot][:]))
        guard_set(bld, "pbt", (bld.s_ve, ov))
        guard_set(bld, f"fxld{slot}", (bld.s_ve, ov))
        W8(bld, "sync", bld.s_ve, ov)
        E(bld, "sync", lambda e, ti=ti: e.dma_start(
            d["out"][128 * ti:128 * (ti + 1), :], orow[:]), inc=bld.d_sy, n=16)
        of = fence(bld, "sync")
        guard_set(bld, "orow", (bld.s_sy, of))


def build_full(n_layers=NLAYER, dbg=False, skip_ffn=False):
    bld = Build()
    build_common(bld)
    if dbg:
        bld.dout("dbg_h", [128, 8 * TLOC], F16)
        bld.dout("dbg_s", [4096, D], BF16)
        bld.dout("dbg_p", [128, 16 * 384], F16)
        bld.dout("dbg_ofm", [128, 8 * 128], F16)
    emit_setup(bld)
    emit_dn_phase(bld)
    phase_barrier(bld)
    bld.es_dn.close()
    tf_alloc(bld)
    emit_tf_setup(bld)
    for l in range(n_layers):
        emit_layer(bld, l, skip_ffn=skip_ffn)
    phase_barrier(bld)
    if dbg:
        for j in range(8):
            E(bld, "sync", lambda e, j=j: e.dma_start(
                bld.dram["dbg_h"][:, TLOC * j:TLOC * (j + 1)],
                bld.h[:, TLOC * j:TLOC * (j + 1)]), inc=bld.d_sy, n=16)
        for i in range(4):
            E(bld, "sync", lambda e, i=i: e.dma_start(
                bld.dram["dbg_s"][1024 * i:1024 * (i + 1), :],
                bld.dram["sc_s"][1024 * i:1024 * (i + 1), :]), inc=bld.d_sy, n=16)
        E(bld, "sync", lambda e: e.dma_start(bld.dram["dbg_p"], bld.dram["sc_p"][10]),
          inc=bld.d_sy, n=16)
        fence(bld, "sync")
    bld.es_tf.close()
    bld.es_fin = contextlib.ExitStack()
    emit_final(bld)
    finish_program(bld)
    with bld.es, bld.es_fin, bld.nc.Block() as blk:
        bld.prog.play(blk)
    return bld


_BUILD_CACHE = {}


def kernel(**inputs):
    if "bld" not in _BUILD_CACHE:
        _BUILD_CACHE["bld"] = build_full()
    bld = _BUILD_CACHE["bld"]
    in_maps = [core_input_map(inputs, c) for c in range(8)]
    res = run_bass_kernel_spmd(bld.nc, in_maps, core_ids=list(range(8)))
    out = np.empty((B, L, D), np.float32)
    for c in range(8):
        b, half = c // 2, c % 2
        out[b, 4096 * half:4096 * (half + 1)] = res.results[c]["out"]
    return out
